# version 03
# baseline (speedup 1.0000x reference)
import numpy as np
from contextlib import ExitStack

import concourse.bass as bass
import concourse.mybir as mybir
from concourse.bass_utils import run_bass_kernel_spmd
from bass_rust import ReduceOp

F32 = mybir.dt.float32
BF16 = mybir.dt.bfloat16
U32 = mybir.dt.uint32
U16 = mybir.dt.uint16
ALU = mybir.AluOpType
ACT = mybir.ActivationFunctionType
AX = mybir.AxisListType.X

NCORES = 8
B, S, DM = 4, 4096, 1024
DI = 2048
NST = 16
DTR = 64
TOK = 2048
DH = DI // 2


class Prog:

    ENGS = ("t", "v", "s", "g", "y")
    NPOOL = {"y": 12, "g": 6, "s": 4, "v": 2}

    def __init__(self, nc, stack):
        self.nc = nc
        self.q = {e: [] for e in self.ENGS}
        self.cnt = {e: 0 for e in self.ENGS}
        self.sem = {e: stack.enter_context(nc.semaphore(f"sem_{e}")) for e in self.ENGS}
        self.dsem = {}
        self.dcnt = {}
        self.drr = {e: 0 for e in self.ENGS}
        for e, np_ in self.NPOOL.items():
            for i in range(np_):
                self.dsem[(e, i)] = stack.enter_context(nc.semaphore(f"dma_{e}{i}"))
                self.dcnt[(e, i)] = 0
        self.state = {}
        self.wm = {e: {} for e in self.ENGS}
        self.n_emitted = 0

    def _sem_of(self, key):
        return self.sem[key] if isinstance(key, str) else self.dsem[key]

    def _need(self, reads, writes):
        need = {}
        for k in reads:
            st = self.state.get(k)
            if st and st["w"] is not None:
                we, wc = st["w"]
                need[we] = max(need.get(we, 0), wc)
        for k in writes:
            st = self.state.get(k)
            if st:
                if st["w"] is not None:
                    we, wc = st["w"]
                    need[we] = max(need.get(we, 0), wc)
                for re_, rc in st["r"].items():
                    need[re_] = max(need.get(re_, 0), rc)
        return need

    def _filter_waits(self, e, need):
        out = []
        for wk, wc in need.items():
            if self.wm[e].get(wk, 0) < wc:
                self.wm[e][wk] = wc
                out.append((self._sem_of(wk), wc))
        return out

    def _bookkeep(self, reads, writes, semkey, target):
        for k in writes:
            self.state[k] = {"w": (semkey, target), "r": {}}
        for k in reads:
            st = self.state.setdefault(k, {"w": None, "r": {}})
            st["r"][semkey] = target
        self.n_emitted += 1

    def emit(self, e, fn, reads=(), writes=()):
        waits = self._filter_waits(e, self._need(reads, writes))
        self.cnt[e] += 1
        target = self.cnt[e]
        sem = self.sem[e]

        def thunk(eng, fn=fn, waits=waits, sem=sem):
            for ws, wc in waits:
                eng.wait_ge(ws, wc)
            fn(eng).then_inc(sem, 1)

        self.q[e].append(thunk)
        self._bookkeep(reads, writes, e, target)
        return (e, target)

    def dma(self, e, fn, reads=(), writes=()):
        need = self._need(reads, writes)
        i = self.drr[e] % self.NPOOL[e]
        self.drr[e] += 1
        key = (e, i)
        if self.dcnt[key] > 0:
            need[key] = max(need.get(key, 0), self.dcnt[key])
        waits = self._filter_waits(e, need)
        self.dcnt[key] += 16
        target = self.dcnt[key]
        sem = self.dsem[key]

        def thunk(eng, fn=fn, waits=waits, sem=sem):
            for ws, wc in waits:
                eng.wait_ge(ws, wc)
            fn(eng).then_inc(sem, 16)

        self.q[e].append(thunk)
        self._bookkeep(reads, writes, key, target)
        return (key, target)

    def raw(self, e, fn):
        self.q[e].append(lambda eng, fn=fn: fn(eng))

    def wait_keys(self, e, keys):
        need = self._need(keys, keys)
        specs = self._filter_waits(e, need)
        if specs:
            def thunk(eng, specs=specs):
                for ws, wc in specs:
                    eng.wait_ge(ws, wc)
            self.q[e].append(thunk)

    def barrier(self):
        snap = [(("",), 0)]
        specs_all = {}
        for e2 in self.ENGS:
            if self.cnt[e2] > 0:
                specs_all[e2] = self.cnt[e2]
        for k, c in self.dcnt.items():
            if c > 0:
                specs_all[k] = c
        for e in self.ENGS:
            specs = []
            for wk, wc in specs_all.items():
                if wk == e:
                    continue
                if self.wm[e].get(wk, 0) < wc:
                    self.wm[e][wk] = wc
                    specs.append((self._sem_of(wk), wc))
            if specs:
                def thunk(eng, specs=specs):
                    for ws, wc in specs:
                        eng.wait_ge(ws, wc)
                self.q[e].append(thunk)

    def finish_outputs(self, e="y"):
        specs = [(self.dsem[k], c) for k, c in self.dcnt.items() if k[0] == e and c > 0]
        def thunk(eng, specs=specs):
            for ws, wc in specs:
                eng.wait_ge(ws, wc)
        self.q[e].append(thunk)

    def build(self):
        nc = self.nc
        names = {"t": "tensor", "v": "vector", "s": "scalar", "g": "gpsimd", "y": "sync"}
        with nc.Block() as blk:
            for e in self.ENGS:
                dec = getattr(blk, names[e])
                def body(eng, _q=tuple(self.q[e])):
                    for th in _q:
                        th(eng)
                dec(body)
        return nc


def mm_accum(p, psum_ap, pairs, key_out, key_reads, tile_position=None):
    def fn(eng):
        ins = None
        n = len(pairs)
        for i, (lt, rh) in enumerate(pairs):
            ins = eng.matmul(psum_ap, lt, rh, start=(i == 0), stop=(i == n - 1))
        return ins
    return p.emit("t", fn, reads=tuple(key_reads), writes=(key_out,))


def transpose_128(p, psum_ap, in_ap, ident_ap, key_out, key_in):
    return p.emit("t", lambda eng: eng.transpose(psum_ap, in_ap, ident_ap),
                  reads=(key_in, "ident"), writes=(key_out,))


def make_identity(p, s_iota, s_ident):
    p.emit("g", lambda eng: eng.iota(s_iota[:], pattern=[[1, 128]], base=0,
                                     channel_multiplier=-1,
                                     allow_small_or_imprecise_dtypes=True),
           writes=("iota_raw",))
    p.emit("v", lambda eng: eng.tensor_scalar(s_ident[:], s_iota[:], 0.0, None, op0=ALU.is_equal),
           reads=("iota_raw",), writes=("ident",))


def build_l1():
    nc = bass.Bass("TRN2", target_bir_lowering=False)
    stack = ExitStack()
    p = Prog(nc, stack)

    x_fm = nc.dram_tensor("x_fm", [DM, S], F32, kind="ExternalInput").ap()
    winx = nc.dram_tensor("winx", [DM, DH], F32, kind="ExternalInput").ap()
    winz = nc.dram_tensor("winz", [DM, DH], F32, kind="ExternalInput").ap()
    binx = nc.dram_tensor("binx", [128, DH // 128], F32, kind="ExternalInput").ap()
    binz = nc.dram_tensor("binz", [128, DH // 128], F32, kind="ExternalInput").ap()
    xs_out = nc.dram_tensor("xs_out", [DH, S], F32, kind="ExternalOutput").ap()
    z_out = nc.dram_tensor("z_out", [DH, S], F32, kind="ExternalOutput").ap()

    TT = 512
    NTB = S // TT
    s_w = stack.enter_context(nc.sbuf_tensor("s_w", [128, 2, 8, 8, 128], F32))
    s_x = stack.enter_context(nc.sbuf_tensor("s_x", [128, 2, 8, TT], F32))
    s_b = stack.enter_context(nc.sbuf_tensor("s_b", [128, 2, 8], F32))
    s_o = stack.enter_context(nc.sbuf_tensor("s_o", [128, 2, 2, 8, TT], F32))
    ps = [stack.enter_context(nc.psum_tensor(f"ps{i}", [128, TT], F32)) for i in range(4)]

    with stack:
        p.dma("y", lambda eng: eng.dma_start(out=s_w[:, 0], in_=winx.rearrange("(kc p) (nc8 f) -> p kc nc8 f", p=128, f=128)),
              writes=("wx",))
        p.dma("y", lambda eng: eng.dma_start(out=s_w[:, 1], in_=winz.rearrange("(kc p) (nc8 f) -> p kc nc8 f", p=128, f=128)),
              writes=("wz",))
        p.dma("y", lambda eng: eng.dma_start(out=s_b[:, 0], in_=binx),
              writes=("bx",))
        p.dma("y", lambda eng: eng.dma_start(out=s_b[:, 1], in_=binz),
              writes=("bz",))

        x_r = x_fm.rearrange("(kc p) t -> p kc t", p=128)
        nb = 0
        for tb in range(NTB):
            pb = tb % 2
            p.dma("y", lambda eng, tb=tb, pb=pb: eng.dma_start(
                      out=s_x[:, pb], in_=x_r[:, :, tb * TT:(tb + 1) * TT]),
                  writes=(f"x{pb}",))
            for xz in range(2):
                for nc8 in range(8):
                    bi = nb % 4
                    bank = ps[bi]
                    nb += 1
                    pairs = [(s_w[:, xz, kc, nc8, :], s_x[:, pb, kc, :]) for kc in range(8)]
                    kmm = f"ps{bi}"
                    mm_accum(p, bank[:], pairs, kmm, (f"x{pb}", "wx" if xz == 0 else "wz"))
                    func = ACT.Silu if xz == 0 else ACT.Identity
                    def ev(eng, bank=bank, xz=xz, nc8=nc8, func=func, pb=pb):
                        return eng.activation(s_o[:, pb, xz, nc8, :], bank[:], func,
                                              bias=s_b[:, xz, nc8:nc8 + 1], scale=1.0)
                    p.emit("s", ev, reads=(kmm, "bx" if xz == 0 else "bz"),
                           writes=(f"o{pb}_{xz}_{nc8}",))
            for xz, out_t in ((0, xs_out), (1, z_out)):
                p.dma("y", lambda eng, tb=tb, pb=pb, xz=xz, out_t=out_t: eng.dma_start(
                          out=out_t.rearrange("(nc8 p) t -> p nc8 t", p=128)[:, :, tb * TT:(tb + 1) * TT],
                          in_=s_o[:, pb, xz]),
                      reads=tuple(f"o{pb}_{xz}_{nc8}" for nc8 in range(8)))
        p.finish_outputs("y")
        return p.build()


def host_l1_inputs(ins):
    x = ins["L0_attn_x"]
    Win = ins["L0_attn_Win"]
    bin_ = ins["L0_attn_bin"]
    maps = []
    for c in range(NCORES):
        b, h = c // 2, c % 2
        maps.append({
            "x_fm": np.ascontiguousarray(x[b].T),
            "winx": np.ascontiguousarray(Win[:, h * DH:(h + 1) * DH]),
            "winz": np.ascontiguousarray(Win[:, DI + h * DH:DI + (h + 1) * DH]),
            "binx": np.ascontiguousarray(bin_[h * DH:(h + 1) * DH].reshape(-1, 128).T),
            "binz": np.ascontiguousarray(bin_[DI + h * DH:DI + (h + 1) * DH].reshape(-1, 128).T),
        })
    return maps


if __name__ == "__main__":
    pass


def build_l2():
    nc = bass.Bass("TRN2", target_bir_lowering=False)
    stack = ExitStack()
    p = Prog(nc, stack)

    TC = 256
    NCH = S // TC
    NT = DH // 128

    xs_full = nc.dram_tensor("xs_full", [DI, S], F32, kind="ExternalInput").ap()
    z_sl = nc.dram_tensor("z_sl", [DH, S], F32, kind="ExternalInput").ap()
    wx = nc.dram_tensor("wx", [DI, 96], F32, kind="ExternalInput").ap()
    bx_fm = nc.dram_tensor("bx_fm", [96, 1], F32, kind="ExternalInput").ap()
    bx_tm = nc.dram_tensor("bx_tm", [128, 96], F32, kind="ExternalInput").ap()
    wdt = nc.dram_tensor("wdt", [64, DH], F32, kind="ExternalInput").ap()
    bdt = nc.dram_tensor("bdt", [128, 8], F32, kind="ExternalInput").ap()
    alog = nc.dram_tensor("alog", [128, 8, NST], F32, kind="ExternalInput").ap()
    dp_r = nc.dram_tensor("dp_r", [128, 8], F32, kind="ExternalInput").ap()
    y_out = nc.dram_tensor("y_out", [DH, S], F32, kind="ExternalOutput").ap()
    DBG = False
    if DBG:
        dbg_xpf = nc.dram_tensor("dbg_xpf", [96, 256], F32, kind="ExternalOutput").ap()
        dbg_xpt = nc.dram_tensor("dbg_xpt", [128, 2, 96], F32, kind="ExternalOutput").ap()
        dbg_dl = nc.dram_tensor("dbg_dl", [128, 8, 256], F32, kind="ExternalOutput").ap()
        dbg_B = nc.dram_tensor("dbg_B", [128, 256, 16], F32, kind="ExternalOutput").ap()
        dbg_C = nc.dram_tensor("dbg_C", [128, 256, 16], F32, kind="ExternalOutput").ap()
        dbg_h = nc.dram_tensor("dbg_h", [128, 256, 16], F32, kind="ExternalOutput").ap()
        dbg_dA = nc.dram_tensor("dbg_dA", [128, 256, 16], F32, kind="ExternalOutput").ap()
        dbg_u = nc.dram_tensor("dbg_u", [128, 256, 16], F32, kind="ExternalOutput").ap()

    s_xs = stack.enter_context(nc.sbuf_tensor("s_xs", [128, 16, TC], F32))
    s_z = stack.enter_context(nc.sbuf_tensor("s_z", [128, 8, TC], F32))
    s_wx = stack.enter_context(nc.sbuf_tensor("s_wx", [128, 16, 96], F32))
    s_bxf = stack.enter_context(nc.sbuf_tensor("s_bxf", [96, 1], F32))
    s_bxt = stack.enter_context(nc.sbuf_tensor("s_bxt", [128, 96], F32))
    s_wdt = stack.enter_context(nc.sbuf_tensor("s_wdt", [64, 8, 128], F32))
    s_bdt = stack.enter_context(nc.sbuf_tensor("s_bdt", [128, 8], F32))
    s_A = stack.enter_context(nc.sbuf_tensor("s_A", [128, 8, NST], F32))
    s_dp = stack.enter_context(nc.sbuf_tensor("s_dp", [128, 8], F32))
    s_xpf = stack.enter_context(nc.sbuf_tensor("s_xpf", [96, TC], F32))
    s_xpt = stack.enter_context(nc.sbuf_tensor("s_xpt", [128, TC // 128, 96], F32))
    s_dl = stack.enter_context(nc.sbuf_tensor("s_dl", [128, 8, TC], F32))
    s_sc1 = stack.enter_context(nc.sbuf_tensor("s_sc1", [128, 8, TC], F32))
    s_dx = stack.enter_context(nc.sbuf_tensor("s_dx", [128, 8, TC], F32))
    s_B = stack.enter_context(nc.sbuf_tensor("s_B", [128, TC, NST], F32))
    s_C = stack.enter_context(nc.sbuf_tensor("s_C", [128, TC, NST], F32))
    s_dA = stack.enter_context(nc.sbuf_tensor("s_dA", [128, TC, NST], F32))
    s_u = stack.enter_context(nc.sbuf_tensor("s_u", [128, TC, NST], F32))
    s_h = stack.enter_context(nc.sbuf_tensor("s_h", [128, TC, NST], F32))
    s_st = stack.enter_context(nc.sbuf_tensor("s_st", [128, 8, NST], F32))
    s_y = stack.enter_context(nc.sbuf_tensor("s_y", [128, 8, TC], F32))
    s_sz = stack.enter_context(nc.sbuf_tensor("s_sz", [128, 8, TC], F32))
    ps = [stack.enter_context(nc.psum_tensor(f"ps{i}", [128, TC], F32)) for i in range(4)]

    with stack:
        p.dma("y", lambda eng: eng.dma_start(out=s_wx[:], in_=wx.rearrange("(kc p) m -> p kc m", p=128)), writes=("wx",))
        p.dma("y", lambda eng: eng.dma_start(out=s_bxf[:], in_=bx_fm), writes=("bxf",))
        p.dma("y", lambda eng: eng.dma_start(out=s_bxt[:], in_=bx_tm), writes=("bxt",))
        p.dma("y", lambda eng: eng.dma_start(out=s_wdt[:], in_=wdt.rearrange("k (nc8 m) -> k nc8 m", m=128)), writes=("wdt",))
        p.dma("y", lambda eng: eng.dma_start(out=s_bdt[:], in_=bdt), writes=("bdt",))
        p.dma("y", lambda eng: eng.dma_start(out=s_A[:], in_=alog), writes=("alog",))
        p.dma("y", lambda eng: eng.dma_start(out=s_dp[:], in_=dp_r), writes=("dpr",))
        p.emit("s", lambda eng: eng.activation(s_A[:], s_A[:], ACT.Exp),
               reads=("alog",), writes=("alog",))
        p.emit("v", lambda eng: eng.tensor_scalar(s_A[:], s_A[:], -1.0, None, op0=ALU.mult),
               reads=("alog",), writes=("A",))
        p.emit("v", lambda eng: eng.memset(s_st[:], 0.0), writes=("st",))

        xs_r = xs_full.rearrange("(kc p) t -> p kc t", p=128)
        z_r = z_sl.rearrange("(kc p) t -> p kc t", p=128)
        y_r = y_out.rearrange("(kc p) t -> p kc t", p=128)

        for cix in range(NCH):
            t0 = cix * TC
            p.dma("y", lambda eng, t0=t0: eng.dma_start(out=s_xs[:], in_=xs_r[:, :, t0:t0 + TC]), writes=("xs",))
            p.dma("y", lambda eng, t0=t0: eng.dma_start(out=s_z[:], in_=z_r[:, :, t0:t0 + TC]), writes=("z",))

            pairs = [(s_wx[:, kc, :], s_xs[:, kc, :]) for kc in range(16)]
            mm_accum(p, ps[0][0:96, :], pairs, "ps0", ("wx", "xs"))
            p.emit("s", lambda eng: eng.activation(s_xpf[:], ps[0][0:96, :], ACT.Identity, bias=s_bxf[:], scale=1.0),
                   reads=("ps0", "bxf"), writes=("xpf",))
            for tb in range(TC // 128):
                pairs = [(s_xs[:, kc, tb * 128:(tb + 1) * 128], s_wx[:, kc, :]) for kc in range(16)]
                mm_accum(p, ps[1][:, 0:96], pairs, "ps1", ("wx", "xs"))
                p.emit("v", lambda eng, tb=tb: eng.tensor_tensor(s_xpt[:, tb, :], ps[1][:, 0:96], s_bxt[:], op=ALU.add),
                       reads=("ps1", "bxt"), writes=(f"xpt{tb}",))

            for nc8 in range(8):
                bi = 2 + (nc8 % 2)
                mm_accum(p, ps[bi][:], [(s_wdt[:, nc8, :], s_xpf[0:64, :])], f"ps{bi}", ("wdt", "xpf"))
                p.emit("s", lambda eng, nc8=nc8, bi=bi: eng.activation(
                           s_dl[:, nc8, :], ps[bi][:], ACT.Identity, bias=s_bdt[:, nc8:nc8 + 1], scale=1.0),
                       reads=(f"ps{bi}", "bdt"), writes=("dl",))
            p.emit("s", lambda eng: eng.activation(s_sc1[:], s_dl[:], ACT.Abs),
                   reads=("dl",), writes=("sc1",))
            p.emit("s", lambda eng: eng.activation(s_sc1[:], s_sc1[:], ACT.Exp, scale=-1.0),
                   reads=("sc1",), writes=("sc1",))
            p.emit("s", lambda eng: eng.activation(s_sc1[:], s_sc1[:], ACT.Ln, bias=1.0, scale=1.0),
                   reads=("sc1",), writes=("sc1",))
            p.emit("v", lambda eng: eng.scalar_tensor_tensor(
                       out=s_dl[:], in0=s_dl[:], scalar=0.0, in1=s_sc1[:], op0=ALU.max, op1=ALU.add),
                   reads=("sc1", "dl"), writes=("dl",))
            p.emit("v", lambda eng: eng.tensor_tensor(s_dx[:], s_dl[:], s_xs[:, 0:8, :], op=ALU.mult),
                   reads=("dl", "xs"), writes=("dx",))

            for nm, sl, keyb in (("B", slice(64, 80), "B"), ("C", slice(80, 96), "C")):
                dst = s_B if nm == "B" else s_C
                for tb in range(TC // 128):
                    def st0(eng, dst=dst, sl=sl, tb=tb):
                        return eng.dma_start(
                            out=dst[0:1, tb * 128:(tb + 1) * 128, :],
                            in_=s_xpt[:, tb, sl])
                    p.dma("g", st0, reads=(f"xpt{tb}",), writes=(keyb,))
                k = 1
                while k < 128:
                    p.dma("g", lambda eng, dst=dst, k=k: eng.dma_start(out=dst[k:2 * k], in_=dst[0:k]),
                          reads=(keyb,), writes=(keyb,))
                    k *= 2

            for dc in range(NT):
                for n in range(NST):
                    p.emit("s", lambda eng, dc=dc, n=n: eng.activation(
                               s_dA[:, :, n], s_dl[:, dc, :], ACT.Exp, scale=s_A[:, dc, n:n + 1]),
                           reads=("dl", "A"), writes=("dA",))
                p.emit("v", lambda eng, dc=dc: eng.tensor_tensor(
                           s_u[:], s_dx[:, dc, :].unsqueeze(2).broadcast_to([128, TC, NST]), s_B[:], op=ALU.mult),
                       reads=("dx", "B"), writes=("u",))
                for n in range(NST):
                    p.emit("v", lambda eng, dc=dc, n=n: eng.tensor_tensor_scan(
                               s_h[:, :, n], s_dA[:, :, n], s_u[:, :, n], s_st[:, dc, n:n + 1],
                               op0=ALU.mult, op1=ALU.add),
                           reads=("dA", "u", "st"), writes=("h",))
                if DBG and cix == 0 and dc == 7:
                    p.dma("y", lambda eng: eng.dma_start(out=dbg_dA, in_=s_dA[:]), reads=("dA",))
                    p.dma("y", lambda eng: eng.dma_start(out=dbg_u, in_=s_u[:]), reads=("u",))
                p.emit("v", lambda eng, dc=dc: eng.tensor_copy(s_st[:, dc, :], s_h[:, TC - 1, :]),
                       reads=("h",), writes=("st",))
                p.emit("v", lambda eng: eng.tensor_tensor(s_dA[:], s_h[:], s_C[:], op=ALU.mult),
                       reads=("C", "h"), writes=("dA",))
                p.emit("v", lambda eng, dc=dc: eng.tensor_reduce(
                           s_y[:, dc, :], s_dA[:], axis=AX, op=ALU.add),
                       reads=("dA",), writes=(f"y{dc}",))

            if DBG and cix == 0:
                p.dma("y", lambda eng: eng.dma_start(out=dbg_xpf, in_=s_xpf[:]), reads=("xpf",))
                p.dma("y", lambda eng: eng.dma_start(out=dbg_xpt, in_=s_xpt[:]), reads=("xpt0", "xpt1"))
                p.dma("y", lambda eng: eng.dma_start(out=dbg_dl, in_=s_dl[:]), reads=("dl",))
                p.dma("y", lambda eng: eng.dma_start(out=dbg_B, in_=s_B[:]), reads=("B",))
                p.dma("y", lambda eng: eng.dma_start(out=dbg_C, in_=s_C[:]), reads=("C",))
                p.dma("y", lambda eng: eng.dma_start(out=dbg_h, in_=s_h[:]), reads=("h",))
            p.emit("s", lambda eng: eng.activation(s_sz[:], s_z[:], ACT.Silu),
                   reads=("z",), writes=("sz",))
            for dc in range(NT):
                p.emit("v", lambda eng, dc=dc: eng.scalar_tensor_tensor(
                           out=s_y[:, dc, :], in0=s_xs[:, dc, :], scalar=s_dp[:, dc:dc + 1],
                           in1=s_y[:, dc, :], op0=ALU.mult, op1=ALU.add),
                       reads=("xs", "dpr", f"y{dc}"), writes=(f"y{dc}",))
            p.emit("v", lambda eng: eng.tensor_tensor(s_y[:], s_y[:], s_sz[:], op=ALU.mult),
                   reads=("sz",) + tuple(f"y{dc}" for dc in range(NT)),
                   writes=tuple(f"y{dc}" for dc in range(NT)))
            p.dma("y", lambda eng, t0=t0: eng.dma_start(out=y_r[:, :, t0:t0 + TC], in_=s_y[:]),
                  reads=tuple(f"y{dc}" for dc in range(NT)))
        p.finish_outputs("y")
        return p.build()


def host_l2_inputs(ins, xs_by_core, z_by_core):
    Wx = ins["L0_attn_Wx"]; bx = ins["L0_attn_bx"]
    Wdt = ins["L0_attn_Wdt"]; bdt = ins["L0_attn_bdt"]
    Alog = ins["L0_attn_A_log"]; Dp = ins["L0_attn_Dp"]
    maps = []
    for c in range(NCORES):
        b, h = c // 2, c % 2
        own = xs_by_core[c]
        other = xs_by_core[2 * b + (1 - h)]
        xs_full = np.concatenate([own, other], axis=0)
        wx_own = Wx[h * DH:(h + 1) * DH]
        wx_oth = Wx[(1 - h) * DH:(2 - h) * DH]
        maps.append({
            "xs_full": np.ascontiguousarray(xs_full),
            "z_sl": np.ascontiguousarray(z_by_core[c]),
            "wx": np.ascontiguousarray(np.concatenate([wx_own, wx_oth], axis=0)),
            "bx_fm": np.ascontiguousarray(bx[:, None]),
            "bx_tm": np.ascontiguousarray(np.broadcast_to(bx, (128, 96))),
            "wdt": np.ascontiguousarray(Wdt[:, h * DH:(h + 1) * DH]),
            "bdt": np.ascontiguousarray(bdt[h * DH:(h + 1) * DH].reshape(8, 128).T),
            "alog": np.ascontiguousarray(Alog[h * DH:(h + 1) * DH].reshape(8, 128, NST).transpose(1, 0, 2)),
            "dp_r": np.ascontiguousarray(Dp[h * DH:(h + 1) * DH].reshape(8, 128).T),
        })
    return maps


def emit_ln_fm(p, tag, s_in, k_in, s_out, k_out, s_sq, k_sq, s_ones128,
               ps_sum, k_psum, ps_sq, k_psq, s_mv, k_mv, TT, nchunk=8, dmodel=1024):
    inv = 1.0 / dmodel
    p.emit("s", lambda eng: eng.activation(s_sq[:], s_in[:], ACT.Square),
           reads=(k_in,), writes=(k_sq,))
    def mmsum(eng):
        ins = None
        for kc in range(nchunk):
            ins = eng.matmul(ps_sum[:, 0:TT], s_ones128[:], s_in[:, kc, :], start=(kc == 0), stop=(kc == nchunk - 1))
        return ins
    p.emit("t", mmsum, reads=(k_in, "ones"), writes=(k_psum,))
    def mmsq(eng):
        ins = None
        for kc in range(nchunk):
            ins = eng.matmul(ps_sq[:, 0:TT], s_ones128[:], s_sq[:, kc, :], start=(kc == 0), stop=(kc == nchunk - 1))
        return ins
    p.emit("t", mmsq, reads=(k_sq, "ones"), writes=(k_psq,))
    p.emit("v", lambda eng: eng.tensor_scalar(s_mv[:, 0, :], ps_sum[:, 0:TT], inv, None, op0=ALU.mult),
           reads=(k_psum,), writes=(k_mv,))
    p.emit("v", lambda eng: eng.tensor_tensor(s_mv[:, 1, :], s_mv[:, 0, :], s_mv[:, 0, :], op=ALU.mult),
           reads=(k_mv,), writes=(k_mv,))
    p.emit("v", lambda eng: eng.scalar_tensor_tensor(out=s_mv[:, 1, :], in0=ps_sq[:, 0:TT], scalar=inv,
                                                     in1=s_mv[:, 1, :], op0=ALU.mult, op1=ALU.subtract),
           reads=(k_psq, k_mv), writes=(k_mv,))
    p.emit("v", lambda eng: eng.tensor_scalar(s_mv[:, 1, :], s_mv[:, 1, :], 1e-5, None, op0=ALU.add),
           reads=(k_mv,), writes=(k_mv,))
    p.emit("s", lambda eng: eng.activation(s_mv[:, 1, :], s_mv[:, 1, :], ACT.Sqrt),
           reads=(k_mv,), writes=(k_mv,))
    p.emit("v", lambda eng: eng.reciprocal(s_mv[:, 1, :], s_mv[:, 1, :]),
           reads=(k_mv,), writes=(k_mv,))
    p.emit("v", lambda eng: eng.tensor_tensor(
               s_out[:], s_in[:], s_mv[:, 0:1, :].broadcast_to([128, nchunk, TT]), op=ALU.subtract),
           reads=(k_in, k_mv), writes=(k_out,))
    p.emit("v", lambda eng: eng.tensor_tensor(
               s_out[:], s_out[:], s_mv[:, 1:2, :].broadcast_to([128, nchunk, TT]), op=ALU.mult),
           reads=(k_out, k_mv), writes=(k_out,))


def emit_quant_f32(p, s_v, k_v, s_q, k_q, lo, hi, scale_ap=None, k_scale=(), scale_const=None):
    if scale_ap is not None:
        p.emit("v", lambda eng: eng.tensor_tensor(s_q[:], s_v[:], scale_ap, op=ALU.mult),
               reads=(k_v,) + tuple(k_scale), writes=(k_q,))
    else:
        p.emit("v", lambda eng: eng.tensor_scalar(s_q[:], s_v[:], scale_const, None, op0=ALU.mult),
               reads=(k_v,), writes=(k_q,))
    p.emit("v", lambda eng: eng.tensor_scalar(s_v[:], s_q[:], 0.5, 1.0, op0=ALU.add, op1=ALU.mod),
           reads=(k_q,), writes=(k_v + "_mod",))
    p.emit("v", lambda eng: eng.scalar_tensor_tensor(out=s_q[:], in0=s_q[:], scalar=0.5, in1=s_v[:],
                                                     op0=ALU.add, op1=ALU.subtract),
           reads=(k_q, k_v + "_mod"), writes=(k_q,))
    p.emit("v", lambda eng: eng.tensor_scalar(s_q[:], s_q[:], float(lo), float(hi), op0=ALU.max, op1=ALU.min),
           reads=(k_q,), writes=(k_q,))




def emit_colmax_bcast(p, tag, m_ap, k_m, s_cm, s_row1, ps_tr, k_pstr, ps_bc, k_psbc,
                      s_ident, out_scale_ap, k_oscale, out_inv_ap, k_oinv, s_one1, extra_row_ops):
    p.emit("t", lambda eng: eng.transpose(ps_tr[:, 0:128], m_ap, s_ident[:]),
           reads=(k_m, "ident"), writes=(k_pstr,))
    p.emit("v", lambda eng: eng.tensor_reduce(s_cm[:, 0:1], ps_tr[:, 0:128], axis=AX, op=ALU.max),
           reads=(k_pstr,), writes=(tag + "cm",))
    p.emit("v", lambda eng: eng.tensor_scalar(s_cm[:, 0:1], s_cm[:, 0:1], 1e-5, None, op0=ALU.max),
           reads=(tag + "cm",), writes=(tag + "cm",))
    p.emit("t", lambda eng: eng.transpose(ps_tr[0:1, 128:256], s_cm[:, 0:1], s_ident[:]),
           reads=(tag + "cm", "ident"), writes=(k_pstr,))
    p.emit("v", lambda eng: eng.tensor_copy(s_row1[0:1, 0:128], ps_tr[0:1, 128:256]),
           reads=(k_pstr,), writes=(tag + "row",))
    if out_inv_ap is not None:
        p.emit("v", lambda eng: eng.tensor_scalar(s_row1[32:33, 0:128], s_row1[0:1, 0:128], 1.0 / 127.0, None, op0=ALU.mult),
               reads=(tag + "row",), writes=(tag + "rowi",))
    p.emit("v", lambda eng: eng.reciprocal(s_row1[0:1, 0:128], s_row1[0:1, 0:128]),
           reads=(tag + "row",), writes=(tag + "row",))
    p.emit("v", lambda eng: eng.tensor_scalar(s_row1[0:1, 0:128], s_row1[0:1, 0:128], 127.0, None, op0=ALU.mult),
           reads=(tag + "row",), writes=(tag + "row",))
    def bc(eng):
        return eng.matmul(ps_bc[:, 0:128], s_one1[0:1, :], s_row1[0:1, 0:128], start=True, stop=True)
    p.emit("t", bc, reads=(tag + "row", "one1"), writes=(k_psbc,))
    p.emit("v", lambda eng: eng.tensor_copy(out_scale_ap, ps_bc[:, 0:128]),
           reads=(k_psbc,), writes=(k_oscale,))
    if out_inv_ap is not None:
        def bc2(eng):
            return eng.matmul(ps_bc[:, 128:256], s_one1[32:33, :], s_row1[32:33, 0:128], start=True, stop=True)
        p.emit("t", bc2, reads=(tag + "rowi", "one1"), writes=(k_psbc,))
        p.emit("v", lambda eng: eng.tensor_copy(out_inv_ap, ps_bc[:, 128:256]),
               reads=(k_psbc,), writes=(k_oinv,))


P1_STAGE = 99
def build_l3(upto=99):
    nc = bass.Bass("TRN2", target_bir_lowering=False)
    stack = ExitStack()
    p = Prog(nc, stack)

    TT = 512
    NTT = TOK // TT

    y_own = nc.dram_tensor("y_own", [DI, TOK], F32, kind="ExternalInput").ap()
    wout = nc.dram_tensor("wout", [DI, DM], F32, kind="ExternalInput").ap()
    boutr = nc.dram_tensor("boutr", [128, 8], F32, kind="ExternalInput").ap()
    x_fm = nc.dram_tensor("x_fm", [DM, TOK], F32, kind="ExternalInput").ap()
    wgT = nc.dram_tensor("wgT", [DM, 4096], F32, kind="ExternalInput").ap()
    wuT = nc.dram_tensor("wuT", [DM, 4096], F32, kind="ExternalInput").ap()
    wdT = nc.dram_tensor("wdT", [4096, DM], F32, kind="ExternalInput").ap()
    s2_out = nc.dram_tensor("s2_out", [DM, TOK], F32, kind="ExternalOutput").ap()
    gu_hbm = nc.dram_tensor("gu_hbm", [4096, TOK], F32, kind="Internal").ap()
    s1_hbm = nc.dram_tensor("s1_hbm", [DM, TOK], F32, kind="Internal").ap()
    wdq_hbm = nc.dram_tensor("wdq_hbm", [4096, DM], BF16, kind="Internal").ap()

    s_ones = stack.enter_context(nc.sbuf_tensor("s_ones", [128, 128], F32))
    s_one1 = stack.enter_context(nc.sbuf_tensor("s_one1", [64, 128], F32))
    s_kq1 = stack.enter_context(nc.sbuf_tensor("s_kq1", [128, 8, TOK], BF16))
    s_inv1 = stack.enter_context(nc.sbuf_tensor("s_inv1", [128, TOK], F32))
    s_m2 = stack.enter_context(nc.sbuf_tensor("s_m2", [128, TOK], F32))
    s_f2s = stack.enter_context(nc.sbuf_tensor("s_f2s", [128, TOK], F32))
    s_inv2 = stack.enter_context(nc.sbuf_tensor("s_inv2", [128, TOK], F32))
    s_wsum = stack.enter_context(nc.sbuf_tensor("s_wsum", [128, 24], F32))
    s_wsc = stack.enter_context(nc.sbuf_tensor("s_wsc", [128, 8], F32))
    s_row = stack.enter_context(nc.sbuf_tensor("s_row", [96, 128], F32))
    s_cmg = stack.enter_context(nc.sbuf_tensor("s_cmg", [128, 2], F32))
    s_iota = stack.enter_context(nc.sbuf_tensor("s_iota", [128, 128], F32))
    s_ident = stack.enter_context(nc.sbuf_tensor("s_ident", [128, 128], F32))
    ps_pool = [stack.enter_context(nc.psum_tensor(f"pp{i}", [128, TT], F32)) for i in range(8)]

    p.emit("v", lambda eng: eng.memset(s_ones[:], 1.0), writes=("ones",))
    p.emit("v", lambda eng: eng.memset(s_one1[:], 1.0), writes=("one1",))
    make_identity(p, s_iota, s_ident)

    with ExitStack() as ph:
        s_wt = ph.enter_context(nc.sbuf_tensor("p0_wt", [128, 2, 4096], F32))
        wviews = [wgT.rearrange("(c p) n -> p c n", p=128),
                  wuT.rearrange("(c p) n -> p c n", p=128),
                  wdT.rearrange("(c p) n -> p c n", p=128)]
        for wi, wv in enumerate(wviews):
            nper = wv.shape[2]
            cper = wv.shape[1] // 8
            for j in range(8):
                pb = j % 2
                p.dma("y", lambda eng, wv=wv, j=j, pb=pb, nper=nper, cper=cper: eng.dma_start(
                          out=s_wt[:, pb, :].rearrange("p (c n) -> p c n", c=cper),
                          in_=wv[:, j * cper:(j + 1) * cper, :]),
                      writes=(f"p0w{pb}",))
                p.emit("v", lambda eng, wi=wi, j=j, pb=pb: eng.tensor_reduce(
                           s_wsum[:, wi * 8 + j:wi * 8 + j + 1], s_wt[:, pb, :], axis=AX,
                           op=ALU.add, apply_absolute_value=True),
                       reads=(f"p0w{pb}",), writes=(f"wsum{wi}_{j}",))
        for wi in range(3):
            p.emit("v", lambda eng, wi=wi: eng.tensor_reduce(
                       s_wsc[:, wi:wi + 1], s_wsum[:, wi * 8:(wi + 1) * 8], axis=AX, op=ALU.add),
                   reads=tuple(f"wsum{wi}_{j}" for j in range(8)), writes=(f"wtot{wi}",))
        p.emit("t", lambda eng: eng.matmul(ps_pool[6][:, 0:3], s_ones[:], s_wsc[:, 0:3], start=True, stop=True),
               reads=("wtot0", "wtot1", "wtot2", "ones"), writes=("pp6w",))
        p.emit("v", lambda eng: eng.tensor_copy(s_wsc[:, 0:3], ps_pool[6][:, 0:3]),
               reads=("pp6w",), writes=("wsc",))
        p.emit("v", lambda eng: eng.tensor_scalar(s_wsc[:, 0:3], s_wsc[:, 0:3], 1.0 / (4096.0 * 1024.0), 1e-5,
                                                  op0=ALU.mult, op1=ALU.max),
               reads=("wsc",), writes=("wsc",))
        p.emit("v", lambda eng: eng.reciprocal(s_wsc[:, 3:6], s_wsc[:, 0:3]),
               reads=("wsc",), writes=("wsc2",))

    p.barrier()
    if upto < 1:
        p.finish_outputs("y")
        with stack:
            return p.build()
    with ExitStack() as ph:
        s_wtb = ph.enter_context(nc.sbuf_tensor("p0b_wt", [128, 2, 2048], F32))
        s_wqb2 = ph.enter_context(nc.sbuf_tensor("p0b_wq", [128, 2, 2048], F32))
        s_wqbb = ph.enter_context(nc.sbuf_tensor("p0b_wqb", [128, 2, 2048], BF16))
        wv = wdT.rearrange("(c p) n -> p c n", p=128)
        qv = wdq_hbm.rearrange("(c p) n -> p c n", p=128)
        for j in range(16):
            pb = j % 2
            p.dma("y", lambda eng, j=j, pb=pb: eng.dma_start(
                      out=s_wtb[:, pb, :].rearrange("p (c n) -> p c n", c=2), in_=wv[:, 2 * j:2 * j + 2, :]),
                  writes=(f"q0w{pb}",))
            p.emit("v", lambda eng, pb=pb: eng.tensor_scalar(s_wqb2[:, pb, :], s_wtb[:, pb, :], s_wsc[:, 5:6], None, op0=ALU.mult),
                   reads=(f"q0w{pb}", "wsc2"), writes=(f"q0q{pb}",))
            p.emit("v", lambda eng, pb=pb: eng.tensor_scalar(s_wtb[:, pb, :], s_wqb2[:, pb, :], -0.5, None, op0=ALU.is_lt),
                   reads=(f"q0q{pb}",), writes=(f"q0w{pb}",))
            p.emit("v", lambda eng, pb=pb: eng.scalar_tensor_tensor(out=s_wqbb[:, pb, :], in0=s_wqb2[:, pb, :], scalar=0.5,
                                                                    in1=s_wtb[:, pb, :], op0=ALU.is_gt, op1=ALU.subtract),
                   reads=(f"q0q{pb}", f"q0w{pb}"), writes=(f"q0b{pb}",))
            p.dma("y", lambda eng, j=j, pb=pb: eng.dma_start(
                      out=qv[:, 2 * j:2 * j + 2, :], in_=s_wqbb[:, pb, :].rearrange("p (c n) -> p c n", c=2)),
                  reads=(f"q0b{pb}",))

    p.barrier()
    if upto < 2:
        p.finish_outputs("y")
        with stack:
            return p.build()
    with ExitStack() as ph:
        s_y = ph.enter_context(nc.sbuf_tensor("p1_y", [128, 16, 256], F32))
        s_x = ph.enter_context(nc.sbuf_tensor("p1_x", [128, 2, 8, 256], F32))
        s_w = ph.enter_context(nc.sbuf_tensor("p1_w", [128, 16, 8, 128], F32))
        s_b = ph.enter_context(nc.sbuf_tensor("p1_b", [128, 8], F32))
        s_r1 = ph.enter_context(nc.sbuf_tensor("p1_r1", [128, 8, 256], F32))
        s_sq = ph.enter_context(nc.sbuf_tensor("p1_sq", [128, 8, 256], F32))
        s_sc = ph.enter_context(nc.sbuf_tensor("p1_sc", [128, 4, 256], F32))
        s_s1loc = ph.enter_context(nc.sbuf_tensor("p1_s1", [128, 8, 256], F32))
        s_ki = ph.enter_context(nc.sbuf_tensor("p1_ki", [128, 8, 256], mybir.dt.int32))
        s_qs = ph.enter_context(nc.sbuf_tensor("p1_qs", [128, 8, 256], F32))
        p.dma("y", lambda eng: eng.dma_start(out=s_w[:], in_=wout.rearrange("(kc p) (nc8 f) -> p kc nc8 f", p=128, f=128)),
              writes=("w",))
        p.dma("y", lambda eng: eng.dma_start(out=s_b[:], in_=boutr), writes=("b",))
        yv = y_own.rearrange("(kc p) t -> p kc t", p=128)
        xv = x_fm.rearrange("(kc p) t -> p kc t", p=128)
        TT1 = 256
        for tt in range(TOK // TT1):
            pb = tt % 2
            p.dma("y", lambda eng, tt=tt: eng.dma_start(out=s_y[:], in_=yv[:, :, tt * TT1:(tt + 1) * TT1]),
                  writes=("yb",))
            p.dma("y", lambda eng, tt=tt, pb=pb: eng.dma_start(out=s_x[:, pb], in_=xv[:, :, tt * TT1:(tt + 1) * TT1]),
                  writes=(f"x{pb}",))
            for nc8 in range(8):
                bi = nc8 % 4
                pairs = [(s_w[:, kc, nc8, :], s_y[:, kc, :]) for kc in range(16)]
                mm_accum(p, ps_pool[bi][:, 0:TT1], pairs, f"pp{bi}", ("yb", "w"))
                p.emit("s", lambda eng, nc8=nc8, bi=bi: eng.activation(
                           s_r1[:, nc8, :], ps_pool[bi][:, 0:TT1], ACT.Identity, bias=s_b[:, nc8:nc8 + 1], scale=1.0),
                       reads=(f"pp{bi}", "b"), writes=("r1",))
            p.emit("v", lambda eng, pb=pb: eng.tensor_tensor(s_r1[:], s_r1[:], s_x[:, pb], op=ALU.add),
                   reads=("r1", f"x{pb}"), writes=("r1",))
            if P1_STAGE < 1:
                continue
            s1_sl = s_s1loc
            emit_ln_fm(p, f"ln1_{tt}", s_r1, "r1", s1_sl, "s1", s_sq, "sq", s_ones,
                       ps_pool[4], "pp4", ps_pool[5], "pp5", s_qs[:, 0:2], "qs", TT1)
            p.dma("y", lambda eng, tt=tt: eng.dma_start(
                      out=s1_hbm.rearrange("(kc p) t -> p kc t", p=128)[:, :, tt * TT1:(tt + 1) * TT1],
                      in_=s_s1loc[:]),
                  reads=("s1",))
            if P1_STAGE < 2:
                continue
            p.emit("s", lambda eng, s1_sl=s1_sl: eng.activation(s_qs[:], s1_sl[:], ACT.Abs),
                   reads=("s1",), writes=("qs",))
            p.emit("v", lambda eng: eng.tensor_tensor(s_sc[:, 0, :], s_qs[:, 0, :], s_qs[:, 1, :], op=ALU.max),
                   reads=("qs",), writes=("am",))
            for kc in range(2, 8):
                p.emit("v", lambda eng, kc=kc: eng.tensor_tensor(s_sc[:, 0, :], s_sc[:, 0, :], s_qs[:, kc, :], op=ALU.max),
                       reads=("am", "qs"), writes=("am",))
            for jb in range(TT1 // 128):
                emit_colmax_bcast(p, f"cm1_{tt}_{jb}", s_sc[:, 0, jb * 128:(jb + 1) * 128], "am",
                                  s_cmg, s_row, ps_pool[6], "pp6c", ps_pool[7], "pp7c",
                                  s_ident, s_sc[:, 2, jb * 128:(jb + 1) * 128], "sc1",
                                  s_inv1[:, tt * TT1 + jb * 128:tt * TT1 + (jb + 1) * 128], "inv1", s_one1, None)
            if P1_STAGE < 3:
                continue
            p.emit("v", lambda eng, s1_sl=s1_sl: eng.tensor_tensor(
                       s_r1[:], s1_sl[:], s_sc[:, 2:3, :].broadcast_to([128, 8, TT1]), op=ALU.mult),
                   reads=("s1", "sc1"), writes=("r1",))
            p.emit("v", lambda eng: eng.tensor_copy(s_ki[:], s_r1[:]), reads=("r1",), writes=("ki",))
            p.emit("v", lambda eng: eng.tensor_copy(s_sq[:], s_ki[:]), reads=("ki",), writes=("sq",))
            p.emit("v", lambda eng: eng.tensor_tensor(s_r1[:], s_r1[:], s_sq[:], op=ALU.subtract),
                   reads=("r1", "sq"), writes=("r1",))
            p.emit("v", lambda eng: eng.tensor_scalar(s_qs[:], s_r1[:], -0.5, None, op0=ALU.is_lt),
                   reads=("r1",), writes=("qs",))
            p.emit("v", lambda eng: eng.scalar_tensor_tensor(out=s_r1[:], in0=s_r1[:], scalar=0.5, in1=s_qs[:],
                                                             op0=ALU.is_gt, op1=ALU.subtract),
                   reads=("r1", "qs"), writes=("r1",))
            p.emit("v", lambda eng: eng.tensor_tensor(s_r1[:], s_r1[:], s_sq[:], op=ALU.add),
                   reads=("r1", "sq"), writes=("r1",))
            p.emit("v", lambda eng: eng.tensor_scalar(s_r1[:], s_r1[:], -128.0, 127.0, op0=ALU.max, op1=ALU.min),
                   reads=("r1",), writes=("r1",))
            p.emit("v", lambda eng, tt=tt: eng.tensor_copy(s_kq1[:, :, tt * TT1:(tt + 1) * TT1], s_r1[:]),
                   reads=("r1",), writes=("kq1",))

    p.barrier()
    if upto < 3:
        p.finish_outputs("y")
        with stack:
            return p.build()
    p.emit("v", lambda eng: eng.memset(s_m2[:], 0.0), writes=("m2",))
    with ExitStack() as ph:
        s_wb = ph.enter_context(nc.sbuf_tensor("p2_wb", [128, 2, 2, 8, 128], F32))
        s_wq2 = ph.enter_context(nc.sbuf_tensor("p2_wq", [128, 2, 8, 128], F32))
        s_wqb3 = ph.enter_context(nc.sbuf_tensor("p2_wqb", [128, 2, 8, 128], BF16))
        s_g = ph.enter_context(nc.sbuf_tensor("p2_g", [128, TT], F32))
        s_u = ph.enter_context(nc.sbuf_tensor("p2_u", [128, TT], F32))
        s_gu2 = ph.enter_context(nc.sbuf_tensor("p2_gu", [128, 2, TT], F32))
        wgv = wgT.rearrange("(kc p) (oc f) -> p kc oc f", p=128, f=128)
        wuv = wuT.rearrange("(kc p) (oc f) -> p kc oc f", p=128, f=128)
        guv = gu_hbm.rearrange("(oc p) t -> p oc t", p=128)
        for oc in range(32):
            pb = oc % 2
            p.dma("y", lambda eng, oc=oc, pb=pb: eng.dma_start(out=s_wb[:, pb, 0], in_=wgv[:, :, oc, :]),
                  writes=(f"wg{pb}",))
            p.dma("y", lambda eng, oc=oc, pb=pb: eng.dma_start(out=s_wb[:, pb, 1], in_=wuv[:, :, oc, :]),
                  writes=(f"wu{pb}",))
            for gi, wkey, mc_col, sw_col in ((0, "wg", 0, 3), (1, "wu", 1, 4)):
                p.emit("v", lambda eng, pb=pb, gi=gi, sw_col=sw_col: eng.tensor_scalar(
                           s_wq2[:, 0], s_wb[:, pb, gi], s_wsc[:, sw_col:sw_col + 1], None, op0=ALU.mult),
                       reads=(f"{wkey}{pb}", "wsc2"), writes=("wqa",))
                p.emit("v", lambda eng: eng.tensor_scalar(s_wq2[:, 1], s_wq2[:, 0], -0.5, None, op0=ALU.is_lt),
                       reads=("wqa",), writes=("wqm",))
                p.emit("v", lambda eng, gi=gi: eng.scalar_tensor_tensor(out=s_wqb3[:, gi], in0=s_wq2[:, 0], scalar=0.5,
                                                                 in1=s_wq2[:, 1], op0=ALU.is_gt, op1=ALU.subtract),
                       reads=("wqa", "wqm"), writes=(f"wqb{gi}",))
            for tt in range(NTT):
                pairs_g = [(s_wqb3[:, 0, kc, :], s_kq1[:, kc, tt * TT:(tt + 1) * TT]) for kc in range(8)]
                pairs_u = [(s_wqb3[:, 1, kc, :], s_kq1[:, kc, tt * TT:(tt + 1) * TT]) for kc in range(8)]
                mm_accum(p, ps_pool[0][:], pairs_g, "pp0", ("wqb0", "kq1"))
                mm_accum(p, ps_pool[1][:], pairs_u, "pp1", ("wqb1", "kq1"))
                p.emit("v", lambda eng, tt=tt: eng.scalar_tensor_tensor(
                           out=s_g[:], in0=ps_pool[0][:], scalar=s_wsc[:, 0:1],
                           in1=s_inv1[:, tt * TT:(tt + 1) * TT], op0=ALU.mult, op1=ALU.mult),
                       reads=("pp0", "wsc", "inv1"), writes=("g",))
                p.emit("s", lambda eng: eng.activation(s_g[:], s_g[:], ACT.Silu),
                       reads=("g",), writes=("g",))
                p.emit("v", lambda eng, tt=tt: eng.scalar_tensor_tensor(
                           out=s_u[:], in0=ps_pool[1][:], scalar=s_wsc[:, 1:2],
                           in1=s_inv1[:, tt * TT:(tt + 1) * TT], op0=ALU.mult, op1=ALU.mult),
                       reads=("pp1", "wsc", "inv1"), writes=("u",))
                gb = tt % 2
                p.emit("v", lambda eng, gb=gb: eng.tensor_tensor(s_gu2[:, gb], s_g[:], s_u[:], op=ALU.mult),
                       reads=("g", "u"), writes=(f"gu{gb}",))
                p.dma("y", lambda eng, oc=oc, tt=tt, gb=gb: eng.dma_start(
                          out=guv[:, oc, tt * TT:(tt + 1) * TT], in_=s_gu2[:, gb]),
                      reads=(f"gu{gb}",))
                p.emit("s", lambda eng, gb=gb: eng.activation(s_g[:], s_gu2[:, gb], ACT.Abs),
                       reads=(f"gu{gb}",), writes=("g",))
                p.emit("v", lambda eng, tt=tt: eng.tensor_tensor(
                           s_m2[:, tt * TT:(tt + 1) * TT], s_m2[:, tt * TT:(tt + 1) * TT], s_g[:], op=ALU.max),
                       reads=("g", "m2"), writes=("m2",))

    p.barrier()
    if upto < 4:
        p.finish_outputs("y")
        with stack:
            return p.build()
    for jb in range(TOK // 128):
        emit_colmax_bcast(p, f"cm2_{jb}", s_m2[:, jb * 128:(jb + 1) * 128], "m2",
                          s_cmg, s_row, ps_pool[6], "pp6c", ps_pool[7], "pp7c",
                          s_ident, s_f2s[:, jb * 128:(jb + 1) * 128], "f2",
                          s_inv2[:, jb * 128:(jb + 1) * 128], "inv2", s_one1, None)

    p.barrier()
    if upto < 5:
        p.finish_outputs("y")
        with stack:
            return p.build()
    with ExitStack() as ph:
        s_wd = ph.enter_context(nc.sbuf_tensor("p4_wd", [128, 2, 8, 128], BF16))
        s_gu4 = ph.enter_context(nc.sbuf_tensor("p4_gu", [128, 2, TT], F32))
        s_q = ph.enter_context(nc.sbuf_tensor("p4_q", [128, 2, TT], F32))
        s_qb = ph.enter_context(nc.sbuf_tensor("p4_qb", [128, TT], BF16))
        s_qi4 = ph.enter_context(nc.sbuf_tensor("p4_qi", [128, TT], mybir.dt.int32))
        s_qf4 = ph.enter_context(nc.sbuf_tensor("p4_qf", [128, TT], F32))
        s_ffn = ph.enter_context(nc.sbuf_tensor("p4_ffn", [128, 8, TT], F32))
        s_sq4 = ph.enter_context(nc.sbuf_tensor("p4_sq", [128, 8, TT], F32))
        s_x4 = ph.enter_context(nc.sbuf_tensor("p4_x", [128, 8, TT], F32))
        s_s1r = ph.enter_context(nc.sbuf_tensor("p4_s1r", [128, 8, TT], F32))
        s_mv4 = ph.enter_context(nc.sbuf_tensor("p4_mv", [128, 2, TT], F32))
        wdv = wdq_hbm.rearrange("(kc p) (oc f) -> p kc oc f", p=128, f=128)
        guv = gu_hbm.rearrange("(kc p) t -> p kc t", p=128)
        s1v = s1_hbm.rearrange("(kc p) t -> p kc t", p=128)
        for tt in range(NTT):
            p.dma("y", lambda eng, tt=tt: eng.dma_start(out=s_s1r[:], in_=s1v[:, :, tt * TT:(tt + 1) * TT]),
                  writes=("s1r",))
            for ic in range(32):
                pb = ic % 2
                p.dma("y", lambda eng, ic=ic, pb=pb: eng.dma_start(out=s_wd[:, pb], in_=wdv[:, ic, :, :]),
                      writes=(f"wd{pb}",))
                p.dma("y", lambda eng, ic=ic, tt=tt, pb=pb: eng.dma_start(
                          out=s_gu4[:, pb], in_=guv[:, ic, tt * TT:(tt + 1) * TT]),
                      writes=(f"gub{pb}",))
                p.emit("v", lambda eng, tt=tt, pb=pb: eng.tensor_tensor(
                           s_q[:, 0], s_gu4[:, pb], s_f2s[:, tt * TT:(tt + 1) * TT], op=ALU.mult),
                       reads=(f"gub{pb}", "f2"), writes=("qa",))
                p.emit("v", lambda eng: eng.tensor_copy(s_qi4[:], s_q[:, 0]), reads=("qa",), writes=("qi4",))
                p.emit("v", lambda eng: eng.tensor_copy(s_qf4[:], s_qi4[:]), reads=("qi4",), writes=("qf4",))
                p.emit("v", lambda eng: eng.tensor_tensor(s_q[:, 0], s_q[:, 0], s_qf4[:], op=ALU.subtract),
                       reads=("qa", "qf4"), writes=("qa",))
                p.emit("v", lambda eng: eng.tensor_scalar(s_q[:, 1], s_q[:, 0], -0.5, None, op0=ALU.is_lt),
                       reads=("qa",), writes=("qm",))
                p.emit("v", lambda eng: eng.scalar_tensor_tensor(out=s_q[:, 0], in0=s_q[:, 0], scalar=0.5,
                                                                 in1=s_q[:, 1], op0=ALU.is_gt, op1=ALU.subtract),
                       reads=("qa", "qm"), writes=("qa",))
                p.emit("v", lambda eng: eng.tensor_tensor(s_q[:, 0], s_q[:, 0], s_qf4[:], op=ALU.add),
                       reads=("qa", "qf4"), writes=("qa",))
                p.emit("v", lambda eng: eng.tensor_scalar(s_qb[:], s_q[:, 0], -128.0, 127.0, op0=ALU.max, op1=ALU.min),
                       reads=("qa",), writes=("qb",))
                def mmdown(eng, ic=ic, pb=pb):
                    ins = None
                    for oc in range(8):
                        ins = eng.matmul(ps_pool[oc][:], s_wd[:, pb, oc, :], s_qb[:],
                                         start=(ic == 0), stop=(ic == 31))
                    return ins
                p.emit("t", mmdown, reads=(f"wd{pb}", "qb"), writes=("ppdown",))
            for oc in range(8):
                p.emit("v", lambda eng, oc=oc, tt=tt: eng.scalar_tensor_tensor(
                           out=s_ffn[:, oc, :], in0=ps_pool[oc][:], scalar=s_wsc[:, 2:3],
                           in1=s_inv2[:, tt * TT:(tt + 1) * TT], op0=ALU.mult, op1=ALU.mult),
                       reads=("ppdown", "wsc", "inv2"), writes=("ffn",))
            p.emit("v", lambda eng, tt=tt: eng.tensor_tensor(
                       s_ffn[:], s_ffn[:], s_s1r[:], op=ALU.add),
                   reads=("ffn", "s1r"), writes=("ffn",))
            emit_ln_fm(p, f"ln2_{tt}", s_ffn, "ffn", s_x4, "s2", s_sq4, "p4sq", s_ones,
                       ps_pool[0], "ppdown", ps_pool[1], "ppdown", s_mv4, "lnmv2", TT)
            p.dma("y", lambda eng, tt=tt: eng.dma_start(
                      out=s2_out.rearrange("(kc p) t -> p kc t", p=128)[:, :, tt * TT:(tt + 1) * TT], in_=s_x4[:]),
                  reads=("s2",))
    p.finish_outputs("y")
    with stack:
        return p.build()


def host_l3_inputs(ins, y_by_core):
    Wout = ins["L0_attn_Wout"]; bout = ins["L0_attn_bout"]
    x = ins["L0_attn_x"]
    wg = ins["L0_ffn_w_gate"]; wu = ins["L0_ffn_w_up"]; wd = ins["L0_ffn_w_down"]
    maps = []
    for c in range(NCORES):
        b, h = c // 2, c % 2
        tsl = slice(h * TOK, (h + 1) * TOK)
        y_full = np.concatenate([y_by_core[2 * b], y_by_core[2 * b + 1]], axis=0)
        maps.append({
            "y_own": np.ascontiguousarray(y_full[:, tsl]),
            "wout": Wout,
            "boutr": np.ascontiguousarray(bout.reshape(8, 128).T),
            "x_fm": np.ascontiguousarray(x[b].T[:, tsl]),
            "wgT": np.ascontiguousarray(wg.T),
            "wuT": np.ascontiguousarray(wu.T),
            "wdT": np.ascontiguousarray(wd.T),
        })
    return maps


def build_l4():
    nc = bass.Bass("TRN2", target_bir_lowering=False)
    stack = ExitStack()
    p = Prog(nc, stack)

    NH, DHD, W = 16, 64, 128
    NB = TOK // W
    KT = TOK + 2 * W

    s2_fm = nc.dram_tensor("s2_fm", [DM, TOK], F32, kind="ExternalInput").ap()
    halo_x = nc.dram_tensor("halo_x", [DM, 2 * W], F32, kind="ExternalInput").ap()
    wqkv = nc.dram_tensor("wqkv", [DM, 3 * DM], F32, kind="ExternalInput").ap()
    bq_r = nc.dram_tensor("bq_r", [128, 8], F32, kind="ExternalInput").ap()
    bk_r = nc.dram_tensor("bk_r", [128, 8], F32, kind="ExternalInput").ap()
    bv_bc = nc.dram_tensor("bv_bc", [128, DM], F32, kind="ExternalInput").ap()
    wout1 = nc.dram_tensor("wout1", [DM, DM], F32, kind="ExternalInput").ap()
    bo_r = nc.dram_tensor("bo_r", [128, 8], F32, kind="ExternalInput").ap()
    mask0 = nc.dram_tensor("mask0", [128, 384], F32, kind="ExternalInput").ap()
    maskm = nc.dram_tensor("maskm", [128, 384], F32, kind="ExternalInput").ap()
    mask15 = nc.dram_tensor("mask15", [128, 384], F32, kind="ExternalInput").ap()
    s3f_out = nc.dram_tensor("s3f_out", [DM, TOK], F32, kind="ExternalOutput").ap()
    s3t_out = nc.dram_tensor("s3t_out", [TOK, DM], F32, kind="ExternalOutput").ap()
    k_hbm = nc.dram_tensor("k_hbm", [DM, KT], F32, kind="Internal").ap()
    v_hbm = nc.dram_tensor("v_hbm", [KT, DM], F32, kind="Internal").ap()

    s_ms = stack.enter_context(nc.sbuf_tensor("s_ms", [128, 3, 384], F32))
    s_ones = stack.enter_context(nc.sbuf_tensor("s_ones", [128, 128], F32))
    s_one1 = stack.enter_context(nc.sbuf_tensor("s_one1", [64, 128], F32))
    s_row = stack.enter_context(nc.sbuf_tensor("s_row", [96, 512], F32))
    s_iota = stack.enter_context(nc.sbuf_tensor("s_iota", [128, 128], F32))
    s_ident = stack.enter_context(nc.sbuf_tensor("s_ident", [128, 128], F32))
    ps = [stack.enter_context(nc.psum_tensor(f"ps{i}", [128, 512], F32)) for i in range(8)]
    p.emit("v", lambda eng: eng.memset(s_ones[:], 1.0), writes=("ones",))
    p.emit("v", lambda eng: eng.memset(s_one1[:], 1.0), writes=("one1",))
    make_identity(p, s_iota, s_ident)
    p.dma("y", lambda eng: eng.dma_start(out=s_ms[:, 0], in_=mask0), writes=("ms",))
    p.dma("y", lambda eng: eng.dma_start(out=s_ms[:, 1], in_=maskm), writes=("ms",))
    p.dma("y", lambda eng: eng.dma_start(out=s_ms[:, 2], in_=mask15), writes=("ms",))

    x_r = s2_fm.rearrange("(kc p) t -> p kc t", p=128)
    hx_r = halo_x.rearrange("(kc p) t -> p kc t", p=128)
    kv = k_hbm.rearrange("(kc p) t -> p kc t", p=128)
    vv_ = v_hbm.rearrange("(tb p) d -> p tb d", p=128)

    with ExitStack() as ph:
        s_wk = ph.enter_context(nc.sbuf_tensor("pa_wk", [128, 8, 8, 128], F32))
        s_wv = ph.enter_context(nc.sbuf_tensor("pa_wv", [128, 8, 8, 128], F32))
        s_bk = ph.enter_context(nc.sbuf_tensor("pa_bk", [128, 8], F32))
        s_bv = ph.enter_context(nc.sbuf_tensor("pa_bv", [128, DM], F32))
        s_xa = ph.enter_context(nc.sbuf_tensor("pa_x", [128, 2, 8, 512], F32))
        s_k = ph.enter_context(nc.sbuf_tensor("pa_k", [128, 8, 512], F32))
        s_v = ph.enter_context(nc.sbuf_tensor("pa_v", [128, 2, 512], F32))
        p.dma("y", lambda eng: eng.dma_start(out=s_wk[:], in_=wqkv.rearrange("(kc p) n -> p kc n", p=128)[:, :, DM:2 * DM].rearrange("p kc (o f) -> p kc o f", f=128)), writes=("wk",))
        p.dma("y", lambda eng: eng.dma_start(out=s_wv[:], in_=wqkv.rearrange("(kc p) n -> p kc n", p=128)[:, :, 2 * DM:3 * DM].rearrange("p kc (o f) -> p kc o f", f=128)), writes=("wv",))
        p.dma("y", lambda eng: eng.dma_start(out=s_bk[:], in_=bk_r), writes=("bk",))
        p.dma("y", lambda eng: eng.dma_start(out=s_bv[:], in_=bv_bc), writes=("bv",))
        groups = [("own", g) for g in range(4)] + [("halo", 0)]
        for gi, (kind, g) in enumerate(groups):
            pb = gi % 2
            ntok = 512 if kind == "own" else 256
            if kind == "own":
                p.dma("y", lambda eng, g=g, pb=pb: eng.dma_start(out=s_xa[:, pb], in_=x_r[:, :, g * 512:(g + 1) * 512]),
                      writes=(f"x{pb}",))
            else:
                p.dma("y", lambda eng, pb=pb: eng.dma_start(out=s_xa[:, pb, :, 0:256], in_=hx_r[:]),
                      writes=(f"x{pb}",))
            for oc in range(8):
                bi = oc % 2
                pairs = [(s_wk[:, kc, oc, :], s_xa[:, pb, kc, 0:ntok]) for kc in range(8)]
                mm_accum(p, ps[bi][:, 0:ntok], pairs, f"pp{bi}", (f"x{pb}", "wk"))
                p.emit("s", lambda eng, oc=oc, bi=bi, ntok=ntok: eng.activation(
                           s_k[:, oc, 0:ntok], ps[bi][:, 0:ntok], ACT.Identity, bias=s_bk[:, oc:oc + 1], scale=1.0),
                       reads=(f"pp{bi}", "bk"), writes=("k",))
            if kind == "own":
                p.dma("y", lambda eng, g=g: eng.dma_start(out=kv[:, :, 128 + g * 512:128 + (g + 1) * 512], in_=s_k[:]),
                      reads=("k",))
            else:
                p.dma("y", lambda eng: eng.dma_start(out=kv[:, :, 0:128], in_=s_k[:, :, 0:128]), reads=("k",))
                p.dma("y", lambda eng: eng.dma_start(out=kv[:, :, KT - 128:KT], in_=s_k[:, :, 128:256]), reads=("k",))
            for tb in range(ntok // 128):
                for vc in range(2):
                    bi = 2 + vc
                    pairs = [(s_xa[:, pb, kc, tb * 128:(tb + 1) * 128], s_wv[:, kc, 4 * vc:4 * vc + 4, :].rearrange("p o f -> p (o f)")) for kc in range(8)]
                    mm_accum(p, ps[bi][:], pairs, f"pp{bi}", (f"x{pb}", "wv"))
                    p.emit("v", lambda eng, vc=vc, bi=bi: eng.tensor_tensor(
                               s_v[:, vc, :], ps[bi][:], s_bv[:, vc * 512:(vc + 1) * 512], op=ALU.add),
                           reads=(f"pp{bi}", "bv"), writes=("v",))
                if kind == "own":
                    p.dma("y", lambda eng, g=g, tb=tb: eng.dma_start(
                              out=vv_[:, 1 + g * 4 + tb, :], in_=s_v[:].rearrange("p a b -> p (a b)")),
                          reads=("v",))
                elif tb == 0:
                    p.dma("y", lambda eng: eng.dma_start(out=vv_[:, 0, :], in_=s_v[:].rearrange("p a b -> p (a b)")),
                          reads=("v",))
                else:
                    p.dma("y", lambda eng: eng.dma_start(out=vv_[:, 17, :], in_=s_v[:].rearrange("p a b -> p (a b)")),
                          reads=("v",))

    p.barrier()
    with ExitStack() as ph:
        s_wq = ph.enter_context(nc.sbuf_tensor("pb_wq", [128, 8, 8, 128], F32))
        s_bq = ph.enter_context(nc.sbuf_tensor("pb_bq", [128, 8], F32))
        s_x = ph.enter_context(nc.sbuf_tensor("pb_x", [128, 1, 8, 512], F32))
        s_q = ph.enter_context(nc.sbuf_tensor("pb_q", [128, 8, 512], F32))
        s_kk = ph.enter_context(nc.sbuf_tensor("pb_kk", [128, 1, 8, 768], F32))
        s_vv = ph.enter_context(nc.sbuf_tensor("pb_vv", [128, 1, 6, DM], F32))
        s_sc = ph.enter_context(nc.sbuf_tensor("pb_sc", [128, 384], F32))
        s_sm = ph.enter_context(nc.sbuf_tensor("pb_sm", [128, 4], F32))
        s_at = ph.enter_context(nc.sbuf_tensor("pb_at", [128, 3, 128], F32))
        s_ao = ph.enter_context(nc.sbuf_tensor("pb_ao", [128, 8, 512], F32))
        s_wo = ph.enter_context(nc.sbuf_tensor("pb_wo", [128, 2, 8, 128], F32))
        s_bo = ph.enter_context(nc.sbuf_tensor("pb_bo", [128, 8], F32))
        s_o = ph.enter_context(nc.sbuf_tensor("pb_o", [128, 8, 512], F32))
        s_sq = ph.enter_context(nc.sbuf_tensor("pb_sq", [128, 8, 512], F32))
        s_s3 = ph.enter_context(nc.sbuf_tensor("pb_s3", [128, 8, 512], F32))
        s_3t = ph.enter_context(nc.sbuf_tensor("pb_3t", [128, 8, 128], F32))
        s_mvb = ph.enter_context(nc.sbuf_tensor("pb_mv", [128, 2, 512], F32))
        p.dma("y", lambda eng: eng.dma_start(out=s_wq[:], in_=wqkv.rearrange("(kc p) n -> p kc n", p=128)[:, :, 0:DM].rearrange("p kc (o f) -> p kc o f", f=128)), writes=("wq",))
        p.dma("y", lambda eng: eng.dma_start(out=s_bq[:], in_=bq_r), writes=("bq",))
        p.dma("y", lambda eng: eng.dma_start(out=s_bo[:], in_=bo_r), writes=("bo",))
        for g in range(4):
            pb = 0
            p.dma("y", lambda eng, g=g, pb=pb: eng.dma_start(out=s_x[:, pb], in_=x_r[:, :, g * 512:(g + 1) * 512]),
                  writes=(f"x{pb}",))
            p.dma("y", lambda eng, g=g, pb=pb: eng.dma_start(out=s_kk[:, pb], in_=kv[:, :, g * 512:g * 512 + 768]),
                  writes=(f"kk{pb}",))
            p.dma("y", lambda eng, g=g, pb=pb: eng.dma_start(out=s_vv[:, pb], in_=vv_[:, g * 4:g * 4 + 6, :]),
                  writes=(f"vv{pb}",))
            for oc in range(8):
                bi = oc % 2
                pairs = [(s_wq[:, kc, oc, :], s_x[:, pb, kc, :]) for kc in range(8)]
                mm_accum(p, ps[bi][:], pairs, f"pp{bi}", (f"x{pb}", "wq"))
                p.emit("s", lambda eng, oc=oc, bi=bi: eng.activation(
                           s_q[:, oc, 0:512], ps[bi][:], ACT.Identity, bias=s_bq[:, oc:oc + 1], scale=1.0),
                       reads=(f"pp{bi}", "bq"), writes=("q",))
            for hd in range(NH):
                hc, hr = hd // 2, (hd % 2) * 64
                for qb in range(4):
                    gqb = g * 4 + qb
                    mi = 0 if gqb == 0 else (2 if gqb == NB - 1 else 1)
                    p.emit("t", lambda eng, hc=hc, hr=hr, qb=qb, pb=pb: eng.matmul(
                               ps[4][:, 0:384], s_q[hr:hr + 64, hc, qb * 128:(qb + 1) * 128],
                               s_kk[hr:hr + 64, pb, hc, qb * 128:qb * 128 + 384], start=True, stop=True),
                           reads=("q", f"kk{pb}"), writes=("pp4",))
                    p.emit("v", lambda eng, mi=mi: eng.scalar_tensor_tensor(
                               out=s_sc[:], in0=ps[4][:, 0:384], scalar=0.03125, in1=s_ms[:, mi],
                               op0=ALU.mult, op1=ALU.add),
                           reads=("pp4", "ms"), writes=("sc",))
                    p.emit("v", lambda eng: eng.tensor_reduce(s_sm[:, 0:1], s_sc[:], axis=AX, op=ALU.max),
                           reads=("sc",), writes=("sm",))
                    p.emit("v", lambda eng: eng.tensor_scalar(s_sm[:, 1:2], s_sm[:, 0:1], -1.0, None, op0=ALU.mult),
                           reads=("sm",), writes=("sm1",))
                    p.emit("s", lambda eng: eng.activation(s_sc[:], s_sc[:], ACT.Exp, bias=s_sm[:, 1:2], scale=1.0),
                           reads=("sc", "sm1"), writes=("sc",))
                    p.emit("v", lambda eng: eng.tensor_reduce(s_sm[:, 2:3], s_sc[:], axis=AX, op=ALU.add),
                           reads=("sc",), writes=("sm2",))
                    p.emit("v", lambda eng: eng.reciprocal(s_sm[:, 3:4], s_sm[:, 2:3]),
                           reads=("sm2",), writes=("sm3",))
                    p.emit("v", lambda eng: eng.tensor_scalar(s_sc[:], s_sc[:], s_sm[:, 3:4], None, op0=ALU.mult),
                           reads=("sc", "sm3"), writes=("sc",))
                    for j in range(3):
                        p.emit("t", lambda eng, j=j: eng.transpose(ps[5][:, j * 128:(j + 1) * 128], s_sc[:, j * 128:(j + 1) * 128], s_ident[:]),
                               reads=("sc", "ident"), writes=("pp5",))
                    p.emit("s", lambda eng: eng.copy(s_at[:].rearrange("p a b -> p (a b)"), ps[5][:, 0:384]),
                           reads=("pp5",), writes=("at",))
                    def av(eng, hc=hc, hr=hr, qb=qb, pb=pb):
                        ins = None
                        for j in range(3):
                            ins = eng.matmul(ps[6][hr:hr + 64, qb * 128:(qb + 1) * 128],
                                             s_vv[:, pb, qb + j, hc * 128 + hr:hc * 128 + hr + 64],
                                             s_at[:, j, :], start=(j == 0), stop=(j == 2))
                        return ins
                    p.emit("t", av, reads=("at", f"vv{pb}"), writes=("pp6",))
                    p.emit("s", lambda eng, hc=hc, hr=hr, qb=qb: eng.copy(
                               s_ao[hr:hr + 64, hc, qb * 128:(qb + 1) * 128], ps[6][hr:hr + 64, qb * 128:(qb + 1) * 128]),
                           reads=("pp6",), writes=("ao",))
            for oc in range(8):
                bi = oc % 2
                p.dma("y", lambda eng, oc=oc, bi=bi: eng.dma_start(
                          out=s_wo[:, bi], in_=wout1.rearrange("(kc p) n -> p kc n", p=128)[:, :, oc * 128:(oc + 1) * 128]),
                      writes=(f"wo{bi}",))
                pairs = [(s_wo[:, bi, kc, :], s_ao[:, kc, :]) for kc in range(8)]
                mm_accum(p, ps[bi][:], pairs, f"pp{bi}", ("ao", f"wo{bi}"))
                p.emit("s", lambda eng, oc=oc, bi=bi: eng.activation(
                           s_o[:, oc, :], ps[bi][:], ACT.Identity, bias=s_bo[:, oc:oc + 1], scale=1.0),
                       reads=(f"pp{bi}", "bo"), writes=("o",))
            p.emit("v", lambda eng, pb=pb: eng.tensor_tensor(s_o[:], s_o[:], s_x[:, pb], op=ALU.add),
                   reads=("o", f"x{pb}"), writes=("o",))
            emit_ln_fm(p, f"ln3_{g}", s_o, "o", s_s3, "s3", s_sq, "sq3", s_ones,
                       ps[4], "pp4", ps[5], "pp5", s_mvb, "lnmv3", 512)
            p.dma("y", lambda eng, g=g: eng.dma_start(
                      out=s3f_out.rearrange("(kc p) t -> p kc t", p=128)[:, :, g * 512:(g + 1) * 512], in_=s_s3[:]),
                  reads=("s3",))
            for tb in range(4):
                for half in range(2):
                    bk = ps[2] if half == 0 else ps[3]
                    bkk = "pp2" if half == 0 else "pp3"
                    for k4 in range(4):
                        kc = half * 4 + k4
                        p.emit("t", lambda eng, tb=tb, kc=kc, k4=k4, bk=bk: eng.transpose(
                                   bk[:, k4 * 128:(k4 + 1) * 128], s_s3[:, kc, tb * 128:(tb + 1) * 128], s_ident[:]),
                               reads=("s3", "ident"), writes=(bkk,))
                    p.emit("s", lambda eng, half=half, bk=bk: eng.copy(
                               s_3t[:, half * 4:(half + 1) * 4, :].rearrange("p a b -> p (a b)"), bk[:, 0:512]),
                           reads=(bkk,), writes=("t3",))
                p.dma("y", lambda eng, g=g, tb=tb: eng.dma_start(
                          out=s3t_out[(g * 4 + tb) * 128:(g * 4 + tb + 1) * 128, :], in_=s_3t[:].rearrange("p a b -> p (a b)")),
                      reads=("t3",))
    p.finish_outputs("y")
    with stack:
        return p.build()


def host_l4_inputs(ins, s2_by_core):
    Wqkv = ins["L1_attn_Wqkv"]; bqkv = ins["L1_attn_bqkv"]
    Wout1 = ins["L1_attn_Wout"]; bout1 = ins["L1_attn_bout"]
    W = 128
    i = np.arange(128)[:, None]; j = np.arange(384)[None, :]
    base = np.where((j >= i) & (j <= i + 256), 0.0, -1e30).astype(np.float32)
    m_first = np.where(j >= 128, base, -1e30).astype(np.float32)
    m_last = np.where(j < 256, base, -1e30).astype(np.float32)
    maps = []
    for c in range(NCORES):
        b, h = c // 2, c % 2
        peer = 2 * b + (1 - h)
        zeros = np.zeros((DM, W), np.float32)
        if h == 0:
            left = zeros
            right = s2_by_core[peer][:, 0:W]
            mask0_, mask15_ = m_first, base
        else:
            left = s2_by_core[peer][:, TOK - W:TOK]
            right = zeros
            mask0_, mask15_ = base, m_last
        maps.append({
            "s2_fm": np.ascontiguousarray(s2_by_core[c]),
            "halo_x": np.ascontiguousarray(np.concatenate([left, right], axis=1)),
            "wqkv": Wqkv,
            "bq_r": np.ascontiguousarray(bqkv[0:DM].reshape(8, 128).T),
            "bk_r": np.ascontiguousarray(bqkv[DM:2 * DM].reshape(8, 128).T),
            "bv_bc": np.ascontiguousarray(np.broadcast_to(bqkv[2 * DM:3 * DM], (128, DM))),
            "wout1": Wout1,
            "bo_r": np.ascontiguousarray(bout1.reshape(8, 128).T),
            "mask0": mask0_, "maskm": base, "mask15": mask15_,
        })
    return maps


def build_l5():
    nc = bass.Bass("TRN2", target_bir_lowering=False)
    stack = ExitStack()
    p = Prog(nc, stack)
    TT = 512

    s3_fm = nc.dram_tensor("s3_fm", [DM, TOK], F32, kind="ExternalInput").ap()
    wq = nc.dram_tensor("wq", [DM, 2048], F32, kind="ExternalInput").ap()
    q_out = nc.dram_tensor("q_out", [2048, TOK], F32, kind="ExternalOutput").ap()
    st_out = nc.dram_tensor("st_out", [128, 32], F32, kind="ExternalOutput").ap()

    s_w = stack.enter_context(nc.sbuf_tensor("s_w", [128, 8, 16, 128], F32))
    s_x = stack.enter_context(nc.sbuf_tensor("s_x", [128, 2, 8, TT], F32))
    s_q = stack.enter_context(nc.sbuf_tensor("s_q", [128, 16, TT], F32))
    s_sq = stack.enter_context(nc.sbuf_tensor("s_sq", [128, 16, TT], F32))
    s_st = stack.enter_context(nc.sbuf_tensor("s_st", [128, 16, 2, 4], F32))
    s_sto = stack.enter_context(nc.sbuf_tensor("s_sto", [128, 32], F32))
    ps = [stack.enter_context(nc.psum_tensor(f"ps{i}", [128, TT], F32)) for i in range(4)]

    with stack:
        p.dma("y", lambda eng: eng.dma_start(out=s_w[:], in_=wq.rearrange("(kc p) (oc f) -> p kc oc f", p=128, f=128)),
              writes=("w",))
        xv = s3_fm.rearrange("(kc p) t -> p kc t", p=128)
        qv = q_out.rearrange("(oc p) t -> p oc t", p=128)
        for tt in range(TOK // TT):
            pb = tt % 2
            p.dma("y", lambda eng, tt=tt, pb=pb: eng.dma_start(out=s_x[:, pb], in_=xv[:, :, tt * TT:(tt + 1) * TT]),
                  writes=(f"x{pb}",))
            for oc in range(16):
                bi = oc % 4
                pairs = [(s_w[:, kc, oc, :], s_x[:, pb, kc, :]) for kc in range(8)]
                mm_accum(p, ps[bi][:], pairs, f"pp{bi}", (f"x{pb}", "w"))
                p.emit("s", lambda eng, oc=oc, bi=bi: eng.copy(s_q[:, oc, :], ps[bi][:]),
                       reads=(f"pp{bi}",), writes=("q",))
            p.dma("y", lambda eng, tt=tt: eng.dma_start(out=qv[:, :, tt * TT:(tt + 1) * TT], in_=s_q[:]),
                  reads=("q",))
            p.emit("s", lambda eng: eng.activation(s_sq[:], s_q[:], ACT.Square),
                   reads=("q",), writes=("sq",))
            for oc in range(16):
                p.emit("v", lambda eng, oc=oc, tt=tt: eng.tensor_reduce(
                           s_st[:, oc, 0, tt:tt + 1], s_q[:, oc, :], axis=AX, op=ALU.add),
                       reads=("q",), writes=("st",))
                p.emit("v", lambda eng, oc=oc, tt=tt: eng.tensor_reduce(
                           s_st[:, oc, 1, tt:tt + 1], s_sq[:, oc, :], axis=AX, op=ALU.add),
                       reads=("sq",), writes=("st",))
        p.emit("v", lambda eng: eng.tensor_reduce(
                   s_sto[:].rearrange("p (a b) -> p a b", b=1), s_st[:].rearrange("p a b c -> p (a b) c"), axis=AX, op=ALU.add),
               reads=("st",), writes=("sto",))
        p.dma("y", lambda eng: eng.dma_start(out=st_out, in_=s_sto[:]), reads=("sto",))
        p.finish_outputs("y")
        return p.build()


def host_l5_inputs(ins, s3f_by_core):
    wq = ins["L1_ffn_w_q"]
    return [{"s3_fm": np.ascontiguousarray(s3f_by_core[c]), "wq": wq} for c in range(NCORES)]


def build_l6():
    nc = bass.Bass("TRN2", target_bir_lowering=False)
    stack = ExitStack()
    p = Prog(nc, stack)
    NTB = TOK // 128

    q_fm = nc.dram_tensor("q_fm", [2048, TOK], F32, kind="ExternalInput").ap()
    stats_all = nc.dram_tensor("stats_all", [NCORES, 128, 32], F32, kind="ExternalInput").ap()
    s3_tm = nc.dram_tensor("s3_tm", [TOK, DM], F32, kind="ExternalInput").ap()
    keys_t = nc.dram_tensor("keys_t", [128, 2, 8, 256], F32, kind="ExternalInput").ap()
    gamma_r = nc.dram_tensor("gamma_r", [128, 16], F32, kind="ExternalInput").ap()
    beta_r = nc.dram_tensor("beta_r", [128, 16], F32, kind="ExternalInput").ap()
    down_e = nc.dram_tensor("down_e", [65536, 1024], F32, kind="ExternalInput").ap()
    up_e = nc.dram_tensor("up_e", [65536, 1024], F32, kind="ExternalInput").ap()
    fin_out = nc.dram_tensor("fin_out", [TOK, DM], F32, kind="ExternalOutput").ap()

    s_keys = stack.enter_context(nc.sbuf_tensor("s_keys", [128, 2, 8, 256], F32))
    s_stats = stack.enter_context(nc.sbuf_tensor("s_stats", [128, 8, 32], F32))
    s_mr = stack.enter_context(nc.sbuf_tensor("s_mr", [128, 16, 2], F32))
    s_gam = stack.enter_context(nc.sbuf_tensor("s_gam", [128, 16], F32))
    s_bet = stack.enter_context(nc.sbuf_tensor("s_bet", [128, 16], F32))
    s_qt = stack.enter_context(nc.sbuf_tensor("s_qt", [128, 2, 16, 128], F32))
    s_x3 = stack.enter_context(nc.sbuf_tensor("s_x3", [128, 2, 1024], F32))
    s_sim = stack.enter_context(nc.sbuf_tensor("s_sim", [128, 256], F32))
    s_tv = stack.enter_context(nc.sbuf_tensor("s_tv", [128, 2, 8, 8], F32))
    s_tiu = stack.enter_context(nc.sbuf_tensor("s_tiu", [128, 2, 8, 8], U32))
    s_tif = stack.enter_context(nc.sbuf_tensor("s_tif", [128, 2, 8, 8], F32))
    s_as = stack.enter_context(nc.sbuf_tensor("s_as", [128, 8, 64], F32))
    s_ai = stack.enter_context(nc.sbuf_tensor("s_ai", [128, 8, 64], F32))
    s_sc8 = stack.enter_context(nc.sbuf_tensor("s_sc8", [128, 8, 8], F32))
    s_pku = stack.enter_context(nc.sbuf_tensor("s_pku", [128, 8, 8], U32))
    s_pkdm = stack.enter_context(nc.sbuf_tensor("s_pkdm", [128, 2, 8, 8], F32))
    s_msk = stack.enter_context(nc.sbuf_tensor("s_msk", [128, 8, 8], F32))
    s_ixg = stack.enter_context(nc.sbuf_tensor("s_ixg", [128, 2, 8, 8], F32))
    s_eid = stack.enter_context(nc.sbuf_tensor("s_eid", [128, 8, 8], U32))
    s_wd = stack.enter_context(nc.sbuf_tensor("s_wd", [128, 2, 1024], F32))
    s_wu = stack.enter_context(nc.sbuf_tensor("s_wu", [128, 2, 1024], F32))
    s_scr = stack.enter_context(nc.sbuf_tensor("s_scr", [128, 2, 1024], F32))
    s_hm = stack.enter_context(nc.sbuf_tensor("s_hm", [128, 64], F32))
    s_sm = stack.enter_context(nc.sbuf_tensor("s_sm", [128, 64], F32))
    s_smx = stack.enter_context(nc.sbuf_tensor("s_smx", [128, 8, 1], F32))
    s_acc = stack.enter_context(nc.sbuf_tensor("s_acc", [128, 2, 1024], F32))
    s_r4 = stack.enter_context(nc.sbuf_tensor("s_r4", [128, 1024], F32))
    s_ln = stack.enter_context(nc.sbuf_tensor("s_ln", [128, 4], F32))
    ps = [stack.enter_context(nc.psum_tensor(f"ps{i}", [128, 256], F32)) for i in range(2)]

    with stack:
        p.dma("y", lambda eng: eng.dma_start(out=s_keys[:], in_=keys_t), writes=("keys",))
        p.dma("y", lambda eng: eng.dma_start(out=s_stats[:], in_=stats_all.rearrange("c p n -> p c n")), writes=("stats",))
        p.dma("y", lambda eng: eng.dma_start(out=s_gam[:], in_=gamma_r), writes=("gb",))
        p.dma("y", lambda eng: eng.dma_start(out=s_bet[:], in_=beta_r), writes=("gb",))
        for c in range(1, NCORES):
            p.emit("v", lambda eng, c=c: eng.tensor_tensor(s_stats[:, 0, :], s_stats[:, 0, :], s_stats[:, c, :], op=ALU.add),
                   reads=("stats",), writes=("stats",))
        stv = s_stats[:, 0, :].rearrange("p (oc two) -> p oc two", two=2)
        inv_n = 1.0 / (B * S)
        p.emit("v", lambda eng: eng.tensor_scalar(s_mr[:, :, 0], stv[:, :, 0], inv_n, None, op0=ALU.mult),
               reads=("stats",), writes=("mr",))
        p.emit("v", lambda eng: eng.tensor_tensor(s_mr[:, :, 1], s_mr[:, :, 0], s_mr[:, :, 0], op=ALU.mult),
               reads=("mr",), writes=("mr2",))
        p.emit("v", lambda eng: eng.scalar_tensor_tensor(out=s_mr[:, :, 1], in0=stv[:, :, 1], scalar=inv_n,
                                                         in1=s_mr[:, :, 1], op0=ALU.mult, op1=ALU.subtract),
               reads=("stats", "mr2"), writes=("mr2",))
        p.emit("v", lambda eng: eng.tensor_scalar(s_mr[:, :, 1], s_mr[:, :, 1], 1e-5, None, op0=ALU.add),
               reads=("mr2",), writes=("mr2",))
        p.emit("s", lambda eng: eng.activation(s_mr[:, :, 1], s_mr[:, :, 1], ACT.Sqrt),
               reads=("mr2",), writes=("mr2",))
        p.emit("v", lambda eng: eng.reciprocal(s_mr[:, :, 1], s_mr[:, :, 1]),
               reads=("mr2",), writes=("mr2",))

        p.emit("v", lambda eng: eng.tensor_tensor(s_gam[:], s_gam[:], s_mr[:, :, 1], op=ALU.mult),
               reads=("gb", "mr2"), writes=("gb",))

        qv = q_fm.rearrange("(oc p) t -> p oc t", p=128)
        for tb in range(NTB):
            pb = tb % 2
            p.dma("y", lambda eng, tb=tb, pb=pb: eng.dma_start(out=s_qt[:, pb], in_=qv[:, :, tb * 128:(tb + 1) * 128]),
                  writes=(f"qt{pb}",))
            p.dma("y", lambda eng, tb=tb, pb=pb: eng.dma_start(out=s_x3[:, pb], in_=s3_tm[tb * 128:(tb + 1) * 128, :]),
                  writes=(f"x3{pb}",))
            p.emit("v", lambda eng, pb=pb: eng.tensor_tensor(
                       s_qt[:, pb], s_qt[:, pb], s_mr[:, :, 0:1].broadcast_to([128, 16, 128]), op=ALU.subtract),
                   reads=(f"qt{pb}", "mr"), writes=(f"qt{pb}",))
            p.emit("v", lambda eng, pb=pb: eng.tensor_tensor(
                       s_qt[:, pb], s_qt[:, pb], s_gam[:].unsqueeze(2).broadcast_to([128, 16, 128]), op=ALU.mult),
                   reads=(f"qt{pb}", "gb"), writes=(f"qt{pb}",))
            p.emit("v", lambda eng, pb=pb: eng.tensor_tensor(
                       s_qt[:, pb], s_qt[:, pb], s_bet[:].unsqueeze(2).broadcast_to([128, 16, 128]), op=ALU.add),
                   reads=(f"qt{pb}", "gb"), writes=(f"qt{pb}",))
            for p2 in range(2):
                for hd in range(8):
                    oc = p2 * 8 + hd
                    bi = oc % 2
                    p.emit("t", lambda eng, oc=oc, p2=p2, hd=hd, pb=pb, bi=bi: eng.matmul(
                               ps[bi][:], s_qt[:, pb, oc, :], s_keys[:, p2, hd, :], start=True, stop=True),
                           reads=(f"qt{pb}", "keys"), writes=(f"pp{bi}",))
                    p.emit("s", lambda eng, bi=bi: eng.copy(s_sim[:], ps[bi][:]),
                           reads=(f"pp{bi}",), writes=("sim",))
                    p.emit("v", lambda eng, p2=p2, hd=hd: eng.max(out=s_tv[:, p2, hd, :], in_=s_sim[:]),
                           reads=("sim",), writes=("tv",))
                    p.emit("v", lambda eng, p2=p2, hd=hd: eng.max_index(s_tiu[:, p2, hd, :], s_tv[:, p2, hd, :], s_sim[:]),
                           reads=("sim", "tv"), writes=("tiu",))
            p.emit("v", lambda eng: eng.tensor_copy(s_tif[:], s_tiu[:]), reads=("tiu",), writes=("tif",))
            for hd in range(8):
                p.emit("v", lambda eng, hd=hd: eng.tensor_tensor(
                           s_as[:, hd, :].rearrange("p (i j) -> p i j", i=8),
                           s_tv[:, 0, hd, :].unsqueeze(2).broadcast_to([128, 8, 8]),
                           s_tv[:, 1, hd, :].unsqueeze(1).broadcast_to([128, 8, 8]), op=ALU.add),
                       reads=("tv",), writes=("as",))
                p.emit("v", lambda eng, hd=hd: eng.scalar_tensor_tensor(
                           out=s_ai[:, hd, :].rearrange("p (i j) -> p i j", i=8),
                           in0=s_tif[:, 0, hd, :].unsqueeze(2).broadcast_to([128, 8, 8]), scalar=256.0,
                           in1=s_tif[:, 1, hd, :].unsqueeze(1).broadcast_to([128, 8, 8]), op0=ALU.mult, op1=ALU.add),
                       reads=("tif",), writes=("ai",))
            for hd in range(8):
                p.emit("v", lambda eng, hd=hd: eng.max(out=s_sc8[:, hd, :], in_=s_as[:, hd, :]),
                       reads=("as",), writes=("sc8",))
                p.emit("v", lambda eng, hd=hd: eng.max_index(s_pku[:, hd, :], s_sc8[:, hd, :], s_as[:, hd, :]),
                       reads=("as", "sc8"), writes=("pku",))
            p.emit("v", lambda eng: eng.tensor_scalar(s_eid[:], s_pku[:], 3, None, op0=ALU.logical_shift_right),
                   reads=("pku",), writes=("eidtmp",))
            p.emit("v", lambda eng: eng.tensor_copy(s_pkdm[:, 0], s_eid[:]), reads=("eidtmp",), writes=("pkdm",))
            p.emit("v", lambda eng: eng.tensor_scalar(s_eid[:], s_pku[:], 7, None, op0=ALU.bitwise_and),
                   reads=("pku",), writes=("eidtmp",))
            p.emit("v", lambda eng: eng.tensor_copy(s_pkdm[:, 1], s_eid[:]), reads=("eidtmp",), writes=("pkdm",))
            p.emit("v", lambda eng: eng.memset(s_ixg[:], 0.0), writes=("ixg",))
            for j in range(8):
                for si in range(2):
                    p.emit("v", lambda eng, j=j, si=si: eng.tensor_scalar(
                               s_msk[:], s_pkdm[:, si], float(j), None, op0=ALU.is_equal),
                           reads=("pkdm",), writes=("msk",))
                    p.emit("v", lambda eng, j=j, si=si: eng.tensor_tensor(
                               s_msk[:], s_msk[:], s_tif[:, si, :, j:j + 1].broadcast_to([128, 8, 8]), op=ALU.mult),
                           reads=("msk", "tif"), writes=("msk",))
                    p.emit("v", lambda eng, si=si: eng.tensor_tensor(
                               s_ixg[:, si], s_ixg[:, si], s_msk[:], op=ALU.add),
                           reads=("msk", "ixg"), writes=("ixg",))
            p.emit("v", lambda eng: eng.scalar_tensor_tensor(
                       out=s_msk[:], in0=s_ixg[:, 0], scalar=256.0, in1=s_ixg[:, 1], op0=ALU.mult, op1=ALU.add),
                   reads=("ixg",), writes=("msk",))
            p.emit("v", lambda eng: eng.tensor_copy(s_eid[:], s_msk[:]), reads=("msk",), writes=("eid",))
            p.emit("v", lambda eng: eng.tensor_reduce(s_smx[:], s_sc8[:], axis=AX, op=ALU.max),
                   reads=("sc8",), writes=("smx",))
            p.emit("v", lambda eng: eng.tensor_tensor(
                       s_sm[:].rearrange("p (h k) -> p h k", h=8), s_sc8[:], s_smx[:].broadcast_to([128, 8, 8]), op=ALU.subtract),
                   reads=("sc8", "smx"), writes=("sm",))
            p.emit("s", lambda eng: eng.activation(s_sm[:], s_sm[:], ACT.Exp),
                   reads=("sm",), writes=("sm",))
            p.emit("v", lambda eng: eng.tensor_reduce(s_smx[:], s_sm[:].rearrange("p (h k) -> p h k", h=8), axis=AX, op=ALU.add),
                   reads=("sm",), writes=("smx",))
            p.emit("v", lambda eng: eng.reciprocal(s_smx[:], s_smx[:]),
                   reads=("smx",), writes=("smx",))
            p.emit("v", lambda eng: eng.tensor_tensor(
                       s_sm[:].rearrange("p (h k) -> p h k", h=8), s_sm[:].rearrange("p (h k) -> p h k", h=8),
                       s_smx[:].broadcast_to([128, 8, 8]), op=ALU.mult),
                   reads=("sm", "smx"), writes=("sm",))
            for hk in range(64):
                gb_ = hk % 2
                hd, kk = hk // 8, hk % 8
                p.dma("g", lambda eng, hd=hd, kk=kk, gb_=gb_: eng.indirect_dma_start(
                          out=s_wd[:, gb_, :], out_offset=None, in_=down_e,
                          in_offset=bass.IndirectOffsetOnAxis(ap=s_eid[:, hd, kk:kk + 1], axis=0)),
                      reads=("eid",), writes=(f"wd{gb_}",))
                p.emit("v", lambda eng, gb_=gb_, pb=pb: eng.tensor_tensor(
                           s_scr[:, gb_, :], s_wd[:, gb_, :], s_x3[:, pb, :], op=ALU.mult),
                       reads=(f"wd{gb_}", f"x3{pb}"), writes=(f"scr{gb_}",))
                p.emit("s", lambda eng, hk=hk, gb_=gb_: eng.activation(
                           s_scr[:, gb_, :], s_scr[:, gb_, :], ACT.Identity, accum_out=s_hm[:, hk:hk + 1]),
                       reads=(f"scr{gb_}",), writes=(f"scr{gb_}", "hm"))
            p.emit("s", lambda eng: eng.activation(s_hm[:], s_hm[:], ACT.Gelu_apprx_tanh),
                   reads=("hm",), writes=("hm",))
            p.emit("v", lambda eng: eng.tensor_tensor(s_sm[:], s_sm[:], s_hm[:], op=ALU.mult),
                   reads=("sm", "hm"), writes=("sm",))
            p.emit("v", lambda eng: eng.memset(s_acc[:, 0, :], 0.0), writes=("acc0",))
            for hk in range(64):
                gb_ = hk % 2
                hd, kk = hk // 8, hk % 8
                p.dma("g", lambda eng, hd=hd, kk=kk, gb_=gb_: eng.indirect_dma_start(
                          out=s_wu[:, gb_, :], out_offset=None, in_=up_e,
                          in_offset=bass.IndirectOffsetOnAxis(ap=s_eid[:, hd, kk:kk + 1], axis=0)),
                      reads=("eid",), writes=(f"wu{gb_}",))
                src_k, dst_k = f"acc{hk % 2}", f"acc{(hk + 1) % 2}"
                p.emit("v", lambda eng, hk=hk, gb_=gb_: eng.scalar_tensor_tensor(
                           out=s_acc[:, (hk + 1) % 2, :], in0=s_wu[:, gb_, :], scalar=s_sm[:, hk:hk + 1],
                           in1=s_acc[:, hk % 2, :], op0=ALU.mult, op1=ALU.add),
                       reads=(f"wu{gb_}", "sm", src_k), writes=(dst_k,))
            p.emit("v", lambda eng, pb=pb: eng.tensor_tensor(s_r4[:], s_x3[:, pb, :], s_acc[:, 0, :], op=ALU.add),
                   reads=(f"x3{pb}", "acc0"), writes=("r4",))
            p.emit("v", lambda eng: eng.tensor_reduce(s_ln[:, 0:1], s_r4[:], axis=AX, op=ALU.add),
                   reads=("r4",), writes=("ln",))
            p.emit("v", lambda eng: eng.tensor_scalar(s_ln[:, 0:1], s_ln[:, 0:1], 1.0 / 1024.0, None, op0=ALU.mult),
                   reads=("ln",), writes=("ln",))
            p.emit("v", lambda eng: eng.tensor_scalar(s_r4[:], s_r4[:], s_ln[:, 0:1], None, op0=ALU.subtract),
                   reads=("r4", "ln"), writes=("r4",))
            p.emit("s", lambda eng: eng.activation(s_acc[:, 1, :], s_r4[:], ACT.Square),
                   reads=("r4",), writes=("acc1",))
            p.emit("v", lambda eng: eng.tensor_reduce(s_ln[:, 1:2], s_acc[:, 1, :], axis=AX, op=ALU.add),
                   reads=("acc1",), writes=("ln2",))
            p.emit("v", lambda eng: eng.tensor_scalar(s_ln[:, 1:2], s_ln[:, 1:2], 1.0 / 1024.0, 1e-5, op0=ALU.mult, op1=ALU.add),
                   reads=("ln2",), writes=("ln2",))
            p.emit("s", lambda eng: eng.activation(s_ln[:, 1:2], s_ln[:, 1:2], ACT.Sqrt),
                   reads=("ln2",), writes=("ln2",))
            p.emit("v", lambda eng: eng.reciprocal(s_ln[:, 1:2], s_ln[:, 1:2]),
                   reads=("ln2",), writes=("ln2",))
            p.emit("v", lambda eng: eng.tensor_scalar(s_r4[:], s_r4[:], s_ln[:, 1:2], None, op0=ALU.mult),
                   reads=("r4", "ln2"), writes=("r4",))
            p.dma("y", lambda eng, tb=tb: eng.dma_start(out=fin_out[tb * 128:(tb + 1) * 128, :], in_=s_r4[:]),
                  reads=("r4",))
        p.finish_outputs("y")
        return p.build()


def host_l6_inputs(ins, q_by_core, stats_by_core, s3t_by_core):
    keys = ins["L1_ffn_keys"]
    keys_t = np.ascontiguousarray(keys.transpose(3, 2, 0, 1))
    stats_all = np.ascontiguousarray(np.stack([stats_by_core[c] for c in range(NCORES)], axis=0))
    gam = ins["L1_ffn_bn_gamma"]; bet = ins["L1_ffn_bn_beta"]
    maps = []
    for c in range(NCORES):
        maps.append({
            "q_fm": np.ascontiguousarray(q_by_core[c]),
            "stats_all": stats_all,
            "s3_tm": np.ascontiguousarray(s3t_by_core[c]),
            "keys_t": keys_t,
            "gamma_r": np.ascontiguousarray(gam.reshape(16, 128).T),
            "beta_r": np.ascontiguousarray(bet.reshape(16, 128).T),
            "down_e": ins["L1_ffn_down_embed"],
            "up_e": ins["L1_ffn_up_embed"],
        })
    return maps


def kernel(**ins):
    core_ids = list(range(NCORES))

    r1 = run_bass_kernel_spmd(build_l1(), host_l1_inputs(ins), core_ids=core_ids)
    xs_by_core = {c: np.asarray(r1.results[c]["xs_out"]) for c in range(NCORES)}
    z_by_core = {c: np.asarray(r1.results[c]["z_out"]) for c in range(NCORES)}

    r2 = run_bass_kernel_spmd(build_l2(), host_l2_inputs(ins, xs_by_core, z_by_core), core_ids=core_ids)
    y_by_core = {c: np.asarray(r2.results[c]["y_out"]) for c in range(NCORES)}

    r3 = run_bass_kernel_spmd(build_l3(), host_l3_inputs(ins, y_by_core), core_ids=core_ids)
    s2_by_core = {c: np.asarray(r3.results[c]["s2_out"]) for c in range(NCORES)}

    r4 = run_bass_kernel_spmd(build_l4(), host_l4_inputs(ins, s2_by_core), core_ids=core_ids)
    s3f_by_core = {c: np.asarray(r4.results[c]["s3f_out"]) for c in range(NCORES)}
    s3t_by_core = {c: np.asarray(r4.results[c]["s3t_out"]) for c in range(NCORES)}

    r5 = run_bass_kernel_spmd(build_l5(), host_l5_inputs(ins, s3f_by_core), core_ids=core_ids)
    q_by_core = {c: np.asarray(r5.results[c]["q_out"]) for c in range(NCORES)}
    stats_by_core = {c: np.asarray(r5.results[c]["st_out"]) for c in range(NCORES)}

    r6 = run_bass_kernel_spmd(build_l6(), host_l6_inputs(ins, q_by_core, stats_by_core, s3t_by_core), core_ids=core_ids)
    out = np.empty((B, S, DM), np.float32)
    for c in range(NCORES):
        b, h = c // 2, c % 2
        out[b, h * TOK:(h + 1) * TOK, :] = np.asarray(r6.results[c]["fin_out"])
    return out


# version 04
# speedup vs baseline: 1.0164x; 1.0164x over previous
import numpy as np
from contextlib import ExitStack

import concourse.bass as bass
import concourse.mybir as mybir
from concourse.bass_utils import run_bass_kernel_spmd
from bass_rust import ReduceOp

F32 = mybir.dt.float32
BF16 = mybir.dt.bfloat16
U32 = mybir.dt.uint32
U16 = mybir.dt.uint16
ALU = mybir.AluOpType
ACT = mybir.ActivationFunctionType
AX = mybir.AxisListType.X

NCORES = 8
B, S, DM = 4, 4096, 1024
DI = 2048
NST = 16
DTR = 64
TOK = 2048
DH = DI // 2


class Prog:

    ENGS = ("t", "v", "s", "g", "y")
    NPOOL = {"y": 12, "g": 6, "s": 4, "v": 2}

    def __init__(self, nc, stack):
        self.nc = nc
        self.q = {e: [] for e in self.ENGS}
        self.cnt = {e: 0 for e in self.ENGS}
        self.sem = {e: stack.enter_context(nc.semaphore(f"sem_{e}")) for e in self.ENGS}
        self.dsem = {}
        self.dcnt = {}
        self.drr = {e: 0 for e in self.ENGS}
        for e, np_ in self.NPOOL.items():
            for i in range(np_):
                self.dsem[(e, i)] = stack.enter_context(nc.semaphore(f"dma_{e}{i}"))
                self.dcnt[(e, i)] = 0
        self.state = {}
        self.wm = {e: {} for e in self.ENGS}
        self.n_emitted = 0

    def _sem_of(self, key):
        return self.sem[key] if isinstance(key, str) else self.dsem[key]

    def _need(self, reads, writes):
        need = {}
        for k in reads:
            st = self.state.get(k)
            if st and st["w"] is not None:
                we, wc = st["w"]
                need[we] = max(need.get(we, 0), wc)
        for k in writes:
            st = self.state.get(k)
            if st:
                if st["w"] is not None:
                    we, wc = st["w"]
                    need[we] = max(need.get(we, 0), wc)
                for re_, rc in st["r"].items():
                    need[re_] = max(need.get(re_, 0), rc)
        return need

    def _filter_waits(self, e, need):
        out = []
        for wk, wc in need.items():
            if self.wm[e].get(wk, 0) < wc:
                self.wm[e][wk] = wc
                out.append((self._sem_of(wk), wc))
        return out

    def _bookkeep(self, reads, writes, semkey, target):
        for k in writes:
            self.state[k] = {"w": (semkey, target), "r": {}}
        for k in reads:
            st = self.state.setdefault(k, {"w": None, "r": {}})
            st["r"][semkey] = target
        self.n_emitted += 1

    def emit(self, e, fn, reads=(), writes=()):
        waits = self._filter_waits(e, self._need(reads, writes))
        self.cnt[e] += 1
        target = self.cnt[e]
        sem = self.sem[e]

        def thunk(eng, fn=fn, waits=waits, sem=sem):
            for ws, wc in waits:
                eng.wait_ge(ws, wc)
            fn(eng).then_inc(sem, 1)

        self.q[e].append(thunk)
        self._bookkeep(reads, writes, e, target)
        return (e, target)

    def dma(self, e, fn, reads=(), writes=()):
        need = self._need(reads, writes)
        i = self.drr[e] % self.NPOOL[e]
        self.drr[e] += 1
        key = (e, i)
        if self.dcnt[key] > 0:
            need[key] = max(need.get(key, 0), self.dcnt[key])
        waits = self._filter_waits(e, need)
        self.dcnt[key] += 16
        target = self.dcnt[key]
        sem = self.dsem[key]

        def thunk(eng, fn=fn, waits=waits, sem=sem):
            for ws, wc in waits:
                eng.wait_ge(ws, wc)
            fn(eng).then_inc(sem, 16)

        self.q[e].append(thunk)
        self._bookkeep(reads, writes, key, target)
        return (key, target)

    def raw(self, e, fn):
        self.q[e].append(lambda eng, fn=fn: fn(eng))

    def wait_keys(self, e, keys):
        need = self._need(keys, keys)
        specs = self._filter_waits(e, need)
        if specs:
            def thunk(eng, specs=specs):
                for ws, wc in specs:
                    eng.wait_ge(ws, wc)
            self.q[e].append(thunk)

    def barrier(self):
        snap = [(("",), 0)]
        specs_all = {}
        for e2 in self.ENGS:
            if self.cnt[e2] > 0:
                specs_all[e2] = self.cnt[e2]
        for k, c in self.dcnt.items():
            if c > 0:
                specs_all[k] = c
        for e in self.ENGS:
            specs = []
            for wk, wc in specs_all.items():
                if wk == e:
                    continue
                if self.wm[e].get(wk, 0) < wc:
                    self.wm[e][wk] = wc
                    specs.append((self._sem_of(wk), wc))
            if specs:
                def thunk(eng, specs=specs):
                    for ws, wc in specs:
                        eng.wait_ge(ws, wc)
                self.q[e].append(thunk)

    def finish_outputs(self, e="y"):
        specs = [(self.dsem[k], c) for k, c in self.dcnt.items() if k[0] == e and c > 0]
        def thunk(eng, specs=specs):
            for ws, wc in specs:
                eng.wait_ge(ws, wc)
        self.q[e].append(thunk)

    def build(self):
        nc = self.nc
        names = {"t": "tensor", "v": "vector", "s": "scalar", "g": "gpsimd", "y": "sync"}
        with nc.Block() as blk:
            for e in self.ENGS:
                dec = getattr(blk, names[e])
                def body(eng, _q=tuple(self.q[e])):
                    for th in _q:
                        th(eng)
                dec(body)
        return nc


def mm_accum(p, psum_ap, pairs, key_out, key_reads, tile_position=None):
    def fn(eng):
        ins = None
        n = len(pairs)
        for i, (lt, rh) in enumerate(pairs):
            ins = eng.matmul(psum_ap, lt, rh, start=(i == 0), stop=(i == n - 1))
        return ins
    return p.emit("t", fn, reads=tuple(key_reads), writes=(key_out,))


def transpose_128(p, psum_ap, in_ap, ident_ap, key_out, key_in):
    return p.emit("t", lambda eng: eng.transpose(psum_ap, in_ap, ident_ap),
                  reads=(key_in, "ident"), writes=(key_out,))


def make_identity(p, s_iota, s_ident):
    p.emit("g", lambda eng: eng.iota(s_iota[:], pattern=[[1, 128]], base=0,
                                     channel_multiplier=-1,
                                     allow_small_or_imprecise_dtypes=True),
           writes=("iota_raw",))
    p.emit("v", lambda eng: eng.tensor_scalar(s_ident[:], s_iota[:], 0.0, None, op0=ALU.is_equal),
           reads=("iota_raw",), writes=("ident",))


def build_l1():
    nc = bass.Bass("TRN2", target_bir_lowering=False)
    stack = ExitStack()
    p = Prog(nc, stack)

    x_fm = nc.dram_tensor("x_fm", [DM, S], F32, kind="ExternalInput").ap()
    winx = nc.dram_tensor("winx", [DM, DH], F32, kind="ExternalInput").ap()
    winz = nc.dram_tensor("winz", [DM, DH], F32, kind="ExternalInput").ap()
    binx = nc.dram_tensor("binx", [128, DH // 128], F32, kind="ExternalInput").ap()
    binz = nc.dram_tensor("binz", [128, DH // 128], F32, kind="ExternalInput").ap()
    xs_out = nc.dram_tensor("xs_out", [DH, S], F32, kind="ExternalOutput").ap()
    z_out = nc.dram_tensor("z_out", [DH, S], F32, kind="ExternalOutput").ap()

    TT = 512
    NTB = S // TT
    s_w = stack.enter_context(nc.sbuf_tensor("s_w", [128, 2, 8, 8, 128], F32))
    s_x = stack.enter_context(nc.sbuf_tensor("s_x", [128, 2, 8, TT], F32))
    s_b = stack.enter_context(nc.sbuf_tensor("s_b", [128, 2, 8], F32))
    s_o = stack.enter_context(nc.sbuf_tensor("s_o", [128, 2, 2, 8, TT], F32))
    ps = [stack.enter_context(nc.psum_tensor(f"ps{i}", [128, TT], F32)) for i in range(4)]

    with stack:
        p.dma("y", lambda eng: eng.dma_start(out=s_w[:, 0], in_=winx.rearrange("(kc p) (nc8 f) -> p kc nc8 f", p=128, f=128)),
              writes=("wx",))
        p.dma("y", lambda eng: eng.dma_start(out=s_w[:, 1], in_=winz.rearrange("(kc p) (nc8 f) -> p kc nc8 f", p=128, f=128)),
              writes=("wz",))
        p.dma("y", lambda eng: eng.dma_start(out=s_b[:, 0], in_=binx),
              writes=("bx",))
        p.dma("y", lambda eng: eng.dma_start(out=s_b[:, 1], in_=binz),
              writes=("bz",))

        x_r = x_fm.rearrange("(kc p) t -> p kc t", p=128)
        nb = 0
        for tb in range(NTB):
            pb = tb % 2
            p.dma("y", lambda eng, tb=tb, pb=pb: eng.dma_start(
                      out=s_x[:, pb], in_=x_r[:, :, tb * TT:(tb + 1) * TT]),
                  writes=(f"x{pb}",))
            for xz in range(2):
                for nc8 in range(8):
                    bi = nb % 4
                    bank = ps[bi]
                    nb += 1
                    pairs = [(s_w[:, xz, kc, nc8, :], s_x[:, pb, kc, :]) for kc in range(8)]
                    kmm = f"ps{bi}"
                    mm_accum(p, bank[:], pairs, kmm, (f"x{pb}", "wx" if xz == 0 else "wz"))
                    func = ACT.Silu if xz == 0 else ACT.Identity
                    def ev(eng, bank=bank, xz=xz, nc8=nc8, func=func, pb=pb):
                        return eng.activation(s_o[:, pb, xz, nc8, :], bank[:], func,
                                              bias=s_b[:, xz, nc8:nc8 + 1], scale=1.0)
                    p.emit("s", ev, reads=(kmm, "bx" if xz == 0 else "bz"),
                           writes=(f"o{pb}_{xz}_{nc8}",))
            for xz, out_t in ((0, xs_out), (1, z_out)):
                p.dma("y", lambda eng, tb=tb, pb=pb, xz=xz, out_t=out_t: eng.dma_start(
                          out=out_t.rearrange("(nc8 p) t -> p nc8 t", p=128)[:, :, tb * TT:(tb + 1) * TT],
                          in_=s_o[:, pb, xz]),
                      reads=tuple(f"o{pb}_{xz}_{nc8}" for nc8 in range(8)))
        p.finish_outputs("y")
        return p.build()


def host_l1_inputs(ins):
    x = ins["L0_attn_x"]
    Win = ins["L0_attn_Win"]
    bin_ = ins["L0_attn_bin"]
    maps = []
    for c in range(NCORES):
        b, h = c // 2, c % 2
        maps.append({
            "x_fm": np.ascontiguousarray(x[b].T),
            "winx": np.ascontiguousarray(Win[:, h * DH:(h + 1) * DH]),
            "winz": np.ascontiguousarray(Win[:, DI + h * DH:DI + (h + 1) * DH]),
            "binx": np.ascontiguousarray(bin_[h * DH:(h + 1) * DH].reshape(-1, 128).T),
            "binz": np.ascontiguousarray(bin_[DI + h * DH:DI + (h + 1) * DH].reshape(-1, 128).T),
        })
    return maps


if __name__ == "__main__":
    pass


def build_l2():
    nc = bass.Bass("TRN2", target_bir_lowering=False)
    stack = ExitStack()
    p = Prog(nc, stack)

    TC = 256
    NCH = S // TC
    NT = DH // 128

    xs_full = nc.dram_tensor("xs_full", [DI, S], F32, kind="ExternalInput").ap()
    z_sl = nc.dram_tensor("z_sl", [DH, S], F32, kind="ExternalInput").ap()
    wx = nc.dram_tensor("wx", [DI, 96], F32, kind="ExternalInput").ap()
    bx_fm = nc.dram_tensor("bx_fm", [96, 1], F32, kind="ExternalInput").ap()
    bx_tm = nc.dram_tensor("bx_tm", [128, 96], F32, kind="ExternalInput").ap()
    wdt = nc.dram_tensor("wdt", [64, DH], F32, kind="ExternalInput").ap()
    bdt = nc.dram_tensor("bdt", [128, 8], F32, kind="ExternalInput").ap()
    alog = nc.dram_tensor("alog", [128, 8, NST], F32, kind="ExternalInput").ap()
    dp_r = nc.dram_tensor("dp_r", [128, 8], F32, kind="ExternalInput").ap()
    y_out = nc.dram_tensor("y_out", [DH, S], F32, kind="ExternalOutput").ap()
    DBG = False
    if DBG:
        dbg_xpf = nc.dram_tensor("dbg_xpf", [96, 256], F32, kind="ExternalOutput").ap()
        dbg_xpt = nc.dram_tensor("dbg_xpt", [128, 2, 96], F32, kind="ExternalOutput").ap()
        dbg_dl = nc.dram_tensor("dbg_dl", [128, 8, 256], F32, kind="ExternalOutput").ap()
        dbg_B = nc.dram_tensor("dbg_B", [128, 256, 16], F32, kind="ExternalOutput").ap()
        dbg_C = nc.dram_tensor("dbg_C", [128, 256, 16], F32, kind="ExternalOutput").ap()
        dbg_h = nc.dram_tensor("dbg_h", [128, 256, 16], F32, kind="ExternalOutput").ap()
        dbg_dA = nc.dram_tensor("dbg_dA", [128, 256, 16], F32, kind="ExternalOutput").ap()
        dbg_u = nc.dram_tensor("dbg_u", [128, 256, 16], F32, kind="ExternalOutput").ap()

    s_xs = stack.enter_context(nc.sbuf_tensor("s_xs", [128, 16, TC], F32))
    s_z = stack.enter_context(nc.sbuf_tensor("s_z", [128, 8, TC], F32))
    s_wx = stack.enter_context(nc.sbuf_tensor("s_wx", [128, 16, 96], F32))
    s_bxf = stack.enter_context(nc.sbuf_tensor("s_bxf", [96, 1], F32))
    s_bxt = stack.enter_context(nc.sbuf_tensor("s_bxt", [128, 96], F32))
    s_wdt = stack.enter_context(nc.sbuf_tensor("s_wdt", [64, 8, 128], F32))
    s_bdt = stack.enter_context(nc.sbuf_tensor("s_bdt", [128, 8], F32))
    s_A = stack.enter_context(nc.sbuf_tensor("s_A", [128, 8, NST], F32))
    s_dp = stack.enter_context(nc.sbuf_tensor("s_dp", [128, 8], F32))
    s_xpf = stack.enter_context(nc.sbuf_tensor("s_xpf", [96, TC], F32))
    s_xpt = stack.enter_context(nc.sbuf_tensor("s_xpt", [128, TC // 128, 96], F32))
    s_dl = stack.enter_context(nc.sbuf_tensor("s_dl", [128, 8, TC], F32))
    s_sc1 = stack.enter_context(nc.sbuf_tensor("s_sc1", [128, 8, TC], F32))
    s_dx = stack.enter_context(nc.sbuf_tensor("s_dx", [128, 8, TC], F32))
    s_B = stack.enter_context(nc.sbuf_tensor("s_B", [128, TC, NST], F32))
    s_C = stack.enter_context(nc.sbuf_tensor("s_C", [128, TC, NST], F32))
    s_dA = stack.enter_context(nc.sbuf_tensor("s_dA", [128, TC, NST], F32))
    s_u = stack.enter_context(nc.sbuf_tensor("s_u", [128, TC, NST], F32))
    s_h = stack.enter_context(nc.sbuf_tensor("s_h", [128, TC, NST], F32))
    s_st = stack.enter_context(nc.sbuf_tensor("s_st", [128, 8, NST], F32))
    s_y = stack.enter_context(nc.sbuf_tensor("s_y", [128, 8, TC], F32))
    s_sz = stack.enter_context(nc.sbuf_tensor("s_sz", [128, 8, TC], F32))
    ps = [stack.enter_context(nc.psum_tensor(f"ps{i}", [128, TC], F32)) for i in range(4)]

    with stack:
        p.dma("y", lambda eng: eng.dma_start(out=s_wx[:], in_=wx.rearrange("(kc p) m -> p kc m", p=128)), writes=("wx",))
        p.dma("y", lambda eng: eng.dma_start(out=s_bxf[:], in_=bx_fm), writes=("bxf",))
        p.dma("y", lambda eng: eng.dma_start(out=s_bxt[:], in_=bx_tm), writes=("bxt",))
        p.dma("y", lambda eng: eng.dma_start(out=s_wdt[:], in_=wdt.rearrange("k (nc8 m) -> k nc8 m", m=128)), writes=("wdt",))
        p.dma("y", lambda eng: eng.dma_start(out=s_bdt[:], in_=bdt), writes=("bdt",))
        p.dma("y", lambda eng: eng.dma_start(out=s_A[:], in_=alog), writes=("alog",))
        p.dma("y", lambda eng: eng.dma_start(out=s_dp[:], in_=dp_r), writes=("dpr",))
        p.emit("s", lambda eng: eng.activation(s_A[:], s_A[:], ACT.Exp),
               reads=("alog",), writes=("alog",))
        p.emit("v", lambda eng: eng.tensor_scalar(s_A[:], s_A[:], -1.0, None, op0=ALU.mult),
               reads=("alog",), writes=("A",))
        p.emit("v", lambda eng: eng.memset(s_st[:], 0.0), writes=("st",))

        xs_r = xs_full.rearrange("(kc p) t -> p kc t", p=128)
        z_r = z_sl.rearrange("(kc p) t -> p kc t", p=128)
        y_r = y_out.rearrange("(kc p) t -> p kc t", p=128)

        for cix in range(NCH):
            t0 = cix * TC
            p.dma("y", lambda eng, t0=t0: eng.dma_start(out=s_xs[:], in_=xs_r[:, :, t0:t0 + TC]), writes=("xs",))
            p.dma("y", lambda eng, t0=t0: eng.dma_start(out=s_z[:], in_=z_r[:, :, t0:t0 + TC]), writes=("z",))

            pairs = [(s_wx[:, kc, :], s_xs[:, kc, :]) for kc in range(16)]
            mm_accum(p, ps[0][0:96, :], pairs, "ps0", ("wx", "xs"))
            p.emit("s", lambda eng: eng.activation(s_xpf[:], ps[0][0:96, :], ACT.Identity, bias=s_bxf[:], scale=1.0),
                   reads=("ps0", "bxf"), writes=("xpf",))
            for tb in range(TC // 128):
                pairs = [(s_xs[:, kc, tb * 128:(tb + 1) * 128], s_wx[:, kc, :]) for kc in range(16)]
                mm_accum(p, ps[1][:, 0:96], pairs, "ps1", ("wx", "xs"))
                p.emit("v", lambda eng, tb=tb: eng.tensor_tensor(s_xpt[:, tb, :], ps[1][:, 0:96], s_bxt[:], op=ALU.add),
                       reads=("ps1", "bxt"), writes=(f"xpt{tb}",))

            for nc8 in range(8):
                bi = 2 + (nc8 % 2)
                mm_accum(p, ps[bi][:], [(s_wdt[:, nc8, :], s_xpf[0:64, :])], f"ps{bi}", ("wdt", "xpf"))
                p.emit("s", lambda eng, nc8=nc8, bi=bi: eng.activation(
                           s_dl[:, nc8, :], ps[bi][:], ACT.Identity, bias=s_bdt[:, nc8:nc8 + 1], scale=1.0),
                       reads=(f"ps{bi}", "bdt"), writes=("dl",))
            p.emit("s", lambda eng: eng.activation(s_sc1[:], s_dl[:], ACT.Abs),
                   reads=("dl",), writes=("sc1",))
            p.emit("s", lambda eng: eng.activation(s_sc1[:], s_sc1[:], ACT.Exp, scale=-1.0),
                   reads=("sc1",), writes=("sc1",))
            p.emit("s", lambda eng: eng.activation(s_sc1[:], s_sc1[:], ACT.Ln, bias=1.0, scale=1.0),
                   reads=("sc1",), writes=("sc1",))
            p.emit("v", lambda eng: eng.scalar_tensor_tensor(
                       out=s_dl[:], in0=s_dl[:], scalar=0.0, in1=s_sc1[:], op0=ALU.max, op1=ALU.add),
                   reads=("sc1", "dl"), writes=("dl",))
            p.emit("v", lambda eng: eng.tensor_tensor(s_dx[:], s_dl[:], s_xs[:, 0:8, :], op=ALU.mult),
                   reads=("dl", "xs"), writes=("dx",))

            for nm, sl, keyb in (("B", slice(64, 80), "B"), ("C", slice(80, 96), "C")):
                dst = s_B if nm == "B" else s_C
                for tb in range(TC // 128):
                    def st0(eng, dst=dst, sl=sl, tb=tb):
                        return eng.dma_start(
                            out=dst[0:1, tb * 128:(tb + 1) * 128, :],
                            in_=s_xpt[:, tb, sl])
                    p.dma("g", st0, reads=(f"xpt{tb}",), writes=(keyb,))
                k = 1
                while k < 128:
                    p.dma("g", lambda eng, dst=dst, k=k: eng.dma_start(out=dst[k:2 * k], in_=dst[0:k]),
                          reads=(keyb,), writes=(keyb,))
                    k *= 2

            for dc in range(NT):
                for n in range(NST):
                    p.emit("s", lambda eng, dc=dc, n=n: eng.activation(
                               s_dA[:, :, n], s_dl[:, dc, :], ACT.Exp, scale=s_A[:, dc, n:n + 1]),
                           reads=("dl", "A"), writes=("dA",))
                p.emit("v", lambda eng, dc=dc: eng.tensor_tensor(
                           s_u[:], s_dx[:, dc, :].unsqueeze(2).broadcast_to([128, TC, NST]), s_B[:], op=ALU.mult),
                       reads=("dx", "B"), writes=("u",))
                for n in range(NST):
                    p.emit("v", lambda eng, dc=dc, n=n: eng.tensor_tensor_scan(
                               s_h[:, :, n], s_dA[:, :, n], s_u[:, :, n], s_st[:, dc, n:n + 1],
                               op0=ALU.mult, op1=ALU.add),
                           reads=("dA", "u", "st"), writes=("h",))
                if DBG and cix == 0 and dc == 7:
                    p.dma("y", lambda eng: eng.dma_start(out=dbg_dA, in_=s_dA[:]), reads=("dA",))
                    p.dma("y", lambda eng: eng.dma_start(out=dbg_u, in_=s_u[:]), reads=("u",))
                p.emit("v", lambda eng, dc=dc: eng.tensor_copy(s_st[:, dc, :], s_h[:, TC - 1, :]),
                       reads=("h",), writes=("st",))
                p.emit("v", lambda eng: eng.tensor_tensor(s_dA[:], s_h[:], s_C[:], op=ALU.mult),
                       reads=("C", "h"), writes=("dA",))
                p.emit("v", lambda eng, dc=dc: eng.tensor_reduce(
                           s_y[:, dc, :], s_dA[:], axis=AX, op=ALU.add),
                       reads=("dA",), writes=(f"y{dc}",))

            if DBG and cix == 0:
                p.dma("y", lambda eng: eng.dma_start(out=dbg_xpf, in_=s_xpf[:]), reads=("xpf",))
                p.dma("y", lambda eng: eng.dma_start(out=dbg_xpt, in_=s_xpt[:]), reads=("xpt0", "xpt1"))
                p.dma("y", lambda eng: eng.dma_start(out=dbg_dl, in_=s_dl[:]), reads=("dl",))
                p.dma("y", lambda eng: eng.dma_start(out=dbg_B, in_=s_B[:]), reads=("B",))
                p.dma("y", lambda eng: eng.dma_start(out=dbg_C, in_=s_C[:]), reads=("C",))
                p.dma("y", lambda eng: eng.dma_start(out=dbg_h, in_=s_h[:]), reads=("h",))
            p.emit("s", lambda eng: eng.activation(s_sz[:], s_z[:], ACT.Silu),
                   reads=("z",), writes=("sz",))
            for dc in range(NT):
                p.emit("v", lambda eng, dc=dc: eng.scalar_tensor_tensor(
                           out=s_y[:, dc, :], in0=s_xs[:, dc, :], scalar=s_dp[:, dc:dc + 1],
                           in1=s_y[:, dc, :], op0=ALU.mult, op1=ALU.add),
                       reads=("xs", "dpr", f"y{dc}"), writes=(f"y{dc}",))
            p.emit("v", lambda eng: eng.tensor_tensor(s_y[:], s_y[:], s_sz[:], op=ALU.mult),
                   reads=("sz",) + tuple(f"y{dc}" for dc in range(NT)),
                   writes=tuple(f"y{dc}" for dc in range(NT)))
            p.dma("y", lambda eng, t0=t0: eng.dma_start(out=y_r[:, :, t0:t0 + TC], in_=s_y[:]),
                  reads=tuple(f"y{dc}" for dc in range(NT)))
        p.finish_outputs("y")
        return p.build()


def host_l2_inputs(ins, xs_by_core, z_by_core):
    Wx = ins["L0_attn_Wx"]; bx = ins["L0_attn_bx"]
    Wdt = ins["L0_attn_Wdt"]; bdt = ins["L0_attn_bdt"]
    Alog = ins["L0_attn_A_log"]; Dp = ins["L0_attn_Dp"]
    maps = []
    for c in range(NCORES):
        b, h = c // 2, c % 2
        own = xs_by_core[c]
        other = xs_by_core[2 * b + (1 - h)]
        xs_full = np.concatenate([own, other], axis=0)
        wx_own = Wx[h * DH:(h + 1) * DH]
        wx_oth = Wx[(1 - h) * DH:(2 - h) * DH]
        maps.append({
            "xs_full": np.ascontiguousarray(xs_full),
            "z_sl": np.ascontiguousarray(z_by_core[c]),
            "wx": np.ascontiguousarray(np.concatenate([wx_own, wx_oth], axis=0)),
            "bx_fm": np.ascontiguousarray(bx[:, None]),
            "bx_tm": np.ascontiguousarray(np.broadcast_to(bx, (128, 96))),
            "wdt": np.ascontiguousarray(Wdt[:, h * DH:(h + 1) * DH]),
            "bdt": np.ascontiguousarray(bdt[h * DH:(h + 1) * DH].reshape(8, 128).T),
            "alog": np.ascontiguousarray(Alog[h * DH:(h + 1) * DH].reshape(8, 128, NST).transpose(1, 0, 2)),
            "dp_r": np.ascontiguousarray(Dp[h * DH:(h + 1) * DH].reshape(8, 128).T),
        })
    return maps


def emit_ln_fm(p, tag, s_in, k_in, s_out, k_out, s_sq, k_sq, s_ones128,
               ps_sum, k_psum, ps_sq, k_psq, s_mv, k_mv, TT, nchunk=8, dmodel=1024):
    inv = 1.0 / dmodel
    p.emit("s", lambda eng: eng.activation(s_sq[:], s_in[:], ACT.Square),
           reads=(k_in,), writes=(k_sq,))
    def mmsum(eng):
        ins = None
        for kc in range(nchunk):
            ins = eng.matmul(ps_sum[:, 0:TT], s_ones128[:], s_in[:, kc, :], start=(kc == 0), stop=(kc == nchunk - 1))
        return ins
    p.emit("t", mmsum, reads=(k_in, "ones"), writes=(k_psum,))
    def mmsq(eng):
        ins = None
        for kc in range(nchunk):
            ins = eng.matmul(ps_sq[:, 0:TT], s_ones128[:], s_sq[:, kc, :], start=(kc == 0), stop=(kc == nchunk - 1))
        return ins
    p.emit("t", mmsq, reads=(k_sq, "ones"), writes=(k_psq,))
    p.emit("v", lambda eng: eng.tensor_scalar(s_mv[:, 0, :], ps_sum[:, 0:TT], inv, None, op0=ALU.mult),
           reads=(k_psum,), writes=(k_mv,))
    p.emit("v", lambda eng: eng.tensor_tensor(s_mv[:, 1, :], s_mv[:, 0, :], s_mv[:, 0, :], op=ALU.mult),
           reads=(k_mv,), writes=(k_mv,))
    p.emit("v", lambda eng: eng.scalar_tensor_tensor(out=s_mv[:, 1, :], in0=ps_sq[:, 0:TT], scalar=inv,
                                                     in1=s_mv[:, 1, :], op0=ALU.mult, op1=ALU.subtract),
           reads=(k_psq, k_mv), writes=(k_mv,))
    p.emit("v", lambda eng: eng.tensor_scalar(s_mv[:, 1, :], s_mv[:, 1, :], 1e-5, None, op0=ALU.add),
           reads=(k_mv,), writes=(k_mv,))
    p.emit("s", lambda eng: eng.activation(s_mv[:, 1, :], s_mv[:, 1, :], ACT.Sqrt),
           reads=(k_mv,), writes=(k_mv,))
    p.emit("v", lambda eng: eng.reciprocal(s_mv[:, 1, :], s_mv[:, 1, :]),
           reads=(k_mv,), writes=(k_mv,))
    p.emit("v", lambda eng: eng.tensor_tensor(
               s_out[:], s_in[:], s_mv[:, 0:1, :].broadcast_to([128, nchunk, TT]), op=ALU.subtract),
           reads=(k_in, k_mv), writes=(k_out,))
    p.emit("v", lambda eng: eng.tensor_tensor(
               s_out[:], s_out[:], s_mv[:, 1:2, :].broadcast_to([128, nchunk, TT]), op=ALU.mult),
           reads=(k_out, k_mv), writes=(k_out,))


def emit_quant_f32(p, s_v, k_v, s_q, k_q, lo, hi, scale_ap=None, k_scale=(), scale_const=None):
    if scale_ap is not None:
        p.emit("v", lambda eng: eng.tensor_tensor(s_q[:], s_v[:], scale_ap, op=ALU.mult),
               reads=(k_v,) + tuple(k_scale), writes=(k_q,))
    else:
        p.emit("v", lambda eng: eng.tensor_scalar(s_q[:], s_v[:], scale_const, None, op0=ALU.mult),
               reads=(k_v,), writes=(k_q,))
    p.emit("v", lambda eng: eng.tensor_scalar(s_v[:], s_q[:], 0.5, 1.0, op0=ALU.add, op1=ALU.mod),
           reads=(k_q,), writes=(k_v + "_mod",))
    p.emit("v", lambda eng: eng.scalar_tensor_tensor(out=s_q[:], in0=s_q[:], scalar=0.5, in1=s_v[:],
                                                     op0=ALU.add, op1=ALU.subtract),
           reads=(k_q, k_v + "_mod"), writes=(k_q,))
    p.emit("v", lambda eng: eng.tensor_scalar(s_q[:], s_q[:], float(lo), float(hi), op0=ALU.max, op1=ALU.min),
           reads=(k_q,), writes=(k_q,))




def emit_colmax_bcast(p, tag, m_ap, k_m, s_cm, s_row1, ps_tr, k_pstr, ps_bc, k_psbc,
                      s_ident, out_scale_ap, k_oscale, out_inv_ap, k_oinv, s_one1, extra_row_ops):
    p.emit("t", lambda eng: eng.transpose(ps_tr[:, 0:128], m_ap, s_ident[:]),
           reads=(k_m, "ident"), writes=(k_pstr,))
    p.emit("v", lambda eng: eng.tensor_reduce(s_cm[:, 0:1], ps_tr[:, 0:128], axis=AX, op=ALU.max),
           reads=(k_pstr,), writes=(tag + "cm",))
    p.emit("v", lambda eng: eng.tensor_scalar(s_cm[:, 0:1], s_cm[:, 0:1], 1e-5, None, op0=ALU.max),
           reads=(tag + "cm",), writes=(tag + "cm",))
    p.emit("t", lambda eng: eng.transpose(ps_tr[0:1, 128:256], s_cm[:, 0:1], s_ident[:]),
           reads=(tag + "cm", "ident"), writes=(k_pstr,))
    p.emit("v", lambda eng: eng.tensor_copy(s_row1[0:1, 0:128], ps_tr[0:1, 128:256]),
           reads=(k_pstr,), writes=(tag + "row",))
    if out_inv_ap is not None:
        p.emit("v", lambda eng: eng.tensor_scalar(s_row1[32:33, 0:128], s_row1[0:1, 0:128], 1.0 / 127.0, None, op0=ALU.mult),
               reads=(tag + "row",), writes=(tag + "rowi",))
    p.emit("v", lambda eng: eng.reciprocal(s_row1[0:1, 0:128], s_row1[0:1, 0:128]),
           reads=(tag + "row",), writes=(tag + "row",))
    p.emit("v", lambda eng: eng.tensor_scalar(s_row1[0:1, 0:128], s_row1[0:1, 0:128], 127.0, None, op0=ALU.mult),
           reads=(tag + "row",), writes=(tag + "row",))
    def bc(eng):
        return eng.matmul(ps_bc[:, 0:128], s_one1[0:1, :], s_row1[0:1, 0:128], start=True, stop=True)
    p.emit("t", bc, reads=(tag + "row", "one1"), writes=(k_psbc,))
    p.emit("v", lambda eng: eng.tensor_copy(out_scale_ap, ps_bc[:, 0:128]),
           reads=(k_psbc,), writes=(k_oscale,))
    if out_inv_ap is not None:
        def bc2(eng):
            return eng.matmul(ps_bc[:, 128:256], s_one1[32:33, :], s_row1[32:33, 0:128], start=True, stop=True)
        p.emit("t", bc2, reads=(tag + "rowi", "one1"), writes=(k_psbc,))
        p.emit("v", lambda eng: eng.tensor_copy(out_inv_ap, ps_bc[:, 128:256]),
               reads=(k_psbc,), writes=(k_oinv,))


P1_STAGE = 99
def build_l3(upto=99):
    nc = bass.Bass("TRN2", target_bir_lowering=False)
    stack = ExitStack()
    p = Prog(nc, stack)

    TT = 512
    NTT = TOK // TT

    y_own = nc.dram_tensor("y_own", [DI, TOK], F32, kind="ExternalInput").ap()
    wout = nc.dram_tensor("wout", [DI, DM], F32, kind="ExternalInput").ap()
    boutr = nc.dram_tensor("boutr", [128, 8], F32, kind="ExternalInput").ap()
    x_fm = nc.dram_tensor("x_fm", [DM, TOK], F32, kind="ExternalInput").ap()
    wgT = nc.dram_tensor("wgT", [DM, 4096], F32, kind="ExternalInput").ap()
    wuT = nc.dram_tensor("wuT", [DM, 4096], F32, kind="ExternalInput").ap()
    wdT = nc.dram_tensor("wdT", [4096, DM], F32, kind="ExternalInput").ap()
    s2_out = nc.dram_tensor("s2_out", [DM, TOK], F32, kind="ExternalOutput").ap()
    gu_hbm = nc.dram_tensor("gu_hbm", [4096, TOK], F32, kind="Internal").ap()
    s1_hbm = nc.dram_tensor("s1_hbm", [DM, TOK], F32, kind="Internal").ap()
    wdq_hbm = nc.dram_tensor("wdq_hbm", [4096, DM], BF16, kind="Internal").ap()

    s_ones = stack.enter_context(nc.sbuf_tensor("s_ones", [128, 128], F32))
    s_one1 = stack.enter_context(nc.sbuf_tensor("s_one1", [64, 128], F32))
    s_kq1 = stack.enter_context(nc.sbuf_tensor("s_kq1", [128, 8, TOK], BF16))
    s_inv1 = stack.enter_context(nc.sbuf_tensor("s_inv1", [128, TOK], F32))
    s_m2 = stack.enter_context(nc.sbuf_tensor("s_m2", [128, TOK], F32))
    s_f2s = stack.enter_context(nc.sbuf_tensor("s_f2s", [128, TOK], F32))
    s_inv2 = stack.enter_context(nc.sbuf_tensor("s_inv2", [128, TOK], F32))
    s_wsum = stack.enter_context(nc.sbuf_tensor("s_wsum", [128, 24], F32))
    s_wsc = stack.enter_context(nc.sbuf_tensor("s_wsc", [128, 8], F32))
    s_row = stack.enter_context(nc.sbuf_tensor("s_row", [96, 128], F32))
    s_cmg = stack.enter_context(nc.sbuf_tensor("s_cmg", [128, 2], F32))
    s_iota = stack.enter_context(nc.sbuf_tensor("s_iota", [128, 128], F32))
    s_ident = stack.enter_context(nc.sbuf_tensor("s_ident", [128, 128], F32))
    ps_pool = [stack.enter_context(nc.psum_tensor(f"pp{i}", [128, TT], F32)) for i in range(8)]

    p.emit("v", lambda eng: eng.memset(s_ones[:], 1.0), writes=("ones",))
    p.emit("v", lambda eng: eng.memset(s_one1[:], 1.0), writes=("one1",))
    make_identity(p, s_iota, s_ident)

    with ExitStack() as ph:
        s_wt = ph.enter_context(nc.sbuf_tensor("p0_wt", [128, 2, 4096], F32))
        wviews = [wgT.rearrange("(c p) n -> p c n", p=128),
                  wuT.rearrange("(c p) n -> p c n", p=128),
                  wdT.rearrange("(c p) n -> p c n", p=128)]
        for wi, wv in enumerate(wviews):
            nper = wv.shape[2]
            cper = wv.shape[1] // 8
            for j in range(8):
                pb = j % 2
                p.dma("y", lambda eng, wv=wv, j=j, pb=pb, nper=nper, cper=cper: eng.dma_start(
                          out=s_wt[:, pb, :].rearrange("p (c n) -> p c n", c=cper),
                          in_=wv[:, j * cper:(j + 1) * cper, :]),
                      writes=(f"p0w{pb}",))
                p.emit("v", lambda eng, wi=wi, j=j, pb=pb: eng.tensor_reduce(
                           s_wsum[:, wi * 8 + j:wi * 8 + j + 1], s_wt[:, pb, :], axis=AX,
                           op=ALU.add, apply_absolute_value=True),
                       reads=(f"p0w{pb}",), writes=(f"wsum{wi}_{j}",))
        for wi in range(3):
            p.emit("v", lambda eng, wi=wi: eng.tensor_reduce(
                       s_wsc[:, wi:wi + 1], s_wsum[:, wi * 8:(wi + 1) * 8], axis=AX, op=ALU.add),
                   reads=tuple(f"wsum{wi}_{j}" for j in range(8)), writes=(f"wtot{wi}",))
        p.emit("t", lambda eng: eng.matmul(ps_pool[6][:, 0:3], s_ones[:], s_wsc[:, 0:3], start=True, stop=True),
               reads=("wtot0", "wtot1", "wtot2", "ones"), writes=("pp6w",))
        p.emit("v", lambda eng: eng.tensor_copy(s_wsc[:, 0:3], ps_pool[6][:, 0:3]),
               reads=("pp6w",), writes=("wsc",))
        p.emit("v", lambda eng: eng.tensor_scalar(s_wsc[:, 0:3], s_wsc[:, 0:3], 1.0 / (4096.0 * 1024.0), 1e-5,
                                                  op0=ALU.mult, op1=ALU.max),
               reads=("wsc",), writes=("wsc",))
        p.emit("v", lambda eng: eng.reciprocal(s_wsc[:, 3:6], s_wsc[:, 0:3]),
               reads=("wsc",), writes=("wsc2",))

    p.barrier()
    if upto < 1:
        p.finish_outputs("y")
        with stack:
            return p.build()
    with ExitStack() as ph:
        s_wtb = ph.enter_context(nc.sbuf_tensor("p0b_wt", [128, 2, 2048], F32))
        s_wqb2 = ph.enter_context(nc.sbuf_tensor("p0b_wq", [128, 2, 2048], F32))
        s_wqbb = ph.enter_context(nc.sbuf_tensor("p0b_wqb", [128, 2, 2048], BF16))
        wv = wdT.rearrange("(c p) n -> p c n", p=128)
        qv = wdq_hbm.rearrange("(c p) n -> p c n", p=128)
        for j in range(16):
            pb = j % 2
            p.dma("y", lambda eng, j=j, pb=pb: eng.dma_start(
                      out=s_wtb[:, pb, :].rearrange("p (c n) -> p c n", c=2), in_=wv[:, 2 * j:2 * j + 2, :]),
                  writes=(f"q0w{pb}",))
            p.emit("v", lambda eng, pb=pb: eng.tensor_scalar(s_wqb2[:, pb, :], s_wtb[:, pb, :], s_wsc[:, 5:6], None, op0=ALU.mult),
                   reads=(f"q0w{pb}", "wsc2"), writes=(f"q0q{pb}",))
            p.emit("v", lambda eng, pb=pb: eng.tensor_scalar(s_wtb[:, pb, :], s_wqb2[:, pb, :], -0.5, None, op0=ALU.is_lt),
                   reads=(f"q0q{pb}",), writes=(f"q0w{pb}",))
            p.emit("v", lambda eng, pb=pb: eng.scalar_tensor_tensor(out=s_wqbb[:, pb, :], in0=s_wqb2[:, pb, :], scalar=0.5,
                                                                    in1=s_wtb[:, pb, :], op0=ALU.is_gt, op1=ALU.subtract),
                   reads=(f"q0q{pb}", f"q0w{pb}"), writes=(f"q0b{pb}",))
            p.dma("y", lambda eng, j=j, pb=pb: eng.dma_start(
                      out=qv[:, 2 * j:2 * j + 2, :], in_=s_wqbb[:, pb, :].rearrange("p (c n) -> p c n", c=2)),
                  reads=(f"q0b{pb}",))

    p.barrier()
    if upto < 2:
        p.finish_outputs("y")
        with stack:
            return p.build()
    with ExitStack() as ph:
        s_y = ph.enter_context(nc.sbuf_tensor("p1_y", [128, 16, 256], F32))
        s_x = ph.enter_context(nc.sbuf_tensor("p1_x", [128, 2, 8, 256], F32))
        s_w = ph.enter_context(nc.sbuf_tensor("p1_w", [128, 16, 8, 128], F32))
        s_b = ph.enter_context(nc.sbuf_tensor("p1_b", [128, 8], F32))
        s_r1 = ph.enter_context(nc.sbuf_tensor("p1_r1", [128, 8, 256], F32))
        s_sq = ph.enter_context(nc.sbuf_tensor("p1_sq", [128, 8, 256], F32))
        s_sc = ph.enter_context(nc.sbuf_tensor("p1_sc", [128, 4, 256], F32))
        s_s1loc = ph.enter_context(nc.sbuf_tensor("p1_s1", [128, 8, 256], F32))
        s_ki = ph.enter_context(nc.sbuf_tensor("p1_ki", [128, 8, 256], mybir.dt.int32))
        s_qs = ph.enter_context(nc.sbuf_tensor("p1_qs", [128, 8, 256], F32))
        p.dma("y", lambda eng: eng.dma_start(out=s_w[:], in_=wout.rearrange("(kc p) (nc8 f) -> p kc nc8 f", p=128, f=128)),
              writes=("w",))
        p.dma("y", lambda eng: eng.dma_start(out=s_b[:], in_=boutr), writes=("b",))
        yv = y_own.rearrange("(kc p) t -> p kc t", p=128)
        xv = x_fm.rearrange("(kc p) t -> p kc t", p=128)
        TT1 = 256
        for tt in range(TOK // TT1):
            pb = tt % 2
            p.dma("y", lambda eng, tt=tt: eng.dma_start(out=s_y[:], in_=yv[:, :, tt * TT1:(tt + 1) * TT1]),
                  writes=("yb",))
            p.dma("y", lambda eng, tt=tt, pb=pb: eng.dma_start(out=s_x[:, pb], in_=xv[:, :, tt * TT1:(tt + 1) * TT1]),
                  writes=(f"x{pb}",))
            for nc8 in range(8):
                bi = nc8 % 4
                pairs = [(s_w[:, kc, nc8, :], s_y[:, kc, :]) for kc in range(16)]
                mm_accum(p, ps_pool[bi][:, 0:TT1], pairs, f"pp{bi}", ("yb", "w"))
                p.emit("s", lambda eng, nc8=nc8, bi=bi: eng.activation(
                           s_r1[:, nc8, :], ps_pool[bi][:, 0:TT1], ACT.Identity, bias=s_b[:, nc8:nc8 + 1], scale=1.0),
                       reads=(f"pp{bi}", "b"), writes=("r1",))
            p.emit("v", lambda eng, pb=pb: eng.tensor_tensor(s_r1[:], s_r1[:], s_x[:, pb], op=ALU.add),
                   reads=("r1", f"x{pb}"), writes=("r1",))
            if P1_STAGE < 1:
                continue
            s1_sl = s_s1loc
            emit_ln_fm(p, f"ln1_{tt}", s_r1, "r1", s1_sl, "s1", s_sq, "sq", s_ones,
                       ps_pool[4], "pp4", ps_pool[5], "pp5", s_qs[:, 0:2], "qs", TT1)
            p.dma("y", lambda eng, tt=tt: eng.dma_start(
                      out=s1_hbm.rearrange("(kc p) t -> p kc t", p=128)[:, :, tt * TT1:(tt + 1) * TT1],
                      in_=s_s1loc[:]),
                  reads=("s1",))
            if P1_STAGE < 2:
                continue
            p.emit("s", lambda eng, s1_sl=s1_sl: eng.activation(s_qs[:], s1_sl[:], ACT.Abs),
                   reads=("s1",), writes=("qs",))
            p.emit("v", lambda eng: eng.tensor_tensor(s_sc[:, 0, :], s_qs[:, 0, :], s_qs[:, 1, :], op=ALU.max),
                   reads=("qs",), writes=("am",))
            for kc in range(2, 8):
                p.emit("v", lambda eng, kc=kc: eng.tensor_tensor(s_sc[:, 0, :], s_sc[:, 0, :], s_qs[:, kc, :], op=ALU.max),
                       reads=("am", "qs"), writes=("am",))
            for jb in range(TT1 // 128):
                emit_colmax_bcast(p, f"cm1_{tt}_{jb}", s_sc[:, 0, jb * 128:(jb + 1) * 128], "am",
                                  s_cmg, s_row, ps_pool[6], "pp6c", ps_pool[7], "pp7c",
                                  s_ident, s_sc[:, 2, jb * 128:(jb + 1) * 128], "sc1",
                                  s_inv1[:, tt * TT1 + jb * 128:tt * TT1 + (jb + 1) * 128], "inv1", s_one1, None)
            if P1_STAGE < 3:
                continue
            p.emit("v", lambda eng, s1_sl=s1_sl: eng.tensor_tensor(
                       s_r1[:], s1_sl[:], s_sc[:, 2:3, :].broadcast_to([128, 8, TT1]), op=ALU.mult),
                   reads=("s1", "sc1"), writes=("r1",))
            p.emit("v", lambda eng: eng.tensor_copy(s_ki[:], s_r1[:]), reads=("r1",), writes=("ki",))
            p.emit("v", lambda eng: eng.tensor_copy(s_sq[:], s_ki[:]), reads=("ki",), writes=("sq",))
            p.emit("v", lambda eng: eng.tensor_tensor(s_r1[:], s_r1[:], s_sq[:], op=ALU.subtract),
                   reads=("r1", "sq"), writes=("r1",))
            p.emit("v", lambda eng: eng.tensor_scalar(s_qs[:], s_r1[:], -0.5, None, op0=ALU.is_lt),
                   reads=("r1",), writes=("qs",))
            p.emit("v", lambda eng: eng.scalar_tensor_tensor(out=s_r1[:], in0=s_r1[:], scalar=0.5, in1=s_qs[:],
                                                             op0=ALU.is_gt, op1=ALU.subtract),
                   reads=("r1", "qs"), writes=("r1",))
            p.emit("v", lambda eng: eng.tensor_tensor(s_r1[:], s_r1[:], s_sq[:], op=ALU.add),
                   reads=("r1", "sq"), writes=("r1",))
            p.emit("v", lambda eng: eng.tensor_scalar(s_r1[:], s_r1[:], -128.0, 127.0, op0=ALU.max, op1=ALU.min),
                   reads=("r1",), writes=("r1",))
            p.emit("v", lambda eng, tt=tt: eng.tensor_copy(s_kq1[:, :, tt * TT1:(tt + 1) * TT1], s_r1[:]),
                   reads=("r1",), writes=("kq1",))

    p.barrier()
    if upto < 3:
        p.finish_outputs("y")
        with stack:
            return p.build()
    p.emit("v", lambda eng: eng.memset(s_m2[:], 0.0), writes=("m2",))
    with ExitStack() as ph:
        s_wb = ph.enter_context(nc.sbuf_tensor("p2_wb", [128, 2, 2, 8, 128], F32))
        s_wq2 = ph.enter_context(nc.sbuf_tensor("p2_wq", [128, 2, 8, 128], F32))
        s_wqb3 = ph.enter_context(nc.sbuf_tensor("p2_wqb", [128, 2, 8, 128], BF16))
        s_g = ph.enter_context(nc.sbuf_tensor("p2_g", [128, TT], F32))
        s_u = ph.enter_context(nc.sbuf_tensor("p2_u", [128, TT], F32))
        s_gu2 = ph.enter_context(nc.sbuf_tensor("p2_gu", [128, 2, TT], F32))
        wgv = wgT.rearrange("(kc p) (oc f) -> p kc oc f", p=128, f=128)
        wuv = wuT.rearrange("(kc p) (oc f) -> p kc oc f", p=128, f=128)
        guv = gu_hbm.rearrange("(oc p) t -> p oc t", p=128)
        for oc in range(32):
            pb = oc % 2
            p.dma("y", lambda eng, oc=oc, pb=pb: eng.dma_start(out=s_wb[:, pb, 0], in_=wgv[:, :, oc, :]),
                  writes=(f"wg{pb}",))
            p.dma("y", lambda eng, oc=oc, pb=pb: eng.dma_start(out=s_wb[:, pb, 1], in_=wuv[:, :, oc, :]),
                  writes=(f"wu{pb}",))
            for gi, wkey, mc_col, sw_col in ((0, "wg", 0, 3), (1, "wu", 1, 4)):
                p.emit("v", lambda eng, pb=pb, gi=gi, sw_col=sw_col: eng.tensor_scalar(
                           s_wq2[:, 0], s_wb[:, pb, gi], s_wsc[:, sw_col:sw_col + 1], None, op0=ALU.mult),
                       reads=(f"{wkey}{pb}", "wsc2"), writes=("wqa",))
                p.emit("v", lambda eng: eng.tensor_scalar(s_wq2[:, 1], s_wq2[:, 0], -0.5, None, op0=ALU.is_lt),
                       reads=("wqa",), writes=("wqm",))
                p.emit("v", lambda eng, gi=gi: eng.scalar_tensor_tensor(out=s_wqb3[:, gi], in0=s_wq2[:, 0], scalar=0.5,
                                                                 in1=s_wq2[:, 1], op0=ALU.is_gt, op1=ALU.subtract),
                       reads=("wqa", "wqm"), writes=(f"wqb{gi}",))
            for tt in range(NTT):
                pairs_g = [(s_wqb3[:, 0, kc, :], s_kq1[:, kc, tt * TT:(tt + 1) * TT]) for kc in range(8)]
                pairs_u = [(s_wqb3[:, 1, kc, :], s_kq1[:, kc, tt * TT:(tt + 1) * TT]) for kc in range(8)]
                mm_accum(p, ps_pool[0][:], pairs_g, "pp0", ("wqb0", "kq1"))
                mm_accum(p, ps_pool[1][:], pairs_u, "pp1", ("wqb1", "kq1"))
                p.emit("v", lambda eng, tt=tt: eng.scalar_tensor_tensor(
                           out=s_g[:], in0=ps_pool[0][:], scalar=s_wsc[:, 0:1],
                           in1=s_inv1[:, tt * TT:(tt + 1) * TT], op0=ALU.mult, op1=ALU.mult),
                       reads=("pp0", "wsc", "inv1"), writes=("g",))
                p.emit("s", lambda eng: eng.activation(s_g[:], s_g[:], ACT.Silu),
                       reads=("g",), writes=("g",))
                p.emit("v", lambda eng, tt=tt: eng.scalar_tensor_tensor(
                           out=s_u[:], in0=ps_pool[1][:], scalar=s_wsc[:, 1:2],
                           in1=s_inv1[:, tt * TT:(tt + 1) * TT], op0=ALU.mult, op1=ALU.mult),
                       reads=("pp1", "wsc", "inv1"), writes=("u",))
                gb = tt % 2
                p.emit("v", lambda eng, gb=gb: eng.tensor_tensor(s_gu2[:, gb], s_g[:], s_u[:], op=ALU.mult),
                       reads=("g", "u"), writes=(f"gu{gb}",))
                p.dma("y", lambda eng, oc=oc, tt=tt, gb=gb: eng.dma_start(
                          out=guv[:, oc, tt * TT:(tt + 1) * TT], in_=s_gu2[:, gb]),
                      reads=(f"gu{gb}",))
                p.emit("s", lambda eng, gb=gb: eng.activation(s_g[:], s_gu2[:, gb], ACT.Abs),
                       reads=(f"gu{gb}",), writes=("g",))
                p.emit("v", lambda eng, tt=tt: eng.tensor_tensor(
                           s_m2[:, tt * TT:(tt + 1) * TT], s_m2[:, tt * TT:(tt + 1) * TT], s_g[:], op=ALU.max),
                       reads=("g", "m2"), writes=("m2",))

    p.barrier()
    if upto < 4:
        p.finish_outputs("y")
        with stack:
            return p.build()
    for jb in range(TOK // 128):
        emit_colmax_bcast(p, f"cm2_{jb}", s_m2[:, jb * 128:(jb + 1) * 128], "m2",
                          s_cmg, s_row, ps_pool[6], "pp6c", ps_pool[7], "pp7c",
                          s_ident, s_f2s[:, jb * 128:(jb + 1) * 128], "f2",
                          s_inv2[:, jb * 128:(jb + 1) * 128], "inv2", s_one1, None)

    p.barrier()
    if upto < 5:
        p.finish_outputs("y")
        with stack:
            return p.build()
    with ExitStack() as ph:
        s_wd = ph.enter_context(nc.sbuf_tensor("p4_wd", [128, 2, 8, 128], BF16))
        s_gu4 = ph.enter_context(nc.sbuf_tensor("p4_gu", [128, 2, TT], F32))
        s_q = ph.enter_context(nc.sbuf_tensor("p4_q", [128, 2, TT], F32))
        s_qb = ph.enter_context(nc.sbuf_tensor("p4_qb", [128, TT], BF16))
        s_qi4 = ph.enter_context(nc.sbuf_tensor("p4_qi", [128, TT], mybir.dt.int32))
        s_qf4 = ph.enter_context(nc.sbuf_tensor("p4_qf", [128, TT], F32))
        s_ffn = ph.enter_context(nc.sbuf_tensor("p4_ffn", [128, 8, TT], F32))
        s_sq4 = ph.enter_context(nc.sbuf_tensor("p4_sq", [128, 8, TT], F32))
        s_x4 = ph.enter_context(nc.sbuf_tensor("p4_x", [128, 8, TT], F32))
        s_s1r = ph.enter_context(nc.sbuf_tensor("p4_s1r", [128, 8, TT], F32))
        s_mv4 = ph.enter_context(nc.sbuf_tensor("p4_mv", [128, 2, TT], F32))
        wdv = wdq_hbm.rearrange("(kc p) (oc f) -> p kc oc f", p=128, f=128)
        guv = gu_hbm.rearrange("(kc p) t -> p kc t", p=128)
        s1v = s1_hbm.rearrange("(kc p) t -> p kc t", p=128)
        for tt in range(NTT):
            p.dma("y", lambda eng, tt=tt: eng.dma_start(out=s_s1r[:], in_=s1v[:, :, tt * TT:(tt + 1) * TT]),
                  writes=("s1r",))
            for ic in range(32):
                pb = ic % 2
                p.dma("y", lambda eng, ic=ic, pb=pb: eng.dma_start(out=s_wd[:, pb], in_=wdv[:, ic, :, :]),
                      writes=(f"wd{pb}",))
                p.dma("y", lambda eng, ic=ic, tt=tt, pb=pb: eng.dma_start(
                          out=s_gu4[:, pb], in_=guv[:, ic, tt * TT:(tt + 1) * TT]),
                      writes=(f"gub{pb}",))
                p.emit("v", lambda eng, tt=tt, pb=pb: eng.tensor_tensor(
                           s_q[:, 0], s_gu4[:, pb], s_f2s[:, tt * TT:(tt + 1) * TT], op=ALU.mult),
                       reads=(f"gub{pb}", "f2"), writes=("qa",))
                p.emit("v", lambda eng: eng.tensor_copy(s_qi4[:], s_q[:, 0]), reads=("qa",), writes=("qi4",))
                p.emit("v", lambda eng: eng.tensor_copy(s_qf4[:], s_qi4[:]), reads=("qi4",), writes=("qf4",))
                p.emit("v", lambda eng: eng.tensor_tensor(s_q[:, 0], s_q[:, 0], s_qf4[:], op=ALU.subtract),
                       reads=("qa", "qf4"), writes=("qa",))
                p.emit("v", lambda eng: eng.tensor_scalar(s_q[:, 1], s_q[:, 0], -0.5, None, op0=ALU.is_lt),
                       reads=("qa",), writes=("qm",))
                p.emit("v", lambda eng: eng.scalar_tensor_tensor(out=s_q[:, 0], in0=s_q[:, 0], scalar=0.5,
                                                                 in1=s_q[:, 1], op0=ALU.is_gt, op1=ALU.subtract),
                       reads=("qa", "qm"), writes=("qa",))
                p.emit("v", lambda eng: eng.tensor_tensor(s_q[:, 0], s_q[:, 0], s_qf4[:], op=ALU.add),
                       reads=("qa", "qf4"), writes=("qa",))
                p.emit("v", lambda eng: eng.tensor_scalar(s_qb[:], s_q[:, 0], -128.0, 127.0, op0=ALU.max, op1=ALU.min),
                       reads=("qa",), writes=("qb",))
                def mmdown(eng, ic=ic, pb=pb):
                    ins = None
                    for oc in range(8):
                        ins = eng.matmul(ps_pool[oc][:], s_wd[:, pb, oc, :], s_qb[:],
                                         start=(ic == 0), stop=(ic == 31))
                    return ins
                p.emit("t", mmdown, reads=(f"wd{pb}", "qb"), writes=("ppdown",))
            for oc in range(8):
                p.emit("v", lambda eng, oc=oc, tt=tt: eng.scalar_tensor_tensor(
                           out=s_ffn[:, oc, :], in0=ps_pool[oc][:], scalar=s_wsc[:, 2:3],
                           in1=s_inv2[:, tt * TT:(tt + 1) * TT], op0=ALU.mult, op1=ALU.mult),
                       reads=("ppdown", "wsc", "inv2"), writes=("ffn",))
            p.emit("v", lambda eng, tt=tt: eng.tensor_tensor(
                       s_ffn[:], s_ffn[:], s_s1r[:], op=ALU.add),
                   reads=("ffn", "s1r"), writes=("ffn",))
            emit_ln_fm(p, f"ln2_{tt}", s_ffn, "ffn", s_x4, "s2", s_sq4, "p4sq", s_ones,
                       ps_pool[0], "ppdown", ps_pool[1], "ppdown", s_mv4, "lnmv2", TT)
            p.dma("y", lambda eng, tt=tt: eng.dma_start(
                      out=s2_out.rearrange("(kc p) t -> p kc t", p=128)[:, :, tt * TT:(tt + 1) * TT], in_=s_x4[:]),
                  reads=("s2",))
    p.finish_outputs("y")
    with stack:
        return p.build()


def host_l3_inputs(ins, y_by_core):
    Wout = ins["L0_attn_Wout"]; bout = ins["L0_attn_bout"]
    x = ins["L0_attn_x"]
    wg = ins["L0_ffn_w_gate"]; wu = ins["L0_ffn_w_up"]; wd = ins["L0_ffn_w_down"]
    maps = []
    for c in range(NCORES):
        b, h = c // 2, c % 2
        tsl = slice(h * TOK, (h + 1) * TOK)
        y_full = np.concatenate([y_by_core[2 * b], y_by_core[2 * b + 1]], axis=0)
        maps.append({
            "y_own": np.ascontiguousarray(y_full[:, tsl]),
            "wout": Wout,
            "boutr": np.ascontiguousarray(bout.reshape(8, 128).T),
            "x_fm": np.ascontiguousarray(x[b].T[:, tsl]),
            "wgT": np.ascontiguousarray(wg.T),
            "wuT": np.ascontiguousarray(wu.T),
            "wdT": np.ascontiguousarray(wd.T),
        })
    return maps


def build_l4():
    nc = bass.Bass("TRN2", target_bir_lowering=False)
    stack = ExitStack()
    p = Prog(nc, stack)

    NH, DHD, W = 16, 64, 128
    NB = TOK // W
    KT = TOK + 2 * W

    s2_fm = nc.dram_tensor("s2_fm", [DM, TOK], F32, kind="ExternalInput").ap()
    halo_x = nc.dram_tensor("halo_x", [DM, 2 * W], F32, kind="ExternalInput").ap()
    wqkv = nc.dram_tensor("wqkv", [DM, 3 * DM], F32, kind="ExternalInput").ap()
    bq_r = nc.dram_tensor("bq_r", [128, 8], F32, kind="ExternalInput").ap()
    bk_r = nc.dram_tensor("bk_r", [128, 8], F32, kind="ExternalInput").ap()
    bv_bc = nc.dram_tensor("bv_bc", [128, DM], F32, kind="ExternalInput").ap()
    wout1 = nc.dram_tensor("wout1", [DM, DM], F32, kind="ExternalInput").ap()
    bo_r = nc.dram_tensor("bo_r", [128, 8], F32, kind="ExternalInput").ap()
    mask0 = nc.dram_tensor("mask0", [128, 384], F32, kind="ExternalInput").ap()
    maskm = nc.dram_tensor("maskm", [128, 384], F32, kind="ExternalInput").ap()
    mask15 = nc.dram_tensor("mask15", [128, 384], F32, kind="ExternalInput").ap()
    s3f_out = nc.dram_tensor("s3f_out", [DM, TOK], F32, kind="ExternalOutput").ap()
    s3t_out = nc.dram_tensor("s3t_out", [TOK, DM], F32, kind="ExternalOutput").ap()
    k_hbm = nc.dram_tensor("k_hbm", [DM, KT], F32, kind="Internal").ap()
    v_hbm = nc.dram_tensor("v_hbm", [KT, DM], F32, kind="Internal").ap()

    s_ms = stack.enter_context(nc.sbuf_tensor("s_ms", [128, 3, 384], F32))
    s_ones = stack.enter_context(nc.sbuf_tensor("s_ones", [128, 128], F32))
    s_one1 = stack.enter_context(nc.sbuf_tensor("s_one1", [64, 128], F32))
    s_row = stack.enter_context(nc.sbuf_tensor("s_row", [96, 512], F32))
    s_iota = stack.enter_context(nc.sbuf_tensor("s_iota", [128, 128], F32))
    s_ident = stack.enter_context(nc.sbuf_tensor("s_ident", [128, 128], F32))
    ps = [stack.enter_context(nc.psum_tensor(f"ps{i}", [128, 512], F32)) for i in range(8)]
    p.emit("v", lambda eng: eng.memset(s_ones[:], 1.0), writes=("ones",))
    p.emit("v", lambda eng: eng.memset(s_one1[:], 1.0), writes=("one1",))
    make_identity(p, s_iota, s_ident)
    p.dma("y", lambda eng: eng.dma_start(out=s_ms[:, 0], in_=mask0), writes=("ms",))
    p.dma("y", lambda eng: eng.dma_start(out=s_ms[:, 1], in_=maskm), writes=("ms",))
    p.dma("y", lambda eng: eng.dma_start(out=s_ms[:, 2], in_=mask15), writes=("ms",))

    x_r = s2_fm.rearrange("(kc p) t -> p kc t", p=128)
    hx_r = halo_x.rearrange("(kc p) t -> p kc t", p=128)
    kv = k_hbm.rearrange("(kc p) t -> p kc t", p=128)
    vv_ = v_hbm.rearrange("(tb p) d -> p tb d", p=128)

    with ExitStack() as ph:
        s_wk = ph.enter_context(nc.sbuf_tensor("pa_wk", [128, 8, 8, 128], F32))
        s_wv = ph.enter_context(nc.sbuf_tensor("pa_wv", [128, 8, 8, 128], F32))
        s_bk = ph.enter_context(nc.sbuf_tensor("pa_bk", [128, 8], F32))
        s_bv = ph.enter_context(nc.sbuf_tensor("pa_bv", [128, DM], F32))
        s_xa = ph.enter_context(nc.sbuf_tensor("pa_x", [128, 2, 8, 512], F32))
        s_k = ph.enter_context(nc.sbuf_tensor("pa_k", [128, 8, 512], F32))
        s_v = ph.enter_context(nc.sbuf_tensor("pa_v", [128, 2, 512], F32))
        p.dma("y", lambda eng: eng.dma_start(out=s_wk[:], in_=wqkv.rearrange("(kc p) n -> p kc n", p=128)[:, :, DM:2 * DM].rearrange("p kc (o f) -> p kc o f", f=128)), writes=("wk",))
        p.dma("y", lambda eng: eng.dma_start(out=s_wv[:], in_=wqkv.rearrange("(kc p) n -> p kc n", p=128)[:, :, 2 * DM:3 * DM].rearrange("p kc (o f) -> p kc o f", f=128)), writes=("wv",))
        p.dma("y", lambda eng: eng.dma_start(out=s_bk[:], in_=bk_r), writes=("bk",))
        p.dma("y", lambda eng: eng.dma_start(out=s_bv[:], in_=bv_bc), writes=("bv",))
        groups = [("own", g) for g in range(4)] + [("halo", 0)]
        for gi, (kind, g) in enumerate(groups):
            pb = gi % 2
            ntok = 512 if kind == "own" else 256
            if kind == "own":
                p.dma("y", lambda eng, g=g, pb=pb: eng.dma_start(out=s_xa[:, pb], in_=x_r[:, :, g * 512:(g + 1) * 512]),
                      writes=(f"x{pb}",))
            else:
                p.dma("y", lambda eng, pb=pb: eng.dma_start(out=s_xa[:, pb, :, 0:256], in_=hx_r[:]),
                      writes=(f"x{pb}",))
            for oc in range(8):
                bi = oc % 2
                pairs = [(s_wk[:, kc, oc, :], s_xa[:, pb, kc, 0:ntok]) for kc in range(8)]
                mm_accum(p, ps[bi][:, 0:ntok], pairs, f"pp{bi}", (f"x{pb}", "wk"))
                p.emit("s", lambda eng, oc=oc, bi=bi, ntok=ntok: eng.activation(
                           s_k[:, oc, 0:ntok], ps[bi][:, 0:ntok], ACT.Identity, bias=s_bk[:, oc:oc + 1], scale=1.0),
                       reads=(f"pp{bi}", "bk"), writes=("k",))
            if kind == "own":
                p.dma("y", lambda eng, g=g: eng.dma_start(out=kv[:, :, 128 + g * 512:128 + (g + 1) * 512], in_=s_k[:]),
                      reads=("k",))
            else:
                p.dma("y", lambda eng: eng.dma_start(out=kv[:, :, 0:128], in_=s_k[:, :, 0:128]), reads=("k",))
                p.dma("y", lambda eng: eng.dma_start(out=kv[:, :, KT - 128:KT], in_=s_k[:, :, 128:256]), reads=("k",))
            for tb in range(ntok // 128):
                for vc in range(2):
                    bi = 2 + vc
                    pairs = [(s_xa[:, pb, kc, tb * 128:(tb + 1) * 128], s_wv[:, kc, 4 * vc:4 * vc + 4, :].rearrange("p o f -> p (o f)")) for kc in range(8)]
                    mm_accum(p, ps[bi][:], pairs, f"pp{bi}", (f"x{pb}", "wv"))
                    p.emit("v", lambda eng, vc=vc, bi=bi: eng.tensor_tensor(
                               s_v[:, vc, :], ps[bi][:], s_bv[:, vc * 512:(vc + 1) * 512], op=ALU.add),
                           reads=(f"pp{bi}", "bv"), writes=("v",))
                if kind == "own":
                    p.dma("y", lambda eng, g=g, tb=tb: eng.dma_start(
                              out=vv_[:, 1 + g * 4 + tb, :], in_=s_v[:].rearrange("p a b -> p (a b)")),
                          reads=("v",))
                elif tb == 0:
                    p.dma("y", lambda eng: eng.dma_start(out=vv_[:, 0, :], in_=s_v[:].rearrange("p a b -> p (a b)")),
                          reads=("v",))
                else:
                    p.dma("y", lambda eng: eng.dma_start(out=vv_[:, 17, :], in_=s_v[:].rearrange("p a b -> p (a b)")),
                          reads=("v",))

    p.barrier()
    with ExitStack() as ph:
        s_wq = ph.enter_context(nc.sbuf_tensor("pb_wq", [128, 8, 8, 128], F32))
        s_bq = ph.enter_context(nc.sbuf_tensor("pb_bq", [128, 8], F32))
        s_x = ph.enter_context(nc.sbuf_tensor("pb_x", [128, 1, 8, 512], F32))
        s_q = ph.enter_context(nc.sbuf_tensor("pb_q", [128, 8, 512], F32))
        s_kk = ph.enter_context(nc.sbuf_tensor("pb_kk", [128, 1, 8, 768], F32))
        s_vv = ph.enter_context(nc.sbuf_tensor("pb_vv", [128, 1, 6, DM], F32))
        s_sc = ph.enter_context(nc.sbuf_tensor("pb_sc", [128, 384], F32))
        s_sm = ph.enter_context(nc.sbuf_tensor("pb_sm", [128, 4], F32))
        s_at = ph.enter_context(nc.sbuf_tensor("pb_at", [128, 3, 128], F32))
        s_ao = ph.enter_context(nc.sbuf_tensor("pb_ao", [128, 8, 512], F32))
        s_wo = ph.enter_context(nc.sbuf_tensor("pb_wo", [128, 2, 8, 128], F32))
        s_bo = ph.enter_context(nc.sbuf_tensor("pb_bo", [128, 8], F32))
        s_o = ph.enter_context(nc.sbuf_tensor("pb_o", [128, 8, 512], F32))
        s_sq = ph.enter_context(nc.sbuf_tensor("pb_sq", [128, 8, 512], F32))
        s_s3 = ph.enter_context(nc.sbuf_tensor("pb_s3", [128, 8, 512], F32))
        s_3t = ph.enter_context(nc.sbuf_tensor("pb_3t", [128, 8, 128], F32))
        s_mvb = ph.enter_context(nc.sbuf_tensor("pb_mv", [128, 2, 512], F32))
        p.dma("y", lambda eng: eng.dma_start(out=s_wq[:], in_=wqkv.rearrange("(kc p) n -> p kc n", p=128)[:, :, 0:DM].rearrange("p kc (o f) -> p kc o f", f=128)), writes=("wq",))
        p.dma("y", lambda eng: eng.dma_start(out=s_bq[:], in_=bq_r), writes=("bq",))
        p.dma("y", lambda eng: eng.dma_start(out=s_bo[:], in_=bo_r), writes=("bo",))
        for g in range(4):
            pb = 0
            p.dma("y", lambda eng, g=g, pb=pb: eng.dma_start(out=s_x[:, pb], in_=x_r[:, :, g * 512:(g + 1) * 512]),
                  writes=(f"x{pb}",))
            p.dma("y", lambda eng, g=g, pb=pb: eng.dma_start(out=s_kk[:, pb], in_=kv[:, :, g * 512:g * 512 + 768]),
                  writes=(f"kk{pb}",))
            p.dma("y", lambda eng, g=g, pb=pb: eng.dma_start(out=s_vv[:, pb], in_=vv_[:, g * 4:g * 4 + 6, :]),
                  writes=(f"vv{pb}",))
            for oc in range(8):
                bi = oc % 2
                pairs = [(s_wq[:, kc, oc, :], s_x[:, pb, kc, :]) for kc in range(8)]
                mm_accum(p, ps[bi][:], pairs, f"pp{bi}", (f"x{pb}", "wq"))
                p.emit("s", lambda eng, oc=oc, bi=bi: eng.activation(
                           s_q[:, oc, 0:512], ps[bi][:], ACT.Identity, bias=s_bq[:, oc:oc + 1], scale=1.0),
                       reads=(f"pp{bi}", "bq"), writes=("q",))
            for hd in range(NH):
                hc, hr = hd // 2, (hd % 2) * 64
                for qb in range(4):
                    gqb = g * 4 + qb
                    mi = 0 if gqb == 0 else (2 if gqb == NB - 1 else 1)
                    p.emit("t", lambda eng, hc=hc, hr=hr, qb=qb, pb=pb: eng.matmul(
                               ps[4][:, 0:384], s_q[hr:hr + 64, hc, qb * 128:(qb + 1) * 128],
                               s_kk[hr:hr + 64, pb, hc, qb * 128:qb * 128 + 384], start=True, stop=True),
                           reads=("q", f"kk{pb}"), writes=("pp4",))
                    p.emit("v", lambda eng, mi=mi: eng.scalar_tensor_tensor(
                               out=s_sc[:], in0=ps[4][:, 0:384], scalar=0.03125, in1=s_ms[:, mi],
                               op0=ALU.mult, op1=ALU.add),
                           reads=("pp4", "ms"), writes=("sc",))
                    p.emit("v", lambda eng: eng.tensor_reduce(s_sm[:, 0:1], s_sc[:], axis=AX, op=ALU.max),
                           reads=("sc",), writes=("sm",))
                    p.emit("v", lambda eng: eng.tensor_scalar(s_sm[:, 1:2], s_sm[:, 0:1], -1.0, None, op0=ALU.mult),
                           reads=("sm",), writes=("sm1",))
                    p.emit("s", lambda eng: eng.activation(s_sc[:], s_sc[:], ACT.Exp, bias=s_sm[:, 1:2], scale=1.0),
                           reads=("sc", "sm1"), writes=("sc",))
                    p.emit("v", lambda eng: eng.tensor_reduce(s_sm[:, 2:3], s_sc[:], axis=AX, op=ALU.add),
                           reads=("sc",), writes=("sm2",))
                    p.emit("v", lambda eng: eng.reciprocal(s_sm[:, 3:4], s_sm[:, 2:3]),
                           reads=("sm2",), writes=("sm3",))
                    p.emit("v", lambda eng: eng.tensor_scalar(s_sc[:], s_sc[:], s_sm[:, 3:4], None, op0=ALU.mult),
                           reads=("sc", "sm3"), writes=("sc",))
                    for j in range(3):
                        p.emit("t", lambda eng, j=j: eng.transpose(ps[5][:, j * 128:(j + 1) * 128], s_sc[:, j * 128:(j + 1) * 128], s_ident[:]),
                               reads=("sc", "ident"), writes=("pp5",))
                    p.emit("s", lambda eng: eng.copy(s_at[:].rearrange("p a b -> p (a b)"), ps[5][:, 0:384]),
                           reads=("pp5",), writes=("at",))
                    def av(eng, hc=hc, hr=hr, qb=qb, pb=pb):
                        ins = None
                        for j in range(3):
                            ins = eng.matmul(ps[6][hr:hr + 64, qb * 128:(qb + 1) * 128],
                                             s_vv[:, pb, qb + j, hc * 128 + hr:hc * 128 + hr + 64],
                                             s_at[:, j, :], start=(j == 0), stop=(j == 2))
                        return ins
                    p.emit("t", av, reads=("at", f"vv{pb}"), writes=("pp6",))
                    p.emit("s", lambda eng, hc=hc, hr=hr, qb=qb: eng.copy(
                               s_ao[hr:hr + 64, hc, qb * 128:(qb + 1) * 128], ps[6][hr:hr + 64, qb * 128:(qb + 1) * 128]),
                           reads=("pp6",), writes=("ao",))
            for oc in range(8):
                bi = oc % 2
                p.dma("y", lambda eng, oc=oc, bi=bi: eng.dma_start(
                          out=s_wo[:, bi], in_=wout1.rearrange("(kc p) n -> p kc n", p=128)[:, :, oc * 128:(oc + 1) * 128]),
                      writes=(f"wo{bi}",))
                pairs = [(s_wo[:, bi, kc, :], s_ao[:, kc, :]) for kc in range(8)]
                mm_accum(p, ps[bi][:], pairs, f"pp{bi}", ("ao", f"wo{bi}"))
                p.emit("s", lambda eng, oc=oc, bi=bi: eng.activation(
                           s_o[:, oc, :], ps[bi][:], ACT.Identity, bias=s_bo[:, oc:oc + 1], scale=1.0),
                       reads=(f"pp{bi}", "bo"), writes=("o",))
            p.emit("v", lambda eng, pb=pb: eng.tensor_tensor(s_o[:], s_o[:], s_x[:, pb], op=ALU.add),
                   reads=("o", f"x{pb}"), writes=("o",))
            emit_ln_fm(p, f"ln3_{g}", s_o, "o", s_s3, "s3", s_sq, "sq3", s_ones,
                       ps[4], "pp4", ps[5], "pp5", s_mvb, "lnmv3", 512)
            p.dma("y", lambda eng, g=g: eng.dma_start(
                      out=s3f_out.rearrange("(kc p) t -> p kc t", p=128)[:, :, g * 512:(g + 1) * 512], in_=s_s3[:]),
                  reads=("s3",))
            for tb in range(4):
                for half in range(2):
                    bk = ps[2] if half == 0 else ps[3]
                    bkk = "pp2" if half == 0 else "pp3"
                    for k4 in range(4):
                        kc = half * 4 + k4
                        p.emit("t", lambda eng, tb=tb, kc=kc, k4=k4, bk=bk: eng.transpose(
                                   bk[:, k4 * 128:(k4 + 1) * 128], s_s3[:, kc, tb * 128:(tb + 1) * 128], s_ident[:]),
                               reads=("s3", "ident"), writes=(bkk,))
                    p.emit("s", lambda eng, half=half, bk=bk: eng.copy(
                               s_3t[:, half * 4:(half + 1) * 4, :].rearrange("p a b -> p (a b)"), bk[:, 0:512]),
                           reads=(bkk,), writes=("t3",))
                p.dma("y", lambda eng, g=g, tb=tb: eng.dma_start(
                          out=s3t_out[(g * 4 + tb) * 128:(g * 4 + tb + 1) * 128, :], in_=s_3t[:].rearrange("p a b -> p (a b)")),
                      reads=("t3",))
    p.finish_outputs("y")
    with stack:
        return p.build()


def host_l4_inputs(ins, s2_by_core):
    Wqkv = ins["L1_attn_Wqkv"]; bqkv = ins["L1_attn_bqkv"]
    Wout1 = ins["L1_attn_Wout"]; bout1 = ins["L1_attn_bout"]
    W = 128
    i = np.arange(128)[:, None]; j = np.arange(384)[None, :]
    base = np.where((j >= i) & (j <= i + 256), 0.0, -1e30).astype(np.float32)
    m_first = np.where(j >= 128, base, -1e30).astype(np.float32)
    m_last = np.where(j < 256, base, -1e30).astype(np.float32)
    maps = []
    for c in range(NCORES):
        b, h = c // 2, c % 2
        peer = 2 * b + (1 - h)
        zeros = np.zeros((DM, W), np.float32)
        if h == 0:
            left = zeros
            right = s2_by_core[peer][:, 0:W]
            mask0_, mask15_ = m_first, base
        else:
            left = s2_by_core[peer][:, TOK - W:TOK]
            right = zeros
            mask0_, mask15_ = base, m_last
        maps.append({
            "s2_fm": np.ascontiguousarray(s2_by_core[c]),
            "halo_x": np.ascontiguousarray(np.concatenate([left, right], axis=1)),
            "wqkv": Wqkv,
            "bq_r": np.ascontiguousarray(bqkv[0:DM].reshape(8, 128).T),
            "bk_r": np.ascontiguousarray(bqkv[DM:2 * DM].reshape(8, 128).T),
            "bv_bc": np.ascontiguousarray(np.broadcast_to(bqkv[2 * DM:3 * DM], (128, DM))),
            "wout1": Wout1,
            "bo_r": np.ascontiguousarray(bout1.reshape(8, 128).T),
            "mask0": mask0_, "maskm": base, "mask15": mask15_,
        })
    return maps


def build_l5():
    nc = bass.Bass("TRN2", target_bir_lowering=False)
    stack = ExitStack()
    p = Prog(nc, stack)
    TT = 512

    s3_fm = nc.dram_tensor("s3_fm", [DM, TOK], F32, kind="ExternalInput").ap()
    wq = nc.dram_tensor("wq", [DM, 2048], F32, kind="ExternalInput").ap()
    q_out = nc.dram_tensor("q_out", [2048, TOK], F32, kind="ExternalOutput").ap()
    st_out = nc.dram_tensor("st_out", [128, 32], F32, kind="ExternalOutput").ap()

    s_w = stack.enter_context(nc.sbuf_tensor("s_w", [128, 8, 16, 128], F32))
    s_x = stack.enter_context(nc.sbuf_tensor("s_x", [128, 2, 8, TT], F32))
    s_q = stack.enter_context(nc.sbuf_tensor("s_q", [128, 16, TT], F32))
    s_sq = stack.enter_context(nc.sbuf_tensor("s_sq", [128, 16, TT], F32))
    s_st = stack.enter_context(nc.sbuf_tensor("s_st", [128, 16, 2, 4], F32))
    s_sto = stack.enter_context(nc.sbuf_tensor("s_sto", [128, 32], F32))
    ps = [stack.enter_context(nc.psum_tensor(f"ps{i}", [128, TT], F32)) for i in range(4)]

    with stack:
        p.dma("y", lambda eng: eng.dma_start(out=s_w[:], in_=wq.rearrange("(kc p) (oc f) -> p kc oc f", p=128, f=128)),
              writes=("w",))
        xv = s3_fm.rearrange("(kc p) t -> p kc t", p=128)
        qv = q_out.rearrange("(oc p) t -> p oc t", p=128)
        for tt in range(TOK // TT):
            pb = tt % 2
            p.dma("y", lambda eng, tt=tt, pb=pb: eng.dma_start(out=s_x[:, pb], in_=xv[:, :, tt * TT:(tt + 1) * TT]),
                  writes=(f"x{pb}",))
            for oc in range(16):
                bi = oc % 4
                pairs = [(s_w[:, kc, oc, :], s_x[:, pb, kc, :]) for kc in range(8)]
                mm_accum(p, ps[bi][:], pairs, f"pp{bi}", (f"x{pb}", "w"))
                p.emit("s", lambda eng, oc=oc, bi=bi: eng.copy(s_q[:, oc, :], ps[bi][:]),
                       reads=(f"pp{bi}",), writes=("q",))
            p.dma("y", lambda eng, tt=tt: eng.dma_start(out=qv[:, :, tt * TT:(tt + 1) * TT], in_=s_q[:]),
                  reads=("q",))
            p.emit("s", lambda eng: eng.activation(s_sq[:], s_q[:], ACT.Square),
                   reads=("q",), writes=("sq",))
            for oc in range(16):
                p.emit("v", lambda eng, oc=oc, tt=tt: eng.tensor_reduce(
                           s_st[:, oc, 0, tt:tt + 1], s_q[:, oc, :], axis=AX, op=ALU.add),
                       reads=("q",), writes=("st",))
                p.emit("v", lambda eng, oc=oc, tt=tt: eng.tensor_reduce(
                           s_st[:, oc, 1, tt:tt + 1], s_sq[:, oc, :], axis=AX, op=ALU.add),
                       reads=("sq",), writes=("st",))
        p.emit("v", lambda eng: eng.tensor_reduce(
                   s_sto[:].rearrange("p (a b) -> p a b", b=1), s_st[:].rearrange("p a b c -> p (a b) c"), axis=AX, op=ALU.add),
               reads=("st",), writes=("sto",))
        p.dma("y", lambda eng: eng.dma_start(out=st_out, in_=s_sto[:]), reads=("sto",))
        p.finish_outputs("y")
        return p.build()


def host_l5_inputs(ins, s3f_by_core):
    wq = ins["L1_ffn_w_q"]
    return [{"s3_fm": np.ascontiguousarray(s3f_by_core[c]), "wq": wq} for c in range(NCORES)]


def build_l6():
    nc = bass.Bass("TRN2", target_bir_lowering=False)
    stack = ExitStack()
    p = Prog(nc, stack)
    NTB = TOK // 128

    q_fm = nc.dram_tensor("q_fm", [2048, TOK], F32, kind="ExternalInput").ap()
    stats_all = nc.dram_tensor("stats_all", [NCORES, 128, 32], F32, kind="ExternalInput").ap()
    s3_tm = nc.dram_tensor("s3_tm", [TOK, DM], F32, kind="ExternalInput").ap()
    keys_t = nc.dram_tensor("keys_t", [128, 2, 8, 256], F32, kind="ExternalInput").ap()
    gamma_r = nc.dram_tensor("gamma_r", [128, 16], F32, kind="ExternalInput").ap()
    beta_r = nc.dram_tensor("beta_r", [128, 16], F32, kind="ExternalInput").ap()
    down_e = nc.dram_tensor("down_e", [65536, 1024], F32, kind="ExternalInput").ap()
    up_e = nc.dram_tensor("up_e", [65536, 1024], F32, kind="ExternalInput").ap()
    fin_out = nc.dram_tensor("fin_out", [TOK, DM], F32, kind="ExternalOutput").ap()

    s_keys = stack.enter_context(nc.sbuf_tensor("s_keys", [128, 2, 8, 256], F32))
    s_stats = stack.enter_context(nc.sbuf_tensor("s_stats", [128, 8, 32], F32))
    s_mr = stack.enter_context(nc.sbuf_tensor("s_mr", [128, 16, 2], F32))
    s_gam = stack.enter_context(nc.sbuf_tensor("s_gam", [128, 16], F32))
    s_bet = stack.enter_context(nc.sbuf_tensor("s_bet", [128, 16], F32))
    s_qt = stack.enter_context(nc.sbuf_tensor("s_qt", [128, 2, 16, 128], F32))
    s_x3 = stack.enter_context(nc.sbuf_tensor("s_x3", [128, 2, 1024], F32))
    s_sim = stack.enter_context(nc.sbuf_tensor("s_sim", [128, 256], F32))
    s_tv = stack.enter_context(nc.sbuf_tensor("s_tv", [128, 2, 8, 8], F32))
    s_tiu = stack.enter_context(nc.sbuf_tensor("s_tiu", [128, 2, 8, 8], U32))
    s_tif = stack.enter_context(nc.sbuf_tensor("s_tif", [128, 2, 8, 8], F32))
    s_as = stack.enter_context(nc.sbuf_tensor("s_as", [128, 8, 64], F32))
    s_ai = stack.enter_context(nc.sbuf_tensor("s_ai", [128, 8, 64], F32))
    s_sc8 = stack.enter_context(nc.sbuf_tensor("s_sc8", [128, 8, 8], F32))
    s_pku = stack.enter_context(nc.sbuf_tensor("s_pku", [128, 8, 8], U32))
    s_pkdm = stack.enter_context(nc.sbuf_tensor("s_pkdm", [128, 2, 8, 8], F32))
    s_msk = stack.enter_context(nc.sbuf_tensor("s_msk", [128, 8, 8], F32))
    s_ixg = stack.enter_context(nc.sbuf_tensor("s_ixg", [128, 2, 8, 8], F32))
    s_eid = stack.enter_context(nc.sbuf_tensor("s_eid", [128, 8, 8], U32))
    s_wd = stack.enter_context(nc.sbuf_tensor("s_wd", [128, 2, 1024], F32))
    s_wu = stack.enter_context(nc.sbuf_tensor("s_wu", [128, 2, 1024], F32))
    s_scr = stack.enter_context(nc.sbuf_tensor("s_scr", [128, 2, 1024], F32))
    s_hm = stack.enter_context(nc.sbuf_tensor("s_hm", [128, 64], F32))
    s_sm = stack.enter_context(nc.sbuf_tensor("s_sm", [128, 64], F32))
    s_smx = stack.enter_context(nc.sbuf_tensor("s_smx", [128, 8, 1], F32))
    s_acc = stack.enter_context(nc.sbuf_tensor("s_acc", [128, 2, 1024], F32))
    s_r4 = stack.enter_context(nc.sbuf_tensor("s_r4", [128, 1024], F32))
    s_ln = stack.enter_context(nc.sbuf_tensor("s_ln", [128, 4], F32))
    ps = [stack.enter_context(nc.psum_tensor(f"ps{i}", [128, 256], F32)) for i in range(2)]

    with stack:
        p.dma("y", lambda eng: eng.dma_start(out=s_keys[:], in_=keys_t), writes=("keys",))
        p.dma("y", lambda eng: eng.dma_start(out=s_stats[:], in_=stats_all.rearrange("c p n -> p c n")), writes=("stats",))
        p.dma("y", lambda eng: eng.dma_start(out=s_gam[:], in_=gamma_r), writes=("gb",))
        p.dma("y", lambda eng: eng.dma_start(out=s_bet[:], in_=beta_r), writes=("gb",))
        for c in range(1, NCORES):
            p.emit("v", lambda eng, c=c: eng.tensor_tensor(s_stats[:, 0, :], s_stats[:, 0, :], s_stats[:, c, :], op=ALU.add),
                   reads=("stats",), writes=("stats",))
        stv = s_stats[:, 0, :].rearrange("p (oc two) -> p oc two", two=2)
        inv_n = 1.0 / (B * S)
        p.emit("v", lambda eng: eng.tensor_scalar(s_mr[:, :, 0], stv[:, :, 0], inv_n, None, op0=ALU.mult),
               reads=("stats",), writes=("mr",))
        p.emit("v", lambda eng: eng.tensor_tensor(s_mr[:, :, 1], s_mr[:, :, 0], s_mr[:, :, 0], op=ALU.mult),
               reads=("mr",), writes=("mr2",))
        p.emit("v", lambda eng: eng.scalar_tensor_tensor(out=s_mr[:, :, 1], in0=stv[:, :, 1], scalar=inv_n,
                                                         in1=s_mr[:, :, 1], op0=ALU.mult, op1=ALU.subtract),
               reads=("stats", "mr2"), writes=("mr2",))
        p.emit("v", lambda eng: eng.tensor_scalar(s_mr[:, :, 1], s_mr[:, :, 1], 1e-5, None, op0=ALU.add),
               reads=("mr2",), writes=("mr2",))
        p.emit("s", lambda eng: eng.activation(s_mr[:, :, 1], s_mr[:, :, 1], ACT.Sqrt),
               reads=("mr2",), writes=("mr2",))
        p.emit("v", lambda eng: eng.reciprocal(s_mr[:, :, 1], s_mr[:, :, 1]),
               reads=("mr2",), writes=("mr2",))

        p.emit("v", lambda eng: eng.tensor_tensor(s_gam[:], s_gam[:], s_mr[:, :, 1], op=ALU.mult),
               reads=("gb", "mr2"), writes=("gb",))

        qv = q_fm.rearrange("(oc p) t -> p oc t", p=128)
        for tb in range(NTB):
            pb = tb % 2
            p.dma("y", lambda eng, tb=tb, pb=pb: eng.dma_start(out=s_qt[:, pb], in_=qv[:, :, tb * 128:(tb + 1) * 128]),
                  writes=(f"qt{pb}",))
            p.dma("y", lambda eng, tb=tb, pb=pb: eng.dma_start(out=s_x3[:, pb], in_=s3_tm[tb * 128:(tb + 1) * 128, :]),
                  writes=(f"x3{pb}",))
            p.emit("v", lambda eng, pb=pb: eng.tensor_tensor(
                       s_qt[:, pb], s_qt[:, pb], s_mr[:, :, 0:1].broadcast_to([128, 16, 128]), op=ALU.subtract),
                   reads=(f"qt{pb}", "mr"), writes=(f"qt{pb}",))
            p.emit("v", lambda eng, pb=pb: eng.tensor_tensor(
                       s_qt[:, pb], s_qt[:, pb], s_gam[:].unsqueeze(2).broadcast_to([128, 16, 128]), op=ALU.mult),
                   reads=(f"qt{pb}", "gb"), writes=(f"qt{pb}",))
            p.emit("v", lambda eng, pb=pb: eng.tensor_tensor(
                       s_qt[:, pb], s_qt[:, pb], s_bet[:].unsqueeze(2).broadcast_to([128, 16, 128]), op=ALU.add),
                   reads=(f"qt{pb}", "gb"), writes=(f"qt{pb}",))
            for p2 in range(2):
                for hd in range(8):
                    oc = p2 * 8 + hd
                    bi = oc % 2
                    p.emit("t", lambda eng, oc=oc, p2=p2, hd=hd, pb=pb, bi=bi: eng.matmul(
                               ps[bi][:], s_qt[:, pb, oc, :], s_keys[:, p2, hd, :], start=True, stop=True),
                           reads=(f"qt{pb}", "keys"), writes=(f"pp{bi}",))
                    p.emit("s", lambda eng, bi=bi: eng.copy(s_sim[:], ps[bi][:]),
                           reads=(f"pp{bi}",), writes=("sim",))
                    p.emit("v", lambda eng, p2=p2, hd=hd: eng.max(out=s_tv[:, p2, hd, :], in_=s_sim[:]),
                           reads=("sim",), writes=("tv",))
                    p.emit("v", lambda eng, p2=p2, hd=hd: eng.max_index(s_tiu[:, p2, hd, :], s_tv[:, p2, hd, :], s_sim[:]),
                           reads=("sim", "tv"), writes=("tiu",))
            p.emit("v", lambda eng: eng.tensor_copy(s_tif[:], s_tiu[:]), reads=("tiu",), writes=("tif",))
            for hd in range(8):
                p.emit("v", lambda eng, hd=hd: eng.tensor_tensor(
                           s_as[:, hd, :].rearrange("p (i j) -> p i j", i=8),
                           s_tv[:, 0, hd, :].unsqueeze(2).broadcast_to([128, 8, 8]),
                           s_tv[:, 1, hd, :].unsqueeze(1).broadcast_to([128, 8, 8]), op=ALU.add),
                       reads=("tv",), writes=("as",))
                p.emit("v", lambda eng, hd=hd: eng.scalar_tensor_tensor(
                           out=s_ai[:, hd, :].rearrange("p (i j) -> p i j", i=8),
                           in0=s_tif[:, 0, hd, :].unsqueeze(2).broadcast_to([128, 8, 8]), scalar=256.0,
                           in1=s_tif[:, 1, hd, :].unsqueeze(1).broadcast_to([128, 8, 8]), op0=ALU.mult, op1=ALU.add),
                       reads=("tif",), writes=("ai",))
            for hd in range(8):
                p.emit("v", lambda eng, hd=hd: eng.max(out=s_sc8[:, hd, :], in_=s_as[:, hd, :]),
                       reads=("as",), writes=("sc8",))
                p.emit("v", lambda eng, hd=hd: eng.max_index(s_pku[:, hd, :], s_sc8[:, hd, :], s_as[:, hd, :]),
                       reads=("as", "sc8"), writes=("pku",))
            p.emit("v", lambda eng: eng.tensor_scalar(s_eid[:], s_pku[:], 3, None, op0=ALU.logical_shift_right),
                   reads=("pku",), writes=("eidtmp",))
            p.emit("v", lambda eng: eng.tensor_copy(s_pkdm[:, 0], s_eid[:]), reads=("eidtmp",), writes=("pkdm",))
            p.emit("v", lambda eng: eng.tensor_scalar(s_eid[:], s_pku[:], 7, None, op0=ALU.bitwise_and),
                   reads=("pku",), writes=("eidtmp",))
            p.emit("v", lambda eng: eng.tensor_copy(s_pkdm[:, 1], s_eid[:]), reads=("eidtmp",), writes=("pkdm",))
            p.emit("v", lambda eng: eng.memset(s_ixg[:], 0.0), writes=("ixg",))
            for j in range(8):
                for si in range(2):
                    p.emit("v", lambda eng, j=j, si=si: eng.tensor_scalar(
                               s_msk[:], s_pkdm[:, si], float(j), None, op0=ALU.is_equal),
                           reads=("pkdm",), writes=("msk",))
                    p.emit("v", lambda eng, j=j, si=si: eng.tensor_tensor(
                               s_msk[:], s_msk[:], s_tif[:, si, :, j:j + 1].broadcast_to([128, 8, 8]), op=ALU.mult),
                           reads=("msk", "tif"), writes=("msk",))
                    p.emit("v", lambda eng, si=si: eng.tensor_tensor(
                               s_ixg[:, si], s_ixg[:, si], s_msk[:], op=ALU.add),
                           reads=("msk", "ixg"), writes=("ixg",))
            p.emit("v", lambda eng: eng.scalar_tensor_tensor(
                       out=s_msk[:], in0=s_ixg[:, 0], scalar=256.0, in1=s_ixg[:, 1], op0=ALU.mult, op1=ALU.add),
                   reads=("ixg",), writes=("msk",))
            p.emit("v", lambda eng: eng.tensor_copy(s_eid[:], s_msk[:]), reads=("msk",), writes=("eid",))
            p.emit("v", lambda eng: eng.tensor_reduce(s_smx[:], s_sc8[:], axis=AX, op=ALU.max),
                   reads=("sc8",), writes=("smx",))
            p.emit("v", lambda eng: eng.tensor_tensor(
                       s_sm[:].rearrange("p (h k) -> p h k", h=8), s_sc8[:], s_smx[:].broadcast_to([128, 8, 8]), op=ALU.subtract),
                   reads=("sc8", "smx"), writes=("sm",))
            p.emit("s", lambda eng: eng.activation(s_sm[:], s_sm[:], ACT.Exp),
                   reads=("sm",), writes=("sm",))
            p.emit("v", lambda eng: eng.tensor_reduce(s_smx[:], s_sm[:].rearrange("p (h k) -> p h k", h=8), axis=AX, op=ALU.add),
                   reads=("sm",), writes=("smx",))
            p.emit("v", lambda eng: eng.reciprocal(s_smx[:], s_smx[:]),
                   reads=("smx",), writes=("smx",))
            p.emit("v", lambda eng: eng.tensor_tensor(
                       s_sm[:].rearrange("p (h k) -> p h k", h=8), s_sm[:].rearrange("p (h k) -> p h k", h=8),
                       s_smx[:].broadcast_to([128, 8, 8]), op=ALU.mult),
                   reads=("sm", "smx"), writes=("sm",))
            for hk in range(64):
                gb_ = hk % 2
                hd, kk = hk // 8, hk % 8
                p.dma("g", lambda eng, hd=hd, kk=kk, gb_=gb_: eng.indirect_dma_start(
                          out=s_wd[:, gb_, :], out_offset=None, in_=down_e,
                          in_offset=bass.IndirectOffsetOnAxis(ap=s_eid[:, hd, kk:kk + 1], axis=0)),
                      reads=("eid",), writes=(f"wd{gb_}",))
                p.emit("v", lambda eng, gb_=gb_, pb=pb: eng.tensor_tensor(
                           s_scr[:, gb_, :], s_wd[:, gb_, :], s_x3[:, pb, :], op=ALU.mult),
                       reads=(f"wd{gb_}", f"x3{pb}"), writes=(f"scr{gb_}",))
                p.emit("s", lambda eng, hk=hk, gb_=gb_: eng.activation(
                           s_scr[:, gb_, :], s_scr[:, gb_, :], ACT.Identity, accum_out=s_hm[:, hk:hk + 1]),
                       reads=(f"scr{gb_}",), writes=(f"scr{gb_}", "hm"))
            p.emit("s", lambda eng: eng.activation(s_hm[:], s_hm[:], ACT.Gelu_apprx_tanh),
                   reads=("hm",), writes=("hm",))
            p.emit("v", lambda eng: eng.tensor_tensor(s_sm[:], s_sm[:], s_hm[:], op=ALU.mult),
                   reads=("sm", "hm"), writes=("sm",))
            p.emit("v", lambda eng: eng.memset(s_acc[:, 0, :], 0.0), writes=("acc0",))
            for hk in range(64):
                gb_ = hk % 2
                hd, kk = hk // 8, hk % 8
                p.dma("g", lambda eng, hd=hd, kk=kk, gb_=gb_: eng.indirect_dma_start(
                          out=s_wu[:, gb_, :], out_offset=None, in_=up_e,
                          in_offset=bass.IndirectOffsetOnAxis(ap=s_eid[:, hd, kk:kk + 1], axis=0)),
                      reads=("eid",), writes=(f"wu{gb_}",))
                src_k, dst_k = f"acc{hk % 2}", f"acc{(hk + 1) % 2}"
                p.emit("v", lambda eng, hk=hk, gb_=gb_: eng.scalar_tensor_tensor(
                           out=s_acc[:, (hk + 1) % 2, :], in0=s_wu[:, gb_, :], scalar=s_sm[:, hk:hk + 1],
                           in1=s_acc[:, hk % 2, :], op0=ALU.mult, op1=ALU.add),
                       reads=(f"wu{gb_}", "sm", src_k), writes=(dst_k,))
            p.emit("v", lambda eng, pb=pb: eng.tensor_tensor(s_r4[:], s_x3[:, pb, :], s_acc[:, 0, :], op=ALU.add),
                   reads=(f"x3{pb}", "acc0"), writes=("r4",))
            p.emit("v", lambda eng: eng.tensor_reduce(s_ln[:, 0:1], s_r4[:], axis=AX, op=ALU.add),
                   reads=("r4",), writes=("ln",))
            p.emit("v", lambda eng: eng.tensor_scalar(s_ln[:, 0:1], s_ln[:, 0:1], 1.0 / 1024.0, None, op0=ALU.mult),
                   reads=("ln",), writes=("ln",))
            p.emit("v", lambda eng: eng.tensor_scalar(s_r4[:], s_r4[:], s_ln[:, 0:1], None, op0=ALU.subtract),
                   reads=("r4", "ln"), writes=("r4",))
            p.emit("s", lambda eng: eng.activation(s_acc[:, 1, :], s_r4[:], ACT.Square),
                   reads=("r4",), writes=("acc1",))
            p.emit("v", lambda eng: eng.tensor_reduce(s_ln[:, 1:2], s_acc[:, 1, :], axis=AX, op=ALU.add),
                   reads=("acc1",), writes=("ln2",))
            p.emit("v", lambda eng: eng.tensor_scalar(s_ln[:, 1:2], s_ln[:, 1:2], 1.0 / 1024.0, 1e-5, op0=ALU.mult, op1=ALU.add),
                   reads=("ln2",), writes=("ln2",))
            p.emit("s", lambda eng: eng.activation(s_ln[:, 1:2], s_ln[:, 1:2], ACT.Sqrt),
                   reads=("ln2",), writes=("ln2",))
            p.emit("v", lambda eng: eng.reciprocal(s_ln[:, 1:2], s_ln[:, 1:2]),
                   reads=("ln2",), writes=("ln2",))
            p.emit("v", lambda eng: eng.tensor_scalar(s_r4[:], s_r4[:], s_ln[:, 1:2], None, op0=ALU.mult),
                   reads=("r4", "ln2"), writes=("r4",))
            p.dma("y", lambda eng, tb=tb: eng.dma_start(out=fin_out[tb * 128:(tb + 1) * 128, :], in_=s_r4[:]),
                  reads=("r4",))
        p.finish_outputs("y")
        return p.build()


def host_l6_inputs(ins, q_by_core, stats_by_core, s3t_by_core):
    keys = ins["L1_ffn_keys"]
    keys_t = np.ascontiguousarray(keys.transpose(3, 2, 0, 1))
    stats_all = np.ascontiguousarray(np.stack([stats_by_core[c] for c in range(NCORES)], axis=0))
    gam = ins["L1_ffn_bn_gamma"]; bet = ins["L1_ffn_bn_beta"]
    maps = []
    for c in range(NCORES):
        maps.append({
            "q_fm": np.ascontiguousarray(q_by_core[c]),
            "stats_all": stats_all,
            "s3_tm": np.ascontiguousarray(s3t_by_core[c]),
            "keys_t": keys_t,
            "gamma_r": np.ascontiguousarray(gam.reshape(16, 128).T),
            "beta_r": np.ascontiguousarray(bet.reshape(16, 128).T),
            "down_e": ins["L1_ffn_down_embed"],
            "up_e": ins["L1_ffn_up_embed"],
        })
    return maps


def kernel(**ins):
    core_ids = list(range(NCORES))

    r1 = run_bass_kernel_spmd(build_l1(), host_l1_inputs(ins), core_ids=core_ids)
    xs_by_core = {c: np.asarray(r1.results[c]["xs_out"]) for c in range(NCORES)}
    z_by_core = {c: np.asarray(r1.results[c]["z_out"]) for c in range(NCORES)}

    r2 = run_bass_kernel_spmd(build_l2(), host_l2_inputs(ins, xs_by_core, z_by_core), core_ids=core_ids)
    y_by_core = {c: np.asarray(r2.results[c]["y_out"]) for c in range(NCORES)}

    r3 = run_bass_kernel_spmd(build_l3(), host_l3_inputs(ins, y_by_core), core_ids=core_ids)
    s2_by_core = {c: np.asarray(r3.results[c]["s2_out"]) for c in range(NCORES)}

    r4 = run_bass_kernel_spmd(build_l4(), host_l4_inputs(ins, s2_by_core), core_ids=core_ids)
    s3f_by_core = {c: np.asarray(r4.results[c]["s3f_out"]) for c in range(NCORES)}
    s3t_by_core = {c: np.asarray(r4.results[c]["s3t_out"]) for c in range(NCORES)}

    r5 = run_bass_kernel_spmd(build_l5(), host_l5_inputs(ins, s3f_by_core), core_ids=core_ids)
    q_by_core = {c: np.asarray(r5.results[c]["q_out"]) for c in range(NCORES)}
    stats_by_core = {c: np.asarray(r5.results[c]["st_out"]) for c in range(NCORES)}

    r6 = run_bass_kernel_spmd(build_l6(), host_l6_inputs(ins, q_by_core, stats_by_core, s3t_by_core), core_ids=core_ids)
    out = np.empty((B, S, DM), np.float32)
    for c in range(NCORES):
        b, h = c // 2, c % 2
        out[b, h * TOK:(h + 1) * TOK, :] = np.asarray(r6.results[c]["fin_out"])
    return out


from concourse.libnrt import physical_to_logical_core_id

MAGIC = 0x51AB0000
NSLOT = 8


def _pnc_logical_pairs():
    out = []
    for pnc in range(8):
        out.append((pnc, physical_to_logical_core_id(pnc, 0, container_num_cores=8) % 8))
    return out


class PairXch:

    def __init__(self, p, nc, stack):
        self.p = p
        self.nc = nc
        self.shf = nc.dram_tensor("sh_flags", [2, NSLOT], mybir.dt.int32,
                                  kind="Internal", addr_space="Shared").ap()
        self.shdone = nc.dram_tensor("sh_done", [2, 4], mybir.dt.int32,
                                     kind="Internal", addr_space="Shared").ap()
        self.s_magic = stack.enter_context(nc.sbuf_tensor("x_magic", [1, NSLOT], mybir.dt.int32))
        self.s_zero = stack.enter_context(nc.sbuf_tensor("x_zero", [1, NSLOT], mybir.dt.int32))
        self.s_fb = stack.enter_context(nc.sbuf_tensor("x_fb", [1, NSLOT], mybir.dt.int32))
        self.s_v0 = stack.enter_context(nc.sbuf_tensor("x_v0", [1, 4], mybir.dt.int32))
        self.sem = stack.enter_context(nc.semaphore("x_sem"))
        self.psem = stack.enter_context(nc.semaphore("x_poll"))
        self.cnt = 0
        self.uid = 0

    def prologue(self):
        p, nc = self.p, self.nc
        p.emit("g", lambda eng: eng.iota(self.s_magic[:], pattern=[[1, NSLOT]], base=MAGIC,
                                         channel_multiplier=0, allow_small_or_imprecise_dtypes=True),
               writes=("x_magic",))
        p.emit("v", lambda eng: eng.memset(self.s_zero[:], 0), writes=("x_zero",))
        p.emit("v", lambda eng: eng.memset(self.s_fb[:], 0), writes=("x_fb",))

        def fn(eng):
            self.cnt += 16
            tgt = self.cnt
            for arm in nc.switch_core_id([eng], n=128):
                par = arm.logical % 2
                eng.dma_start(out=self.s_v0[0:1, 0:1],
                              in_=self.shdone[par:par + 1, 0:1]).then_inc(self.sem, 16)
            eng.wait_ge(self.sem, tgt)
            self.cnt += 16
            tgt = self.cnt
            for arm in nc.switch_core_id([eng], n=128):
                par = arm.logical % 2
                eng.dma_start(out=self.s_v0[0:1, 2:3],
                              in_=self.shdone[1 - par:2 - par, 0:1]).then_inc(self.sem, 16)
            eng.wait_ge(self.sem, tgt)
        p.raw("g", fn)
        p.emit("g", lambda eng: eng.engine_nop(), writes=("x_v0",))

    def send(self, slot, data_pairs, reads):
        p, nc = self.p, self.nc
        p.wait_keys("g", list(reads) + ["x_magic"])
        ndma = len(data_pairs)

        def fn(eng, slot=slot, data_pairs=data_pairs, ndma=ndma):
            self.cnt += 16 * ndma
            tgt = self.cnt
            for arm in nc.switch_core_id([eng], n=128):
                par = arm.logical % 2
                for dst_fn, src_ap in data_pairs:
                    src = src_ap(par) if callable(src_ap) else src_ap
                    eng.dma_start(out=dst_fn(par), in_=src).then_inc(self.sem, 16)
            eng.wait_ge(self.sem, tgt)
            self.cnt += 16
            tgt = self.cnt
            for arm in nc.switch_core_id([eng], n=128):
                par = arm.logical % 2
                eng.dma_start(out=self.shf[par:par + 1, slot:slot + 1],
                              in_=self.s_magic[0:1, slot:slot + 1]).then_inc(self.sem, 16)
            eng.wait_ge(self.sem, tgt)
        p.raw("g", fn)
        p.emit("g", lambda eng: eng.engine_nop(), writes=(f"x_sent{slot}",))

    def poll(self, slot):
        p, nc = self.p, self.nc
        magic = MAGIC + slot
        p.wait_keys("g", ["x_fb"])

        def fn(eng, slot=slot, magic=magic):
            self.uid += 1
            u = self.uid
            for arm in nc.switch_core_id([eng], n=128):
                k = int(arm)
                par = arm.logical % 2
                with eng.register(f"xrf{u}_{k}") as rf, eng.register(f"xrt{u}_{k}") as rt:
                    eng.reg_mov(rf, 0)
                    eng.reg_mov(rt, 0)
                    eng.br(f"xpo{u}_{k}")
                    with nc.bb(f"xpo{u}_{k}", valid_engines=[mybir.EngineType.Pool]):
                        eng.reg_add(rt, rt, 16)
                        eng.dma_start(out=self.s_fb[0:1, slot:slot + 1],
                                      in_=self.shf[1 - par:2 - par, slot:slot + 1]).then_inc(self.psem, 16)
                        eng.wait_ge(self.psem, rt)
                        eng.load(rf, self.s_fb[0:1, slot:slot + 1])
                        eng.br_eq(rf, magic, f"xgo{u}_{k}", f"xpo{u}_{k}")
                    nc.enter_bb(f"xgo{u}_{k}", parent=nc.cur_bb, valid_engines=[mybir.EngineType.Pool])
                    eng.engine_nop()
        p.raw("g", fn)
        p.emit("g", lambda eng: eng.engine_nop(), writes=(f"x_recv{slot}",))

    def recv_dma(self, fn_of_par, out_ap_fn, slot, writes, nd=1):
        p, nc = self.p, self.nc
        p.wait_keys("g", [f"x_recv{slot}"] + list(writes))

        def fn(eng):
            self.cnt += 16 * nd
            tgt = self.cnt
            for arm in nc.switch_core_id([eng], n=128):
                par = arm.logical % 2
                ins = fn_of_par(par)
                outs = out_ap_fn(par)
                if not isinstance(ins, (list, tuple)):
                    ins, outs = [ins], [outs]
                for i_ap, o_ap in zip(ins, outs):
                    eng.dma_start(out=o_ap, in_=i_ap).then_inc(self.sem, 16)
            eng.wait_ge(self.sem, tgt)
        p.raw("g", fn)
        p.emit("g", lambda eng: eng.engine_nop(), writes=tuple(writes))

    def epilogue(self, after_keys):
        p, nc = self.p, self.nc
        p.wait_keys("g", list(after_keys) + ["x_v0", "x_zero"])

        def fn(eng):
            self.uid += 1
            u = self.uid
            self.cnt += 16
            t1 = self.cnt
            self.cnt += 16
            t2 = self.cnt
            for arm in nc.switch_core_id([eng], n=128):
                k = int(arm)
                par = arm.logical % 2
                with eng.register(f"xdn{u}_{k}") as rd:
                    eng.load(rd, self.s_v0[0:1, 0:1])
                    eng.reg_add(rd, rd, 1)
                    eng.store(self.s_v0[0:1, 1:2], rd)
                eng.dma_start(out=self.shdone[par:par + 1, 0:1],
                              in_=self.s_v0[0:1, 1:2]).then_inc(self.sem, 16)
                eng.wait_ge(self.sem, t1)
                with eng.register(f"xdf{u}_{k}") as rf, eng.register(f"xdv{u}_{k}") as rv, \
                        eng.register(f"xdt{u}_{k}") as rt:
                    eng.load(rv, self.s_v0[0:1, 2:3])
                    eng.reg_mov(rt, 0)
                    eng.br(f"xdp{u}_{k}")
                    with nc.bb(f"xdp{u}_{k}", valid_engines=[mybir.EngineType.Pool]):
                        eng.reg_add(rt, rt, 16)
                        eng.dma_start(out=self.s_fb[0:1, 0:1],
                                      in_=self.shdone[1 - par:2 - par, 0:1]).then_inc(self.psem, 16)
                        eng.wait_ge(self.psem, rt)
                        eng.load(rf, self.s_fb[0:1, 0:1])
                        eng.br_eq(rf, rv, f"xdp{u}_{k}", f"xdz{u}_{k}")
                    nc.enter_bb(f"xdz{u}_{k}", parent=nc.cur_bb, valid_engines=[mybir.EngineType.Pool])
                eng.dma_start(out=self.shf[par:par + 1, :], in_=self.s_zero[0:1, :]).then_inc(self.sem, 16)
                eng.wait_ge(self.sem, t2)
        p.raw("g", fn)
        p.emit("g", lambda eng: eng.engine_nop(), writes=("x_done",))


def build_fusedA():
    nc = bass.Bass("TRN2", target_bir_lowering=False, detect_race_conditions=False)
    stack = ExitStack()
    p = Prog(nc, stack)
    xch = PairXch(p, nc, stack)

    x_fm = nc.dram_tensor("x_fm", [DM, S], F32, kind="ExternalInput").ap()
    x_res = nc.dram_tensor("x_res", [DM, TOK], F32, kind="ExternalInput").ap()
    winx = nc.dram_tensor("winx", [DM, DH], F32, kind="ExternalInput").ap()
    winz = nc.dram_tensor("winz", [DM, DH], F32, kind="ExternalInput").ap()
    binx = nc.dram_tensor("binx", [128, DH // 128], F32, kind="ExternalInput").ap()
    binz = nc.dram_tensor("binz", [128, DH // 128], F32, kind="ExternalInput").ap()
    wx = nc.dram_tensor("wx", [DH, 96], F32, kind="ExternalInput").ap()
    bx_fm = nc.dram_tensor("bx_fm", [96, 1], F32, kind="ExternalInput").ap()
    bx_tm = nc.dram_tensor("bx_tm", [128, 96], F32, kind="ExternalInput").ap()
    wdt = nc.dram_tensor("wdt", [64, DH], F32, kind="ExternalInput").ap()
    bdt = nc.dram_tensor("bdt", [128, 8], F32, kind="ExternalInput").ap()
    alog = nc.dram_tensor("alog", [128, 8, NST], F32, kind="ExternalInput").ap()
    dp_r = nc.dram_tensor("dp_r", [128, 8], F32, kind="ExternalInput").ap()
    wout = nc.dram_tensor("wout", [DH, DM], F32, kind="ExternalInput").ap()
    boutr = nc.dram_tensor("boutr", [128, 8], F32, kind="ExternalInput").ap()
    wgT = nc.dram_tensor("wgT", [DM, 4096], F32, kind="ExternalInput").ap()
    wuT = nc.dram_tensor("wuT", [DM, 4096], F32, kind="ExternalInput").ap()
    wdT = nc.dram_tensor("wdT", [4096, DM], F32, kind="ExternalInput").ap()
    wqkv = nc.dram_tensor("wqkv", [DM, 3 * DM], F32, kind="ExternalInput").ap()
    bq_r = nc.dram_tensor("bq_r", [128, 8], F32, kind="ExternalInput").ap()
    bk_r = nc.dram_tensor("bk_r", [128, 8], F32, kind="ExternalInput").ap()
    bv_bc = nc.dram_tensor("bv_bc", [128, DM], F32, kind="ExternalInput").ap()
    wout1 = nc.dram_tensor("wout1", [DM, DM], F32, kind="ExternalInput").ap()
    bo_r = nc.dram_tensor("bo_r", [128, 8], F32, kind="ExternalInput").ap()
    mask0 = nc.dram_tensor("mask0", [128, 384], F32, kind="ExternalInput").ap()
    maskm = nc.dram_tensor("maskm", [128, 384], F32, kind="ExternalInput").ap()
    mask15 = nc.dram_tensor("mask15", [128, 384], F32, kind="ExternalInput").ap()
    zeros_col = nc.dram_tensor("zeros_col", [DM, 128], F32, kind="ExternalInput").ap()
    wq = nc.dram_tensor("wq", [DM, 2048], F32, kind="ExternalInput").ap()

    q_out = nc.dram_tensor("q_out", [2048, TOK], F32, kind="ExternalOutput").ap()
    st_out = nc.dram_tensor("st_out", [128, 32], F32, kind="ExternalOutput").ap()
    s3t_out = nc.dram_tensor("s3t_out", [TOK, DM], F32, kind="ExternalOutput").ap()

    xs_hbm = nc.dram_tensor("xs_hbm", [DH, S], F32, kind="Internal").ap()
    z_hbm = nc.dram_tensor("z_hbm", [DH, S], F32, kind="Internal").ap()
    p_hbm = nc.dram_tensor("p_hbm", [128, 8, S], F32, kind="Internal").ap()
    s1_hbm = nc.dram_tensor("s1_hbm", [DM, TOK], F32, kind="Internal").ap()
    gu_hbm = nc.dram_tensor("gu_hbm", [4096, TOK], F32, kind="Internal").ap()
    wdq_hbm = nc.dram_tensor("wdq_hbm", [4096, DM], BF16, kind="Internal").ap()
    s2_hbm = nc.dram_tensor("s2_hbm", [DM, TOK], F32, kind="Internal").ap()
    halo_hbm = nc.dram_tensor("halo_hbm", [DM, 256], F32, kind="Internal").ap()
    k_hbm = nc.dram_tensor("k_hbm", [DM, TOK + 256], F32, kind="Internal").ap()
    v_hbm = nc.dram_tensor("v_hbm", [TOK + 256, DM], F32, kind="Internal").ap()
    s3f_hbm = nc.dram_tensor("s3f_hbm", [DM, TOK], F32, kind="Internal").ap()

    sh_xpf = nc.dram_tensor("sh_xpf", [2, 96, S], F32, kind="Internal", addr_space="Shared").ap()
    sh_xpt = nc.dram_tensor("sh_xpt", [2, 128, 32, 96], F32, kind="Internal", addr_space="Shared").ap()
    sh_P = nc.dram_tensor("sh_P", [2, 128, 8, S], F32, kind="Internal", addr_space="Shared").ap()
    sh_halo = nc.dram_tensor("sh_halo", [2, DM, 128], F32, kind="Internal", addr_space="Shared").ap()

    s_ones = stack.enter_context(nc.sbuf_tensor("s_ones", [128, 128], F32))
    s_one1 = stack.enter_context(nc.sbuf_tensor("s_one1", [64, 128], F32))
    s_iota = stack.enter_context(nc.sbuf_tensor("s_iota", [128, 128], F32))
    s_ident = stack.enter_context(nc.sbuf_tensor("s_ident", [128, 128], F32))
    s_row = stack.enter_context(nc.sbuf_tensor("s_row", [96, 128], F32))
    s_cmg = stack.enter_context(nc.sbuf_tensor("s_cmg", [128, 2], F32))
    s_wsc = stack.enter_context(nc.sbuf_tensor("s_wsc", [128, 8], F32))
    ps_pool = [stack.enter_context(nc.psum_tensor(f"pp{i}", [128, 512], F32)) for i in range(8)]
    ps = ps_pool

    p.emit("v", lambda eng: eng.memset(s_ones[:], 1.0), writes=("ones",))
    p.emit("v", lambda eng: eng.memset(s_one1[:], 1.0), writes=("one1",))
    make_identity(p, s_iota, s_ident)
    xch.prologue()

    with ExitStack() as ph:
        f0_wt = ph.enter_context(nc.sbuf_tensor("f0_wt", [128, 2, 4096], F32))
        f0_sum = ph.enter_context(nc.sbuf_tensor("f0_sum", [128, 24], F32))
        wviews = [wgT.rearrange("(c p) n -> p c n", p=128),
                  wuT.rearrange("(c p) n -> p c n", p=128),
                  wdT.rearrange("(c p) n -> p c n", p=128)]
        for wi, wv in enumerate(wviews):
            cper = wv.shape[1] // 8
            for j in range(8):
                pb = j % 2
                p.dma("y", lambda eng, wv=wv, j=j, pb=pb, cper=cper: eng.dma_start(
                          out=f0_wt[:, pb, :].rearrange("p (c n) -> p c n", c=cper),
                          in_=wv[:, j * cper:(j + 1) * cper, :]),
                      writes=(f"f0w{pb}",))
                p.emit("v", lambda eng, wi=wi, j=j, pb=pb: eng.tensor_reduce(
                           f0_sum[:, wi * 8 + j:wi * 8 + j + 1], f0_wt[:, pb, :], axis=AX,
                           op=ALU.add, apply_absolute_value=True),
                       reads=(f"f0w{pb}",), writes=(f"f0s{wi}_{j}",))
        for wi in range(3):
            p.emit("v", lambda eng, wi=wi: eng.tensor_reduce(
                       s_wsc[:, wi:wi + 1], f0_sum[:, wi * 8:(wi + 1) * 8], axis=AX, op=ALU.add),
                   reads=tuple(f"f0s{wi}_{j}" for j in range(8)), writes=(f"f0t{wi}",))
        p.emit("t", lambda eng: eng.matmul(ps[6][:, 0:3], s_ones[:], s_wsc[:, 0:3], start=True, stop=True),
               reads=("f0t0", "f0t1", "f0t2", "ones"), writes=("pp6",))
        p.emit("v", lambda eng: eng.tensor_copy(s_wsc[:, 0:3], ps[6][:, 0:3]),
               reads=("pp6",), writes=("wsc",))
        p.emit("v", lambda eng: eng.tensor_scalar(s_wsc[:, 0:3], s_wsc[:, 0:3], 1.0 / (4096.0 * 1024.0), 1e-5,
                                                  op0=ALU.mult, op1=ALU.max),
               reads=("wsc",), writes=("wsc",))
        p.emit("v", lambda eng: eng.reciprocal(s_wsc[:, 3:6], s_wsc[:, 0:3]),
               reads=("wsc",), writes=("wsc2",))
    p.barrier()
    with ExitStack() as ph:
        f0b_wt = ph.enter_context(nc.sbuf_tensor("f0b_wt", [128, 2, 2048], F32))
        f0b_wq = ph.enter_context(nc.sbuf_tensor("f0b_wq", [128, 2, 2048], F32))
        f0b_wb = ph.enter_context(nc.sbuf_tensor("f0b_wb", [128, 2, 2048], BF16))
        wv = wdT.rearrange("(c p) n -> p c n", p=128)
        qv = wdq_hbm.rearrange("(c p) n -> p c n", p=128)
        for j in range(16):
            pb = j % 2
            p.dma("y", lambda eng, j=j, pb=pb: eng.dma_start(
                      out=f0b_wt[:, pb, :].rearrange("p (c n) -> p c n", c=2), in_=wv[:, 2 * j:2 * j + 2, :]),
                  writes=(f"f0bw{pb}",))
            p.emit("v", lambda eng, pb=pb: eng.tensor_scalar(f0b_wq[:, pb, :], f0b_wt[:, pb, :], s_wsc[:, 5:6], None, op0=ALU.mult),
                   reads=(f"f0bw{pb}", "wsc2"), writes=(f"f0bq{pb}",))
            p.emit("v", lambda eng, pb=pb: eng.tensor_scalar(f0b_wt[:, pb, :], f0b_wq[:, pb, :], -0.5, None, op0=ALU.is_lt),
                   reads=(f"f0bq{pb}",), writes=(f"f0bw{pb}",))
            p.emit("v", lambda eng, pb=pb: eng.scalar_tensor_tensor(out=f0b_wb[:, pb, :], in0=f0b_wq[:, pb, :], scalar=0.5,
                                                                    in1=f0b_wt[:, pb, :], op0=ALU.is_gt, op1=ALU.subtract),
                   reads=(f"f0bq{pb}", f"f0bw{pb}"), writes=(f"f0bb{pb}",))
            p.dma("y", lambda eng, j=j, pb=pb: eng.dma_start(
                      out=qv[:, 2 * j:2 * j + 2, :], in_=f0b_wb[:, pb, :].rearrange("p (c n) -> p c n", c=2)),
                  reads=(f"f0bb{pb}",))
    p.barrier()

    xp_stack = ExitStack()
    s_xpf = xp_stack.enter_context(nc.sbuf_tensor("s_xpf", [96, S], F32))
    s_xpt = xp_stack.enter_context(nc.sbuf_tensor("s_xpt", [128, 32, 96], F32))
    with ExitStack() as ph:
        TT = 512
        f1_w = ph.enter_context(nc.sbuf_tensor("f1_w", [128, 2, 8, 8, 128], F32))
        f1_wx = ph.enter_context(nc.sbuf_tensor("f1_wx", [128, 8, 96], F32))
        f1_x = ph.enter_context(nc.sbuf_tensor("f1_x", [128, 2, 8, TT], F32))
        f1_b = ph.enter_context(nc.sbuf_tensor("f1_b", [128, 2, 8], F32))
        f1_o = ph.enter_context(nc.sbuf_tensor("f1_o", [128, 2, 2, 8, TT], F32))
        p.dma("y", lambda eng: eng.dma_start(out=f1_w[:, 0], in_=winx.rearrange("(kc p) (nc8 f) -> p kc nc8 f", p=128, f=128)),
              writes=("wxx",))
        p.dma("y", lambda eng: eng.dma_start(out=f1_w[:, 1], in_=winz.rearrange("(kc p) (nc8 f) -> p kc nc8 f", p=128, f=128)),
              writes=("wzz",))
        p.dma("y", lambda eng: eng.dma_start(out=f1_b[:, 0], in_=binx), writes=("bxx",))
        p.dma("y", lambda eng: eng.dma_start(out=f1_b[:, 1], in_=binz), writes=("bzz",))
        p.dma("y", lambda eng: eng.dma_start(out=f1_wx[:], in_=wx.rearrange("(kc p) m -> p kc m", p=128)), writes=("wxp",))
        x_r = x_fm.rearrange("(kc p) t -> p kc t", p=128)
        xs_w = xs_hbm.rearrange("(nc8 p) t -> p nc8 t", p=128)
        z_w = z_hbm.rearrange("(nc8 p) t -> p nc8 t", p=128)
        nb = 0
        for tb in range(S // TT):
            pb = tb % 2
            p.dma("y", lambda eng, tb=tb, pb=pb: eng.dma_start(
                      out=f1_x[:, pb], in_=x_r[:, :, tb * TT:(tb + 1) * TT]),
                  writes=(f"x{pb}",))
            for xz in range(2):
                for nc8 in range(8):
                    bi = nb % 4
                    nb += 1
                    pairs = [(f1_w[:, xz, kc, nc8, :], f1_x[:, pb, kc, :]) for kc in range(8)]
                    mm_accum(p, ps[bi][:], pairs, f"pp{bi}", (f"x{pb}", "wxx" if xz == 0 else "wzz"))
                    func = ACT.Silu if xz == 0 else ACT.Identity
                    def ev(eng, bi=bi, xz=xz, nc8=nc8, func=func, pb=pb):
                        return eng.activation(f1_o[:, pb, xz, nc8, :], ps[bi][:], func,
                                              bias=f1_b[:, xz, nc8:nc8 + 1], scale=1.0)
                    p.emit("s", ev, reads=(f"pp{bi}", "bxx" if xz == 0 else "bzz"),
                           writes=(f"o{pb}_{xz}_{nc8}",))
            for xz, out_t in ((0, xs_w), (1, z_w)):
                p.dma("y", lambda eng, tb=tb, pb=pb, xz=xz, out_t=out_t: eng.dma_start(
                          out=out_t[:, :, tb * TT:(tb + 1) * TT], in_=f1_o[:, pb, xz]),
                      reads=tuple(f"o{pb}_{xz}_{nc8}" for nc8 in range(8)))
            pairs = [(f1_wx[:, kc, :], f1_o[:, pb, 0, kc, :]) for kc in range(8)]
            mm_accum(p, ps[4][0:96, :], pairs, "pp4", ("wxp",) + tuple(f"o{pb}_0_{nc8}" for nc8 in range(8)))
            p.emit("v", lambda eng, tb=tb: eng.tensor_copy(s_xpf[:, tb * TT:(tb + 1) * TT], ps[4][0:96, :]),
                   reads=("pp4",), writes=("xpf",))
            for j in range(TT // 128):
                pairs = [(f1_o[:, pb, 0, kc, j * 128:(j + 1) * 128], f1_wx[:, kc, :]) for kc in range(8)]
                mm_accum(p, ps[5][:, 0:96], pairs, "pp5", ("wxp",) + tuple(f"o{pb}_0_{nc8}" for nc8 in range(8)))
                p.emit("v", lambda eng, tb=tb, j=j: eng.tensor_copy(s_xpt[:, tb * (TT // 128) + j, :], ps[5][:, 0:96]),
                       reads=("pp5",), writes=("xpt",))
    xch.send(0, [(lambda par: sh_xpf[par], s_xpf[:]), (lambda par: sh_xpt[par], s_xpt[:])],
             reads=["xpf", "xpt"])
    xch.poll(0)
    with ExitStack() as ph:
        fx_f = ph.enter_context(nc.sbuf_tensor("fx_f", [96, S], F32))
        fx_t = ph.enter_context(nc.sbuf_tensor("fx_t", [128, 32, 96], F32))
        fx_bt = ph.enter_context(nc.sbuf_tensor("fx_bt", [128, 96], F32))
        fx_bf = ph.enter_context(nc.sbuf_tensor("fx_bf", [96, 1], F32))
        xch.recv_dma(lambda par: [sh_xpf[1 - par], sh_xpt[1 - par]],
                     lambda par: [fx_f[:], fx_t[:]], 0, ["xpf2", "xpt2"], nd=2)
        p.dma("y", lambda eng: eng.dma_start(out=fx_bf[:], in_=bx_fm), writes=("bxf",))
        p.dma("y", lambda eng: eng.dma_start(out=fx_bt[:], in_=bx_tm), writes=("bxt",))
        p.emit("v", lambda eng: eng.tensor_tensor(s_xpf[:], s_xpf[:], fx_f[:], op=ALU.add),
               reads=("xpf", "xpf2"), writes=("xpf",))
        p.emit("v", lambda eng: eng.tensor_scalar(s_xpf[:], s_xpf[:], fx_bf[:], None, op0=ALU.add),
               reads=("xpf", "bxf"), writes=("xpf",))
        p.emit("v", lambda eng: eng.tensor_tensor(s_xpt[:], s_xpt[:], fx_t[:], op=ALU.add),
               reads=("xpt", "xpt2"), writes=("xpt",))
        p.emit("v", lambda eng: eng.tensor_tensor(s_xpt[:], s_xpt[:], fx_bt[:].unsqueeze(1).broadcast_to([128, 32, 96]), op=ALU.add),
               reads=("xpt", "bxt"), writes=("xpt",))
    p.barrier()
    return _fusedA_part2(nc, stack, p, xch, locals())


def _fusedA_part2(nc, stack, p, xch, env):
    g = env
    s_xpf, s_xpt = g["s_xpf"], g["s_xpt"]
    s_ones, s_one1, s_ident = g["s_ones"], g["s_one1"], g["s_ident"]
    s_row, s_cmg, s_wsc = g["s_row"], g["s_cmg"], g["s_wsc"]
    ps = g["ps"]
    xs_hbm, z_hbm, p_hbm, s1_hbm = g["xs_hbm"], g["z_hbm"], g["p_hbm"], g["s1_hbm"]
    gu_hbm, wdq_hbm, s2_hbm, halo_hbm = g["gu_hbm"], g["wdq_hbm"], g["s2_hbm"], g["halo_hbm"]
    k_hbm, v_hbm, s3f_hbm = g["k_hbm"], g["v_hbm"], g["s3f_hbm"]
    sh_P, sh_halo = g["sh_P"], g["sh_halo"]
    wdt, bdt, alog, dp_r, wout, boutr = g["wdt"], g["bdt"], g["alog"], g["dp_r"], g["wout"], g["boutr"]
    x_res, wgT, wuT, wdT = g["x_res"], g["wgT"], g["wuT"], g["wdT"]
    wqkv, bq_r, bk_r, bv_bc, wout1, bo_r = g["wqkv"], g["bq_r"], g["bk_r"], g["bv_bc"], g["wout1"], g["bo_r"]
    zeros_col, wq = g["zeros_col"], g["wq"]
    q_out, st_out, s3t_out = g["q_out"], g["st_out"], g["s3t_out"]

    TC = 256
    NCH = S // TC
    NT = DH // 128
    with ExitStack() as ph:
        s_xs = ph.enter_context(nc.sbuf_tensor("s_xs", [128, 8, TC], F32))
        s_z = ph.enter_context(nc.sbuf_tensor("s_z", [128, 8, TC], F32))
        s_wdt = ph.enter_context(nc.sbuf_tensor("s_wdt", [64, 8, 128], F32))
        s_bdt = ph.enter_context(nc.sbuf_tensor("s_bdt", [128, 8], F32))
        s_A = ph.enter_context(nc.sbuf_tensor("s_A", [128, 8, NST], F32))
        s_dp = ph.enter_context(nc.sbuf_tensor("s_dp", [128, 8], F32))
        s_dl = ph.enter_context(nc.sbuf_tensor("s_dl", [128, 8, TC], F32))
        s_scz = ph.enter_context(nc.sbuf_tensor("s_scz", [128, 8, TC], F32))
        s_dx = ph.enter_context(nc.sbuf_tensor("s_dx", [128, 8, TC], F32))
        s_B = ph.enter_context(nc.sbuf_tensor("s_B", [128, TC, NST], F32))
        s_C = ph.enter_context(nc.sbuf_tensor("s_C", [128, TC, NST], F32))
        s_dA = ph.enter_context(nc.sbuf_tensor("s_dA", [128, TC, NST], F32))
        s_u = ph.enter_context(nc.sbuf_tensor("s_u", [128, TC, NST], F32))
        s_h = ph.enter_context(nc.sbuf_tensor("s_h", [128, TC, NST], F32))
        s_st = ph.enter_context(nc.sbuf_tensor("s_st", [128, 8, NST], F32))
        s_y = ph.enter_context(nc.sbuf_tensor("s_y", [128, 8, TC], F32))
        s_wo = ph.enter_context(nc.sbuf_tensor("s_wo", [128, 8, 8, 128], F32))
        s_pe = ph.enter_context(nc.sbuf_tensor("s_pe", [128, 8, TC], F32))
        p.dma("y", lambda eng: eng.dma_start(out=s_wdt[:], in_=wdt.rearrange("k (nc8 m) -> k nc8 m", m=128)), writes=("wdt",))
        p.dma("y", lambda eng: eng.dma_start(out=s_bdt[:], in_=bdt), writes=("bdt",))
        p.dma("y", lambda eng: eng.dma_start(out=s_A[:], in_=alog), writes=("alog",))
        p.dma("y", lambda eng: eng.dma_start(out=s_dp[:], in_=dp_r), writes=("dpr",))
        p.dma("y", lambda eng: eng.dma_start(out=s_wo[:], in_=wout.rearrange("(kc p) (nc8 f) -> p kc nc8 f", p=128, f=128)), writes=("wo",))
        p.emit("s", lambda eng: eng.activation(s_A[:], s_A[:], ACT.Exp),
               reads=("alog",), writes=("alog",))
        p.emit("v", lambda eng: eng.tensor_scalar(s_A[:], s_A[:], -1.0, None, op0=ALU.mult),
               reads=("alog",), writes=("A",))
        p.emit("v", lambda eng: eng.memset(s_st[:], 0.0), writes=("st",))
        xs_r = xs_hbm.rearrange("(kc p) t -> p kc t", p=128)
        z_r = z_hbm.rearrange("(kc p) t -> p kc t", p=128)
        for cix in range(NCH):
            t0 = cix * TC
            p.dma("y", lambda eng, t0=t0: eng.dma_start(out=s_xs[:], in_=xs_r[:, :, t0:t0 + TC]), writes=("xs",))
            p.dma("y", lambda eng, t0=t0: eng.dma_start(out=s_z[:], in_=z_r[:, :, t0:t0 + TC]), writes=("z",))
            for nc8 in range(8):
                bi = 2 + (nc8 % 2)
                mm_accum(p, ps[bi][:, 0:TC], [(s_wdt[:, nc8, :], s_xpf[0:64, t0:t0 + TC])], f"pp{bi}", ("wdt", "xpf"))
                p.emit("s", lambda eng, nc8=nc8, bi=bi: eng.activation(
                           s_dl[:, nc8, :], ps[bi][:, 0:TC], ACT.Identity, bias=s_bdt[:, nc8:nc8 + 1], scale=1.0),
                       reads=(f"pp{bi}", "bdt"), writes=("dl",))
            p.emit("s", lambda eng: eng.activation(s_scz[:], s_dl[:], ACT.Abs),
                   reads=("dl",), writes=("scz",))
            p.emit("s", lambda eng: eng.activation(s_scz[:], s_scz[:], ACT.Exp, scale=-1.0),
                   reads=("scz",), writes=("scz",))
            p.emit("s", lambda eng: eng.activation(s_scz[:], s_scz[:], ACT.Ln, bias=1.0, scale=1.0),
                   reads=("scz",), writes=("scz",))
            p.emit("v", lambda eng: eng.scalar_tensor_tensor(
                       out=s_dl[:], in0=s_dl[:], scalar=0.0, in1=s_scz[:], op0=ALU.max, op1=ALU.add),
                   reads=("scz", "dl"), writes=("dl",))
            p.emit("v", lambda eng: eng.tensor_tensor(s_dx[:], s_dl[:], s_xs[:], op=ALU.mult),
                   reads=("dl", "xs"), writes=("dx",))
            for nm, sl, keyb in (("B", slice(64, 80), "B"), ("C", slice(80, 96), "C")):
                dst = s_B if nm == "B" else s_C
                for j in range(TC // 128):
                    gtb = t0 // 128 + j
                    def st0(eng, dst=dst, sl=sl, j=j, gtb=gtb):
                        return eng.dma_start(out=dst[0:1, j * 128:(j + 1) * 128, :], in_=s_xpt[:, gtb, sl])
                    p.dma("g", st0, reads=("xpt",), writes=(keyb,))
                kk = 1
                while kk < 128:
                    p.dma("g", lambda eng, dst=dst, kk=kk: eng.dma_start(out=dst[kk:2 * kk], in_=dst[0:kk]),
                          reads=(keyb,), writes=(keyb,))
                    kk *= 2
            for dc in range(NT):
                for n in range(NST):
                    p.emit("s", lambda eng, dc=dc, n=n: eng.activation(
                               s_dA[:, :, n], s_dl[:, dc, :], ACT.Exp, scale=s_A[:, dc, n:n + 1]),
                           reads=("dl", "A"), writes=("dA",))
                p.emit("v", lambda eng, dc=dc: eng.tensor_tensor(
                           s_u[:], s_dx[:, dc, :].unsqueeze(2).broadcast_to([128, TC, NST]), s_B[:], op=ALU.mult),
                       reads=("dx", "B"), writes=("u",))
                for n in range(NST):
                    p.emit("v", lambda eng, dc=dc, n=n: eng.tensor_tensor_scan(
                               s_h[:, :, n], s_dA[:, :, n], s_u[:, :, n], s_st[:, dc, n:n + 1],
                               op0=ALU.mult, op1=ALU.add),
                           reads=("dA", "u", "st"), writes=("h",))
                p.emit("v", lambda eng, dc=dc: eng.tensor_copy(s_st[:, dc, :], s_h[:, TC - 1, :]),
                       reads=("h",), writes=("st",))
                p.emit("v", lambda eng: eng.tensor_tensor(s_dA[:], s_h[:], s_C[:], op=ALU.mult),
                       reads=("C", "h"), writes=("dA",))
                p.emit("v", lambda eng, dc=dc: eng.tensor_reduce(
                           s_y[:, dc, :], s_dA[:], axis=AX, op=ALU.add),
                       reads=("dA",), writes=(f"y{dc}",))
            p.emit("s", lambda eng: eng.activation(s_scz[:], s_z[:], ACT.Silu),
                   reads=("z",), writes=("scz",))
            for dc in range(NT):
                p.emit("v", lambda eng, dc=dc: eng.scalar_tensor_tensor(
                           out=s_y[:, dc, :], in0=s_xs[:, dc, :], scalar=s_dp[:, dc:dc + 1],
                           in1=s_y[:, dc, :], op0=ALU.mult, op1=ALU.add),
                       reads=("xs", "dpr", f"y{dc}"), writes=(f"y{dc}",))
            p.emit("v", lambda eng: eng.tensor_tensor(s_y[:], s_y[:], s_scz[:], op=ALU.mult),
                   reads=("scz",) + tuple(f"y{dc}" for dc in range(NT)),
                   writes=tuple(f"y{dc}" for dc in range(NT)))
            for oc in range(8):
                bi = 4 + (oc % 2)
                pairs = [(s_wo[:, kc, oc, :], s_y[:, kc, :]) for kc in range(8)]
                mm_accum(p, ps[bi][:, 0:TC], pairs, f"pp{bi}", ("wo",) + tuple(f"y{dc}" for dc in range(NT)))
                p.emit("s", lambda eng, oc=oc, bi=bi: eng.copy(s_pe[:, oc, :], ps[bi][:, 0:TC]),
                       reads=(f"pp{bi}",), writes=("pe",))
            p.dma("y", lambda eng, t0=t0: eng.dma_start(out=p_hbm[:, :, t0:t0 + TC], in_=s_pe[:]),
                  reads=("pe",))
    p.wait_keys("g", ["pe"])
    xch.send(1, [(lambda par: sh_P[par], p_hbm)], reads=["pe"])
    xch.poll(1)
    p.barrier()
    g["xp_stack"].close()
    return _fusedA_part3(nc, stack, p, xch, g)


def _fusedA_part3(nc, stack, p, xch, g):
    s_ones, s_one1, s_ident = g["s_ones"], g["s_one1"], g["s_ident"]
    s_row, s_cmg, s_wsc = g["s_row"], g["s_cmg"], g["s_wsc"]
    ps = g["ps"]
    p_hbm, s1_hbm, gu_hbm, wdq_hbm = g["p_hbm"], g["s1_hbm"], g["gu_hbm"], g["wdq_hbm"]
    s2_hbm, halo_hbm, k_hbm, v_hbm, s3f_hbm = g["s2_hbm"], g["halo_hbm"], g["k_hbm"], g["v_hbm"], g["s3f_hbm"]
    sh_P, sh_halo = g["sh_P"], g["sh_halo"]
    x_res, wgT, wuT, wdT, boutr = g["x_res"], g["wgT"], g["wuT"], g["wdT"], g["boutr"]
    wqkv, bq_r, bk_r, bv_bc, wout1, bo_r = g["wqkv"], g["bq_r"], g["bk_r"], g["bv_bc"], g["wout1"], g["bo_r"]
    zeros_col, wq = g["zeros_col"], g["wq"]
    q_out, st_out, s3t_out = g["q_out"], g["st_out"], g["s3t_out"]

    ln_stack = ExitStack()
    s_inv1 = ln_stack.enter_context(nc.sbuf_tensor("s_inv1", [128, TOK], F32))
    s_kq1 = ln_stack.enter_context(nc.sbuf_tensor("s_kq1", [128, 8, TOK], BF16))
    g["s_inv1"], g["s_kq1"], g["ln_stack"] = s_inv1, s_kq1, ln_stack

    with ExitStack() as ph:
        TT1 = 256
        f3_pa = ph.enter_context(nc.sbuf_tensor("f3_pa", [128, 8, TT1], F32))
        f3_pb = ph.enter_context(nc.sbuf_tensor("f3_pb", [128, 8, TT1], F32))
        f3_x = ph.enter_context(nc.sbuf_tensor("f3_x", [128, 8, TT1], F32))
        f3_b = ph.enter_context(nc.sbuf_tensor("f3_b", [128, 8], F32))
        f3_r1 = ph.enter_context(nc.sbuf_tensor("f3_r1", [128, 8, TT1], F32))
        f3_sq = ph.enter_context(nc.sbuf_tensor("f3_sq", [128, 8, TT1], F32))
        f3_sc = ph.enter_context(nc.sbuf_tensor("f3_sc", [128, 4, TT1], F32))
        f3_s1 = ph.enter_context(nc.sbuf_tensor("f3_s1", [128, 8, TT1], F32))
        f3_ki = ph.enter_context(nc.sbuf_tensor("f3_ki", [128, 8, TT1], mybir.dt.int32))
        f3_qs = ph.enter_context(nc.sbuf_tensor("f3_qs", [128, 8, TT1], F32))
        p.dma("y", lambda eng: eng.dma_start(out=f3_b[:], in_=boutr), writes=("bo",))
        xv = x_res.rearrange("(kc p) t -> p kc t", p=128)
        s1v = s1_hbm.rearrange("(kc p) t -> p kc t", p=128)
        for tt in range(TOK // TT1):
            xch.recv_dma(
                lambda par, tt=tt: [sh_P[par][:, :, par * TOK + tt * TT1: par * TOK + (tt + 1) * TT1],
                                    sh_P[1 - par][:, :, par * TOK + tt * TT1: par * TOK + (tt + 1) * TT1]],
                lambda par: [f3_pa[:], f3_pb[:]], 1, ["pa", "pb"], nd=2)
            p.dma("y", lambda eng, tt=tt: eng.dma_start(out=f3_x[:], in_=xv[:, :, tt * TT1:(tt + 1) * TT1]),
                  writes=("xr",))
            p.emit("v", lambda eng: eng.tensor_tensor(f3_r1[:], f3_pa[:], f3_pb[:], op=ALU.add),
                   reads=("pa", "pb"), writes=("r1",))
            for nc8 in range(8):
                p.emit("v", lambda eng, nc8=nc8: eng.tensor_scalar(
                           f3_r1[:, nc8, :], f3_r1[:, nc8, :], f3_b[:, nc8:nc8 + 1], None, op0=ALU.add),
                       reads=("r1", "bo"), writes=("r1",))
            p.emit("v", lambda eng: eng.tensor_tensor(f3_r1[:], f3_r1[:], f3_x[:], op=ALU.add),
                   reads=("r1", "xr"), writes=("r1",))
            emit_ln_fm(p, f"f3ln_{tt}", f3_r1, "r1", f3_s1, "s1", f3_sq, "sq", s_ones,
                       ps[4], "pp4", ps[5], "pp5", f3_qs[:, 0:2], "qs", TT1)
            p.dma("y", lambda eng, tt=tt: eng.dma_start(out=s1v[:, :, tt * TT1:(tt + 1) * TT1], in_=f3_s1[:]),
                  reads=("s1",))
            p.emit("s", lambda eng: eng.activation(f3_qs[:], f3_s1[:], ACT.Abs),
                   reads=("s1",), writes=("qs",))
            p.emit("v", lambda eng: eng.tensor_tensor(f3_sc[:, 0, :], f3_qs[:, 0, :], f3_qs[:, 1, :], op=ALU.max),
                   reads=("qs",), writes=("am",))
            for kc in range(2, 8):
                p.emit("v", lambda eng, kc=kc: eng.tensor_tensor(f3_sc[:, 0, :], f3_sc[:, 0, :], f3_qs[:, kc, :], op=ALU.max),
                       reads=("am", "qs"), writes=("am",))
            for jb in range(TT1 // 128):
                emit_colmax_bcast(p, f"f3cm_{tt}_{jb}", f3_sc[:, 0, jb * 128:(jb + 1) * 128], "am",
                                  s_cmg, s_row, ps[6], "pp6", ps[7], "pp7",
                                  s_ident, f3_sc[:, 2, jb * 128:(jb + 1) * 128], "sc1",
                                  s_inv1[:, tt * TT1 + jb * 128:tt * TT1 + (jb + 1) * 128], "inv1", s_one1, None)
            p.emit("v", lambda eng: eng.tensor_tensor(
                       f3_r1[:], f3_s1[:], f3_sc[:, 2:3, :].broadcast_to([128, 8, TT1]), op=ALU.mult),
                   reads=("s1", "sc1"), writes=("r1",))
            p.emit("v", lambda eng: eng.tensor_copy(f3_ki[:], f3_r1[:]), reads=("r1",), writes=("ki",))
            p.emit("v", lambda eng: eng.tensor_copy(f3_sq[:], f3_ki[:]), reads=("ki",), writes=("sq",))
            p.emit("v", lambda eng: eng.tensor_tensor(f3_r1[:], f3_r1[:], f3_sq[:], op=ALU.subtract),
                   reads=("r1", "sq"), writes=("r1",))
            p.emit("v", lambda eng: eng.tensor_scalar(f3_qs[:], f3_r1[:], -0.5, None, op0=ALU.is_lt),
                   reads=("r1",), writes=("qs",))
            p.emit("v", lambda eng: eng.scalar_tensor_tensor(out=f3_r1[:], in0=f3_r1[:], scalar=0.5, in1=f3_qs[:],
                                                             op0=ALU.is_gt, op1=ALU.subtract),
                   reads=("r1", "qs"), writes=("r1",))
            p.emit("v", lambda eng: eng.tensor_tensor(f3_r1[:], f3_r1[:], f3_sq[:], op=ALU.add),
                   reads=("r1", "sq"), writes=("r1",))
            p.emit("v", lambda eng: eng.tensor_scalar(f3_r1[:], f3_r1[:], -128.0, 127.0, op0=ALU.max, op1=ALU.min),
                   reads=("r1",), writes=("r1",))
            p.emit("v", lambda eng, tt=tt: eng.tensor_copy(s_kq1[:, :, tt * TT1:(tt + 1) * TT1], f3_r1[:]),
                   reads=("r1",), writes=("kq1",))
    p.barrier()
    return _fusedA_part4(nc, stack, p, xch, g)


def _fusedA_part4(nc, stack, p, xch, g):
    s_ones, s_one1, s_ident = g["s_ones"], g["s_one1"], g["s_ident"]
    s_row, s_cmg, s_wsc = g["s_row"], g["s_cmg"], g["s_wsc"]
    s_inv1, s_kq1 = g["s_inv1"], g["s_kq1"]
    bn_stack = ExitStack()
    s_m2 = bn_stack.enter_context(nc.sbuf_tensor("s_m2", [128, TOK], F32))
    s_f2s = bn_stack.enter_context(nc.sbuf_tensor("s_f2s", [128, TOK], F32))
    s_inv2 = bn_stack.enter_context(nc.sbuf_tensor("s_inv2", [128, TOK], F32))
    g["bn_stack"] = bn_stack
    ps = g["ps"]
    s1_hbm, gu_hbm, wdq_hbm = g["s1_hbm"], g["gu_hbm"], g["wdq_hbm"]
    s2_hbm, halo_hbm, k_hbm, v_hbm, s3f_hbm = g["s2_hbm"], g["halo_hbm"], g["k_hbm"], g["v_hbm"], g["s3f_hbm"]
    sh_halo = g["sh_halo"]
    wgT, wuT, wdT = g["wgT"], g["wuT"], g["wdT"]
    wqkv, bq_r, bk_r, bv_bc, wout1, bo_r = g["wqkv"], g["bq_r"], g["bk_r"], g["bv_bc"], g["wout1"], g["bo_r"]
    zeros_col, wq = g["zeros_col"], g["wq"]
    q_out, st_out, s3t_out = g["q_out"], g["st_out"], g["s3t_out"]
    TT = 512
    NTT = TOK // TT

    p.emit("v", lambda eng: eng.memset(s_m2[:], 0.0), writes=("m2",))
    with ExitStack() as ph:
        f4_wb = ph.enter_context(nc.sbuf_tensor("f4_wb", [128, 2, 2, 8, 128], F32))
        f4_wq = ph.enter_context(nc.sbuf_tensor("f4_wq", [128, 2, 8, 128], F32))
        f4_qb = ph.enter_context(nc.sbuf_tensor("f4_qb", [128, 2, 8, 128], BF16))
        f4_g = ph.enter_context(nc.sbuf_tensor("f4_g", [128, TT], F32))
        f4_u = ph.enter_context(nc.sbuf_tensor("f4_u", [128, TT], F32))
        f4_gu = ph.enter_context(nc.sbuf_tensor("f4_gu", [128, 2, TT], F32))
        wgv = wgT.rearrange("(kc p) (oc f) -> p kc oc f", p=128, f=128)
        wuv = wuT.rearrange("(kc p) (oc f) -> p kc oc f", p=128, f=128)
        guv = gu_hbm.rearrange("(oc p) t -> p oc t", p=128)
        for oc in range(32):
            pb = oc % 2
            p.dma("y", lambda eng, oc=oc, pb=pb: eng.dma_start(out=f4_wb[:, pb, 0], in_=wgv[:, :, oc, :]),
                  writes=(f"wg{pb}",))
            p.dma("y", lambda eng, oc=oc, pb=pb: eng.dma_start(out=f4_wb[:, pb, 1], in_=wuv[:, :, oc, :]),
                  writes=(f"wu{pb}",))
            for gi, wkey, sw_col in ((0, "wg", 3), (1, "wu", 4)):
                p.emit("v", lambda eng, pb=pb, gi=gi, sw_col=sw_col: eng.tensor_scalar(
                           f4_wq[:, 0], f4_wb[:, pb, gi], s_wsc[:, sw_col:sw_col + 1], None, op0=ALU.mult),
                       reads=(f"{wkey}{pb}", "wsc2"), writes=("wqa",))
                p.emit("v", lambda eng: eng.tensor_scalar(f4_wq[:, 1], f4_wq[:, 0], -0.5, None, op0=ALU.is_lt),
                       reads=("wqa",), writes=("wqm",))
                p.emit("v", lambda eng, gi=gi: eng.scalar_tensor_tensor(out=f4_qb[:, gi], in0=f4_wq[:, 0], scalar=0.5,
                                                                 in1=f4_wq[:, 1], op0=ALU.is_gt, op1=ALU.subtract),
                       reads=("wqa", "wqm"), writes=(f"wqb{gi}",))
            for tt in range(NTT):
                pairs_g = [(f4_qb[:, 0, kc, :], s_kq1[:, kc, tt * TT:(tt + 1) * TT]) for kc in range(8)]
                pairs_u = [(f4_qb[:, 1, kc, :], s_kq1[:, kc, tt * TT:(tt + 1) * TT]) for kc in range(8)]
                mm_accum(p, ps[0][:], pairs_g, "pp0", ("wqb0", "kq1"))
                mm_accum(p, ps[1][:], pairs_u, "pp1", ("wqb1", "kq1"))
                p.emit("v", lambda eng, tt=tt: eng.scalar_tensor_tensor(
                           out=f4_g[:], in0=ps[0][:], scalar=s_wsc[:, 0:1],
                           in1=s_inv1[:, tt * TT:(tt + 1) * TT], op0=ALU.mult, op1=ALU.mult),
                       reads=("pp0", "wsc", "inv1"), writes=("g",))
                p.emit("s", lambda eng: eng.activation(f4_g[:], f4_g[:], ACT.Silu),
                       reads=("g",), writes=("g",))
                p.emit("v", lambda eng, tt=tt: eng.scalar_tensor_tensor(
                           out=f4_u[:], in0=ps[1][:], scalar=s_wsc[:, 1:2],
                           in1=s_inv1[:, tt * TT:(tt + 1) * TT], op0=ALU.mult, op1=ALU.mult),
                       reads=("pp1", "wsc", "inv1"), writes=("u",))
                gb = tt % 2
                p.emit("v", lambda eng, gb=gb: eng.tensor_tensor(f4_gu[:, gb], f4_g[:], f4_u[:], op=ALU.mult),
                       reads=("g", "u"), writes=(f"gu{gb}",))
                p.dma("y", lambda eng, oc=oc, tt=tt, gb=gb: eng.dma_start(
                          out=guv[:, oc, tt * TT:(tt + 1) * TT], in_=f4_gu[:, gb]),
                      reads=(f"gu{gb}",))
                p.emit("s", lambda eng, gb=gb: eng.activation(f4_g[:], f4_gu[:, gb], ACT.Abs),
                       reads=(f"gu{gb}",), writes=("g",))
                p.emit("v", lambda eng, tt=tt: eng.tensor_tensor(
                           s_m2[:, tt * TT:(tt + 1) * TT], s_m2[:, tt * TT:(tt + 1) * TT], f4_g[:], op=ALU.max),
                       reads=("g", "m2"), writes=("m2",))
    p.barrier()
    for jb in range(TOK // 128):
        emit_colmax_bcast(p, f"f4cm_{jb}", s_m2[:, jb * 128:(jb + 1) * 128], "m2",
                          s_cmg, s_row, ps[6], "pp6", ps[7], "pp7",
                          s_ident, s_f2s[:, jb * 128:(jb + 1) * 128], "f2",
                          s_inv2[:, jb * 128:(jb + 1) * 128], "inv2", s_one1, None)
    p.barrier()

    with ExitStack() as ph:
        f5_wd = ph.enter_context(nc.sbuf_tensor("f5_wd", [128, 2, 8, 128], BF16))
        f5_gu = ph.enter_context(nc.sbuf_tensor("f5_gu", [128, 2, TT], F32))
        f5_q = ph.enter_context(nc.sbuf_tensor("f5_q", [128, 2, TT], F32))
        f5_qb = ph.enter_context(nc.sbuf_tensor("f5_qb", [128, TT], BF16))
        f5_qi = ph.enter_context(nc.sbuf_tensor("f5_qi", [128, TT], mybir.dt.int32))
        f5_qf = ph.enter_context(nc.sbuf_tensor("f5_qf", [128, TT], F32))
        f5_ffn = ph.enter_context(nc.sbuf_tensor("f5_ffn", [128, 8, TT], F32))
        f5_sq = ph.enter_context(nc.sbuf_tensor("f5_sq", [128, 8, TT], F32))
        f5_s2 = ph.enter_context(nc.sbuf_tensor("f5_s2", [128, 8, TT], F32))
        f5_s1r = ph.enter_context(nc.sbuf_tensor("f5_s1r", [128, 8, TT], F32))
        f5_mv = ph.enter_context(nc.sbuf_tensor("f5_mv", [128, 2, TT], F32))
        wdv = wdq_hbm.rearrange("(kc p) (oc f) -> p kc oc f", p=128, f=128)
        guv = gu_hbm.rearrange("(kc p) t -> p kc t", p=128)
        s1v = s1_hbm.rearrange("(kc p) t -> p kc t", p=128)
        s2v = s2_hbm.rearrange("(kc p) t -> p kc t", p=128)
        for tt in range(NTT):
            p.dma("y", lambda eng, tt=tt: eng.dma_start(out=f5_s1r[:], in_=s1v[:, :, tt * TT:(tt + 1) * TT]),
                  writes=("s1r",))
            for ic in range(32):
                pb = ic % 2
                p.dma("y", lambda eng, ic=ic, pb=pb: eng.dma_start(out=f5_wd[:, pb], in_=wdv[:, ic, :, :]),
                      writes=(f"wd{pb}",))
                p.dma("y", lambda eng, ic=ic, tt=tt, pb=pb: eng.dma_start(
                          out=f5_gu[:, pb], in_=guv[:, ic, tt * TT:(tt + 1) * TT]),
                      writes=(f"gub{pb}",))
                p.emit("v", lambda eng, tt=tt, pb=pb: eng.tensor_tensor(
                           f5_q[:, 0], f5_gu[:, pb], s_f2s[:, tt * TT:(tt + 1) * TT], op=ALU.mult),
                       reads=(f"gub{pb}", "f2"), writes=("qa",))
                p.emit("v", lambda eng: eng.tensor_copy(f5_qi[:], f5_q[:, 0]), reads=("qa",), writes=("qi",))
                p.emit("v", lambda eng: eng.tensor_copy(f5_qf[:], f5_qi[:]), reads=("qi",), writes=("qf",))
                p.emit("v", lambda eng: eng.tensor_tensor(f5_q[:, 0], f5_q[:, 0], f5_qf[:], op=ALU.subtract),
                       reads=("qa", "qf"), writes=("qa",))
                p.emit("v", lambda eng: eng.tensor_scalar(f5_q[:, 1], f5_q[:, 0], -0.5, None, op0=ALU.is_lt),
                       reads=("qa",), writes=("qm",))
                p.emit("v", lambda eng: eng.scalar_tensor_tensor(out=f5_q[:, 0], in0=f5_q[:, 0], scalar=0.5,
                                                                 in1=f5_q[:, 1], op0=ALU.is_gt, op1=ALU.subtract),
                       reads=("qa", "qm"), writes=("qa",))
                p.emit("v", lambda eng: eng.tensor_tensor(f5_q[:, 0], f5_q[:, 0], f5_qf[:], op=ALU.add),
                       reads=("qa", "qf"), writes=("qa",))
                p.emit("v", lambda eng: eng.tensor_scalar(f5_qb[:], f5_q[:, 0], -128.0, 127.0, op0=ALU.max, op1=ALU.min),
                       reads=("qa",), writes=("qb",))
                def mmdown(eng, ic=ic, pb=pb):
                    ins = None
                    for oc in range(8):
                        ins = eng.matmul(ps[oc][:], f5_wd[:, pb, oc, :], f5_qb[:],
                                         start=(ic == 0), stop=(ic == 31))
                    return ins
                p.emit("t", mmdown, reads=(f"wd{pb}", "qb"), writes=("ppdown",))
            for oc in range(8):
                p.emit("v", lambda eng, oc=oc, tt=tt: eng.scalar_tensor_tensor(
                           out=f5_ffn[:, oc, :], in0=ps[oc][:], scalar=s_wsc[:, 2:3],
                           in1=s_inv2[:, tt * TT:(tt + 1) * TT], op0=ALU.mult, op1=ALU.mult),
                       reads=("ppdown", "wsc", "inv2"), writes=("ffn",))
            p.emit("v", lambda eng: eng.tensor_tensor(f5_ffn[:], f5_ffn[:], f5_s1r[:], op=ALU.add),
                   reads=("ffn", "s1r"), writes=("ffn",))
            emit_ln_fm(p, f"f5ln_{tt}", f5_ffn, "ffn", f5_s2, "s2", f5_sq, "f5sq", s_ones,
                       ps[0], "ppdown", ps[1], "ppdown", f5_mv, "f5mv", TT)
            p.dma("y", lambda eng, tt=tt: eng.dma_start(out=s2v[:, :, tt * TT:(tt + 1) * TT], in_=f5_s2[:]),
                  reads=("s2",))
    p.barrier()

    xch.send(2, [(lambda par: sh_halo[par],
                  lambda par: s2_hbm[:, (TOK - 128) * (1 - par):(TOK - 128) * (1 - par) + 128])],
             reads=["s2"])
    xch.poll(2)
    xch.recv_dma(lambda par: [sh_halo[1 - par], zeros_col],
                 lambda par: [halo_hbm[:, 128 * (1 - par):128 * (1 - par) + 128],
                              halo_hbm[:, 128 * par:128 * par + 128]],
                 2, ["halo"], nd=2)
    p.barrier()
    return _fusedA_part5(nc, stack, p, xch, g)


def _fusedA_part5(nc, stack, p, xch, g):
    g["bn_stack"].close()
    g["ln_stack"].close()
    s_ones, s_one1, s_ident = g["s_ones"], g["s_one1"], g["s_ident"]
    s_ms = stack.enter_context(nc.sbuf_tensor("s_ms", [128, 3, 384], F32))
    p.dma("y", lambda eng: eng.dma_start(out=s_ms[:, 0], in_=g["mask0"]), writes=("ms",))
    p.dma("y", lambda eng: eng.dma_start(out=s_ms[:, 1], in_=g["maskm"]), writes=("ms",))
    p.dma("y", lambda eng: eng.dma_start(out=s_ms[:, 2], in_=g["mask15"]), writes=("ms",))
    ps = g["ps"]
    s2_hbm, halo_hbm, k_hbm, v_hbm, s3f_hbm = g["s2_hbm"], g["halo_hbm"], g["k_hbm"], g["v_hbm"], g["s3f_hbm"]
    wqkv, bq_r, bk_r, bv_bc, wout1, bo_r = g["wqkv"], g["bq_r"], g["bk_r"], g["bv_bc"], g["wout1"], g["bo_r"]
    wq = g["wq"]
    q_out, st_out, s3t_out = g["q_out"], g["st_out"], g["s3t_out"]
    NH, DHD, W = 16, 64, 128
    NB = TOK // W
    KT = TOK + 2 * W

    x_r = s2_hbm.rearrange("(kc p) t -> p kc t", p=128)
    hx_r = halo_hbm.rearrange("(kc p) t -> p kc t", p=128)
    kv = k_hbm.rearrange("(kc p) t -> p kc t", p=128)
    vv_ = v_hbm.rearrange("(tb p) d -> p tb d", p=128)

    with ExitStack() as ph:
        s_wk = ph.enter_context(nc.sbuf_tensor("fa_wk", [128, 8, 8, 128], F32))
        s_wv = ph.enter_context(nc.sbuf_tensor("fa_wv", [128, 8, 8, 128], F32))
        s_bk = ph.enter_context(nc.sbuf_tensor("fa_bk", [128, 8], F32))
        s_bv = ph.enter_context(nc.sbuf_tensor("fa_bv", [128, DM], F32))
        s_xa = ph.enter_context(nc.sbuf_tensor("fa_x", [128, 2, 8, 512], F32))
        s_k = ph.enter_context(nc.sbuf_tensor("fa_k", [128, 8, 512], F32))
        s_v = ph.enter_context(nc.sbuf_tensor("fa_v", [128, 2, 512], F32))
        p.dma("y", lambda eng: eng.dma_start(out=s_wk[:], in_=wqkv.rearrange("(kc p) n -> p kc n", p=128)[:, :, DM:2 * DM].rearrange("p kc (o f) -> p kc o f", f=128)), writes=("wk",))
        p.dma("y", lambda eng: eng.dma_start(out=s_wv[:], in_=wqkv.rearrange("(kc p) n -> p kc n", p=128)[:, :, 2 * DM:3 * DM].rearrange("p kc (o f) -> p kc o f", f=128)), writes=("wv",))
        p.dma("y", lambda eng: eng.dma_start(out=s_bk[:], in_=bk_r), writes=("bk",))
        p.dma("y", lambda eng: eng.dma_start(out=s_bv[:], in_=bv_bc), writes=("bv",))
        groups = [("own", gg) for gg in range(4)] + [("halo", 0)]
        for gi, (kind, gg) in enumerate(groups):
            pb = gi % 2
            ntok = 512 if kind == "own" else 256
            if kind == "own":
                p.dma("y", lambda eng, gg=gg, pb=pb: eng.dma_start(out=s_xa[:, pb], in_=x_r[:, :, gg * 512:(gg + 1) * 512]),
                      writes=(f"x{pb}",))
            else:
                p.dma("y", lambda eng, pb=pb: eng.dma_start(out=s_xa[:, pb, :, 0:256], in_=hx_r[:]),
                      writes=(f"x{pb}",))
            for oc in range(8):
                bi = oc % 2
                pairs = [(s_wk[:, kc, oc, :], s_xa[:, pb, kc, 0:ntok]) for kc in range(8)]
                mm_accum(p, ps[bi][:, 0:ntok], pairs, f"pp{bi}", (f"x{pb}", "wk"))
                p.emit("s", lambda eng, oc=oc, bi=bi, ntok=ntok: eng.activation(
                           s_k[:, oc, 0:ntok], ps[bi][:, 0:ntok], ACT.Identity, bias=s_bk[:, oc:oc + 1], scale=1.0),
                       reads=(f"pp{bi}", "bk"), writes=("k",))
            if kind == "own":
                p.dma("y", lambda eng, gg=gg: eng.dma_start(out=kv[:, :, 128 + gg * 512:128 + (gg + 1) * 512], in_=s_k[:]),
                      reads=("k",))
            else:
                p.dma("y", lambda eng: eng.dma_start(out=kv[:, :, 0:128], in_=s_k[:, :, 0:128]), reads=("k",))
                p.dma("y", lambda eng: eng.dma_start(out=kv[:, :, KT - 128:KT], in_=s_k[:, :, 128:256]), reads=("k",))
            for tb in range(ntok // 128):
                for vc in range(2):
                    bi = 2 + vc
                    pairs = [(s_xa[:, pb, kc, tb * 128:(tb + 1) * 128], s_wv[:, kc, 4 * vc:4 * vc + 4, :].rearrange("p o f -> p (o f)")) for kc in range(8)]
                    mm_accum(p, ps[bi][:], pairs, f"pp{bi}", (f"x{pb}", "wv"))
                    p.emit("v", lambda eng, vc=vc, bi=bi: eng.tensor_tensor(
                               s_v[:, vc, :], ps[bi][:], s_bv[:, vc * 512:(vc + 1) * 512], op=ALU.add),
                           reads=(f"pp{bi}", "bv"), writes=("v",))
                if kind == "own":
                    p.dma("y", lambda eng, gg=gg, tb=tb: eng.dma_start(
                              out=vv_[:, 1 + gg * 4 + tb, :], in_=s_v[:].rearrange("p a b -> p (a b)")),
                          reads=("v",))
                elif tb == 0:
                    p.dma("y", lambda eng: eng.dma_start(out=vv_[:, 0, :], in_=s_v[:].rearrange("p a b -> p (a b)")),
                          reads=("v",))
                else:
                    p.dma("y", lambda eng: eng.dma_start(out=vv_[:, 17, :], in_=s_v[:].rearrange("p a b -> p (a b)")),
                          reads=("v",))
    p.barrier()

    with ExitStack() as ph:
        s_wq4 = ph.enter_context(nc.sbuf_tensor("fb_wq", [128, 8, 8, 128], F32))
        s_bq = ph.enter_context(nc.sbuf_tensor("fb_bq", [128, 8], F32))
        s_x = ph.enter_context(nc.sbuf_tensor("fb_x", [128, 1, 8, 512], F32))
        s_q = ph.enter_context(nc.sbuf_tensor("fb_q", [128, 8, 512], F32))
        s_kk = ph.enter_context(nc.sbuf_tensor("fb_kk", [128, 1, 8, 768], F32))
        s_vv = ph.enter_context(nc.sbuf_tensor("fb_vv", [128, 1, 6, DM], F32))
        s_sc = ph.enter_context(nc.sbuf_tensor("fb_sc", [128, 384], F32))
        s_sm = ph.enter_context(nc.sbuf_tensor("fb_sm", [128, 4], F32))
        s_at = ph.enter_context(nc.sbuf_tensor("fb_at", [128, 3, 128], F32))
        s_ao = ph.enter_context(nc.sbuf_tensor("fb_ao", [128, 8, 512], F32))
        s_wo = ph.enter_context(nc.sbuf_tensor("fb_wo", [128, 2, 8, 128], F32))
        s_bo = ph.enter_context(nc.sbuf_tensor("fb_bo", [128, 8], F32))
        s_o = ph.enter_context(nc.sbuf_tensor("fb_o", [128, 8, 512], F32))
        s_sq = ph.enter_context(nc.sbuf_tensor("fb_sq", [128, 8, 512], F32))
        s_s3 = ph.enter_context(nc.sbuf_tensor("fb_s3", [128, 8, 512], F32))
        s_3t = ph.enter_context(nc.sbuf_tensor("fb_3t", [128, 8, 128], F32))
        s_mvb = ph.enter_context(nc.sbuf_tensor("fb_mv", [128, 2, 512], F32))
        p.dma("y", lambda eng: eng.dma_start(out=s_wq4[:], in_=wqkv.rearrange("(kc p) n -> p kc n", p=128)[:, :, 0:DM].rearrange("p kc (o f) -> p kc o f", f=128)), writes=("wqq",))
        p.dma("y", lambda eng: eng.dma_start(out=s_bq[:], in_=bq_r), writes=("bq",))
        p.dma("y", lambda eng: eng.dma_start(out=s_bo[:], in_=bo_r), writes=("bo2",))
        s3fv = s3f_hbm.rearrange("(kc p) t -> p kc t", p=128)
        for gg in range(4):
            pb = 0
            p.dma("y", lambda eng, gg=gg: eng.dma_start(out=s_x[:, 0], in_=x_r[:, :, gg * 512:(gg + 1) * 512]),
                  writes=("xb0",))
            p.dma("y", lambda eng, gg=gg: eng.dma_start(out=s_kk[:, 0], in_=kv[:, :, gg * 512:gg * 512 + 768]),
                  writes=("kk0",))
            p.dma("y", lambda eng, gg=gg: eng.dma_start(out=s_vv[:, 0], in_=vv_[:, gg * 4:gg * 4 + 6, :]),
                  writes=("vv0",))
            for oc in range(8):
                bi = oc % 2
                pairs = [(s_wq4[:, kc, oc, :], s_x[:, 0, kc, :]) for kc in range(8)]
                mm_accum(p, ps[bi][:], pairs, f"pp{bi}", ("xb0", "wqq"))
                p.emit("s", lambda eng, oc=oc, bi=bi: eng.activation(
                           s_q[:, oc, 0:512], ps[bi][:], ACT.Identity, bias=s_bq[:, oc:oc + 1], scale=1.0),
                       reads=(f"pp{bi}", "bq"), writes=("q",))
            for hd in range(NH):
                hc, hr = hd // 2, (hd % 2) * 64
                for qb in range(4):
                    gqb = gg * 4 + qb
                    mi = 0 if gqb == 0 else (2 if gqb == NB - 1 else 1)
                    p.emit("t", lambda eng, hc=hc, hr=hr, qb=qb: eng.matmul(
                               ps[4][:, 0:384], s_q[hr:hr + 64, hc, qb * 128:(qb + 1) * 128],
                               s_kk[hr:hr + 64, 0, hc, qb * 128:qb * 128 + 384], start=True, stop=True),
                           reads=("q", "kk0"), writes=("pp4",))
                    p.emit("v", lambda eng, mi=mi: eng.scalar_tensor_tensor(
                               out=s_sc[:], in0=ps[4][:, 0:384], scalar=0.03125, in1=s_ms[:, mi],
                               op0=ALU.mult, op1=ALU.add),
                           reads=("pp4", "ms"), writes=("sc",))
                    p.emit("v", lambda eng: eng.tensor_reduce(s_sm[:, 0:1], s_sc[:], axis=AX, op=ALU.max),
                           reads=("sc",), writes=("sm",))
                    p.emit("v", lambda eng: eng.tensor_scalar(s_sm[:, 1:2], s_sm[:, 0:1], -1.0, None, op0=ALU.mult),
                           reads=("sm",), writes=("sm1",))
                    p.emit("s", lambda eng: eng.activation(s_sc[:], s_sc[:], ACT.Exp, bias=s_sm[:, 1:2], scale=1.0),
                           reads=("sc", "sm1"), writes=("sc",))
                    p.emit("v", lambda eng: eng.tensor_reduce(s_sm[:, 2:3], s_sc[:], axis=AX, op=ALU.add),
                           reads=("sc",), writes=("sm2",))
                    p.emit("v", lambda eng: eng.reciprocal(s_sm[:, 3:4], s_sm[:, 2:3]),
                           reads=("sm2",), writes=("sm3",))
                    p.emit("v", lambda eng: eng.tensor_scalar(s_sc[:], s_sc[:], s_sm[:, 3:4], None, op0=ALU.mult),
                           reads=("sc", "sm3"), writes=("sc",))
                    for j in range(3):
                        p.emit("t", lambda eng, j=j: eng.transpose(ps[5][:, j * 128:(j + 1) * 128], s_sc[:, j * 128:(j + 1) * 128], s_ident[:]),
                               reads=("sc", "ident"), writes=("pp5",))
                    p.emit("s", lambda eng: eng.copy(s_at[:].rearrange("p a b -> p (a b)"), ps[5][:, 0:384]),
                           reads=("pp5",), writes=("at",))
                    def av(eng, hc=hc, hr=hr, qb=qb):
                        ins = None
                        for j in range(3):
                            ins = eng.matmul(ps[6][hr:hr + 64, qb * 128:(qb + 1) * 128],
                                             s_vv[:, 0, qb + j, hc * 128 + hr:hc * 128 + hr + 64],
                                             s_at[:, j, :], start=(j == 0), stop=(j == 2))
                        return ins
                    p.emit("t", av, reads=("at", "vv0"), writes=("pp6",))
                    p.emit("s", lambda eng, hc=hc, hr=hr, qb=qb: eng.copy(
                               s_ao[hr:hr + 64, hc, qb * 128:(qb + 1) * 128], ps[6][hr:hr + 64, qb * 128:(qb + 1) * 128]),
                           reads=("pp6",), writes=("ao",))
            for oc in range(8):
                bi = oc % 2
                p.dma("y", lambda eng, oc=oc, bi=bi: eng.dma_start(
                          out=s_wo[:, bi], in_=wout1.rearrange("(kc p) n -> p kc n", p=128)[:, :, oc * 128:(oc + 1) * 128]),
                      writes=(f"wo1{bi}",))
                pairs = [(s_wo[:, bi, kc, :], s_ao[:, kc, :]) for kc in range(8)]
                mm_accum(p, ps[bi][:], pairs, f"pp{bi}", ("ao", f"wo1{bi}"))
                p.emit("s", lambda eng, oc=oc, bi=bi: eng.activation(
                           s_o[:, oc, :], ps[bi][:], ACT.Identity, bias=s_bo[:, oc:oc + 1], scale=1.0),
                       reads=(f"pp{bi}", "bo2"), writes=("o",))
            p.emit("v", lambda eng: eng.tensor_tensor(s_o[:], s_o[:], s_x[:, 0], op=ALU.add),
                   reads=("o", "xb0"), writes=("o",))
            emit_ln_fm(p, f"f7ln_{gg}", s_o, "o", s_s3, "s3", s_sq, "sq3", s_ones,
                       ps[4], "pp4", ps[5], "pp5", s_mvb, "f7mv", 512)
            p.dma("y", lambda eng, gg=gg: eng.dma_start(
                      out=s3fv[:, :, gg * 512:(gg + 1) * 512], in_=s_s3[:]),
                  reads=("s3",))
            for tb in range(4):
                for half in range(2):
                    bk = ps[2] if half == 0 else ps[3]
                    bkk = "pp2" if half == 0 else "pp3"
                    for k4 in range(4):
                        kc = half * 4 + k4
                        p.emit("t", lambda eng, tb=tb, kc=kc, k4=k4, bk=bk: eng.transpose(
                                   bk[:, k4 * 128:(k4 + 1) * 128], s_s3[:, kc, tb * 128:(tb + 1) * 128], s_ident[:]),
                               reads=("s3", "ident"), writes=(bkk,))
                    p.emit("s", lambda eng, half=half, bk=bk: eng.copy(
                               s_3t[:, half * 4:(half + 1) * 4, :].rearrange("p a b -> p (a b)"), bk[:, 0:512]),
                           reads=(bkk,), writes=("t3",))
                p.dma("y", lambda eng, gg=gg, tb=tb: eng.dma_start(
                          out=s3t_out[(gg * 4 + tb) * 128:(gg * 4 + tb + 1) * 128, :], in_=s_3t[:].rearrange("p a b -> p (a b)")),
                      reads=("t3",))
    p.barrier()

    with ExitStack() as ph:
        TTQ = 512
        f8_w = ph.enter_context(nc.sbuf_tensor("f8_w", [128, 8, 16, 128], F32))
        f8_x = ph.enter_context(nc.sbuf_tensor("f8_x", [128, 2, 8, TTQ], F32))
        f8_q = ph.enter_context(nc.sbuf_tensor("f8_q", [128, 16, TTQ], F32))
        f8_sq = ph.enter_context(nc.sbuf_tensor("f8_sq", [128, 16, TTQ], F32))
        f8_st = ph.enter_context(nc.sbuf_tensor("f8_st", [128, 16, 2, 4], F32))
        f8_sto = ph.enter_context(nc.sbuf_tensor("f8_sto", [128, 32], F32))
        p.dma("y", lambda eng: eng.dma_start(out=f8_w[:], in_=wq.rearrange("(kc p) (oc f) -> p kc oc f", p=128, f=128)),
              writes=("w8",))
        xv = s3f_hbm.rearrange("(kc p) t -> p kc t", p=128)
        qv = q_out.rearrange("(oc p) t -> p oc t", p=128)
        for tt in range(TOK // TTQ):
            pb = tt % 2
            p.dma("y", lambda eng, tt=tt, pb=pb: eng.dma_start(out=f8_x[:, pb], in_=xv[:, :, tt * TTQ:(tt + 1) * TTQ]),
                  writes=(f"x8{pb}",))
            for oc in range(16):
                bi = oc % 4
                pairs = [(f8_w[:, kc, oc, :], f8_x[:, pb, kc, :]) for kc in range(8)]
                mm_accum(p, ps[bi][:], pairs, f"pp{bi}", (f"x8{pb}", "w8"))
                p.emit("s", lambda eng, oc=oc, bi=bi: eng.copy(f8_q[:, oc, :], ps[bi][:]),
                       reads=(f"pp{bi}",), writes=("q8",))
            p.dma("y", lambda eng, tt=tt: eng.dma_start(out=qv[:, :, tt * TTQ:(tt + 1) * TTQ], in_=f8_q[:]),
                  reads=("q8",))
            p.emit("s", lambda eng: eng.activation(f8_sq[:], f8_q[:], ACT.Square),
                   reads=("q8",), writes=("sq8",))
            for oc in range(16):
                p.emit("v", lambda eng, oc=oc, tt=tt: eng.tensor_reduce(
                           f8_st[:, oc, 0, tt:tt + 1], f8_q[:, oc, :], axis=AX, op=ALU.add),
                       reads=("q8",), writes=("st8",))
                p.emit("v", lambda eng, oc=oc, tt=tt: eng.tensor_reduce(
                           f8_st[:, oc, 1, tt:tt + 1], f8_sq[:, oc, :], axis=AX, op=ALU.add),
                       reads=("sq8",), writes=("st8",))
        p.emit("v", lambda eng: eng.tensor_reduce(
                   f8_sto[:].rearrange("p (a b) -> p a b", b=1), f8_st[:].rearrange("p a b c -> p (a b) c"), axis=AX, op=ALU.add),
               reads=("st8",), writes=("sto8",))
        p.dma("y", lambda eng: eng.dma_start(out=st_out, in_=f8_sto[:]), reads=("sto8",))

    xch.epilogue(["sto8", "t3", "q8"])
    p.finish_outputs("y")
    with stack:
        return p.build()


def host_fusedA_inputs(ins):
    Win = ins["L0_attn_Win"]; bin_ = ins["L0_attn_bin"]
    Wx = ins["L0_attn_Wx"]; bx = ins["L0_attn_bx"]
    Wdt = ins["L0_attn_Wdt"]; bdt = ins["L0_attn_bdt"]
    Alog = ins["L0_attn_A_log"]; Dp = ins["L0_attn_Dp"]
    Wout = ins["L0_attn_Wout"]; bout = ins["L0_attn_bout"]
    x = ins["L0_attn_x"]
    wg = ins["L0_ffn_w_gate"]; wu = ins["L0_ffn_w_up"]; wd = ins["L0_ffn_w_down"]
    Wqkv = ins["L1_attn_Wqkv"]; bqkv = ins["L1_attn_bqkv"]
    Wout1 = ins["L1_attn_Wout"]; bout1 = ins["L1_attn_bout"]
    wq = ins["L1_ffn_w_q"]
    i = np.arange(128)[:, None]; j = np.arange(384)[None, :]
    base = np.where((j >= i) & (j <= i + 256), 0.0, -1e30).astype(np.float32)
    m_first = np.where(j >= 128, base, -1e30).astype(np.float32)
    m_last = np.where(j < 256, base, -1e30).astype(np.float32)
    maps = []
    for c in range(NCORES):
        b, h = c // 2, c % 2
        tsl = slice(h * TOK, (h + 1) * TOK)
        maps.append({
            "x_fm": np.ascontiguousarray(x[b].T),
            "x_res": np.ascontiguousarray(x[b].T[:, tsl]),
            "winx": np.ascontiguousarray(Win[:, h * DH:(h + 1) * DH]),
            "winz": np.ascontiguousarray(Win[:, DI + h * DH:DI + (h + 1) * DH]),
            "binx": np.ascontiguousarray(bin_[h * DH:(h + 1) * DH].reshape(-1, 128).T),
            "binz": np.ascontiguousarray(bin_[DI + h * DH:DI + (h + 1) * DH].reshape(-1, 128).T),
            "wx": np.ascontiguousarray(Wx[h * DH:(h + 1) * DH]),
            "bx_fm": np.ascontiguousarray(bx[:, None]),
            "bx_tm": np.ascontiguousarray(np.broadcast_to(bx, (128, 96))),
            "wdt": np.ascontiguousarray(Wdt[:, h * DH:(h + 1) * DH]),
            "bdt": np.ascontiguousarray(bdt[h * DH:(h + 1) * DH].reshape(8, 128).T),
            "alog": np.ascontiguousarray(Alog[h * DH:(h + 1) * DH].reshape(8, 128, NST).transpose(1, 0, 2)),
            "dp_r": np.ascontiguousarray(Dp[h * DH:(h + 1) * DH].reshape(8, 128).T),
            "wout": np.ascontiguousarray(Wout[h * DH:(h + 1) * DH]),
            "boutr": np.ascontiguousarray(bout.reshape(8, 128).T),
            "wgT": np.ascontiguousarray(wg.T),
            "wuT": np.ascontiguousarray(wu.T),
            "wdT": np.ascontiguousarray(wd.T),
            "wqkv": Wqkv,
            "bq_r": np.ascontiguousarray(bqkv[0:DM].reshape(8, 128).T),
            "bk_r": np.ascontiguousarray(bqkv[DM:2 * DM].reshape(8, 128).T),
            "bv_bc": np.ascontiguousarray(np.broadcast_to(bqkv[2 * DM:3 * DM], (128, DM))),
            "wout1": Wout1,
            "bo_r": np.ascontiguousarray(bout1.reshape(8, 128).T),
            "mask0": m_first if h == 0 else base,
            "maskm": base,
            "mask15": m_last if h == 1 else base,
            "zeros_col": np.zeros((DM, 128), np.float32),
            "wq": wq,
        })
    return maps


def kernel_fused(**ins):
    core_ids = list(range(NCORES))
    rA = run_bass_kernel_spmd(build_fusedA(), host_fusedA_inputs(ins), core_ids=core_ids)
    q_by_core = {c: np.asarray(rA.results[c]["q_out"]) for c in range(NCORES)}
    stats_by_core = {c: np.asarray(rA.results[c]["st_out"]) for c in range(NCORES)}
    s3t_by_core = {c: np.asarray(rA.results[c]["s3t_out"]) for c in range(NCORES)}
    rB = run_bass_kernel_spmd(build_l6(), host_l6_inputs(ins, q_by_core, stats_by_core, s3t_by_core),
                              core_ids=core_ids)
    out = np.empty((B, S, DM), np.float32)
    for c in range(NCORES):
        b, h = c // 2, c % 2
        out[b, h * TOK:(h + 1) * TOK, :] = np.asarray(rB.results[c]["fin_out"])
    return out


kernel = kernel_fused


# version 05 (final)
# speedup vs baseline: 1.1845x; 1.1654x over previous
import numpy as np
from contextlib import ExitStack

import concourse.bass as bass
import concourse.mybir as mybir
from concourse.bass_utils import run_bass_kernel_spmd
from bass_rust import ReduceOp

F32 = mybir.dt.float32
BF16 = mybir.dt.bfloat16
U32 = mybir.dt.uint32
U16 = mybir.dt.uint16
ALU = mybir.AluOpType
ACT = mybir.ActivationFunctionType
AX = mybir.AxisListType.X

NCORES = 8
B, S, DM = 4, 4096, 1024
DI = 2048
NST = 16
DTR = 64
TOK = 2048
DH = DI // 2


class Prog:

    ENGS = ("t", "v", "s", "g", "y")
    NPOOL = {"y": 12, "g": 6, "s": 4, "v": 2}

    def __init__(self, nc, stack):
        self.nc = nc
        self.q = {e: [] for e in self.ENGS}
        self.cnt = {e: 0 for e in self.ENGS}
        self.sem = {e: stack.enter_context(nc.semaphore(f"sem_{e}")) for e in self.ENGS}
        self.dsem = {}
        self.dcnt = {}
        self.drr = {e: 0 for e in self.ENGS}
        for e, np_ in self.NPOOL.items():
            for i in range(np_):
                self.dsem[(e, i)] = stack.enter_context(nc.semaphore(f"dma_{e}{i}"))
                self.dcnt[(e, i)] = 0
        self.state = {}
        self.wm = {e: {} for e in self.ENGS}
        self.n_emitted = 0

    def _sem_of(self, key):
        return self.sem[key] if isinstance(key, str) else self.dsem[key]

    def _need(self, reads, writes):
        need = {}
        for k in reads:
            st = self.state.get(k)
            if st and st["w"] is not None:
                we, wc = st["w"]
                need[we] = max(need.get(we, 0), wc)
        for k in writes:
            st = self.state.get(k)
            if st:
                if st["w"] is not None:
                    we, wc = st["w"]
                    need[we] = max(need.get(we, 0), wc)
                for re_, rc in st["r"].items():
                    need[re_] = max(need.get(re_, 0), rc)
        return need

    def _filter_waits(self, e, need):
        out = []
        for wk, wc in need.items():
            if self.wm[e].get(wk, 0) < wc:
                self.wm[e][wk] = wc
                out.append((self._sem_of(wk), wc))
        return out

    def _bookkeep(self, reads, writes, semkey, target):
        for k in writes:
            self.state[k] = {"w": (semkey, target), "r": {}}
        for k in reads:
            st = self.state.setdefault(k, {"w": None, "r": {}})
            st["r"][semkey] = target
        self.n_emitted += 1

    def emit(self, e, fn, reads=(), writes=()):
        waits = self._filter_waits(e, self._need(reads, writes))
        self.cnt[e] += 1
        target = self.cnt[e]
        sem = self.sem[e]

        def thunk(eng, fn=fn, waits=waits, sem=sem):
            for ws, wc in waits:
                eng.wait_ge(ws, wc)
            fn(eng).then_inc(sem, 1)

        self.q[e].append(thunk)
        self._bookkeep(reads, writes, e, target)
        return (e, target)

    def dma(self, e, fn, reads=(), writes=()):
        need = self._need(reads, writes)
        i = self.drr[e] % self.NPOOL[e]
        self.drr[e] += 1
        key = (e, i)
        if self.dcnt[key] > 0:
            need[key] = max(need.get(key, 0), self.dcnt[key])
        waits = self._filter_waits(e, need)
        self.dcnt[key] += 16
        target = self.dcnt[key]
        sem = self.dsem[key]

        def thunk(eng, fn=fn, waits=waits, sem=sem):
            for ws, wc in waits:
                eng.wait_ge(ws, wc)
            fn(eng).then_inc(sem, 16)

        self.q[e].append(thunk)
        self._bookkeep(reads, writes, key, target)
        return (key, target)

    def raw(self, e, fn):
        self.q[e].append(lambda eng, fn=fn: fn(eng))

    def wait_keys(self, e, keys):
        need = self._need(keys, keys)
        specs = self._filter_waits(e, need)
        if specs:
            def thunk(eng, specs=specs):
                for ws, wc in specs:
                    eng.wait_ge(ws, wc)
            self.q[e].append(thunk)

    def barrier(self):
        snap = [(("",), 0)]
        specs_all = {}
        for e2 in self.ENGS:
            if self.cnt[e2] > 0:
                specs_all[e2] = self.cnt[e2]
        for k, c in self.dcnt.items():
            if c > 0:
                specs_all[k] = c
        for e in self.ENGS:
            specs = []
            for wk, wc in specs_all.items():
                if wk == e:
                    continue
                if self.wm[e].get(wk, 0) < wc:
                    self.wm[e][wk] = wc
                    specs.append((self._sem_of(wk), wc))
            if specs:
                def thunk(eng, specs=specs):
                    for ws, wc in specs:
                        eng.wait_ge(ws, wc)
                self.q[e].append(thunk)

    def finish_outputs(self, e="y"):
        specs = [(self.dsem[k], c) for k, c in self.dcnt.items() if k[0] == e and c > 0]
        def thunk(eng, specs=specs):
            for ws, wc in specs:
                eng.wait_ge(ws, wc)
        self.q[e].append(thunk)

    def build(self):
        nc = self.nc
        names = {"t": "tensor", "v": "vector", "s": "scalar", "g": "gpsimd", "y": "sync"}
        with nc.Block() as blk:
            for e in self.ENGS:
                dec = getattr(blk, names[e])
                def body(eng, _q=tuple(self.q[e])):
                    for th in _q:
                        th(eng)
                dec(body)
        return nc


def mm_accum(p, psum_ap, pairs, key_out, key_reads, tile_position=None):
    def fn(eng):
        ins = None
        n = len(pairs)
        for i, (lt, rh) in enumerate(pairs):
            ins = eng.matmul(psum_ap, lt, rh, start=(i == 0), stop=(i == n - 1))
        return ins
    return p.emit("t", fn, reads=tuple(key_reads), writes=(key_out,))


def transpose_128(p, psum_ap, in_ap, ident_ap, key_out, key_in):
    return p.emit("t", lambda eng: eng.transpose(psum_ap, in_ap, ident_ap),
                  reads=(key_in, "ident"), writes=(key_out,))


def make_identity(p, s_iota, s_ident):
    p.emit("g", lambda eng: eng.iota(s_iota[:], pattern=[[1, 128]], base=0,
                                     channel_multiplier=-1,
                                     allow_small_or_imprecise_dtypes=True),
           writes=("iota_raw",))
    p.emit("v", lambda eng: eng.tensor_scalar(s_ident[:], s_iota[:], 0.0, None, op0=ALU.is_equal),
           reads=("iota_raw",), writes=("ident",))


def build_l1():
    nc = bass.Bass("TRN2", target_bir_lowering=False)
    stack = ExitStack()
    p = Prog(nc, stack)

    x_fm = nc.dram_tensor("x_fm", [DM, S], F32, kind="ExternalInput").ap()
    winx = nc.dram_tensor("winx", [DM, DH], F32, kind="ExternalInput").ap()
    winz = nc.dram_tensor("winz", [DM, DH], F32, kind="ExternalInput").ap()
    binx = nc.dram_tensor("binx", [128, DH // 128], F32, kind="ExternalInput").ap()
    binz = nc.dram_tensor("binz", [128, DH // 128], F32, kind="ExternalInput").ap()
    xs_out = nc.dram_tensor("xs_out", [DH, S], F32, kind="ExternalOutput").ap()
    z_out = nc.dram_tensor("z_out", [DH, S], F32, kind="ExternalOutput").ap()

    TT = 512
    NTB = S // TT
    s_w = stack.enter_context(nc.sbuf_tensor("s_w", [128, 2, 8, 8, 128], F32))
    s_x = stack.enter_context(nc.sbuf_tensor("s_x", [128, 2, 8, TT], F32))
    s_b = stack.enter_context(nc.sbuf_tensor("s_b", [128, 2, 8], F32))
    s_o = stack.enter_context(nc.sbuf_tensor("s_o", [128, 2, 2, 8, TT], F32))
    ps = [stack.enter_context(nc.psum_tensor(f"ps{i}", [128, TT], F32)) for i in range(4)]

    with stack:
        p.dma("y", lambda eng: eng.dma_start(out=s_w[:, 0], in_=winx.rearrange("(kc p) (nc8 f) -> p kc nc8 f", p=128, f=128)),
              writes=("wx",))
        p.dma("y", lambda eng: eng.dma_start(out=s_w[:, 1], in_=winz.rearrange("(kc p) (nc8 f) -> p kc nc8 f", p=128, f=128)),
              writes=("wz",))
        p.dma("y", lambda eng: eng.dma_start(out=s_b[:, 0], in_=binx),
              writes=("bx",))
        p.dma("y", lambda eng: eng.dma_start(out=s_b[:, 1], in_=binz),
              writes=("bz",))

        x_r = x_fm.rearrange("(kc p) t -> p kc t", p=128)
        nb = 0
        for tb in range(NTB):
            pb = tb % 2
            p.dma("y", lambda eng, tb=tb, pb=pb: eng.dma_start(
                      out=s_x[:, pb], in_=x_r[:, :, tb * TT:(tb + 1) * TT]),
                  writes=(f"x{pb}",))
            for xz in range(2):
                for nc8 in range(8):
                    bi = nb % 4
                    bank = ps[bi]
                    nb += 1
                    pairs = [(s_w[:, xz, kc, nc8, :], s_x[:, pb, kc, :]) for kc in range(8)]
                    kmm = f"ps{bi}"
                    mm_accum(p, bank[:], pairs, kmm, (f"x{pb}", "wx" if xz == 0 else "wz"))
                    func = ACT.Silu if xz == 0 else ACT.Identity
                    def ev(eng, bank=bank, xz=xz, nc8=nc8, func=func, pb=pb):
                        return eng.activation(s_o[:, pb, xz, nc8, :], bank[:], func,
                                              bias=s_b[:, xz, nc8:nc8 + 1], scale=1.0)
                    p.emit("s", ev, reads=(kmm, "bx" if xz == 0 else "bz"),
                           writes=(f"o{pb}_{xz}_{nc8}",))
            for xz, out_t in ((0, xs_out), (1, z_out)):
                p.dma("y", lambda eng, tb=tb, pb=pb, xz=xz, out_t=out_t: eng.dma_start(
                          out=out_t.rearrange("(nc8 p) t -> p nc8 t", p=128)[:, :, tb * TT:(tb + 1) * TT],
                          in_=s_o[:, pb, xz]),
                      reads=tuple(f"o{pb}_{xz}_{nc8}" for nc8 in range(8)))
        p.finish_outputs("y")
        return p.build()


def host_l1_inputs(ins):
    x = ins["L0_attn_x"]
    Win = ins["L0_attn_Win"]
    bin_ = ins["L0_attn_bin"]
    maps = []
    for c in range(NCORES):
        b, h = c // 2, c % 2
        maps.append({
            "x_fm": np.ascontiguousarray(x[b].T),
            "winx": np.ascontiguousarray(Win[:, h * DH:(h + 1) * DH]),
            "winz": np.ascontiguousarray(Win[:, DI + h * DH:DI + (h + 1) * DH]),
            "binx": np.ascontiguousarray(bin_[h * DH:(h + 1) * DH].reshape(-1, 128).T),
            "binz": np.ascontiguousarray(bin_[DI + h * DH:DI + (h + 1) * DH].reshape(-1, 128).T),
        })
    return maps


if __name__ == "__main__":
    pass


def build_l2():
    nc = bass.Bass("TRN2", target_bir_lowering=False)
    stack = ExitStack()
    p = Prog(nc, stack)

    TC = 256
    NCH = S // TC
    NT = DH // 128

    xs_full = nc.dram_tensor("xs_full", [DI, S], F32, kind="ExternalInput").ap()
    z_sl = nc.dram_tensor("z_sl", [DH, S], F32, kind="ExternalInput").ap()
    wx = nc.dram_tensor("wx", [DI, 96], F32, kind="ExternalInput").ap()
    bx_fm = nc.dram_tensor("bx_fm", [96, 1], F32, kind="ExternalInput").ap()
    bx_tm = nc.dram_tensor("bx_tm", [128, 96], F32, kind="ExternalInput").ap()
    wdt = nc.dram_tensor("wdt", [64, DH], F32, kind="ExternalInput").ap()
    bdt = nc.dram_tensor("bdt", [128, 8], F32, kind="ExternalInput").ap()
    alog = nc.dram_tensor("alog", [128, 8, NST], F32, kind="ExternalInput").ap()
    dp_r = nc.dram_tensor("dp_r", [128, 8], F32, kind="ExternalInput").ap()
    y_out = nc.dram_tensor("y_out", [DH, S], F32, kind="ExternalOutput").ap()
    DBG = False
    if DBG:
        dbg_xpf = nc.dram_tensor("dbg_xpf", [96, 256], F32, kind="ExternalOutput").ap()
        dbg_xpt = nc.dram_tensor("dbg_xpt", [128, 2, 96], F32, kind="ExternalOutput").ap()
        dbg_dl = nc.dram_tensor("dbg_dl", [128, 8, 256], F32, kind="ExternalOutput").ap()
        dbg_B = nc.dram_tensor("dbg_B", [128, 256, 16], F32, kind="ExternalOutput").ap()
        dbg_C = nc.dram_tensor("dbg_C", [128, 256, 16], F32, kind="ExternalOutput").ap()
        dbg_h = nc.dram_tensor("dbg_h", [128, 256, 16], F32, kind="ExternalOutput").ap()
        dbg_dA = nc.dram_tensor("dbg_dA", [128, 256, 16], F32, kind="ExternalOutput").ap()
        dbg_u = nc.dram_tensor("dbg_u", [128, 256, 16], F32, kind="ExternalOutput").ap()

    s_xs = stack.enter_context(nc.sbuf_tensor("s_xs", [128, 16, TC], F32))
    s_z = stack.enter_context(nc.sbuf_tensor("s_z", [128, 8, TC], F32))
    s_wx = stack.enter_context(nc.sbuf_tensor("s_wx", [128, 16, 96], F32))
    s_bxf = stack.enter_context(nc.sbuf_tensor("s_bxf", [96, 1], F32))
    s_bxt = stack.enter_context(nc.sbuf_tensor("s_bxt", [128, 96], F32))
    s_wdt = stack.enter_context(nc.sbuf_tensor("s_wdt", [64, 8, 128], F32))
    s_bdt = stack.enter_context(nc.sbuf_tensor("s_bdt", [128, 8], F32))
    s_A = stack.enter_context(nc.sbuf_tensor("s_A", [128, 8, NST], F32))
    s_dp = stack.enter_context(nc.sbuf_tensor("s_dp", [128, 8], F32))
    s_xpf = stack.enter_context(nc.sbuf_tensor("s_xpf", [96, TC], F32))
    s_xpt = stack.enter_context(nc.sbuf_tensor("s_xpt", [128, TC // 128, 96], F32))
    s_dl = stack.enter_context(nc.sbuf_tensor("s_dl", [128, 8, TC], F32))
    s_sc1 = stack.enter_context(nc.sbuf_tensor("s_sc1", [128, 8, TC], F32))
    s_dx = stack.enter_context(nc.sbuf_tensor("s_dx", [128, 8, TC], F32))
    s_B = stack.enter_context(nc.sbuf_tensor("s_B", [128, TC, NST], F32))
    s_C = stack.enter_context(nc.sbuf_tensor("s_C", [128, TC, NST], F32))
    s_dA = stack.enter_context(nc.sbuf_tensor("s_dA", [128, TC, NST], F32))
    s_u = stack.enter_context(nc.sbuf_tensor("s_u", [128, TC, NST], F32))
    s_h = stack.enter_context(nc.sbuf_tensor("s_h", [128, TC, NST], F32))
    s_st = stack.enter_context(nc.sbuf_tensor("s_st", [128, 8, NST], F32))
    s_y = stack.enter_context(nc.sbuf_tensor("s_y", [128, 8, TC], F32))
    s_sz = stack.enter_context(nc.sbuf_tensor("s_sz", [128, 8, TC], F32))
    ps = [stack.enter_context(nc.psum_tensor(f"ps{i}", [128, TC], F32)) for i in range(4)]

    with stack:
        p.dma("y", lambda eng: eng.dma_start(out=s_wx[:], in_=wx.rearrange("(kc p) m -> p kc m", p=128)), writes=("wx",))
        p.dma("y", lambda eng: eng.dma_start(out=s_bxf[:], in_=bx_fm), writes=("bxf",))
        p.dma("y", lambda eng: eng.dma_start(out=s_bxt[:], in_=bx_tm), writes=("bxt",))
        p.dma("y", lambda eng: eng.dma_start(out=s_wdt[:], in_=wdt.rearrange("k (nc8 m) -> k nc8 m", m=128)), writes=("wdt",))
        p.dma("y", lambda eng: eng.dma_start(out=s_bdt[:], in_=bdt), writes=("bdt",))
        p.dma("y", lambda eng: eng.dma_start(out=s_A[:], in_=alog), writes=("alog",))
        p.dma("y", lambda eng: eng.dma_start(out=s_dp[:], in_=dp_r), writes=("dpr",))
        p.emit("s", lambda eng: eng.activation(s_A[:], s_A[:], ACT.Exp),
               reads=("alog",), writes=("alog",))
        p.emit("v", lambda eng: eng.tensor_scalar(s_A[:], s_A[:], -1.0, None, op0=ALU.mult),
               reads=("alog",), writes=("A",))
        p.emit("v", lambda eng: eng.memset(s_st[:], 0.0), writes=("st",))

        xs_r = xs_full.rearrange("(kc p) t -> p kc t", p=128)
        z_r = z_sl.rearrange("(kc p) t -> p kc t", p=128)
        y_r = y_out.rearrange("(kc p) t -> p kc t", p=128)

        for cix in range(NCH):
            t0 = cix * TC
            p.dma("y", lambda eng, t0=t0: eng.dma_start(out=s_xs[:], in_=xs_r[:, :, t0:t0 + TC]), writes=("xs",))
            p.dma("y", lambda eng, t0=t0: eng.dma_start(out=s_z[:], in_=z_r[:, :, t0:t0 + TC]), writes=("z",))

            pairs = [(s_wx[:, kc, :], s_xs[:, kc, :]) for kc in range(16)]
            mm_accum(p, ps[0][0:96, :], pairs, "ps0", ("wx", "xs"))
            p.emit("s", lambda eng: eng.activation(s_xpf[:], ps[0][0:96, :], ACT.Identity, bias=s_bxf[:], scale=1.0),
                   reads=("ps0", "bxf"), writes=("xpf",))
            for tb in range(TC // 128):
                pairs = [(s_xs[:, kc, tb * 128:(tb + 1) * 128], s_wx[:, kc, :]) for kc in range(16)]
                mm_accum(p, ps[1][:, 0:96], pairs, "ps1", ("wx", "xs"))
                p.emit("v", lambda eng, tb=tb: eng.tensor_tensor(s_xpt[:, tb, :], ps[1][:, 0:96], s_bxt[:], op=ALU.add),
                       reads=("ps1", "bxt"), writes=(f"xpt{tb}",))

            for nc8 in range(8):
                bi = 2 + (nc8 % 2)
                mm_accum(p, ps[bi][:], [(s_wdt[:, nc8, :], s_xpf[0:64, :])], f"ps{bi}", ("wdt", "xpf"))
                p.emit("s", lambda eng, nc8=nc8, bi=bi: eng.activation(
                           s_dl[:, nc8, :], ps[bi][:], ACT.Identity, bias=s_bdt[:, nc8:nc8 + 1], scale=1.0),
                       reads=(f"ps{bi}", "bdt"), writes=("dl",))
            p.emit("s", lambda eng: eng.activation(s_sc1[:], s_dl[:], ACT.Abs),
                   reads=("dl",), writes=("sc1",))
            p.emit("s", lambda eng: eng.activation(s_sc1[:], s_sc1[:], ACT.Exp, scale=-1.0),
                   reads=("sc1",), writes=("sc1",))
            p.emit("s", lambda eng: eng.activation(s_sc1[:], s_sc1[:], ACT.Ln, bias=1.0, scale=1.0),
                   reads=("sc1",), writes=("sc1",))
            p.emit("v", lambda eng: eng.scalar_tensor_tensor(
                       out=s_dl[:], in0=s_dl[:], scalar=0.0, in1=s_sc1[:], op0=ALU.max, op1=ALU.add),
                   reads=("sc1", "dl"), writes=("dl",))
            p.emit("v", lambda eng: eng.tensor_tensor(s_dx[:], s_dl[:], s_xs[:, 0:8, :], op=ALU.mult),
                   reads=("dl", "xs"), writes=("dx",))

            for nm, sl, keyb in (("B", slice(64, 80), "B"), ("C", slice(80, 96), "C")):
                dst = s_B if nm == "B" else s_C
                for tb in range(TC // 128):
                    def st0(eng, dst=dst, sl=sl, tb=tb):
                        return eng.dma_start(
                            out=dst[0:1, tb * 128:(tb + 1) * 128, :],
                            in_=s_xpt[:, tb, sl])
                    p.dma("g", st0, reads=(f"xpt{tb}",), writes=(keyb,))
                k = 1
                while k < 128:
                    p.dma("g", lambda eng, dst=dst, k=k: eng.dma_start(out=dst[k:2 * k], in_=dst[0:k]),
                          reads=(keyb,), writes=(keyb,))
                    k *= 2

            for dc in range(NT):
                for n in range(NST):
                    p.emit("s", lambda eng, dc=dc, n=n: eng.activation(
                               s_dA[:, :, n], s_dl[:, dc, :], ACT.Exp, scale=s_A[:, dc, n:n + 1]),
                           reads=("dl", "A"), writes=("dA",))
                p.emit("v", lambda eng, dc=dc: eng.tensor_tensor(
                           s_u[:], s_dx[:, dc, :].unsqueeze(2).broadcast_to([128, TC, NST]), s_B[:], op=ALU.mult),
                       reads=("dx", "B"), writes=("u",))
                for n in range(NST):
                    p.emit("v", lambda eng, dc=dc, n=n: eng.tensor_tensor_scan(
                               s_h[:, :, n], s_dA[:, :, n], s_u[:, :, n], s_st[:, dc, n:n + 1],
                               op0=ALU.mult, op1=ALU.add),
                           reads=("dA", "u", "st"), writes=("h",))
                if DBG and cix == 0 and dc == 7:
                    p.dma("y", lambda eng: eng.dma_start(out=dbg_dA, in_=s_dA[:]), reads=("dA",))
                    p.dma("y", lambda eng: eng.dma_start(out=dbg_u, in_=s_u[:]), reads=("u",))
                p.emit("v", lambda eng, dc=dc: eng.tensor_copy(s_st[:, dc, :], s_h[:, TC - 1, :]),
                       reads=("h",), writes=("st",))
                p.emit("v", lambda eng: eng.tensor_tensor(s_dA[:], s_h[:], s_C[:], op=ALU.mult),
                       reads=("C", "h"), writes=("dA",))
                p.emit("v", lambda eng, dc=dc: eng.tensor_reduce(
                           s_y[:, dc, :], s_dA[:], axis=AX, op=ALU.add),
                       reads=("dA",), writes=(f"y{dc}",))

            if DBG and cix == 0:
                p.dma("y", lambda eng: eng.dma_start(out=dbg_xpf, in_=s_xpf[:]), reads=("xpf",))
                p.dma("y", lambda eng: eng.dma_start(out=dbg_xpt, in_=s_xpt[:]), reads=("xpt0", "xpt1"))
                p.dma("y", lambda eng: eng.dma_start(out=dbg_dl, in_=s_dl[:]), reads=("dl",))
                p.dma("y", lambda eng: eng.dma_start(out=dbg_B, in_=s_B[:]), reads=("B",))
                p.dma("y", lambda eng: eng.dma_start(out=dbg_C, in_=s_C[:]), reads=("C",))
                p.dma("y", lambda eng: eng.dma_start(out=dbg_h, in_=s_h[:]), reads=("h",))
            p.emit("s", lambda eng: eng.activation(s_sz[:], s_z[:], ACT.Silu),
                   reads=("z",), writes=("sz",))
            for dc in range(NT):
                p.emit("v", lambda eng, dc=dc: eng.scalar_tensor_tensor(
                           out=s_y[:, dc, :], in0=s_xs[:, dc, :], scalar=s_dp[:, dc:dc + 1],
                           in1=s_y[:, dc, :], op0=ALU.mult, op1=ALU.add),
                       reads=("xs", "dpr", f"y{dc}"), writes=(f"y{dc}",))
            p.emit("v", lambda eng: eng.tensor_tensor(s_y[:], s_y[:], s_sz[:], op=ALU.mult),
                   reads=("sz",) + tuple(f"y{dc}" for dc in range(NT)),
                   writes=tuple(f"y{dc}" for dc in range(NT)))
            p.dma("y", lambda eng, t0=t0: eng.dma_start(out=y_r[:, :, t0:t0 + TC], in_=s_y[:]),
                  reads=tuple(f"y{dc}" for dc in range(NT)))
        p.finish_outputs("y")
        return p.build()


def host_l2_inputs(ins, xs_by_core, z_by_core):
    Wx = ins["L0_attn_Wx"]; bx = ins["L0_attn_bx"]
    Wdt = ins["L0_attn_Wdt"]; bdt = ins["L0_attn_bdt"]
    Alog = ins["L0_attn_A_log"]; Dp = ins["L0_attn_Dp"]
    maps = []
    for c in range(NCORES):
        b, h = c // 2, c % 2
        own = xs_by_core[c]
        other = xs_by_core[2 * b + (1 - h)]
        xs_full = np.concatenate([own, other], axis=0)
        wx_own = Wx[h * DH:(h + 1) * DH]
        wx_oth = Wx[(1 - h) * DH:(2 - h) * DH]
        maps.append({
            "xs_full": np.ascontiguousarray(xs_full),
            "z_sl": np.ascontiguousarray(z_by_core[c]),
            "wx": np.ascontiguousarray(np.concatenate([wx_own, wx_oth], axis=0)),
            "bx_fm": np.ascontiguousarray(bx[:, None]),
            "bx_tm": np.ascontiguousarray(np.broadcast_to(bx, (128, 96))),
            "wdt": np.ascontiguousarray(Wdt[:, h * DH:(h + 1) * DH]),
            "bdt": np.ascontiguousarray(bdt[h * DH:(h + 1) * DH].reshape(8, 128).T),
            "alog": np.ascontiguousarray(Alog[h * DH:(h + 1) * DH].reshape(8, 128, NST).transpose(1, 0, 2)),
            "dp_r": np.ascontiguousarray(Dp[h * DH:(h + 1) * DH].reshape(8, 128).T),
        })
    return maps


def emit_ln_fm(p, tag, s_in, k_in, s_out, k_out, s_sq, k_sq, s_ones128,
               ps_sum, k_psum, ps_sq, k_psq, s_mv, k_mv, TT, nchunk=8, dmodel=1024):
    inv = 1.0 / dmodel
    p.emit("s", lambda eng: eng.activation(s_sq[:], s_in[:], ACT.Square),
           reads=(k_in,), writes=(k_sq,))
    def mmsum(eng):
        ins = None
        for kc in range(nchunk):
            ins = eng.matmul(ps_sum[:, 0:TT], s_ones128[:], s_in[:, kc, :], start=(kc == 0), stop=(kc == nchunk - 1))
        return ins
    p.emit("t", mmsum, reads=(k_in, "ones"), writes=(k_psum,))
    def mmsq(eng):
        ins = None
        for kc in range(nchunk):
            ins = eng.matmul(ps_sq[:, 0:TT], s_ones128[:], s_sq[:, kc, :], start=(kc == 0), stop=(kc == nchunk - 1))
        return ins
    p.emit("t", mmsq, reads=(k_sq, "ones"), writes=(k_psq,))
    p.emit("v", lambda eng: eng.tensor_scalar(s_mv[:, 0, :], ps_sum[:, 0:TT], inv, None, op0=ALU.mult),
           reads=(k_psum,), writes=(k_mv,))
    p.emit("v", lambda eng: eng.tensor_tensor(s_mv[:, 1, :], s_mv[:, 0, :], s_mv[:, 0, :], op=ALU.mult),
           reads=(k_mv,), writes=(k_mv,))
    p.emit("v", lambda eng: eng.scalar_tensor_tensor(out=s_mv[:, 1, :], in0=ps_sq[:, 0:TT], scalar=inv,
                                                     in1=s_mv[:, 1, :], op0=ALU.mult, op1=ALU.subtract),
           reads=(k_psq, k_mv), writes=(k_mv,))
    p.emit("v", lambda eng: eng.tensor_scalar(s_mv[:, 1, :], s_mv[:, 1, :], 1e-5, None, op0=ALU.add),
           reads=(k_mv,), writes=(k_mv,))
    p.emit("s", lambda eng: eng.activation(s_mv[:, 1, :], s_mv[:, 1, :], ACT.Sqrt),
           reads=(k_mv,), writes=(k_mv,))
    p.emit("v", lambda eng: eng.reciprocal(s_mv[:, 1, :], s_mv[:, 1, :]),
           reads=(k_mv,), writes=(k_mv,))
    p.emit("v", lambda eng: eng.tensor_tensor(
               s_out[:], s_in[:], s_mv[:, 0:1, :].broadcast_to([128, nchunk, TT]), op=ALU.subtract),
           reads=(k_in, k_mv), writes=(k_out,))
    p.emit("v", lambda eng: eng.tensor_tensor(
               s_out[:], s_out[:], s_mv[:, 1:2, :].broadcast_to([128, nchunk, TT]), op=ALU.mult),
           reads=(k_out, k_mv), writes=(k_out,))


def emit_quant_f32(p, s_v, k_v, s_q, k_q, lo, hi, scale_ap=None, k_scale=(), scale_const=None):
    if scale_ap is not None:
        p.emit("v", lambda eng: eng.tensor_tensor(s_q[:], s_v[:], scale_ap, op=ALU.mult),
               reads=(k_v,) + tuple(k_scale), writes=(k_q,))
    else:
        p.emit("v", lambda eng: eng.tensor_scalar(s_q[:], s_v[:], scale_const, None, op0=ALU.mult),
               reads=(k_v,), writes=(k_q,))
    p.emit("v", lambda eng: eng.tensor_scalar(s_v[:], s_q[:], 0.5, 1.0, op0=ALU.add, op1=ALU.mod),
           reads=(k_q,), writes=(k_v + "_mod",))
    p.emit("v", lambda eng: eng.scalar_tensor_tensor(out=s_q[:], in0=s_q[:], scalar=0.5, in1=s_v[:],
                                                     op0=ALU.add, op1=ALU.subtract),
           reads=(k_q, k_v + "_mod"), writes=(k_q,))
    p.emit("v", lambda eng: eng.tensor_scalar(s_q[:], s_q[:], float(lo), float(hi), op0=ALU.max, op1=ALU.min),
           reads=(k_q,), writes=(k_q,))




def emit_colmax_bcast(p, tag, m_ap, k_m, s_cm, s_row1, ps_tr, k_pstr, ps_bc, k_psbc,
                      s_ident, out_scale_ap, k_oscale, out_inv_ap, k_oinv, s_one1, extra_row_ops):
    p.emit("t", lambda eng: eng.transpose(ps_tr[:, 0:128], m_ap, s_ident[:]),
           reads=(k_m, "ident"), writes=(k_pstr,))
    p.emit("v", lambda eng: eng.tensor_reduce(s_cm[:, 0:1], ps_tr[:, 0:128], axis=AX, op=ALU.max),
           reads=(k_pstr,), writes=(tag + "cm",))
    p.emit("v", lambda eng: eng.tensor_scalar(s_cm[:, 0:1], s_cm[:, 0:1], 1e-5, None, op0=ALU.max),
           reads=(tag + "cm",), writes=(tag + "cm",))
    p.emit("t", lambda eng: eng.transpose(ps_tr[0:1, 128:256], s_cm[:, 0:1], s_ident[:]),
           reads=(tag + "cm", "ident"), writes=(k_pstr,))
    p.emit("v", lambda eng: eng.tensor_copy(s_row1[0:1, 0:128], ps_tr[0:1, 128:256]),
           reads=(k_pstr,), writes=(tag + "row",))
    if out_inv_ap is not None:
        p.emit("v", lambda eng: eng.tensor_scalar(s_row1[32:33, 0:128], s_row1[0:1, 0:128], 1.0 / 127.0, None, op0=ALU.mult),
               reads=(tag + "row",), writes=(tag + "rowi",))
    p.emit("v", lambda eng: eng.reciprocal(s_row1[0:1, 0:128], s_row1[0:1, 0:128]),
           reads=(tag + "row",), writes=(tag + "row",))
    p.emit("v", lambda eng: eng.tensor_scalar(s_row1[0:1, 0:128], s_row1[0:1, 0:128], 127.0, None, op0=ALU.mult),
           reads=(tag + "row",), writes=(tag + "row",))
    def bc(eng):
        return eng.matmul(ps_bc[:, 0:128], s_one1[0:1, :], s_row1[0:1, 0:128], start=True, stop=True)
    p.emit("t", bc, reads=(tag + "row", "one1"), writes=(k_psbc,))
    p.emit("v", lambda eng: eng.tensor_copy(out_scale_ap, ps_bc[:, 0:128]),
           reads=(k_psbc,), writes=(k_oscale,))
    if out_inv_ap is not None:
        def bc2(eng):
            return eng.matmul(ps_bc[:, 128:256], s_one1[32:33, :], s_row1[32:33, 0:128], start=True, stop=True)
        p.emit("t", bc2, reads=(tag + "rowi", "one1"), writes=(k_psbc,))
        p.emit("v", lambda eng: eng.tensor_copy(out_inv_ap, ps_bc[:, 128:256]),
               reads=(k_psbc,), writes=(k_oinv,))


P1_STAGE = 99
def build_l3(upto=99):
    nc = bass.Bass("TRN2", target_bir_lowering=False)
    stack = ExitStack()
    p = Prog(nc, stack)

    TT = 512
    NTT = TOK // TT

    y_own = nc.dram_tensor("y_own", [DI, TOK], F32, kind="ExternalInput").ap()
    wout = nc.dram_tensor("wout", [DI, DM], F32, kind="ExternalInput").ap()
    boutr = nc.dram_tensor("boutr", [128, 8], F32, kind="ExternalInput").ap()
    x_fm = nc.dram_tensor("x_fm", [DM, TOK], F32, kind="ExternalInput").ap()
    wgT = nc.dram_tensor("wgT", [DM, 4096], F32, kind="ExternalInput").ap()
    wuT = nc.dram_tensor("wuT", [DM, 4096], F32, kind="ExternalInput").ap()
    wdT = nc.dram_tensor("wdT", [4096, DM], F32, kind="ExternalInput").ap()
    s2_out = nc.dram_tensor("s2_out", [DM, TOK], F32, kind="ExternalOutput").ap()
    gu_hbm = nc.dram_tensor("gu_hbm", [4096, TOK], F32, kind="Internal").ap()
    s1_hbm = nc.dram_tensor("s1_hbm", [DM, TOK], F32, kind="Internal").ap()
    wdq_hbm = nc.dram_tensor("wdq_hbm", [4096, DM], BF16, kind="Internal").ap()

    s_ones = stack.enter_context(nc.sbuf_tensor("s_ones", [128, 128], F32))
    s_one1 = stack.enter_context(nc.sbuf_tensor("s_one1", [64, 128], F32))
    s_kq1 = stack.enter_context(nc.sbuf_tensor("s_kq1", [128, 8, TOK], BF16))
    s_inv1 = stack.enter_context(nc.sbuf_tensor("s_inv1", [128, TOK], F32))
    s_m2 = stack.enter_context(nc.sbuf_tensor("s_m2", [128, TOK], F32))
    s_f2s = stack.enter_context(nc.sbuf_tensor("s_f2s", [128, TOK], F32))
    s_inv2 = stack.enter_context(nc.sbuf_tensor("s_inv2", [128, TOK], F32))
    s_wsum = stack.enter_context(nc.sbuf_tensor("s_wsum", [128, 24], F32))
    s_wsc = stack.enter_context(nc.sbuf_tensor("s_wsc", [128, 8], F32))
    s_row = stack.enter_context(nc.sbuf_tensor("s_row", [96, 128], F32))
    s_cmg = stack.enter_context(nc.sbuf_tensor("s_cmg", [128, 2], F32))
    s_iota = stack.enter_context(nc.sbuf_tensor("s_iota", [128, 128], F32))
    s_ident = stack.enter_context(nc.sbuf_tensor("s_ident", [128, 128], F32))
    ps_pool = [stack.enter_context(nc.psum_tensor(f"pp{i}", [128, TT], F32)) for i in range(8)]

    p.emit("v", lambda eng: eng.memset(s_ones[:], 1.0), writes=("ones",))
    p.emit("v", lambda eng: eng.memset(s_one1[:], 1.0), writes=("one1",))
    make_identity(p, s_iota, s_ident)

    with ExitStack() as ph:
        s_wt = ph.enter_context(nc.sbuf_tensor("p0_wt", [128, 2, 4096], F32))
        wviews = [wgT.rearrange("(c p) n -> p c n", p=128),
                  wuT.rearrange("(c p) n -> p c n", p=128),
                  wdT.rearrange("(c p) n -> p c n", p=128)]
        for wi, wv in enumerate(wviews):
            nper = wv.shape[2]
            cper = wv.shape[1] // 8
            for j in range(8):
                pb = j % 2
                p.dma("y", lambda eng, wv=wv, j=j, pb=pb, nper=nper, cper=cper: eng.dma_start(
                          out=s_wt[:, pb, :].rearrange("p (c n) -> p c n", c=cper),
                          in_=wv[:, j * cper:(j + 1) * cper, :]),
                      writes=(f"p0w{pb}",))
                p.emit("v", lambda eng, wi=wi, j=j, pb=pb: eng.tensor_reduce(
                           s_wsum[:, wi * 8 + j:wi * 8 + j + 1], s_wt[:, pb, :], axis=AX,
                           op=ALU.add, apply_absolute_value=True),
                       reads=(f"p0w{pb}",), writes=(f"wsum{wi}_{j}",))
        for wi in range(3):
            p.emit("v", lambda eng, wi=wi: eng.tensor_reduce(
                       s_wsc[:, wi:wi + 1], s_wsum[:, wi * 8:(wi + 1) * 8], axis=AX, op=ALU.add),
                   reads=tuple(f"wsum{wi}_{j}" for j in range(8)), writes=(f"wtot{wi}",))
        p.emit("t", lambda eng: eng.matmul(ps_pool[6][:, 0:3], s_ones[:], s_wsc[:, 0:3], start=True, stop=True),
               reads=("wtot0", "wtot1", "wtot2", "ones"), writes=("pp6w",))
        p.emit("v", lambda eng: eng.tensor_copy(s_wsc[:, 0:3], ps_pool[6][:, 0:3]),
               reads=("pp6w",), writes=("wsc",))
        p.emit("v", lambda eng: eng.tensor_scalar(s_wsc[:, 0:3], s_wsc[:, 0:3], 1.0 / (4096.0 * 1024.0), 1e-5,
                                                  op0=ALU.mult, op1=ALU.max),
               reads=("wsc",), writes=("wsc",))
        p.emit("v", lambda eng: eng.reciprocal(s_wsc[:, 3:6], s_wsc[:, 0:3]),
               reads=("wsc",), writes=("wsc2",))

    p.barrier()
    if upto < 1:
        p.finish_outputs("y")
        with stack:
            return p.build()
    with ExitStack() as ph:
        s_wtb = ph.enter_context(nc.sbuf_tensor("p0b_wt", [128, 2, 2048], F32))
        s_wqb2 = ph.enter_context(nc.sbuf_tensor("p0b_wq", [128, 2, 2048], F32))
        s_wqbb = ph.enter_context(nc.sbuf_tensor("p0b_wqb", [128, 2, 2048], BF16))
        wv = wdT.rearrange("(c p) n -> p c n", p=128)
        qv = wdq_hbm.rearrange("(c p) n -> p c n", p=128)
        for j in range(16):
            pb = j % 2
            p.dma("y", lambda eng, j=j, pb=pb: eng.dma_start(
                      out=s_wtb[:, pb, :].rearrange("p (c n) -> p c n", c=2), in_=wv[:, 2 * j:2 * j + 2, :]),
                  writes=(f"q0w{pb}",))
            p.emit("v", lambda eng, pb=pb: eng.tensor_scalar(s_wqb2[:, pb, :], s_wtb[:, pb, :], s_wsc[:, 5:6], None, op0=ALU.mult),
                   reads=(f"q0w{pb}", "wsc2"), writes=(f"q0q{pb}",))
            p.emit("v", lambda eng, pb=pb: eng.tensor_scalar(s_wtb[:, pb, :], s_wqb2[:, pb, :], -0.5, None, op0=ALU.is_lt),
                   reads=(f"q0q{pb}",), writes=(f"q0w{pb}",))
            p.emit("v", lambda eng, pb=pb: eng.scalar_tensor_tensor(out=s_wqbb[:, pb, :], in0=s_wqb2[:, pb, :], scalar=0.5,
                                                                    in1=s_wtb[:, pb, :], op0=ALU.is_gt, op1=ALU.subtract),
                   reads=(f"q0q{pb}", f"q0w{pb}"), writes=(f"q0b{pb}",))
            p.dma("y", lambda eng, j=j, pb=pb: eng.dma_start(
                      out=qv[:, 2 * j:2 * j + 2, :], in_=s_wqbb[:, pb, :].rearrange("p (c n) -> p c n", c=2)),
                  reads=(f"q0b{pb}",))

    p.barrier()
    if upto < 2:
        p.finish_outputs("y")
        with stack:
            return p.build()
    with ExitStack() as ph:
        s_y = ph.enter_context(nc.sbuf_tensor("p1_y", [128, 16, 256], F32))
        s_x = ph.enter_context(nc.sbuf_tensor("p1_x", [128, 2, 8, 256], F32))
        s_w = ph.enter_context(nc.sbuf_tensor("p1_w", [128, 16, 8, 128], F32))
        s_b = ph.enter_context(nc.sbuf_tensor("p1_b", [128, 8], F32))
        s_r1 = ph.enter_context(nc.sbuf_tensor("p1_r1", [128, 8, 256], F32))
        s_sq = ph.enter_context(nc.sbuf_tensor("p1_sq", [128, 8, 256], F32))
        s_sc = ph.enter_context(nc.sbuf_tensor("p1_sc", [128, 4, 256], F32))
        s_s1loc = ph.enter_context(nc.sbuf_tensor("p1_s1", [128, 8, 256], F32))
        s_ki = ph.enter_context(nc.sbuf_tensor("p1_ki", [128, 8, 256], mybir.dt.int32))
        s_qs = ph.enter_context(nc.sbuf_tensor("p1_qs", [128, 8, 256], F32))
        p.dma("y", lambda eng: eng.dma_start(out=s_w[:], in_=wout.rearrange("(kc p) (nc8 f) -> p kc nc8 f", p=128, f=128)),
              writes=("w",))
        p.dma("y", lambda eng: eng.dma_start(out=s_b[:], in_=boutr), writes=("b",))
        yv = y_own.rearrange("(kc p) t -> p kc t", p=128)
        xv = x_fm.rearrange("(kc p) t -> p kc t", p=128)
        TT1 = 256
        for tt in range(TOK // TT1):
            pb = tt % 2
            p.dma("y", lambda eng, tt=tt: eng.dma_start(out=s_y[:], in_=yv[:, :, tt * TT1:(tt + 1) * TT1]),
                  writes=("yb",))
            p.dma("y", lambda eng, tt=tt, pb=pb: eng.dma_start(out=s_x[:, pb], in_=xv[:, :, tt * TT1:(tt + 1) * TT1]),
                  writes=(f"x{pb}",))
            for nc8 in range(8):
                bi = nc8 % 4
                pairs = [(s_w[:, kc, nc8, :], s_y[:, kc, :]) for kc in range(16)]
                mm_accum(p, ps_pool[bi][:, 0:TT1], pairs, f"pp{bi}", ("yb", "w"))
                p.emit("s", lambda eng, nc8=nc8, bi=bi: eng.activation(
                           s_r1[:, nc8, :], ps_pool[bi][:, 0:TT1], ACT.Identity, bias=s_b[:, nc8:nc8 + 1], scale=1.0),
                       reads=(f"pp{bi}", "b"), writes=("r1",))
            p.emit("v", lambda eng, pb=pb: eng.tensor_tensor(s_r1[:], s_r1[:], s_x[:, pb], op=ALU.add),
                   reads=("r1", f"x{pb}"), writes=("r1",))
            if P1_STAGE < 1:
                continue
            s1_sl = s_s1loc
            emit_ln_fm(p, f"ln1_{tt}", s_r1, "r1", s1_sl, "s1", s_sq, "sq", s_ones,
                       ps_pool[4], "pp4", ps_pool[5], "pp5", s_qs[:, 0:2], "qs", TT1)
            p.dma("y", lambda eng, tt=tt: eng.dma_start(
                      out=s1_hbm.rearrange("(kc p) t -> p kc t", p=128)[:, :, tt * TT1:(tt + 1) * TT1],
                      in_=s_s1loc[:]),
                  reads=("s1",))
            if P1_STAGE < 2:
                continue
            p.emit("s", lambda eng, s1_sl=s1_sl: eng.activation(s_qs[:], s1_sl[:], ACT.Abs),
                   reads=("s1",), writes=("qs",))
            p.emit("v", lambda eng: eng.tensor_tensor(s_sc[:, 0, :], s_qs[:, 0, :], s_qs[:, 1, :], op=ALU.max),
                   reads=("qs",), writes=("am",))
            for kc in range(2, 8):
                p.emit("v", lambda eng, kc=kc: eng.tensor_tensor(s_sc[:, 0, :], s_sc[:, 0, :], s_qs[:, kc, :], op=ALU.max),
                       reads=("am", "qs"), writes=("am",))
            for jb in range(TT1 // 128):
                emit_colmax_bcast(p, f"cm1_{tt}_{jb}", s_sc[:, 0, jb * 128:(jb + 1) * 128], "am",
                                  s_cmg, s_row, ps_pool[6], "pp6c", ps_pool[7], "pp7c",
                                  s_ident, s_sc[:, 2, jb * 128:(jb + 1) * 128], "sc1",
                                  s_inv1[:, tt * TT1 + jb * 128:tt * TT1 + (jb + 1) * 128], "inv1", s_one1, None)
            if P1_STAGE < 3:
                continue
            p.emit("v", lambda eng, s1_sl=s1_sl: eng.tensor_tensor(
                       s_r1[:], s1_sl[:], s_sc[:, 2:3, :].broadcast_to([128, 8, TT1]), op=ALU.mult),
                   reads=("s1", "sc1"), writes=("r1",))
            p.emit("v", lambda eng: eng.tensor_copy(s_ki[:], s_r1[:]), reads=("r1",), writes=("ki",))
            p.emit("v", lambda eng: eng.tensor_copy(s_sq[:], s_ki[:]), reads=("ki",), writes=("sq",))
            p.emit("v", lambda eng: eng.tensor_tensor(s_r1[:], s_r1[:], s_sq[:], op=ALU.subtract),
                   reads=("r1", "sq"), writes=("r1",))
            p.emit("v", lambda eng: eng.tensor_scalar(s_qs[:], s_r1[:], -0.5, None, op0=ALU.is_lt),
                   reads=("r1",), writes=("qs",))
            p.emit("v", lambda eng: eng.scalar_tensor_tensor(out=s_r1[:], in0=s_r1[:], scalar=0.5, in1=s_qs[:],
                                                             op0=ALU.is_gt, op1=ALU.subtract),
                   reads=("r1", "qs"), writes=("r1",))
            p.emit("v", lambda eng: eng.tensor_tensor(s_r1[:], s_r1[:], s_sq[:], op=ALU.add),
                   reads=("r1", "sq"), writes=("r1",))
            p.emit("v", lambda eng: eng.tensor_scalar(s_r1[:], s_r1[:], -128.0, 127.0, op0=ALU.max, op1=ALU.min),
                   reads=("r1",), writes=("r1",))
            p.emit("v", lambda eng, tt=tt: eng.tensor_copy(s_kq1[:, :, tt * TT1:(tt + 1) * TT1], s_r1[:]),
                   reads=("r1",), writes=("kq1",))

    p.barrier()
    if upto < 3:
        p.finish_outputs("y")
        with stack:
            return p.build()
    p.emit("v", lambda eng: eng.memset(s_m2[:], 0.0), writes=("m2",))
    with ExitStack() as ph:
        s_wb = ph.enter_context(nc.sbuf_tensor("p2_wb", [128, 2, 2, 8, 128], F32))
        s_wq2 = ph.enter_context(nc.sbuf_tensor("p2_wq", [128, 2, 8, 128], F32))
        s_wqb3 = ph.enter_context(nc.sbuf_tensor("p2_wqb", [128, 2, 8, 128], BF16))
        s_g = ph.enter_context(nc.sbuf_tensor("p2_g", [128, TT], F32))
        s_u = ph.enter_context(nc.sbuf_tensor("p2_u", [128, TT], F32))
        s_gu2 = ph.enter_context(nc.sbuf_tensor("p2_gu", [128, 2, TT], F32))
        wgv = wgT.rearrange("(kc p) (oc f) -> p kc oc f", p=128, f=128)
        wuv = wuT.rearrange("(kc p) (oc f) -> p kc oc f", p=128, f=128)
        guv = gu_hbm.rearrange("(oc p) t -> p oc t", p=128)
        for oc in range(32):
            pb = oc % 2
            p.dma("y", lambda eng, oc=oc, pb=pb: eng.dma_start(out=s_wb[:, pb, 0], in_=wgv[:, :, oc, :]),
                  writes=(f"wg{pb}",))
            p.dma("y", lambda eng, oc=oc, pb=pb: eng.dma_start(out=s_wb[:, pb, 1], in_=wuv[:, :, oc, :]),
                  writes=(f"wu{pb}",))
            for gi, wkey, mc_col, sw_col in ((0, "wg", 0, 3), (1, "wu", 1, 4)):
                p.emit("v", lambda eng, pb=pb, gi=gi, sw_col=sw_col: eng.tensor_scalar(
                           s_wq2[:, 0], s_wb[:, pb, gi], s_wsc[:, sw_col:sw_col + 1], None, op0=ALU.mult),
                       reads=(f"{wkey}{pb}", "wsc2"), writes=("wqa",))
                p.emit("v", lambda eng: eng.tensor_scalar(s_wq2[:, 1], s_wq2[:, 0], -0.5, None, op0=ALU.is_lt),
                       reads=("wqa",), writes=("wqm",))
                p.emit("v", lambda eng, gi=gi: eng.scalar_tensor_tensor(out=s_wqb3[:, gi], in0=s_wq2[:, 0], scalar=0.5,
                                                                 in1=s_wq2[:, 1], op0=ALU.is_gt, op1=ALU.subtract),
                       reads=("wqa", "wqm"), writes=(f"wqb{gi}",))
            for tt in range(NTT):
                pairs_g = [(s_wqb3[:, 0, kc, :], s_kq1[:, kc, tt * TT:(tt + 1) * TT]) for kc in range(8)]
                pairs_u = [(s_wqb3[:, 1, kc, :], s_kq1[:, kc, tt * TT:(tt + 1) * TT]) for kc in range(8)]
                mm_accum(p, ps_pool[0][:], pairs_g, "pp0", ("wqb0", "kq1"))
                mm_accum(p, ps_pool[1][:], pairs_u, "pp1", ("wqb1", "kq1"))
                p.emit("v", lambda eng, tt=tt: eng.scalar_tensor_tensor(
                           out=s_g[:], in0=ps_pool[0][:], scalar=s_wsc[:, 0:1],
                           in1=s_inv1[:, tt * TT:(tt + 1) * TT], op0=ALU.mult, op1=ALU.mult),
                       reads=("pp0", "wsc", "inv1"), writes=("g",))
                p.emit("s", lambda eng: eng.activation(s_g[:], s_g[:], ACT.Silu),
                       reads=("g",), writes=("g",))
                p.emit("v", lambda eng, tt=tt: eng.scalar_tensor_tensor(
                           out=s_u[:], in0=ps_pool[1][:], scalar=s_wsc[:, 1:2],
                           in1=s_inv1[:, tt * TT:(tt + 1) * TT], op0=ALU.mult, op1=ALU.mult),
                       reads=("pp1", "wsc", "inv1"), writes=("u",))
                gb = tt % 2
                p.emit("v", lambda eng, gb=gb: eng.tensor_tensor(s_gu2[:, gb], s_g[:], s_u[:], op=ALU.mult),
                       reads=("g", "u"), writes=(f"gu{gb}",))
                p.dma("y", lambda eng, oc=oc, tt=tt, gb=gb: eng.dma_start(
                          out=guv[:, oc, tt * TT:(tt + 1) * TT], in_=s_gu2[:, gb]),
                      reads=(f"gu{gb}",))
                p.emit("s", lambda eng, gb=gb: eng.activation(s_g[:], s_gu2[:, gb], ACT.Abs),
                       reads=(f"gu{gb}",), writes=("g",))
                p.emit("v", lambda eng, tt=tt: eng.tensor_tensor(
                           s_m2[:, tt * TT:(tt + 1) * TT], s_m2[:, tt * TT:(tt + 1) * TT], s_g[:], op=ALU.max),
                       reads=("g", "m2"), writes=("m2",))

    p.barrier()
    if upto < 4:
        p.finish_outputs("y")
        with stack:
            return p.build()
    for jb in range(TOK // 128):
        emit_colmax_bcast(p, f"cm2_{jb}", s_m2[:, jb * 128:(jb + 1) * 128], "m2",
                          s_cmg, s_row, ps_pool[6], "pp6c", ps_pool[7], "pp7c",
                          s_ident, s_f2s[:, jb * 128:(jb + 1) * 128], "f2",
                          s_inv2[:, jb * 128:(jb + 1) * 128], "inv2", s_one1, None)

    p.barrier()
    if upto < 5:
        p.finish_outputs("y")
        with stack:
            return p.build()
    with ExitStack() as ph:
        s_wd = ph.enter_context(nc.sbuf_tensor("p4_wd", [128, 2, 8, 128], BF16))
        s_gu4 = ph.enter_context(nc.sbuf_tensor("p4_gu", [128, 2, TT], F32))
        s_q = ph.enter_context(nc.sbuf_tensor("p4_q", [128, 2, TT], F32))
        s_qb = ph.enter_context(nc.sbuf_tensor("p4_qb", [128, TT], BF16))
        s_qi4 = ph.enter_context(nc.sbuf_tensor("p4_qi", [128, TT], mybir.dt.int32))
        s_qf4 = ph.enter_context(nc.sbuf_tensor("p4_qf", [128, TT], F32))
        s_ffn = ph.enter_context(nc.sbuf_tensor("p4_ffn", [128, 8, TT], F32))
        s_sq4 = ph.enter_context(nc.sbuf_tensor("p4_sq", [128, 8, TT], F32))
        s_x4 = ph.enter_context(nc.sbuf_tensor("p4_x", [128, 8, TT], F32))
        s_s1r = ph.enter_context(nc.sbuf_tensor("p4_s1r", [128, 8, TT], F32))
        s_mv4 = ph.enter_context(nc.sbuf_tensor("p4_mv", [128, 2, TT], F32))
        wdv = wdq_hbm.rearrange("(kc p) (oc f) -> p kc oc f", p=128, f=128)
        guv = gu_hbm.rearrange("(kc p) t -> p kc t", p=128)
        s1v = s1_hbm.rearrange("(kc p) t -> p kc t", p=128)
        for tt in range(NTT):
            p.dma("y", lambda eng, tt=tt: eng.dma_start(out=s_s1r[:], in_=s1v[:, :, tt * TT:(tt + 1) * TT]),
                  writes=("s1r",))
            for ic in range(32):
                pb = ic % 2
                p.dma("y", lambda eng, ic=ic, pb=pb: eng.dma_start(out=s_wd[:, pb], in_=wdv[:, ic, :, :]),
                      writes=(f"wd{pb}",))
                p.dma("y", lambda eng, ic=ic, tt=tt, pb=pb: eng.dma_start(
                          out=s_gu4[:, pb], in_=guv[:, ic, tt * TT:(tt + 1) * TT]),
                      writes=(f"gub{pb}",))
                p.emit("v", lambda eng, tt=tt, pb=pb: eng.tensor_tensor(
                           s_q[:, 0], s_gu4[:, pb], s_f2s[:, tt * TT:(tt + 1) * TT], op=ALU.mult),
                       reads=(f"gub{pb}", "f2"), writes=("qa",))
                p.emit("v", lambda eng: eng.tensor_copy(s_qi4[:], s_q[:, 0]), reads=("qa",), writes=("qi4",))
                p.emit("v", lambda eng: eng.tensor_copy(s_qf4[:], s_qi4[:]), reads=("qi4",), writes=("qf4",))
                p.emit("v", lambda eng: eng.tensor_tensor(s_q[:, 0], s_q[:, 0], s_qf4[:], op=ALU.subtract),
                       reads=("qa", "qf4"), writes=("qa",))
                p.emit("v", lambda eng: eng.tensor_scalar(s_q[:, 1], s_q[:, 0], -0.5, None, op0=ALU.is_lt),
                       reads=("qa",), writes=("qm",))
                p.emit("v", lambda eng: eng.scalar_tensor_tensor(out=s_q[:, 0], in0=s_q[:, 0], scalar=0.5,
                                                                 in1=s_q[:, 1], op0=ALU.is_gt, op1=ALU.subtract),
                       reads=("qa", "qm"), writes=("qa",))
                p.emit("v", lambda eng: eng.tensor_tensor(s_q[:, 0], s_q[:, 0], s_qf4[:], op=ALU.add),
                       reads=("qa", "qf4"), writes=("qa",))
                p.emit("v", lambda eng: eng.tensor_scalar(s_qb[:], s_q[:, 0], -128.0, 127.0, op0=ALU.max, op1=ALU.min),
                       reads=("qa",), writes=("qb",))
                def mmdown(eng, ic=ic, pb=pb):
                    ins = None
                    for oc in range(8):
                        ins = eng.matmul(ps_pool[oc][:], s_wd[:, pb, oc, :], s_qb[:],
                                         start=(ic == 0), stop=(ic == 31))
                    return ins
                p.emit("t", mmdown, reads=(f"wd{pb}", "qb"), writes=("ppdown",))
            for oc in range(8):
                p.emit("v", lambda eng, oc=oc, tt=tt: eng.scalar_tensor_tensor(
                           out=s_ffn[:, oc, :], in0=ps_pool[oc][:], scalar=s_wsc[:, 2:3],
                           in1=s_inv2[:, tt * TT:(tt + 1) * TT], op0=ALU.mult, op1=ALU.mult),
                       reads=("ppdown", "wsc", "inv2"), writes=("ffn",))
            p.emit("v", lambda eng, tt=tt: eng.tensor_tensor(
                       s_ffn[:], s_ffn[:], s_s1r[:], op=ALU.add),
                   reads=("ffn", "s1r"), writes=("ffn",))
            emit_ln_fm(p, f"ln2_{tt}", s_ffn, "ffn", s_x4, "s2", s_sq4, "p4sq", s_ones,
                       ps_pool[0], "ppdown", ps_pool[1], "ppdown", s_mv4, "lnmv2", TT)
            p.dma("y", lambda eng, tt=tt: eng.dma_start(
                      out=s2_out.rearrange("(kc p) t -> p kc t", p=128)[:, :, tt * TT:(tt + 1) * TT], in_=s_x4[:]),
                  reads=("s2",))
    p.finish_outputs("y")
    with stack:
        return p.build()


def host_l3_inputs(ins, y_by_core):
    Wout = ins["L0_attn_Wout"]; bout = ins["L0_attn_bout"]
    x = ins["L0_attn_x"]
    wg = ins["L0_ffn_w_gate"]; wu = ins["L0_ffn_w_up"]; wd = ins["L0_ffn_w_down"]
    maps = []
    for c in range(NCORES):
        b, h = c // 2, c % 2
        tsl = slice(h * TOK, (h + 1) * TOK)
        y_full = np.concatenate([y_by_core[2 * b], y_by_core[2 * b + 1]], axis=0)
        maps.append({
            "y_own": np.ascontiguousarray(y_full[:, tsl]),
            "wout": Wout,
            "boutr": np.ascontiguousarray(bout.reshape(8, 128).T),
            "x_fm": np.ascontiguousarray(x[b].T[:, tsl]),
            "wgT": np.ascontiguousarray(wg.T),
            "wuT": np.ascontiguousarray(wu.T),
            "wdT": np.ascontiguousarray(wd.T),
        })
    return maps


def build_l4():
    nc = bass.Bass("TRN2", target_bir_lowering=False)
    stack = ExitStack()
    p = Prog(nc, stack)

    NH, DHD, W = 16, 64, 128
    NB = TOK // W
    KT = TOK + 2 * W

    s2_fm = nc.dram_tensor("s2_fm", [DM, TOK], F32, kind="ExternalInput").ap()
    halo_x = nc.dram_tensor("halo_x", [DM, 2 * W], F32, kind="ExternalInput").ap()
    wqkv = nc.dram_tensor("wqkv", [DM, 3 * DM], F32, kind="ExternalInput").ap()
    bq_r = nc.dram_tensor("bq_r", [128, 8], F32, kind="ExternalInput").ap()
    bk_r = nc.dram_tensor("bk_r", [128, 8], F32, kind="ExternalInput").ap()
    bv_bc = nc.dram_tensor("bv_bc", [128, DM], F32, kind="ExternalInput").ap()
    wout1 = nc.dram_tensor("wout1", [DM, DM], F32, kind="ExternalInput").ap()
    bo_r = nc.dram_tensor("bo_r", [128, 8], F32, kind="ExternalInput").ap()
    mask0 = nc.dram_tensor("mask0", [128, 384], F32, kind="ExternalInput").ap()
    maskm = nc.dram_tensor("maskm", [128, 384], F32, kind="ExternalInput").ap()
    mask15 = nc.dram_tensor("mask15", [128, 384], F32, kind="ExternalInput").ap()
    s3f_out = nc.dram_tensor("s3f_out", [DM, TOK], F32, kind="ExternalOutput").ap()
    s3t_out = nc.dram_tensor("s3t_out", [TOK, DM], F32, kind="ExternalOutput").ap()
    k_hbm = nc.dram_tensor("k_hbm", [DM, KT], F32, kind="Internal").ap()
    v_hbm = nc.dram_tensor("v_hbm", [KT, DM], F32, kind="Internal").ap()

    s_ms = stack.enter_context(nc.sbuf_tensor("s_ms", [128, 3, 384], F32))
    s_ones = stack.enter_context(nc.sbuf_tensor("s_ones", [128, 128], F32))
    s_one1 = stack.enter_context(nc.sbuf_tensor("s_one1", [64, 128], F32))
    s_row = stack.enter_context(nc.sbuf_tensor("s_row", [96, 512], F32))
    s_iota = stack.enter_context(nc.sbuf_tensor("s_iota", [128, 128], F32))
    s_ident = stack.enter_context(nc.sbuf_tensor("s_ident", [128, 128], F32))
    ps = [stack.enter_context(nc.psum_tensor(f"ps{i}", [128, 512], F32)) for i in range(8)]
    p.emit("v", lambda eng: eng.memset(s_ones[:], 1.0), writes=("ones",))
    p.emit("v", lambda eng: eng.memset(s_one1[:], 1.0), writes=("one1",))
    make_identity(p, s_iota, s_ident)
    p.dma("y", lambda eng: eng.dma_start(out=s_ms[:, 0], in_=mask0), writes=("ms",))
    p.dma("y", lambda eng: eng.dma_start(out=s_ms[:, 1], in_=maskm), writes=("ms",))
    p.dma("y", lambda eng: eng.dma_start(out=s_ms[:, 2], in_=mask15), writes=("ms",))

    x_r = s2_fm.rearrange("(kc p) t -> p kc t", p=128)
    hx_r = halo_x.rearrange("(kc p) t -> p kc t", p=128)
    kv = k_hbm.rearrange("(kc p) t -> p kc t", p=128)
    vv_ = v_hbm.rearrange("(tb p) d -> p tb d", p=128)

    with ExitStack() as ph:
        s_wk = ph.enter_context(nc.sbuf_tensor("pa_wk", [128, 8, 8, 128], F32))
        s_wv = ph.enter_context(nc.sbuf_tensor("pa_wv", [128, 8, 8, 128], F32))
        s_bk = ph.enter_context(nc.sbuf_tensor("pa_bk", [128, 8], F32))
        s_bv = ph.enter_context(nc.sbuf_tensor("pa_bv", [128, DM], F32))
        s_xa = ph.enter_context(nc.sbuf_tensor("pa_x", [128, 2, 8, 512], F32))
        s_k = ph.enter_context(nc.sbuf_tensor("pa_k", [128, 8, 512], F32))
        s_v = ph.enter_context(nc.sbuf_tensor("pa_v", [128, 2, 512], F32))
        p.dma("y", lambda eng: eng.dma_start(out=s_wk[:], in_=wqkv.rearrange("(kc p) n -> p kc n", p=128)[:, :, DM:2 * DM].rearrange("p kc (o f) -> p kc o f", f=128)), writes=("wk",))
        p.dma("y", lambda eng: eng.dma_start(out=s_wv[:], in_=wqkv.rearrange("(kc p) n -> p kc n", p=128)[:, :, 2 * DM:3 * DM].rearrange("p kc (o f) -> p kc o f", f=128)), writes=("wv",))
        p.dma("y", lambda eng: eng.dma_start(out=s_bk[:], in_=bk_r), writes=("bk",))
        p.dma("y", lambda eng: eng.dma_start(out=s_bv[:], in_=bv_bc), writes=("bv",))
        groups = [("own", g) for g in range(4)] + [("halo", 0)]
        for gi, (kind, g) in enumerate(groups):
            pb = gi % 2
            ntok = 512 if kind == "own" else 256
            if kind == "own":
                p.dma("y", lambda eng, g=g, pb=pb: eng.dma_start(out=s_xa[:, pb], in_=x_r[:, :, g * 512:(g + 1) * 512]),
                      writes=(f"x{pb}",))
            else:
                p.dma("y", lambda eng, pb=pb: eng.dma_start(out=s_xa[:, pb, :, 0:256], in_=hx_r[:]),
                      writes=(f"x{pb}",))
            for oc in range(8):
                bi = oc % 2
                pairs = [(s_wk[:, kc, oc, :], s_xa[:, pb, kc, 0:ntok]) for kc in range(8)]
                mm_accum(p, ps[bi][:, 0:ntok], pairs, f"pp{bi}", (f"x{pb}", "wk"))
                p.emit("s", lambda eng, oc=oc, bi=bi, ntok=ntok: eng.activation(
                           s_k[:, oc, 0:ntok], ps[bi][:, 0:ntok], ACT.Identity, bias=s_bk[:, oc:oc + 1], scale=1.0),
                       reads=(f"pp{bi}", "bk"), writes=("k",))
            if kind == "own":
                p.dma("y", lambda eng, g=g: eng.dma_start(out=kv[:, :, 128 + g * 512:128 + (g + 1) * 512], in_=s_k[:]),
                      reads=("k",))
            else:
                p.dma("y", lambda eng: eng.dma_start(out=kv[:, :, 0:128], in_=s_k[:, :, 0:128]), reads=("k",))
                p.dma("y", lambda eng: eng.dma_start(out=kv[:, :, KT - 128:KT], in_=s_k[:, :, 128:256]), reads=("k",))
            for tb in range(ntok // 128):
                for vc in range(2):
                    bi = 2 + vc
                    pairs = [(s_xa[:, pb, kc, tb * 128:(tb + 1) * 128], s_wv[:, kc, 4 * vc:4 * vc + 4, :].rearrange("p o f -> p (o f)")) for kc in range(8)]
                    mm_accum(p, ps[bi][:], pairs, f"pp{bi}", (f"x{pb}", "wv"))
                    p.emit("v", lambda eng, vc=vc, bi=bi: eng.tensor_tensor(
                               s_v[:, vc, :], ps[bi][:], s_bv[:, vc * 512:(vc + 1) * 512], op=ALU.add),
                           reads=(f"pp{bi}", "bv"), writes=("v",))
                if kind == "own":
                    p.dma("y", lambda eng, g=g, tb=tb: eng.dma_start(
                              out=vv_[:, 1 + g * 4 + tb, :], in_=s_v[:].rearrange("p a b -> p (a b)")),
                          reads=("v",))
                elif tb == 0:
                    p.dma("y", lambda eng: eng.dma_start(out=vv_[:, 0, :], in_=s_v[:].rearrange("p a b -> p (a b)")),
                          reads=("v",))
                else:
                    p.dma("y", lambda eng: eng.dma_start(out=vv_[:, 17, :], in_=s_v[:].rearrange("p a b -> p (a b)")),
                          reads=("v",))

    p.barrier()
    with ExitStack() as ph:
        s_wq = ph.enter_context(nc.sbuf_tensor("pb_wq", [128, 8, 8, 128], F32))
        s_bq = ph.enter_context(nc.sbuf_tensor("pb_bq", [128, 8], F32))
        s_x = ph.enter_context(nc.sbuf_tensor("pb_x", [128, 1, 8, 512], F32))
        s_q = ph.enter_context(nc.sbuf_tensor("pb_q", [128, 8, 512], F32))
        s_kk = ph.enter_context(nc.sbuf_tensor("pb_kk", [128, 1, 8, 768], F32))
        s_vv = ph.enter_context(nc.sbuf_tensor("pb_vv", [128, 1, 6, DM], F32))
        s_sc = ph.enter_context(nc.sbuf_tensor("pb_sc", [128, 384], F32))
        s_sm = ph.enter_context(nc.sbuf_tensor("pb_sm", [128, 4], F32))
        s_at = ph.enter_context(nc.sbuf_tensor("pb_at", [128, 3, 128], F32))
        s_ao = ph.enter_context(nc.sbuf_tensor("pb_ao", [128, 8, 512], F32))
        s_wo = ph.enter_context(nc.sbuf_tensor("pb_wo", [128, 2, 8, 128], F32))
        s_bo = ph.enter_context(nc.sbuf_tensor("pb_bo", [128, 8], F32))
        s_o = ph.enter_context(nc.sbuf_tensor("pb_o", [128, 8, 512], F32))
        s_sq = ph.enter_context(nc.sbuf_tensor("pb_sq", [128, 8, 512], F32))
        s_s3 = ph.enter_context(nc.sbuf_tensor("pb_s3", [128, 8, 512], F32))
        s_3t = ph.enter_context(nc.sbuf_tensor("pb_3t", [128, 8, 128], F32))
        s_mvb = ph.enter_context(nc.sbuf_tensor("pb_mv", [128, 2, 512], F32))
        p.dma("y", lambda eng: eng.dma_start(out=s_wq[:], in_=wqkv.rearrange("(kc p) n -> p kc n", p=128)[:, :, 0:DM].rearrange("p kc (o f) -> p kc o f", f=128)), writes=("wq",))
        p.dma("y", lambda eng: eng.dma_start(out=s_bq[:], in_=bq_r), writes=("bq",))
        p.dma("y", lambda eng: eng.dma_start(out=s_bo[:], in_=bo_r), writes=("bo",))
        for g in range(4):
            pb = 0
            p.dma("y", lambda eng, g=g, pb=pb: eng.dma_start(out=s_x[:, pb], in_=x_r[:, :, g * 512:(g + 1) * 512]),
                  writes=(f"x{pb}",))
            p.dma("y", lambda eng, g=g, pb=pb: eng.dma_start(out=s_kk[:, pb], in_=kv[:, :, g * 512:g * 512 + 768]),
                  writes=(f"kk{pb}",))
            p.dma("y", lambda eng, g=g, pb=pb: eng.dma_start(out=s_vv[:, pb], in_=vv_[:, g * 4:g * 4 + 6, :]),
                  writes=(f"vv{pb}",))
            for oc in range(8):
                bi = oc % 2
                pairs = [(s_wq[:, kc, oc, :], s_x[:, pb, kc, :]) for kc in range(8)]
                mm_accum(p, ps[bi][:], pairs, f"pp{bi}", (f"x{pb}", "wq"))
                p.emit("s", lambda eng, oc=oc, bi=bi: eng.activation(
                           s_q[:, oc, 0:512], ps[bi][:], ACT.Identity, bias=s_bq[:, oc:oc + 1], scale=1.0),
                       reads=(f"pp{bi}", "bq"), writes=("q",))
            for hd in range(NH):
                hc, hr = hd // 2, (hd % 2) * 64
                for qb in range(4):
                    gqb = g * 4 + qb
                    mi = 0 if gqb == 0 else (2 if gqb == NB - 1 else 1)
                    p.emit("t", lambda eng, hc=hc, hr=hr, qb=qb, pb=pb: eng.matmul(
                               ps[4][:, 0:384], s_q[hr:hr + 64, hc, qb * 128:(qb + 1) * 128],
                               s_kk[hr:hr + 64, pb, hc, qb * 128:qb * 128 + 384], start=True, stop=True),
                           reads=("q", f"kk{pb}"), writes=("pp4",))
                    p.emit("v", lambda eng, mi=mi: eng.scalar_tensor_tensor(
                               out=s_sc[:], in0=ps[4][:, 0:384], scalar=0.03125, in1=s_ms[:, mi],
                               op0=ALU.mult, op1=ALU.add),
                           reads=("pp4", "ms"), writes=("sc",))
                    p.emit("v", lambda eng: eng.tensor_reduce(s_sm[:, 0:1], s_sc[:], axis=AX, op=ALU.max),
                           reads=("sc",), writes=("sm",))
                    p.emit("v", lambda eng: eng.tensor_scalar(s_sm[:, 1:2], s_sm[:, 0:1], -1.0, None, op0=ALU.mult),
                           reads=("sm",), writes=("sm1",))
                    p.emit("s", lambda eng: eng.activation(s_sc[:], s_sc[:], ACT.Exp, bias=s_sm[:, 1:2], scale=1.0),
                           reads=("sc", "sm1"), writes=("sc",))
                    p.emit("v", lambda eng: eng.tensor_reduce(s_sm[:, 2:3], s_sc[:], axis=AX, op=ALU.add),
                           reads=("sc",), writes=("sm2",))
                    p.emit("v", lambda eng: eng.reciprocal(s_sm[:, 3:4], s_sm[:, 2:3]),
                           reads=("sm2",), writes=("sm3",))
                    p.emit("v", lambda eng: eng.tensor_scalar(s_sc[:], s_sc[:], s_sm[:, 3:4], None, op0=ALU.mult),
                           reads=("sc", "sm3"), writes=("sc",))
                    for j in range(3):
                        p.emit("t", lambda eng, j=j: eng.transpose(ps[5][:, j * 128:(j + 1) * 128], s_sc[:, j * 128:(j + 1) * 128], s_ident[:]),
                               reads=("sc", "ident"), writes=("pp5",))
                    p.emit("s", lambda eng: eng.copy(s_at[:].rearrange("p a b -> p (a b)"), ps[5][:, 0:384]),
                           reads=("pp5",), writes=("at",))
                    def av(eng, hc=hc, hr=hr, qb=qb, pb=pb):
                        ins = None
                        for j in range(3):
                            ins = eng.matmul(ps[6][hr:hr + 64, qb * 128:(qb + 1) * 128],
                                             s_vv[:, pb, qb + j, hc * 128 + hr:hc * 128 + hr + 64],
                                             s_at[:, j, :], start=(j == 0), stop=(j == 2))
                        return ins
                    p.emit("t", av, reads=("at", f"vv{pb}"), writes=("pp6",))
                    p.emit("s", lambda eng, hc=hc, hr=hr, qb=qb: eng.copy(
                               s_ao[hr:hr + 64, hc, qb * 128:(qb + 1) * 128], ps[6][hr:hr + 64, qb * 128:(qb + 1) * 128]),
                           reads=("pp6",), writes=("ao",))
            for oc in range(8):
                bi = oc % 2
                p.dma("y", lambda eng, oc=oc, bi=bi: eng.dma_start(
                          out=s_wo[:, bi], in_=wout1.rearrange("(kc p) n -> p kc n", p=128)[:, :, oc * 128:(oc + 1) * 128]),
                      writes=(f"wo{bi}",))
                pairs = [(s_wo[:, bi, kc, :], s_ao[:, kc, :]) for kc in range(8)]
                mm_accum(p, ps[bi][:], pairs, f"pp{bi}", ("ao", f"wo{bi}"))
                p.emit("s", lambda eng, oc=oc, bi=bi: eng.activation(
                           s_o[:, oc, :], ps[bi][:], ACT.Identity, bias=s_bo[:, oc:oc + 1], scale=1.0),
                       reads=(f"pp{bi}", "bo"), writes=("o",))
            p.emit("v", lambda eng, pb=pb: eng.tensor_tensor(s_o[:], s_o[:], s_x[:, pb], op=ALU.add),
                   reads=("o", f"x{pb}"), writes=("o",))
            emit_ln_fm(p, f"ln3_{g}", s_o, "o", s_s3, "s3", s_sq, "sq3", s_ones,
                       ps[4], "pp4", ps[5], "pp5", s_mvb, "lnmv3", 512)
            p.dma("y", lambda eng, g=g: eng.dma_start(
                      out=s3f_out.rearrange("(kc p) t -> p kc t", p=128)[:, :, g * 512:(g + 1) * 512], in_=s_s3[:]),
                  reads=("s3",))
            for tb in range(4):
                for half in range(2):
                    bk = ps[2] if half == 0 else ps[3]
                    bkk = "pp2" if half == 0 else "pp3"
                    for k4 in range(4):
                        kc = half * 4 + k4
                        p.emit("t", lambda eng, tb=tb, kc=kc, k4=k4, bk=bk: eng.transpose(
                                   bk[:, k4 * 128:(k4 + 1) * 128], s_s3[:, kc, tb * 128:(tb + 1) * 128], s_ident[:]),
                               reads=("s3", "ident"), writes=(bkk,))
                    p.emit("s", lambda eng, half=half, bk=bk: eng.copy(
                               s_3t[:, half * 4:(half + 1) * 4, :].rearrange("p a b -> p (a b)"), bk[:, 0:512]),
                           reads=(bkk,), writes=("t3",))
                p.dma("y", lambda eng, g=g, tb=tb: eng.dma_start(
                          out=s3t_out[(g * 4 + tb) * 128:(g * 4 + tb + 1) * 128, :], in_=s_3t[:].rearrange("p a b -> p (a b)")),
                      reads=("t3",))
    p.finish_outputs("y")
    with stack:
        return p.build()


def host_l4_inputs(ins, s2_by_core):
    Wqkv = ins["L1_attn_Wqkv"]; bqkv = ins["L1_attn_bqkv"]
    Wout1 = ins["L1_attn_Wout"]; bout1 = ins["L1_attn_bout"]
    W = 128
    i = np.arange(128)[:, None]; j = np.arange(384)[None, :]
    base = np.where((j >= i) & (j <= i + 256), 0.0, -1e30).astype(np.float32)
    m_first = np.where(j >= 128, base, -1e30).astype(np.float32)
    m_last = np.where(j < 256, base, -1e30).astype(np.float32)
    maps = []
    for c in range(NCORES):
        b, h = c // 2, c % 2
        peer = 2 * b + (1 - h)
        zeros = np.zeros((DM, W), np.float32)
        if h == 0:
            left = zeros
            right = s2_by_core[peer][:, 0:W]
            mask0_, mask15_ = m_first, base
        else:
            left = s2_by_core[peer][:, TOK - W:TOK]
            right = zeros
            mask0_, mask15_ = base, m_last
        maps.append({
            "s2_fm": np.ascontiguousarray(s2_by_core[c]),
            "halo_x": np.ascontiguousarray(np.concatenate([left, right], axis=1)),
            "wqkv": Wqkv,
            "bq_r": np.ascontiguousarray(bqkv[0:DM].reshape(8, 128).T),
            "bk_r": np.ascontiguousarray(bqkv[DM:2 * DM].reshape(8, 128).T),
            "bv_bc": np.ascontiguousarray(np.broadcast_to(bqkv[2 * DM:3 * DM], (128, DM))),
            "wout1": Wout1,
            "bo_r": np.ascontiguousarray(bout1.reshape(8, 128).T),
            "mask0": mask0_, "maskm": base, "mask15": mask15_,
        })
    return maps


def build_l5():
    nc = bass.Bass("TRN2", target_bir_lowering=False)
    stack = ExitStack()
    p = Prog(nc, stack)
    TT = 512

    s3_fm = nc.dram_tensor("s3_fm", [DM, TOK], F32, kind="ExternalInput").ap()
    wq = nc.dram_tensor("wq", [DM, 2048], F32, kind="ExternalInput").ap()
    q_out = nc.dram_tensor("q_out", [2048, TOK], F32, kind="ExternalOutput").ap()
    st_out = nc.dram_tensor("st_out", [128, 32], F32, kind="ExternalOutput").ap()

    s_w = stack.enter_context(nc.sbuf_tensor("s_w", [128, 8, 16, 128], F32))
    s_x = stack.enter_context(nc.sbuf_tensor("s_x", [128, 2, 8, TT], F32))
    s_q = stack.enter_context(nc.sbuf_tensor("s_q", [128, 16, TT], F32))
    s_sq = stack.enter_context(nc.sbuf_tensor("s_sq", [128, 16, TT], F32))
    s_st = stack.enter_context(nc.sbuf_tensor("s_st", [128, 16, 2, 4], F32))
    s_sto = stack.enter_context(nc.sbuf_tensor("s_sto", [128, 32], F32))
    ps = [stack.enter_context(nc.psum_tensor(f"ps{i}", [128, TT], F32)) for i in range(4)]

    with stack:
        p.dma("y", lambda eng: eng.dma_start(out=s_w[:], in_=wq.rearrange("(kc p) (oc f) -> p kc oc f", p=128, f=128)),
              writes=("w",))
        xv = s3_fm.rearrange("(kc p) t -> p kc t", p=128)
        qv = q_out.rearrange("(oc p) t -> p oc t", p=128)
        for tt in range(TOK // TT):
            pb = tt % 2
            p.dma("y", lambda eng, tt=tt, pb=pb: eng.dma_start(out=s_x[:, pb], in_=xv[:, :, tt * TT:(tt + 1) * TT]),
                  writes=(f"x{pb}",))
            for oc in range(16):
                bi = oc % 4
                pairs = [(s_w[:, kc, oc, :], s_x[:, pb, kc, :]) for kc in range(8)]
                mm_accum(p, ps[bi][:], pairs, f"pp{bi}", (f"x{pb}", "w"))
                p.emit("s", lambda eng, oc=oc, bi=bi: eng.copy(s_q[:, oc, :], ps[bi][:]),
                       reads=(f"pp{bi}",), writes=("q",))
            p.dma("y", lambda eng, tt=tt: eng.dma_start(out=qv[:, :, tt * TT:(tt + 1) * TT], in_=s_q[:]),
                  reads=("q",))
            p.emit("s", lambda eng: eng.activation(s_sq[:], s_q[:], ACT.Square),
                   reads=("q",), writes=("sq",))
            for oc in range(16):
                p.emit("v", lambda eng, oc=oc, tt=tt: eng.tensor_reduce(
                           s_st[:, oc, 0, tt:tt + 1], s_q[:, oc, :], axis=AX, op=ALU.add),
                       reads=("q",), writes=("st",))
                p.emit("v", lambda eng, oc=oc, tt=tt: eng.tensor_reduce(
                           s_st[:, oc, 1, tt:tt + 1], s_sq[:, oc, :], axis=AX, op=ALU.add),
                       reads=("sq",), writes=("st",))
        p.emit("v", lambda eng: eng.tensor_reduce(
                   s_sto[:].rearrange("p (a b) -> p a b", b=1), s_st[:].rearrange("p a b c -> p (a b) c"), axis=AX, op=ALU.add),
               reads=("st",), writes=("sto",))
        p.dma("y", lambda eng: eng.dma_start(out=st_out, in_=s_sto[:]), reads=("sto",))
        p.finish_outputs("y")
        return p.build()


def host_l5_inputs(ins, s3f_by_core):
    wq = ins["L1_ffn_w_q"]
    return [{"s3_fm": np.ascontiguousarray(s3f_by_core[c]), "wq": wq} for c in range(NCORES)]


def build_l6():
    nc = bass.Bass("TRN2", target_bir_lowering=False)
    stack = ExitStack()
    p = Prog(nc, stack)
    NTB = TOK // 128

    q_fm = nc.dram_tensor("q_fm", [2048, TOK], F32, kind="ExternalInput").ap()
    stats_all = nc.dram_tensor("stats_all", [NCORES, 128, 32], F32, kind="ExternalInput").ap()
    s3_tm = nc.dram_tensor("s3_tm", [TOK, DM], F32, kind="ExternalInput").ap()
    keys_t = nc.dram_tensor("keys_t", [128, 2, 8, 256], F32, kind="ExternalInput").ap()
    gamma_r = nc.dram_tensor("gamma_r", [128, 16], F32, kind="ExternalInput").ap()
    beta_r = nc.dram_tensor("beta_r", [128, 16], F32, kind="ExternalInput").ap()
    down_e = nc.dram_tensor("down_e", [65536, 1024], F32, kind="ExternalInput").ap()
    up_e = nc.dram_tensor("up_e", [65536, 1024], F32, kind="ExternalInput").ap()
    fin_out = nc.dram_tensor("fin_out", [TOK, DM], F32, kind="ExternalOutput").ap()

    s_keys = stack.enter_context(nc.sbuf_tensor("s_keys", [128, 2, 8, 256], F32))
    s_stats = stack.enter_context(nc.sbuf_tensor("s_stats", [128, 8, 32], F32))
    s_mr = stack.enter_context(nc.sbuf_tensor("s_mr", [128, 16, 2], F32))
    s_gam = stack.enter_context(nc.sbuf_tensor("s_gam", [128, 16], F32))
    s_bet = stack.enter_context(nc.sbuf_tensor("s_bet", [128, 16], F32))
    s_qt = stack.enter_context(nc.sbuf_tensor("s_qt", [128, 2, 16, 128], F32))
    s_x3 = stack.enter_context(nc.sbuf_tensor("s_x3", [128, 2, 1024], F32))
    s_sim = stack.enter_context(nc.sbuf_tensor("s_sim", [128, 256], F32))
    s_tv = stack.enter_context(nc.sbuf_tensor("s_tv", [128, 2, 8, 8], F32))
    s_tiu = stack.enter_context(nc.sbuf_tensor("s_tiu", [128, 2, 8, 8], U32))
    s_tif = stack.enter_context(nc.sbuf_tensor("s_tif", [128, 2, 8, 8], F32))
    s_as = stack.enter_context(nc.sbuf_tensor("s_as", [128, 8, 64], F32))
    s_ai = stack.enter_context(nc.sbuf_tensor("s_ai", [128, 8, 64], F32))
    s_sc8 = stack.enter_context(nc.sbuf_tensor("s_sc8", [128, 8, 8], F32))
    s_pku = stack.enter_context(nc.sbuf_tensor("s_pku", [128, 8, 8], U32))
    s_pkdm = stack.enter_context(nc.sbuf_tensor("s_pkdm", [128, 2, 8, 8], F32))
    s_msk = stack.enter_context(nc.sbuf_tensor("s_msk", [128, 8, 8], F32))
    s_ixg = stack.enter_context(nc.sbuf_tensor("s_ixg", [128, 2, 8, 8], F32))
    s_eid = stack.enter_context(nc.sbuf_tensor("s_eid", [128, 2, 8, 8], U32))
    s_wd = stack.enter_context(nc.sbuf_tensor("s_wd", [128, 6, 1024], F32))
    s_wu = stack.enter_context(nc.sbuf_tensor("s_wu", [128, 6, 1024], F32))
    s_scr = stack.enter_context(nc.sbuf_tensor("s_scr", [128, 6, 1024], F32))
    s_hm = stack.enter_context(nc.sbuf_tensor("s_hm", [128, 64], F32))
    s_sm = stack.enter_context(nc.sbuf_tensor("s_sm", [128, 2, 64], F32))
    s_smx = stack.enter_context(nc.sbuf_tensor("s_smx", [128, 8, 1], F32))
    s_acc = stack.enter_context(nc.sbuf_tensor("s_acc", [128, 4, 1024], F32))
    s_r4 = stack.enter_context(nc.sbuf_tensor("s_r4", [128, 1024], F32))
    s_ln = stack.enter_context(nc.sbuf_tensor("s_ln", [128, 4], F32))
    ps = [stack.enter_context(nc.psum_tensor(f"ps{i}", [128, 256], F32)) for i in range(2)]

    with stack:
        p.dma("y", lambda eng: eng.dma_start(out=s_keys[:], in_=keys_t), writes=("keys",))
        p.dma("y", lambda eng: eng.dma_start(out=s_stats[:], in_=stats_all.rearrange("c p n -> p c n")), writes=("stats",))
        p.dma("y", lambda eng: eng.dma_start(out=s_gam[:], in_=gamma_r), writes=("gb",))
        p.dma("y", lambda eng: eng.dma_start(out=s_bet[:], in_=beta_r), writes=("gb",))
        for c in range(1, NCORES):
            p.emit("v", lambda eng, c=c: eng.tensor_tensor(s_stats[:, 0, :], s_stats[:, 0, :], s_stats[:, c, :], op=ALU.add),
                   reads=("stats",), writes=("stats",))
        stv = s_stats[:, 0, :].rearrange("p (oc two) -> p oc two", two=2)
        inv_n = 1.0 / (B * S)
        p.emit("v", lambda eng: eng.tensor_scalar(s_mr[:, :, 0], stv[:, :, 0], inv_n, None, op0=ALU.mult),
               reads=("stats",), writes=("mr",))
        p.emit("v", lambda eng: eng.tensor_tensor(s_mr[:, :, 1], s_mr[:, :, 0], s_mr[:, :, 0], op=ALU.mult),
               reads=("mr",), writes=("mr2",))
        p.emit("v", lambda eng: eng.scalar_tensor_tensor(out=s_mr[:, :, 1], in0=stv[:, :, 1], scalar=inv_n,
                                                         in1=s_mr[:, :, 1], op0=ALU.mult, op1=ALU.subtract),
               reads=("stats", "mr2"), writes=("mr2",))
        p.emit("v", lambda eng: eng.tensor_scalar(s_mr[:, :, 1], s_mr[:, :, 1], 1e-5, None, op0=ALU.add),
               reads=("mr2",), writes=("mr2",))
        p.emit("s", lambda eng: eng.activation(s_mr[:, :, 1], s_mr[:, :, 1], ACT.Sqrt),
               reads=("mr2",), writes=("mr2",))
        p.emit("v", lambda eng: eng.reciprocal(s_mr[:, :, 1], s_mr[:, :, 1]),
               reads=("mr2",), writes=("mr2",))

        p.emit("v", lambda eng: eng.tensor_tensor(s_gam[:], s_gam[:], s_mr[:, :, 1], op=ALU.mult),
               reads=("gb", "mr2"), writes=("gb",))

        qv = q_fm.rearrange("(oc p) t -> p oc t", p=128)
        for tb in range(NTB):
            pb = tb % 2
            p.dma("y", lambda eng, tb=tb, pb=pb: eng.dma_start(out=s_qt[:, pb], in_=qv[:, :, tb * 128:(tb + 1) * 128]),
                  writes=(f"qt{pb}",))
            p.dma("y", lambda eng, tb=tb, pb=pb: eng.dma_start(out=s_x3[:, pb], in_=s3_tm[tb * 128:(tb + 1) * 128, :]),
                  writes=(f"x3{pb}",))
            p.emit("v", lambda eng, pb=pb: eng.tensor_tensor(
                       s_qt[:, pb], s_qt[:, pb], s_mr[:, :, 0:1].broadcast_to([128, 16, 128]), op=ALU.subtract),
                   reads=(f"qt{pb}", "mr"), writes=(f"qt{pb}",))
            p.emit("v", lambda eng, pb=pb: eng.tensor_tensor(
                       s_qt[:, pb], s_qt[:, pb], s_gam[:].unsqueeze(2).broadcast_to([128, 16, 128]), op=ALU.mult),
                   reads=(f"qt{pb}", "gb"), writes=(f"qt{pb}",))
            p.emit("v", lambda eng, pb=pb: eng.tensor_tensor(
                       s_qt[:, pb], s_qt[:, pb], s_bet[:].unsqueeze(2).broadcast_to([128, 16, 128]), op=ALU.add),
                   reads=(f"qt{pb}", "gb"), writes=(f"qt{pb}",))
            for p2 in range(2):
                for hd in range(8):
                    oc = p2 * 8 + hd
                    bi = oc % 2
                    p.emit("t", lambda eng, oc=oc, p2=p2, hd=hd, pb=pb, bi=bi: eng.matmul(
                               ps[bi][:], s_qt[:, pb, oc, :], s_keys[:, p2, hd, :], start=True, stop=True),
                           reads=(f"qt{pb}", "keys"), writes=(f"pp{bi}",))
                    p.emit("s", lambda eng, bi=bi: eng.copy(s_sim[:], ps[bi][:]),
                           reads=(f"pp{bi}",), writes=("sim",))
                    p.emit("v", lambda eng, p2=p2, hd=hd: eng.max(out=s_tv[:, p2, hd, :], in_=s_sim[:]),
                           reads=("sim",), writes=("tv",))
                    p.emit("v", lambda eng, p2=p2, hd=hd: eng.max_index(s_tiu[:, p2, hd, :], s_tv[:, p2, hd, :], s_sim[:]),
                           reads=("sim", "tv"), writes=("tiu",))
            p.emit("v", lambda eng: eng.tensor_copy(s_tif[:], s_tiu[:]), reads=("tiu",), writes=("tif",))
            for hd in range(8):
                p.emit("v", lambda eng, hd=hd: eng.tensor_tensor(
                           s_as[:, hd, :].rearrange("p (i j) -> p i j", i=8),
                           s_tv[:, 0, hd, :].unsqueeze(2).broadcast_to([128, 8, 8]),
                           s_tv[:, 1, hd, :].unsqueeze(1).broadcast_to([128, 8, 8]), op=ALU.add),
                       reads=("tv",), writes=("as",))
                p.emit("v", lambda eng, hd=hd: eng.scalar_tensor_tensor(
                           out=s_ai[:, hd, :].rearrange("p (i j) -> p i j", i=8),
                           in0=s_tif[:, 0, hd, :].unsqueeze(2).broadcast_to([128, 8, 8]), scalar=256.0,
                           in1=s_tif[:, 1, hd, :].unsqueeze(1).broadcast_to([128, 8, 8]), op0=ALU.mult, op1=ALU.add),
                       reads=("tif",), writes=("ai",))
            for hd in range(8):
                p.emit("v", lambda eng, hd=hd: eng.max(out=s_sc8[:, hd, :], in_=s_as[:, hd, :]),
                       reads=("as",), writes=("sc8",))
                p.emit("v", lambda eng, hd=hd: eng.max_index(s_pku[:, hd, :], s_sc8[:, hd, :], s_as[:, hd, :]),
                       reads=("as", "sc8"), writes=("pku",))
            p.emit("v", lambda eng, pb=pb: eng.tensor_scalar(s_eid[:, pb], s_pku[:], 3, None, op0=ALU.logical_shift_right),
                   reads=("pku",), writes=(f"eid{pb}",))
            p.emit("v", lambda eng, pb=pb: eng.tensor_copy(s_pkdm[:, 0], s_eid[:, pb]), reads=(f"eid{pb}",), writes=("pkdm",))
            p.emit("v", lambda eng, pb=pb: eng.tensor_scalar(s_eid[:, pb], s_pku[:], 7, None, op0=ALU.bitwise_and),
                   reads=("pku",), writes=(f"eid{pb}",))
            p.emit("v", lambda eng, pb=pb: eng.tensor_copy(s_pkdm[:, 1], s_eid[:, pb]), reads=(f"eid{pb}",), writes=("pkdm",))
            p.emit("v", lambda eng: eng.memset(s_ixg[:], 0.0), writes=("ixg",))
            for j in range(8):
                for si in range(2):
                    p.emit("v", lambda eng, j=j, si=si: eng.tensor_scalar(
                               s_msk[:], s_pkdm[:, si], float(j), None, op0=ALU.is_equal),
                           reads=("pkdm",), writes=("msk",))
                    p.emit("v", lambda eng, j=j, si=si: eng.tensor_tensor(
                               s_msk[:], s_msk[:], s_tif[:, si, :, j:j + 1].broadcast_to([128, 8, 8]), op=ALU.mult),
                           reads=("msk", "tif"), writes=("msk",))
                    p.emit("v", lambda eng, si=si: eng.tensor_tensor(
                               s_ixg[:, si], s_ixg[:, si], s_msk[:], op=ALU.add),
                           reads=("msk", "ixg"), writes=("ixg",))
            p.emit("v", lambda eng: eng.scalar_tensor_tensor(
                       out=s_msk[:], in0=s_ixg[:, 0], scalar=256.0, in1=s_ixg[:, 1], op0=ALU.mult, op1=ALU.add),
                   reads=("ixg",), writes=("msk",))
            p.emit("v", lambda eng, pb=pb: eng.tensor_copy(s_eid[:, pb], s_msk[:]), reads=("msk",), writes=(f"eid{pb}",))
            p.emit("v", lambda eng: eng.tensor_reduce(s_smx[:], s_sc8[:], axis=AX, op=ALU.max),
                   reads=("sc8",), writes=("smx",))
            p.emit("v", lambda eng, pb=pb: eng.tensor_tensor(
                       s_sm[:, pb].rearrange("p (h k) -> p h k", h=8), s_sc8[:], s_smx[:].broadcast_to([128, 8, 8]), op=ALU.subtract),
                   reads=("sc8", "smx"), writes=(f"sm{pb}",))
            p.emit("s", lambda eng, pb=pb: eng.activation(s_sm[:, pb], s_sm[:, pb], ACT.Exp),
                   reads=(f"sm{pb}",), writes=(f"sm{pb}",))
            p.emit("v", lambda eng, pb=pb: eng.tensor_reduce(s_smx[:], s_sm[:, pb].rearrange("p (h k) -> p h k", h=8), axis=AX, op=ALU.add),
                   reads=(f"sm{pb}",), writes=("smx",))
            p.emit("v", lambda eng: eng.reciprocal(s_smx[:], s_smx[:]),
                   reads=("smx",), writes=("smx",))
            p.emit("v", lambda eng, pb=pb: eng.tensor_tensor(
                       s_sm[:, pb].rearrange("p (h k) -> p h k", h=8), s_sm[:, pb].rearrange("p (h k) -> p h k", h=8),
                       s_smx[:].broadcast_to([128, 8, 8]), op=ALU.mult),
                   reads=(f"sm{pb}", "smx"), writes=(f"sm{pb}",))
            for hk in range(64):
                gb_ = hk % 6
                hd, kk = hk // 8, hk % 8
                p.dma("g", lambda eng, hd=hd, kk=kk, gb_=gb_, pb=pb: eng.indirect_dma_start(
                          out=s_wd[:, gb_, :], out_offset=None, in_=down_e,
                          in_offset=bass.IndirectOffsetOnAxis(ap=s_eid[:, pb, hd, kk:kk + 1], axis=0)),
                      reads=(f"eid{pb}",), writes=(f"wd{gb_}",))
                p.emit("v", lambda eng, gb_=gb_, pb=pb: eng.tensor_tensor(
                           s_scr[:, gb_, :], s_wd[:, gb_, :], s_x3[:, pb, :], op=ALU.mult),
                       reads=(f"wd{gb_}", f"x3{pb}"), writes=(f"scr{gb_}",))
                p.emit("s", lambda eng, hk=hk, gb_=gb_: eng.activation(
                           s_scr[:, gb_, :], s_scr[:, gb_, :], ACT.Identity, accum_out=s_hm[:, hk:hk + 1]),
                       reads=(f"scr{gb_}",), writes=(f"scr{gb_}", "hm"))
            p.emit("s", lambda eng: eng.activation(s_hm[:], s_hm[:], ACT.Gelu_apprx_tanh),
                   reads=("hm",), writes=("hm",))
            p.emit("v", lambda eng, pb=pb: eng.tensor_tensor(s_sm[:, pb], s_sm[:, pb], s_hm[:], op=ALU.mult),
                   reads=(f"sm{pb}", "hm"), writes=(f"sm{pb}",))
            p.emit("v", lambda eng: eng.memset(s_acc[:, 0, :], 0.0), writes=("acc0_0",))
            p.emit("v", lambda eng: eng.memset(s_acc[:, 2, :], 0.0), writes=("acc1_0",))
            for hk in range(64):
                gb_ = hk % 6
                hd, kk = hk // 8, hk % 8
                p.dma("g", lambda eng, hd=hd, kk=kk, gb_=gb_, pb=pb: eng.indirect_dma_start(
                          out=s_wu[:, gb_, :], out_offset=None, in_=up_e,
                          in_offset=bass.IndirectOffsetOnAxis(ap=s_eid[:, pb, hd, kk:kk + 1], axis=0)),
                      reads=(f"eid{pb}",), writes=(f"wu{gb_}",))
                ch, st = hk % 2, hk // 2
                src_j, dst_j = st % 2, (st + 1) % 2
                src_k, dst_k = f"acc{ch}_{src_j}", f"acc{ch}_{dst_j}"
                p.emit("v", lambda eng, hk=hk, gb_=gb_, ch=ch, src_j=src_j, dst_j=dst_j, pb=pb: eng.scalar_tensor_tensor(
                           out=s_acc[:, ch * 2 + dst_j, :], in0=s_wu[:, gb_, :], scalar=s_sm[:, pb, hk:hk + 1],
                           in1=s_acc[:, ch * 2 + src_j, :], op0=ALU.mult, op1=ALU.add),
                       reads=(f"wu{gb_}", f"sm{pb}", src_k), writes=(dst_k,))
            p.emit("v", lambda eng, pb=pb: eng.tensor_tensor(s_r4[:], s_x3[:, pb, :], s_acc[:, 0, :], op=ALU.add),
                   reads=(f"x3{pb}", "acc0_0"), writes=("r4",))
            p.emit("v", lambda eng: eng.tensor_tensor(s_r4[:], s_r4[:], s_acc[:, 2, :], op=ALU.add),
                   reads=("r4", "acc1_0"), writes=("r4",))
            p.emit("v", lambda eng: eng.tensor_reduce(s_ln[:, 0:1], s_r4[:], axis=AX, op=ALU.add),
                   reads=("r4",), writes=("ln",))
            p.emit("v", lambda eng: eng.tensor_scalar(s_ln[:, 0:1], s_ln[:, 0:1], 1.0 / 1024.0, None, op0=ALU.mult),
                   reads=("ln",), writes=("ln",))
            p.emit("v", lambda eng: eng.tensor_scalar(s_r4[:], s_r4[:], s_ln[:, 0:1], None, op0=ALU.subtract),
                   reads=("r4", "ln"), writes=("r4",))
            p.emit("s", lambda eng: eng.activation(s_acc[:, 1, :], s_r4[:], ACT.Square),
                   reads=("r4",), writes=("acc0_1",))
            p.emit("v", lambda eng: eng.tensor_reduce(s_ln[:, 1:2], s_acc[:, 1, :], axis=AX, op=ALU.add),
                   reads=("acc0_1",), writes=("ln2",))
            p.emit("v", lambda eng: eng.tensor_scalar(s_ln[:, 1:2], s_ln[:, 1:2], 1.0 / 1024.0, 1e-5, op0=ALU.mult, op1=ALU.add),
                   reads=("ln2",), writes=("ln2",))
            p.emit("s", lambda eng: eng.activation(s_ln[:, 1:2], s_ln[:, 1:2], ACT.Sqrt),
                   reads=("ln2",), writes=("ln2",))
            p.emit("v", lambda eng: eng.reciprocal(s_ln[:, 1:2], s_ln[:, 1:2]),
                   reads=("ln2",), writes=("ln2",))
            p.emit("v", lambda eng: eng.tensor_scalar(s_r4[:], s_r4[:], s_ln[:, 1:2], None, op0=ALU.mult),
                   reads=("r4", "ln2"), writes=("r4",))
            p.dma("y", lambda eng, tb=tb: eng.dma_start(out=fin_out[tb * 128:(tb + 1) * 128, :], in_=s_r4[:]),
                  reads=("r4",))
        p.finish_outputs("y")
        return p.build()


def host_l6_inputs(ins, q_by_core, stats_by_core, s3t_by_core):
    keys = ins["L1_ffn_keys"]
    keys_t = np.ascontiguousarray(keys.transpose(3, 2, 0, 1))
    stats_all = np.ascontiguousarray(np.stack([stats_by_core[c] for c in range(NCORES)], axis=0))
    gam = ins["L1_ffn_bn_gamma"]; bet = ins["L1_ffn_bn_beta"]
    maps = []
    for c in range(NCORES):
        maps.append({
            "q_fm": np.ascontiguousarray(q_by_core[c]),
            "stats_all": stats_all,
            "s3_tm": np.ascontiguousarray(s3t_by_core[c]),
            "keys_t": keys_t,
            "gamma_r": np.ascontiguousarray(gam.reshape(16, 128).T),
            "beta_r": np.ascontiguousarray(bet.reshape(16, 128).T),
            "down_e": ins["L1_ffn_down_embed"],
            "up_e": ins["L1_ffn_up_embed"],
        })
    return maps


def kernel(**ins):
    core_ids = list(range(NCORES))

    r1 = run_bass_kernel_spmd(build_l1(), host_l1_inputs(ins), core_ids=core_ids)
    xs_by_core = {c: np.asarray(r1.results[c]["xs_out"]) for c in range(NCORES)}
    z_by_core = {c: np.asarray(r1.results[c]["z_out"]) for c in range(NCORES)}

    r2 = run_bass_kernel_spmd(build_l2(), host_l2_inputs(ins, xs_by_core, z_by_core), core_ids=core_ids)
    y_by_core = {c: np.asarray(r2.results[c]["y_out"]) for c in range(NCORES)}

    r3 = run_bass_kernel_spmd(build_l3(), host_l3_inputs(ins, y_by_core), core_ids=core_ids)
    s2_by_core = {c: np.asarray(r3.results[c]["s2_out"]) for c in range(NCORES)}

    r4 = run_bass_kernel_spmd(build_l4(), host_l4_inputs(ins, s2_by_core), core_ids=core_ids)
    s3f_by_core = {c: np.asarray(r4.results[c]["s3f_out"]) for c in range(NCORES)}
    s3t_by_core = {c: np.asarray(r4.results[c]["s3t_out"]) for c in range(NCORES)}

    r5 = run_bass_kernel_spmd(build_l5(), host_l5_inputs(ins, s3f_by_core), core_ids=core_ids)
    q_by_core = {c: np.asarray(r5.results[c]["q_out"]) for c in range(NCORES)}
    stats_by_core = {c: np.asarray(r5.results[c]["st_out"]) for c in range(NCORES)}

    r6 = run_bass_kernel_spmd(build_l6(), host_l6_inputs(ins, q_by_core, stats_by_core, s3t_by_core), core_ids=core_ids)
    out = np.empty((B, S, DM), np.float32)
    for c in range(NCORES):
        b, h = c // 2, c % 2
        out[b, h * TOK:(h + 1) * TOK, :] = np.asarray(r6.results[c]["fin_out"])
    return out


from concourse.libnrt import physical_to_logical_core_id

MAGIC = 0x51AB0000
NSLOT = 8


def _pnc_logical_pairs():
    out = []
    for pnc in range(8):
        out.append((pnc, physical_to_logical_core_id(pnc, 0, container_num_cores=8) % 8))
    return out


class PairXch:

    def __init__(self, p, nc, stack):
        self.p = p
        self.nc = nc
        self.shf = nc.dram_tensor("sh_flags", [2, NSLOT], mybir.dt.int32,
                                  kind="Internal", addr_space="Shared").ap()
        self.shdone = nc.dram_tensor("sh_done", [2, 4], mybir.dt.int32,
                                     kind="Internal", addr_space="Shared").ap()
        self.s_magic = stack.enter_context(nc.sbuf_tensor("x_magic", [1, NSLOT], mybir.dt.int32))
        self.s_zero = stack.enter_context(nc.sbuf_tensor("x_zero", [1, NSLOT], mybir.dt.int32))
        self.s_fb = stack.enter_context(nc.sbuf_tensor("x_fb", [1, NSLOT], mybir.dt.int32))
        self.s_v0 = stack.enter_context(nc.sbuf_tensor("x_v0", [1, 4], mybir.dt.int32))
        self.sem = stack.enter_context(nc.semaphore("x_sem"))
        self.psem = stack.enter_context(nc.semaphore("x_poll"))
        self.cnt = 0
        self.uid = 0

    def prologue(self):
        p, nc = self.p, self.nc
        p.emit("g", lambda eng: eng.iota(self.s_magic[:], pattern=[[1, NSLOT]], base=MAGIC,
                                         channel_multiplier=0, allow_small_or_imprecise_dtypes=True),
               writes=("x_magic",))
        p.emit("v", lambda eng: eng.memset(self.s_zero[:], 0), writes=("x_zero",))
        p.emit("v", lambda eng: eng.memset(self.s_fb[:], 0), writes=("x_fb",))

        def fn(eng):
            self.cnt += 16
            tgt = self.cnt
            for arm in nc.switch_core_id([eng], n=128):
                par = arm.logical % 2
                eng.dma_start(out=self.s_v0[0:1, 0:1],
                              in_=self.shdone[par:par + 1, 0:1]).then_inc(self.sem, 16)
            eng.wait_ge(self.sem, tgt)
            self.cnt += 16
            tgt = self.cnt
            for arm in nc.switch_core_id([eng], n=128):
                par = arm.logical % 2
                eng.dma_start(out=self.s_v0[0:1, 2:3],
                              in_=self.shdone[1 - par:2 - par, 0:1]).then_inc(self.sem, 16)
            eng.wait_ge(self.sem, tgt)
        p.raw("g", fn)
        p.emit("g", lambda eng: eng.engine_nop(), writes=("x_v0",))

    def send(self, slot, data_pairs, reads):
        p, nc = self.p, self.nc
        p.wait_keys("g", list(reads) + ["x_magic"])
        ndma = len(data_pairs)

        def fn(eng, slot=slot, data_pairs=data_pairs, ndma=ndma):
            self.cnt += 16 * ndma
            tgt = self.cnt
            for arm in nc.switch_core_id([eng], n=128):
                par = arm.logical % 2
                for dst_fn, src_ap in data_pairs:
                    src = src_ap(par) if callable(src_ap) else src_ap
                    eng.dma_start(out=dst_fn(par), in_=src).then_inc(self.sem, 16)
            eng.wait_ge(self.sem, tgt)
            self.cnt += 16
            tgt = self.cnt
            for arm in nc.switch_core_id([eng], n=128):
                par = arm.logical % 2
                eng.dma_start(out=self.shf[par:par + 1, slot:slot + 1],
                              in_=self.s_magic[0:1, slot:slot + 1]).then_inc(self.sem, 16)
            eng.wait_ge(self.sem, tgt)
        p.raw("g", fn)
        p.emit("g", lambda eng: eng.engine_nop(), writes=(f"x_sent{slot}",))

    def poll(self, slot):
        p, nc = self.p, self.nc
        magic = MAGIC + slot
        p.wait_keys("g", ["x_fb"])

        def fn(eng, slot=slot, magic=magic):
            self.uid += 1
            u = self.uid
            for arm in nc.switch_core_id([eng], n=128):
                k = int(arm)
                par = arm.logical % 2
                with eng.register(f"xrf{u}_{k}") as rf, eng.register(f"xrt{u}_{k}") as rt:
                    eng.reg_mov(rf, 0)
                    eng.reg_mov(rt, 0)
                    eng.br(f"xpo{u}_{k}")
                    with nc.bb(f"xpo{u}_{k}", valid_engines=[mybir.EngineType.Pool]):
                        eng.reg_add(rt, rt, 16)
                        eng.dma_start(out=self.s_fb[0:1, slot:slot + 1],
                                      in_=self.shf[1 - par:2 - par, slot:slot + 1]).then_inc(self.psem, 16)
                        eng.wait_ge(self.psem, rt)
                        eng.load(rf, self.s_fb[0:1, slot:slot + 1])
                        eng.br_eq(rf, magic, f"xgo{u}_{k}", f"xpo{u}_{k}")
                    nc.enter_bb(f"xgo{u}_{k}", parent=nc.cur_bb, valid_engines=[mybir.EngineType.Pool])
                    eng.engine_nop()
        p.raw("g", fn)
        p.emit("g", lambda eng: eng.engine_nop(), writes=(f"x_recv{slot}",))

    def recv_dma(self, fn_of_par, out_ap_fn, slot, writes, nd=1):
        p, nc = self.p, self.nc
        p.wait_keys("g", [f"x_recv{slot}"] + list(writes))

        def fn(eng):
            self.cnt += 16 * nd
            tgt = self.cnt
            for arm in nc.switch_core_id([eng], n=128):
                par = arm.logical % 2
                ins = fn_of_par(par)
                outs = out_ap_fn(par)
                if not isinstance(ins, (list, tuple)):
                    ins, outs = [ins], [outs]
                for i_ap, o_ap in zip(ins, outs):
                    eng.dma_start(out=o_ap, in_=i_ap).then_inc(self.sem, 16)
            eng.wait_ge(self.sem, tgt)
        p.raw("g", fn)
        p.emit("g", lambda eng: eng.engine_nop(), writes=tuple(writes))

    def epilogue(self, after_keys):
        p, nc = self.p, self.nc
        p.wait_keys("g", list(after_keys) + ["x_v0", "x_zero"])

        def fn(eng):
            self.uid += 1
            u = self.uid
            self.cnt += 16
            t1 = self.cnt
            self.cnt += 16
            t2 = self.cnt
            for arm in nc.switch_core_id([eng], n=128):
                k = int(arm)
                par = arm.logical % 2
                with eng.register(f"xdn{u}_{k}") as rd:
                    eng.load(rd, self.s_v0[0:1, 0:1])
                    eng.reg_add(rd, rd, 1)
                    eng.store(self.s_v0[0:1, 1:2], rd)
                eng.dma_start(out=self.shdone[par:par + 1, 0:1],
                              in_=self.s_v0[0:1, 1:2]).then_inc(self.sem, 16)
                eng.wait_ge(self.sem, t1)
                with eng.register(f"xdf{u}_{k}") as rf, eng.register(f"xdv{u}_{k}") as rv, \
                        eng.register(f"xdt{u}_{k}") as rt:
                    eng.load(rv, self.s_v0[0:1, 2:3])
                    eng.reg_mov(rt, 0)
                    eng.br(f"xdp{u}_{k}")
                    with nc.bb(f"xdp{u}_{k}", valid_engines=[mybir.EngineType.Pool]):
                        eng.reg_add(rt, rt, 16)
                        eng.dma_start(out=self.s_fb[0:1, 0:1],
                                      in_=self.shdone[1 - par:2 - par, 0:1]).then_inc(self.psem, 16)
                        eng.wait_ge(self.psem, rt)
                        eng.load(rf, self.s_fb[0:1, 0:1])
                        eng.br_eq(rf, rv, f"xdp{u}_{k}", f"xdz{u}_{k}")
                    nc.enter_bb(f"xdz{u}_{k}", parent=nc.cur_bb, valid_engines=[mybir.EngineType.Pool])
                eng.dma_start(out=self.shf[par:par + 1, :], in_=self.s_zero[0:1, :]).then_inc(self.sem, 16)
                eng.wait_ge(self.sem, t2)
        p.raw("g", fn)
        p.emit("g", lambda eng: eng.engine_nop(), writes=("x_done",))


def build_fusedA():
    nc = bass.Bass("TRN2", target_bir_lowering=False, detect_race_conditions=False)
    stack = ExitStack()
    p = Prog(nc, stack)
    xch = PairXch(p, nc, stack)

    x_fm = nc.dram_tensor("x_fm", [DM, S], F32, kind="ExternalInput").ap()
    x_res = nc.dram_tensor("x_res", [DM, TOK], F32, kind="ExternalInput").ap()
    winx = nc.dram_tensor("winx", [DM, DH], F32, kind="ExternalInput").ap()
    winz = nc.dram_tensor("winz", [DM, DH], F32, kind="ExternalInput").ap()
    binx = nc.dram_tensor("binx", [128, DH // 128], F32, kind="ExternalInput").ap()
    binz = nc.dram_tensor("binz", [128, DH // 128], F32, kind="ExternalInput").ap()
    wx = nc.dram_tensor("wx", [DH, 96], F32, kind="ExternalInput").ap()
    bx_fm = nc.dram_tensor("bx_fm", [96, 1], F32, kind="ExternalInput").ap()
    bx_tm = nc.dram_tensor("bx_tm", [128, 96], F32, kind="ExternalInput").ap()
    wdt = nc.dram_tensor("wdt", [64, DH], F32, kind="ExternalInput").ap()
    bdt = nc.dram_tensor("bdt", [128, 8], F32, kind="ExternalInput").ap()
    alog = nc.dram_tensor("alog", [128, 8, NST], F32, kind="ExternalInput").ap()
    dp_r = nc.dram_tensor("dp_r", [128, 8], F32, kind="ExternalInput").ap()
    wout = nc.dram_tensor("wout", [DH, DM], F32, kind="ExternalInput").ap()
    boutr = nc.dram_tensor("boutr", [128, 8], F32, kind="ExternalInput").ap()
    wgT = nc.dram_tensor("wgT", [DM, 4096], F32, kind="ExternalInput").ap()
    wuT = nc.dram_tensor("wuT", [DM, 4096], F32, kind="ExternalInput").ap()
    wdT = nc.dram_tensor("wdT", [4096, DM], F32, kind="ExternalInput").ap()
    wqkv = nc.dram_tensor("wqkv", [DM, 3 * DM], F32, kind="ExternalInput").ap()
    bq_r = nc.dram_tensor("bq_r", [128, 8], F32, kind="ExternalInput").ap()
    bk_r = nc.dram_tensor("bk_r", [128, 8], F32, kind="ExternalInput").ap()
    bv_bc = nc.dram_tensor("bv_bc", [128, DM], F32, kind="ExternalInput").ap()
    wout1 = nc.dram_tensor("wout1", [DM, DM], F32, kind="ExternalInput").ap()
    bo_r = nc.dram_tensor("bo_r", [128, 8], F32, kind="ExternalInput").ap()
    mask0 = nc.dram_tensor("mask0", [128, 384], F32, kind="ExternalInput").ap()
    maskm = nc.dram_tensor("maskm", [128, 384], F32, kind="ExternalInput").ap()
    mask15 = nc.dram_tensor("mask15", [128, 384], F32, kind="ExternalInput").ap()
    zeros_col = nc.dram_tensor("zeros_col", [DM, 128], F32, kind="ExternalInput").ap()
    wq = nc.dram_tensor("wq", [DM, 2048], F32, kind="ExternalInput").ap()

    q_out = nc.dram_tensor("q_out", [2048, TOK], F32, kind="ExternalOutput").ap()
    st_out = nc.dram_tensor("st_out", [128, 32], F32, kind="ExternalOutput").ap()
    s3t_out = nc.dram_tensor("s3t_out", [TOK, DM], F32, kind="ExternalOutput").ap()

    xs_hbm = nc.dram_tensor("xs_hbm", [DH, S], F32, kind="Internal").ap()
    z_hbm = nc.dram_tensor("z_hbm", [DH, S], F32, kind="Internal").ap()
    p_hbm = nc.dram_tensor("p_hbm", [128, 8, S], F32, kind="Internal").ap()
    s1_hbm = nc.dram_tensor("s1_hbm", [DM, TOK], F32, kind="Internal").ap()
    gu_hbm = nc.dram_tensor("gu_hbm", [4096, TOK], F32, kind="Internal").ap()
    wdq_hbm = nc.dram_tensor("wdq_hbm", [4096, DM], BF16, kind="Internal").ap()
    s2_hbm = nc.dram_tensor("s2_hbm", [DM, TOK], F32, kind="Internal").ap()
    halo_hbm = nc.dram_tensor("halo_hbm", [DM, 256], F32, kind="Internal").ap()
    k_hbm = nc.dram_tensor("k_hbm", [DM, TOK + 256], F32, kind="Internal").ap()
    v_hbm = nc.dram_tensor("v_hbm", [TOK + 256, DM], F32, kind="Internal").ap()
    s3f_hbm = nc.dram_tensor("s3f_hbm", [DM, TOK], F32, kind="Internal").ap()

    sh_xpf = nc.dram_tensor("sh_xpf", [2, 96, S], F32, kind="Internal", addr_space="Shared").ap()
    sh_xpt = nc.dram_tensor("sh_xpt", [2, 128, 32, 96], F32, kind="Internal", addr_space="Shared").ap()
    sh_P = nc.dram_tensor("sh_P", [2, 128, 8, S], F32, kind="Internal", addr_space="Shared").ap()
    sh_halo = nc.dram_tensor("sh_halo", [2, DM, 128], F32, kind="Internal", addr_space="Shared").ap()

    s_ones = stack.enter_context(nc.sbuf_tensor("s_ones", [128, 128], F32))
    s_one1 = stack.enter_context(nc.sbuf_tensor("s_one1", [64, 128], F32))
    s_iota = stack.enter_context(nc.sbuf_tensor("s_iota", [128, 128], F32))
    s_ident = stack.enter_context(nc.sbuf_tensor("s_ident", [128, 128], F32))
    s_row = stack.enter_context(nc.sbuf_tensor("s_row", [96, 128], F32))
    s_cmg = stack.enter_context(nc.sbuf_tensor("s_cmg", [128, 2], F32))
    s_wsc = stack.enter_context(nc.sbuf_tensor("s_wsc", [128, 8], F32))
    ps_pool = [stack.enter_context(nc.psum_tensor(f"pp{i}", [128, 512], F32)) for i in range(8)]
    ps = ps_pool

    p.emit("v", lambda eng: eng.memset(s_ones[:], 1.0), writes=("ones",))
    p.emit("v", lambda eng: eng.memset(s_one1[:], 1.0), writes=("one1",))
    make_identity(p, s_iota, s_ident)
    xch.prologue()

    with ExitStack() as ph:
        f0_wt = ph.enter_context(nc.sbuf_tensor("f0_wt", [128, 2, 4096], F32))
        f0_sum = ph.enter_context(nc.sbuf_tensor("f0_sum", [128, 24], F32))
        wviews = [wgT.rearrange("(c p) n -> p c n", p=128),
                  wuT.rearrange("(c p) n -> p c n", p=128),
                  wdT.rearrange("(c p) n -> p c n", p=128)]
        for wi, wv in enumerate(wviews):
            cper = wv.shape[1] // 8
            for j in range(8):
                pb = j % 2
                p.dma("y", lambda eng, wv=wv, j=j, pb=pb, cper=cper: eng.dma_start(
                          out=f0_wt[:, pb, :].rearrange("p (c n) -> p c n", c=cper),
                          in_=wv[:, j * cper:(j + 1) * cper, :]),
                      writes=(f"f0w{pb}",))
                p.emit("v", lambda eng, wi=wi, j=j, pb=pb: eng.tensor_reduce(
                           f0_sum[:, wi * 8 + j:wi * 8 + j + 1], f0_wt[:, pb, :], axis=AX,
                           op=ALU.add, apply_absolute_value=True),
                       reads=(f"f0w{pb}",), writes=(f"f0s{wi}_{j}",))
        for wi in range(3):
            p.emit("v", lambda eng, wi=wi: eng.tensor_reduce(
                       s_wsc[:, wi:wi + 1], f0_sum[:, wi * 8:(wi + 1) * 8], axis=AX, op=ALU.add),
                   reads=tuple(f"f0s{wi}_{j}" for j in range(8)), writes=(f"f0t{wi}",))
        p.emit("t", lambda eng: eng.matmul(ps[6][:, 0:3], s_ones[:], s_wsc[:, 0:3], start=True, stop=True),
               reads=("f0t0", "f0t1", "f0t2", "ones"), writes=("pp6",))
        p.emit("v", lambda eng: eng.tensor_copy(s_wsc[:, 0:3], ps[6][:, 0:3]),
               reads=("pp6",), writes=("wsc",))
        p.emit("v", lambda eng: eng.tensor_scalar(s_wsc[:, 0:3], s_wsc[:, 0:3], 1.0 / (4096.0 * 1024.0), 1e-5,
                                                  op0=ALU.mult, op1=ALU.max),
               reads=("wsc",), writes=("wsc",))
        p.emit("v", lambda eng: eng.reciprocal(s_wsc[:, 3:6], s_wsc[:, 0:3]),
               reads=("wsc",), writes=("wsc2",))
    p.barrier()
    with ExitStack() as ph:
        f0b_wt = ph.enter_context(nc.sbuf_tensor("f0b_wt", [128, 2, 2048], F32))
        f0b_wq = ph.enter_context(nc.sbuf_tensor("f0b_wq", [128, 2, 2048], F32))
        f0b_wb = ph.enter_context(nc.sbuf_tensor("f0b_wb", [128, 2, 2048], BF16))
        wv = wdT.rearrange("(c p) n -> p c n", p=128)
        qv = wdq_hbm.rearrange("(c p) n -> p c n", p=128)
        for j in range(16):
            pb = j % 2
            p.dma("y", lambda eng, j=j, pb=pb: eng.dma_start(
                      out=f0b_wt[:, pb, :].rearrange("p (c n) -> p c n", c=2), in_=wv[:, 2 * j:2 * j + 2, :]),
                  writes=(f"f0bw{pb}",))
            p.emit("v", lambda eng, pb=pb: eng.tensor_scalar(f0b_wq[:, pb, :], f0b_wt[:, pb, :], s_wsc[:, 5:6], None, op0=ALU.mult),
                   reads=(f"f0bw{pb}", "wsc2"), writes=(f"f0bq{pb}",))
            p.emit("v", lambda eng, pb=pb: eng.tensor_scalar(f0b_wt[:, pb, :], f0b_wq[:, pb, :], -0.5, None, op0=ALU.is_lt),
                   reads=(f"f0bq{pb}",), writes=(f"f0bw{pb}",))
            p.emit("v", lambda eng, pb=pb: eng.scalar_tensor_tensor(out=f0b_wb[:, pb, :], in0=f0b_wq[:, pb, :], scalar=0.5,
                                                                    in1=f0b_wt[:, pb, :], op0=ALU.is_gt, op1=ALU.subtract),
                   reads=(f"f0bq{pb}", f"f0bw{pb}"), writes=(f"f0bb{pb}",))
            p.dma("y", lambda eng, j=j, pb=pb: eng.dma_start(
                      out=qv[:, 2 * j:2 * j + 2, :], in_=f0b_wb[:, pb, :].rearrange("p (c n) -> p c n", c=2)),
                  reads=(f"f0bb{pb}",))
    p.barrier()

    xp_stack = ExitStack()
    s_xpf = xp_stack.enter_context(nc.sbuf_tensor("s_xpf", [96, S], F32))
    s_xpt = xp_stack.enter_context(nc.sbuf_tensor("s_xpt", [128, 32, 96], F32))
    with ExitStack() as ph:
        TT = 512
        f1_w = ph.enter_context(nc.sbuf_tensor("f1_w", [128, 2, 8, 8, 128], F32))
        f1_wx = ph.enter_context(nc.sbuf_tensor("f1_wx", [128, 8, 96], F32))
        f1_x = ph.enter_context(nc.sbuf_tensor("f1_x", [128, 2, 8, TT], F32))
        f1_b = ph.enter_context(nc.sbuf_tensor("f1_b", [128, 2, 8], F32))
        f1_o = ph.enter_context(nc.sbuf_tensor("f1_o", [128, 2, 2, 8, TT], F32))
        p.dma("y", lambda eng: eng.dma_start(out=f1_w[:, 0], in_=winx.rearrange("(kc p) (nc8 f) -> p kc nc8 f", p=128, f=128)),
              writes=("wxx",))
        p.dma("y", lambda eng: eng.dma_start(out=f1_w[:, 1], in_=winz.rearrange("(kc p) (nc8 f) -> p kc nc8 f", p=128, f=128)),
              writes=("wzz",))
        p.dma("y", lambda eng: eng.dma_start(out=f1_b[:, 0], in_=binx), writes=("bxx",))
        p.dma("y", lambda eng: eng.dma_start(out=f1_b[:, 1], in_=binz), writes=("bzz",))
        p.dma("y", lambda eng: eng.dma_start(out=f1_wx[:], in_=wx.rearrange("(kc p) m -> p kc m", p=128)), writes=("wxp",))
        x_r = x_fm.rearrange("(kc p) t -> p kc t", p=128)
        xs_w = xs_hbm.rearrange("(nc8 p) t -> p nc8 t", p=128)
        z_w = z_hbm.rearrange("(nc8 p) t -> p nc8 t", p=128)
        nb = 0
        for tb in range(S // TT):
            pb = tb % 2
            p.dma("y", lambda eng, tb=tb, pb=pb: eng.dma_start(
                      out=f1_x[:, pb], in_=x_r[:, :, tb * TT:(tb + 1) * TT]),
                  writes=(f"x{pb}",))
            for xz in range(2):
                for nc8 in range(8):
                    bi = nb % 4
                    nb += 1
                    pairs = [(f1_w[:, xz, kc, nc8, :], f1_x[:, pb, kc, :]) for kc in range(8)]
                    mm_accum(p, ps[bi][:], pairs, f"pp{bi}", (f"x{pb}", "wxx" if xz == 0 else "wzz"))
                    func = ACT.Silu if xz == 0 else ACT.Identity
                    def ev(eng, bi=bi, xz=xz, nc8=nc8, func=func, pb=pb):
                        return eng.activation(f1_o[:, pb, xz, nc8, :], ps[bi][:], func,
                                              bias=f1_b[:, xz, nc8:nc8 + 1], scale=1.0)
                    p.emit("s", ev, reads=(f"pp{bi}", "bxx" if xz == 0 else "bzz"),
                           writes=(f"o{pb}_{xz}_{nc8}",))
            for xz, out_t in ((0, xs_w), (1, z_w)):
                p.dma("y", lambda eng, tb=tb, pb=pb, xz=xz, out_t=out_t: eng.dma_start(
                          out=out_t[:, :, tb * TT:(tb + 1) * TT], in_=f1_o[:, pb, xz]),
                      reads=tuple(f"o{pb}_{xz}_{nc8}" for nc8 in range(8)))
            pairs = [(f1_wx[:, kc, :], f1_o[:, pb, 0, kc, :]) for kc in range(8)]
            mm_accum(p, ps[4][0:96, :], pairs, "pp4", ("wxp",) + tuple(f"o{pb}_0_{nc8}" for nc8 in range(8)))
            p.emit("v", lambda eng, tb=tb: eng.tensor_copy(s_xpf[:, tb * TT:(tb + 1) * TT], ps[4][0:96, :]),
                   reads=("pp4",), writes=("xpf",))
            for j in range(TT // 128):
                pairs = [(f1_o[:, pb, 0, kc, j * 128:(j + 1) * 128], f1_wx[:, kc, :]) for kc in range(8)]
                mm_accum(p, ps[5][:, 0:96], pairs, "pp5", ("wxp",) + tuple(f"o{pb}_0_{nc8}" for nc8 in range(8)))
                p.emit("v", lambda eng, tb=tb, j=j: eng.tensor_copy(s_xpt[:, tb * (TT // 128) + j, :], ps[5][:, 0:96]),
                       reads=("pp5",), writes=("xpt",))
    xch.send(0, [(lambda par: sh_xpf[par], s_xpf[:]), (lambda par: sh_xpt[par], s_xpt[:])],
             reads=["xpf", "xpt"])
    xch.poll(0)
    with ExitStack() as ph:
        fx_f = ph.enter_context(nc.sbuf_tensor("fx_f", [96, S], F32))
        fx_t = ph.enter_context(nc.sbuf_tensor("fx_t", [128, 32, 96], F32))
        fx_bt = ph.enter_context(nc.sbuf_tensor("fx_bt", [128, 96], F32))
        fx_bf = ph.enter_context(nc.sbuf_tensor("fx_bf", [96, 1], F32))
        xch.recv_dma(lambda par: [sh_xpf[1 - par], sh_xpt[1 - par]],
                     lambda par: [fx_f[:], fx_t[:]], 0, ["xpf2", "xpt2"], nd=2)
        p.dma("y", lambda eng: eng.dma_start(out=fx_bf[:], in_=bx_fm), writes=("bxf",))
        p.dma("y", lambda eng: eng.dma_start(out=fx_bt[:], in_=bx_tm), writes=("bxt",))
        p.emit("v", lambda eng: eng.tensor_tensor(s_xpf[:], s_xpf[:], fx_f[:], op=ALU.add),
               reads=("xpf", "xpf2"), writes=("xpf",))
        p.emit("v", lambda eng: eng.tensor_scalar(s_xpf[:], s_xpf[:], fx_bf[:], None, op0=ALU.add),
               reads=("xpf", "bxf"), writes=("xpf",))
        p.emit("v", lambda eng: eng.tensor_tensor(s_xpt[:], s_xpt[:], fx_t[:], op=ALU.add),
               reads=("xpt", "xpt2"), writes=("xpt",))
        p.emit("v", lambda eng: eng.tensor_tensor(s_xpt[:], s_xpt[:], fx_bt[:].unsqueeze(1).broadcast_to([128, 32, 96]), op=ALU.add),
               reads=("xpt", "bxt"), writes=("xpt",))
    p.barrier()
    return _fusedA_part2(nc, stack, p, xch, locals())


def _fusedA_part2(nc, stack, p, xch, env):
    g = env
    s_xpf, s_xpt = g["s_xpf"], g["s_xpt"]
    s_ones, s_one1, s_ident = g["s_ones"], g["s_one1"], g["s_ident"]
    s_row, s_cmg, s_wsc = g["s_row"], g["s_cmg"], g["s_wsc"]
    ps = g["ps"]
    xs_hbm, z_hbm, p_hbm, s1_hbm = g["xs_hbm"], g["z_hbm"], g["p_hbm"], g["s1_hbm"]
    gu_hbm, wdq_hbm, s2_hbm, halo_hbm = g["gu_hbm"], g["wdq_hbm"], g["s2_hbm"], g["halo_hbm"]
    k_hbm, v_hbm, s3f_hbm = g["k_hbm"], g["v_hbm"], g["s3f_hbm"]
    sh_P, sh_halo = g["sh_P"], g["sh_halo"]
    wdt, bdt, alog, dp_r, wout, boutr = g["wdt"], g["bdt"], g["alog"], g["dp_r"], g["wout"], g["boutr"]
    x_res, wgT, wuT, wdT = g["x_res"], g["wgT"], g["wuT"], g["wdT"]
    wqkv, bq_r, bk_r, bv_bc, wout1, bo_r = g["wqkv"], g["bq_r"], g["bk_r"], g["bv_bc"], g["wout1"], g["bo_r"]
    zeros_col, wq = g["zeros_col"], g["wq"]
    q_out, st_out, s3t_out = g["q_out"], g["st_out"], g["s3t_out"]

    TC = 256
    NCH = S // TC
    NT = DH // 128
    with ExitStack() as ph:
        s_xs = ph.enter_context(nc.sbuf_tensor("s_xs", [128, 8, TC], F32))
        s_z = ph.enter_context(nc.sbuf_tensor("s_z", [128, 8, TC], F32))
        s_wdt = ph.enter_context(nc.sbuf_tensor("s_wdt", [64, 8, 128], F32))
        s_bdt = ph.enter_context(nc.sbuf_tensor("s_bdt", [128, 8], F32))
        s_A = ph.enter_context(nc.sbuf_tensor("s_A", [128, 8, NST], F32))
        s_dp = ph.enter_context(nc.sbuf_tensor("s_dp", [128, 8], F32))
        s_dl = ph.enter_context(nc.sbuf_tensor("s_dl", [128, 8, TC], F32))
        s_scz = ph.enter_context(nc.sbuf_tensor("s_scz", [128, 8, TC], F32))
        s_dx = ph.enter_context(nc.sbuf_tensor("s_dx", [128, 8, TC], F32))
        s_B = ph.enter_context(nc.sbuf_tensor("s_B", [128, TC, NST], F32))
        s_C = ph.enter_context(nc.sbuf_tensor("s_C", [128, TC, NST], F32))
        s_dA = ph.enter_context(nc.sbuf_tensor("s_dA", [128, TC, NST], F32))
        s_u = ph.enter_context(nc.sbuf_tensor("s_u", [128, TC, NST], F32))
        s_h = ph.enter_context(nc.sbuf_tensor("s_h", [128, TC, NST], F32))
        s_st = ph.enter_context(nc.sbuf_tensor("s_st", [128, 8, NST], F32))
        s_y = ph.enter_context(nc.sbuf_tensor("s_y", [128, 8, TC], F32))
        s_wo = ph.enter_context(nc.sbuf_tensor("s_wo", [128, 8, 8, 128], F32))
        s_pe = ph.enter_context(nc.sbuf_tensor("s_pe", [128, 8, TC], F32))
        p.dma("y", lambda eng: eng.dma_start(out=s_wdt[:], in_=wdt.rearrange("k (nc8 m) -> k nc8 m", m=128)), writes=("wdt",))
        p.dma("y", lambda eng: eng.dma_start(out=s_bdt[:], in_=bdt), writes=("bdt",))
        p.dma("y", lambda eng: eng.dma_start(out=s_A[:], in_=alog), writes=("alog",))
        p.dma("y", lambda eng: eng.dma_start(out=s_dp[:], in_=dp_r), writes=("dpr",))
        p.dma("y", lambda eng: eng.dma_start(out=s_wo[:], in_=wout.rearrange("(kc p) (nc8 f) -> p kc nc8 f", p=128, f=128)), writes=("wo",))
        p.emit("s", lambda eng: eng.activation(s_A[:], s_A[:], ACT.Exp),
               reads=("alog",), writes=("alog",))
        p.emit("v", lambda eng: eng.tensor_scalar(s_A[:], s_A[:], -1.0, None, op0=ALU.mult),
               reads=("alog",), writes=("A",))
        p.emit("v", lambda eng: eng.memset(s_st[:], 0.0), writes=("st",))
        xs_r = xs_hbm.rearrange("(kc p) t -> p kc t", p=128)
        z_r = z_hbm.rearrange("(kc p) t -> p kc t", p=128)
        for cix in range(NCH):
            t0 = cix * TC
            p.dma("y", lambda eng, t0=t0: eng.dma_start(out=s_xs[:], in_=xs_r[:, :, t0:t0 + TC]), writes=("xs",))
            p.dma("y", lambda eng, t0=t0: eng.dma_start(out=s_z[:], in_=z_r[:, :, t0:t0 + TC]), writes=("z",))
            for nc8 in range(8):
                bi = 2 + (nc8 % 2)
                mm_accum(p, ps[bi][:, 0:TC], [(s_wdt[:, nc8, :], s_xpf[0:64, t0:t0 + TC])], f"pp{bi}", ("wdt", "xpf"))
                p.emit("s", lambda eng, nc8=nc8, bi=bi: eng.activation(
                           s_dl[:, nc8, :], ps[bi][:, 0:TC], ACT.Identity, bias=s_bdt[:, nc8:nc8 + 1], scale=1.0),
                       reads=(f"pp{bi}", "bdt"), writes=("dl",))
            p.emit("s", lambda eng: eng.activation(s_scz[:], s_dl[:], ACT.Abs),
                   reads=("dl",), writes=("scz",))
            p.emit("s", lambda eng: eng.activation(s_scz[:], s_scz[:], ACT.Exp, scale=-1.0),
                   reads=("scz",), writes=("scz",))
            p.emit("s", lambda eng: eng.activation(s_scz[:], s_scz[:], ACT.Ln, bias=1.0, scale=1.0),
                   reads=("scz",), writes=("scz",))
            p.emit("v", lambda eng: eng.scalar_tensor_tensor(
                       out=s_dl[:], in0=s_dl[:], scalar=0.0, in1=s_scz[:], op0=ALU.max, op1=ALU.add),
                   reads=("scz", "dl"), writes=("dl",))
            p.emit("v", lambda eng: eng.tensor_tensor(s_dx[:], s_dl[:], s_xs[:], op=ALU.mult),
                   reads=("dl", "xs"), writes=("dx",))
            for nm, sl, keyb in (("B", slice(64, 80), "B"), ("C", slice(80, 96), "C")):
                dst = s_B if nm == "B" else s_C
                for j in range(TC // 128):
                    gtb = t0 // 128 + j
                    def st0(eng, dst=dst, sl=sl, j=j, gtb=gtb):
                        return eng.dma_start(out=dst[0:1, j * 128:(j + 1) * 128, :], in_=s_xpt[:, gtb, sl])
                    p.dma("g", st0, reads=("xpt",), writes=(keyb,))
                kk = 1
                while kk < 128:
                    p.dma("g", lambda eng, dst=dst, kk=kk: eng.dma_start(out=dst[kk:2 * kk], in_=dst[0:kk]),
                          reads=(keyb,), writes=(keyb,))
                    kk *= 2
            for dc in range(NT):
                for n in range(NST):
                    p.emit("s", lambda eng, dc=dc, n=n: eng.activation(
                               s_dA[:, :, n], s_dl[:, dc, :], ACT.Exp, scale=s_A[:, dc, n:n + 1]),
                           reads=("dl", "A"), writes=("dA",))
                p.emit("v", lambda eng, dc=dc: eng.tensor_tensor(
                           s_u[:], s_dx[:, dc, :].unsqueeze(2).broadcast_to([128, TC, NST]), s_B[:], op=ALU.mult),
                       reads=("dx", "B"), writes=("u",))
                for n in range(NST):
                    p.emit("v", lambda eng, dc=dc, n=n: eng.tensor_tensor_scan(
                               s_h[:, :, n], s_dA[:, :, n], s_u[:, :, n], s_st[:, dc, n:n + 1],
                               op0=ALU.mult, op1=ALU.add),
                           reads=("dA", "u", "st"), writes=("h",))
                p.emit("v", lambda eng, dc=dc: eng.tensor_copy(s_st[:, dc, :], s_h[:, TC - 1, :]),
                       reads=("h",), writes=("st",))
                p.emit("v", lambda eng: eng.tensor_tensor(s_dA[:], s_h[:], s_C[:], op=ALU.mult),
                       reads=("C", "h"), writes=("dA",))
                p.emit("v", lambda eng, dc=dc: eng.tensor_reduce(
                           s_y[:, dc, :], s_dA[:], axis=AX, op=ALU.add),
                       reads=("dA",), writes=(f"y{dc}",))
            p.emit("s", lambda eng: eng.activation(s_scz[:], s_z[:], ACT.Silu),
                   reads=("z",), writes=("scz",))
            for dc in range(NT):
                p.emit("v", lambda eng, dc=dc: eng.scalar_tensor_tensor(
                           out=s_y[:, dc, :], in0=s_xs[:, dc, :], scalar=s_dp[:, dc:dc + 1],
                           in1=s_y[:, dc, :], op0=ALU.mult, op1=ALU.add),
                       reads=("xs", "dpr", f"y{dc}"), writes=(f"y{dc}",))
            p.emit("v", lambda eng: eng.tensor_tensor(s_y[:], s_y[:], s_scz[:], op=ALU.mult),
                   reads=("scz",) + tuple(f"y{dc}" for dc in range(NT)),
                   writes=tuple(f"y{dc}" for dc in range(NT)))
            for oc in range(8):
                bi = 4 + (oc % 2)
                pairs = [(s_wo[:, kc, oc, :], s_y[:, kc, :]) for kc in range(8)]
                mm_accum(p, ps[bi][:, 0:TC], pairs, f"pp{bi}", ("wo",) + tuple(f"y{dc}" for dc in range(NT)))
                p.emit("s", lambda eng, oc=oc, bi=bi: eng.copy(s_pe[:, oc, :], ps[bi][:, 0:TC]),
                       reads=(f"pp{bi}",), writes=("pe",))
            p.dma("y", lambda eng, t0=t0: eng.dma_start(out=p_hbm[:, :, t0:t0 + TC], in_=s_pe[:]),
                  reads=("pe",))
    p.wait_keys("g", ["pe"])
    xch.send(1, [(lambda par: sh_P[par], p_hbm)], reads=["pe"])
    xch.poll(1)
    p.barrier()
    g["xp_stack"].close()
    return _fusedA_part3(nc, stack, p, xch, g)


def _fusedA_part3(nc, stack, p, xch, g):
    s_ones, s_one1, s_ident = g["s_ones"], g["s_one1"], g["s_ident"]
    s_row, s_cmg, s_wsc = g["s_row"], g["s_cmg"], g["s_wsc"]
    ps = g["ps"]
    p_hbm, s1_hbm, gu_hbm, wdq_hbm = g["p_hbm"], g["s1_hbm"], g["gu_hbm"], g["wdq_hbm"]
    s2_hbm, halo_hbm, k_hbm, v_hbm, s3f_hbm = g["s2_hbm"], g["halo_hbm"], g["k_hbm"], g["v_hbm"], g["s3f_hbm"]
    sh_P, sh_halo = g["sh_P"], g["sh_halo"]
    x_res, wgT, wuT, wdT, boutr = g["x_res"], g["wgT"], g["wuT"], g["wdT"], g["boutr"]
    wqkv, bq_r, bk_r, bv_bc, wout1, bo_r = g["wqkv"], g["bq_r"], g["bk_r"], g["bv_bc"], g["wout1"], g["bo_r"]
    zeros_col, wq = g["zeros_col"], g["wq"]
    q_out, st_out, s3t_out = g["q_out"], g["st_out"], g["s3t_out"]

    ln_stack = ExitStack()
    s_inv1 = ln_stack.enter_context(nc.sbuf_tensor("s_inv1", [128, TOK], F32))
    s_kq1 = ln_stack.enter_context(nc.sbuf_tensor("s_kq1", [128, 8, TOK], BF16))
    g["s_inv1"], g["s_kq1"], g["ln_stack"] = s_inv1, s_kq1, ln_stack

    with ExitStack() as ph:
        TT1 = 256
        f3_pa = ph.enter_context(nc.sbuf_tensor("f3_pa", [128, 8, TT1], F32))
        f3_pb = ph.enter_context(nc.sbuf_tensor("f3_pb", [128, 8, TT1], F32))
        f3_x = ph.enter_context(nc.sbuf_tensor("f3_x", [128, 8, TT1], F32))
        f3_b = ph.enter_context(nc.sbuf_tensor("f3_b", [128, 8], F32))
        f3_r1 = ph.enter_context(nc.sbuf_tensor("f3_r1", [128, 8, TT1], F32))
        f3_sq = ph.enter_context(nc.sbuf_tensor("f3_sq", [128, 8, TT1], F32))
        f3_sc = ph.enter_context(nc.sbuf_tensor("f3_sc", [128, 4, TT1], F32))
        f3_s1 = ph.enter_context(nc.sbuf_tensor("f3_s1", [128, 8, TT1], F32))
        f3_ki = ph.enter_context(nc.sbuf_tensor("f3_ki", [128, 8, TT1], mybir.dt.int32))
        f3_qs = ph.enter_context(nc.sbuf_tensor("f3_qs", [128, 8, TT1], F32))
        p.dma("y", lambda eng: eng.dma_start(out=f3_b[:], in_=boutr), writes=("bo",))
        xv = x_res.rearrange("(kc p) t -> p kc t", p=128)
        s1v = s1_hbm.rearrange("(kc p) t -> p kc t", p=128)
        for tt in range(TOK // TT1):
            xch.recv_dma(
                lambda par, tt=tt: [sh_P[par][:, :, par * TOK + tt * TT1: par * TOK + (tt + 1) * TT1],
                                    sh_P[1 - par][:, :, par * TOK + tt * TT1: par * TOK + (tt + 1) * TT1]],
                lambda par: [f3_pa[:], f3_pb[:]], 1, ["pa", "pb"], nd=2)
            p.dma("y", lambda eng, tt=tt: eng.dma_start(out=f3_x[:], in_=xv[:, :, tt * TT1:(tt + 1) * TT1]),
                  writes=("xr",))
            p.emit("v", lambda eng: eng.tensor_tensor(f3_r1[:], f3_pa[:], f3_pb[:], op=ALU.add),
                   reads=("pa", "pb"), writes=("r1",))
            for nc8 in range(8):
                p.emit("v", lambda eng, nc8=nc8: eng.tensor_scalar(
                           f3_r1[:, nc8, :], f3_r1[:, nc8, :], f3_b[:, nc8:nc8 + 1], None, op0=ALU.add),
                       reads=("r1", "bo"), writes=("r1",))
            p.emit("v", lambda eng: eng.tensor_tensor(f3_r1[:], f3_r1[:], f3_x[:], op=ALU.add),
                   reads=("r1", "xr"), writes=("r1",))
            emit_ln_fm(p, f"f3ln_{tt}", f3_r1, "r1", f3_s1, "s1", f3_sq, "sq", s_ones,
                       ps[4], "pp4", ps[5], "pp5", f3_qs[:, 0:2], "qs", TT1)
            p.dma("y", lambda eng, tt=tt: eng.dma_start(out=s1v[:, :, tt * TT1:(tt + 1) * TT1], in_=f3_s1[:]),
                  reads=("s1",))
            p.emit("s", lambda eng: eng.activation(f3_qs[:], f3_s1[:], ACT.Abs),
                   reads=("s1",), writes=("qs",))
            p.emit("v", lambda eng: eng.tensor_tensor(f3_sc[:, 0, :], f3_qs[:, 0, :], f3_qs[:, 1, :], op=ALU.max),
                   reads=("qs",), writes=("am",))
            for kc in range(2, 8):
                p.emit("v", lambda eng, kc=kc: eng.tensor_tensor(f3_sc[:, 0, :], f3_sc[:, 0, :], f3_qs[:, kc, :], op=ALU.max),
                       reads=("am", "qs"), writes=("am",))
            for jb in range(TT1 // 128):
                emit_colmax_bcast(p, f"f3cm_{tt}_{jb}", f3_sc[:, 0, jb * 128:(jb + 1) * 128], "am",
                                  s_cmg, s_row, ps[6], "pp6", ps[7], "pp7",
                                  s_ident, f3_sc[:, 2, jb * 128:(jb + 1) * 128], "sc1",
                                  s_inv1[:, tt * TT1 + jb * 128:tt * TT1 + (jb + 1) * 128], "inv1", s_one1, None)
            p.emit("v", lambda eng: eng.tensor_tensor(
                       f3_r1[:], f3_s1[:], f3_sc[:, 2:3, :].broadcast_to([128, 8, TT1]), op=ALU.mult),
                   reads=("s1", "sc1"), writes=("r1",))
            p.emit("v", lambda eng: eng.tensor_copy(f3_ki[:], f3_r1[:]), reads=("r1",), writes=("ki",))
            p.emit("v", lambda eng: eng.tensor_copy(f3_sq[:], f3_ki[:]), reads=("ki",), writes=("sq",))
            p.emit("v", lambda eng: eng.tensor_tensor(f3_r1[:], f3_r1[:], f3_sq[:], op=ALU.subtract),
                   reads=("r1", "sq"), writes=("r1",))
            p.emit("v", lambda eng: eng.tensor_scalar(f3_qs[:], f3_r1[:], -0.5, None, op0=ALU.is_lt),
                   reads=("r1",), writes=("qs",))
            p.emit("v", lambda eng: eng.scalar_tensor_tensor(out=f3_r1[:], in0=f3_r1[:], scalar=0.5, in1=f3_qs[:],
                                                             op0=ALU.is_gt, op1=ALU.subtract),
                   reads=("r1", "qs"), writes=("r1",))
            p.emit("v", lambda eng: eng.tensor_tensor(f3_r1[:], f3_r1[:], f3_sq[:], op=ALU.add),
                   reads=("r1", "sq"), writes=("r1",))
            p.emit("v", lambda eng: eng.tensor_scalar(f3_r1[:], f3_r1[:], -128.0, 127.0, op0=ALU.max, op1=ALU.min),
                   reads=("r1",), writes=("r1",))
            p.emit("v", lambda eng, tt=tt: eng.tensor_copy(s_kq1[:, :, tt * TT1:(tt + 1) * TT1], f3_r1[:]),
                   reads=("r1",), writes=("kq1",))
    p.barrier()
    return _fusedA_part4(nc, stack, p, xch, g)


def _fusedA_part4(nc, stack, p, xch, g):
    s_ones, s_one1, s_ident = g["s_ones"], g["s_one1"], g["s_ident"]
    s_row, s_cmg, s_wsc = g["s_row"], g["s_cmg"], g["s_wsc"]
    s_inv1, s_kq1 = g["s_inv1"], g["s_kq1"]
    bn_stack = ExitStack()
    s_m2 = bn_stack.enter_context(nc.sbuf_tensor("s_m2", [128, TOK], F32))
    s_f2s = bn_stack.enter_context(nc.sbuf_tensor("s_f2s", [128, TOK], F32))
    s_inv2 = bn_stack.enter_context(nc.sbuf_tensor("s_inv2", [128, TOK], F32))
    g["bn_stack"] = bn_stack
    ps = g["ps"]
    s1_hbm, gu_hbm, wdq_hbm = g["s1_hbm"], g["gu_hbm"], g["wdq_hbm"]
    s2_hbm, halo_hbm, k_hbm, v_hbm, s3f_hbm = g["s2_hbm"], g["halo_hbm"], g["k_hbm"], g["v_hbm"], g["s3f_hbm"]
    sh_halo = g["sh_halo"]
    wgT, wuT, wdT = g["wgT"], g["wuT"], g["wdT"]
    wqkv, bq_r, bk_r, bv_bc, wout1, bo_r = g["wqkv"], g["bq_r"], g["bk_r"], g["bv_bc"], g["wout1"], g["bo_r"]
    zeros_col, wq = g["zeros_col"], g["wq"]
    q_out, st_out, s3t_out = g["q_out"], g["st_out"], g["s3t_out"]
    TT = 512
    NTT = TOK // TT

    p.emit("v", lambda eng: eng.memset(s_m2[:], 0.0), writes=("m2",))
    with ExitStack() as ph:
        f4_wb = ph.enter_context(nc.sbuf_tensor("f4_wb", [128, 2, 2, 8, 128], F32))
        f4_wq = ph.enter_context(nc.sbuf_tensor("f4_wq", [128, 2, 8, 128], F32))
        f4_qb = ph.enter_context(nc.sbuf_tensor("f4_qb", [128, 2, 8, 128], BF16))
        f4_g = ph.enter_context(nc.sbuf_tensor("f4_g", [128, TT], F32))
        f4_u = ph.enter_context(nc.sbuf_tensor("f4_u", [128, TT], F32))
        f4_gu = ph.enter_context(nc.sbuf_tensor("f4_gu", [128, 2, TT], F32))
        wgv = wgT.rearrange("(kc p) (oc f) -> p kc oc f", p=128, f=128)
        wuv = wuT.rearrange("(kc p) (oc f) -> p kc oc f", p=128, f=128)
        guv = gu_hbm.rearrange("(oc p) t -> p oc t", p=128)
        for oc in range(32):
            pb = oc % 2
            p.dma("y", lambda eng, oc=oc, pb=pb: eng.dma_start(out=f4_wb[:, pb, 0], in_=wgv[:, :, oc, :]),
                  writes=(f"wg{pb}",))
            p.dma("y", lambda eng, oc=oc, pb=pb: eng.dma_start(out=f4_wb[:, pb, 1], in_=wuv[:, :, oc, :]),
                  writes=(f"wu{pb}",))
            for gi, wkey, sw_col in ((0, "wg", 3), (1, "wu", 4)):
                p.emit("v", lambda eng, pb=pb, gi=gi, sw_col=sw_col: eng.tensor_scalar(
                           f4_wq[:, 0], f4_wb[:, pb, gi], s_wsc[:, sw_col:sw_col + 1], None, op0=ALU.mult),
                       reads=(f"{wkey}{pb}", "wsc2"), writes=("wqa",))
                p.emit("v", lambda eng: eng.tensor_scalar(f4_wq[:, 1], f4_wq[:, 0], -0.5, None, op0=ALU.is_lt),
                       reads=("wqa",), writes=("wqm",))
                p.emit("v", lambda eng, gi=gi: eng.scalar_tensor_tensor(out=f4_qb[:, gi], in0=f4_wq[:, 0], scalar=0.5,
                                                                 in1=f4_wq[:, 1], op0=ALU.is_gt, op1=ALU.subtract),
                       reads=("wqa", "wqm"), writes=(f"wqb{gi}",))
            for tt in range(NTT):
                pairs_g = [(f4_qb[:, 0, kc, :], s_kq1[:, kc, tt * TT:(tt + 1) * TT]) for kc in range(8)]
                pairs_u = [(f4_qb[:, 1, kc, :], s_kq1[:, kc, tt * TT:(tt + 1) * TT]) for kc in range(8)]
                mm_accum(p, ps[0][:], pairs_g, "pp0", ("wqb0", "kq1"))
                mm_accum(p, ps[1][:], pairs_u, "pp1", ("wqb1", "kq1"))
                p.emit("v", lambda eng, tt=tt: eng.scalar_tensor_tensor(
                           out=f4_g[:], in0=ps[0][:], scalar=s_wsc[:, 0:1],
                           in1=s_inv1[:, tt * TT:(tt + 1) * TT], op0=ALU.mult, op1=ALU.mult),
                       reads=("pp0", "wsc", "inv1"), writes=("g",))
                p.emit("s", lambda eng: eng.activation(f4_g[:], f4_g[:], ACT.Silu),
                       reads=("g",), writes=("g",))
                p.emit("v", lambda eng, tt=tt: eng.scalar_tensor_tensor(
                           out=f4_u[:], in0=ps[1][:], scalar=s_wsc[:, 1:2],
                           in1=s_inv1[:, tt * TT:(tt + 1) * TT], op0=ALU.mult, op1=ALU.mult),
                       reads=("pp1", "wsc", "inv1"), writes=("u",))
                gb = tt % 2
                p.emit("v", lambda eng, gb=gb: eng.tensor_tensor(f4_gu[:, gb], f4_g[:], f4_u[:], op=ALU.mult),
                       reads=("g", "u"), writes=(f"gu{gb}",))
                p.dma("y", lambda eng, oc=oc, tt=tt, gb=gb: eng.dma_start(
                          out=guv[:, oc, tt * TT:(tt + 1) * TT], in_=f4_gu[:, gb]),
                      reads=(f"gu{gb}",))
                p.emit("s", lambda eng, gb=gb: eng.activation(f4_g[:], f4_gu[:, gb], ACT.Abs),
                       reads=(f"gu{gb}",), writes=("g",))
                p.emit("v", lambda eng, tt=tt: eng.tensor_tensor(
                           s_m2[:, tt * TT:(tt + 1) * TT], s_m2[:, tt * TT:(tt + 1) * TT], f4_g[:], op=ALU.max),
                       reads=("g", "m2"), writes=("m2",))
    p.barrier()
    for jb in range(TOK // 128):
        emit_colmax_bcast(p, f"f4cm_{jb}", s_m2[:, jb * 128:(jb + 1) * 128], "m2",
                          s_cmg, s_row, ps[6], "pp6", ps[7], "pp7",
                          s_ident, s_f2s[:, jb * 128:(jb + 1) * 128], "f2",
                          s_inv2[:, jb * 128:(jb + 1) * 128], "inv2", s_one1, None)
    p.barrier()

    with ExitStack() as ph:
        f5_wd = ph.enter_context(nc.sbuf_tensor("f5_wd", [128, 2, 8, 128], BF16))
        f5_gu = ph.enter_context(nc.sbuf_tensor("f5_gu", [128, 2, TT], F32))
        f5_q = ph.enter_context(nc.sbuf_tensor("f5_q", [128, 2, TT], F32))
        f5_qb = ph.enter_context(nc.sbuf_tensor("f5_qb", [128, TT], BF16))
        f5_qi = ph.enter_context(nc.sbuf_tensor("f5_qi", [128, TT], mybir.dt.int32))
        f5_qf = ph.enter_context(nc.sbuf_tensor("f5_qf", [128, TT], F32))
        f5_ffn = ph.enter_context(nc.sbuf_tensor("f5_ffn", [128, 8, TT], F32))
        f5_sq = ph.enter_context(nc.sbuf_tensor("f5_sq", [128, 8, TT], F32))
        f5_s2 = ph.enter_context(nc.sbuf_tensor("f5_s2", [128, 8, TT], F32))
        f5_s1r = ph.enter_context(nc.sbuf_tensor("f5_s1r", [128, 8, TT], F32))
        f5_mv = ph.enter_context(nc.sbuf_tensor("f5_mv", [128, 2, TT], F32))
        wdv = wdq_hbm.rearrange("(kc p) (oc f) -> p kc oc f", p=128, f=128)
        guv = gu_hbm.rearrange("(kc p) t -> p kc t", p=128)
        s1v = s1_hbm.rearrange("(kc p) t -> p kc t", p=128)
        s2v = s2_hbm.rearrange("(kc p) t -> p kc t", p=128)
        for tt in range(NTT):
            p.dma("y", lambda eng, tt=tt: eng.dma_start(out=f5_s1r[:], in_=s1v[:, :, tt * TT:(tt + 1) * TT]),
                  writes=("s1r",))
            for ic in range(32):
                pb = ic % 2
                p.dma("y", lambda eng, ic=ic, pb=pb: eng.dma_start(out=f5_wd[:, pb], in_=wdv[:, ic, :, :]),
                      writes=(f"wd{pb}",))
                p.dma("y", lambda eng, ic=ic, tt=tt, pb=pb: eng.dma_start(
                          out=f5_gu[:, pb], in_=guv[:, ic, tt * TT:(tt + 1) * TT]),
                      writes=(f"gub{pb}",))
                p.emit("v", lambda eng, tt=tt, pb=pb: eng.tensor_tensor(
                           f5_q[:, 0], f5_gu[:, pb], s_f2s[:, tt * TT:(tt + 1) * TT], op=ALU.mult),
                       reads=(f"gub{pb}", "f2"), writes=("qa",))
                p.emit("v", lambda eng: eng.tensor_copy(f5_qi[:], f5_q[:, 0]), reads=("qa",), writes=("qi",))
                p.emit("v", lambda eng: eng.tensor_copy(f5_qf[:], f5_qi[:]), reads=("qi",), writes=("qf",))
                p.emit("v", lambda eng: eng.tensor_tensor(f5_q[:, 0], f5_q[:, 0], f5_qf[:], op=ALU.subtract),
                       reads=("qa", "qf"), writes=("qa",))
                p.emit("v", lambda eng: eng.tensor_scalar(f5_q[:, 1], f5_q[:, 0], -0.5, None, op0=ALU.is_lt),
                       reads=("qa",), writes=("qm",))
                p.emit("v", lambda eng: eng.scalar_tensor_tensor(out=f5_q[:, 0], in0=f5_q[:, 0], scalar=0.5,
                                                                 in1=f5_q[:, 1], op0=ALU.is_gt, op1=ALU.subtract),
                       reads=("qa", "qm"), writes=("qa",))
                p.emit("v", lambda eng: eng.tensor_tensor(f5_q[:, 0], f5_q[:, 0], f5_qf[:], op=ALU.add),
                       reads=("qa", "qf"), writes=("qa",))
                p.emit("v", lambda eng: eng.tensor_scalar(f5_qb[:], f5_q[:, 0], -128.0, 127.0, op0=ALU.max, op1=ALU.min),
                       reads=("qa",), writes=("qb",))
                def mmdown(eng, ic=ic, pb=pb):
                    ins = None
                    for oc in range(8):
                        ins = eng.matmul(ps[oc][:], f5_wd[:, pb, oc, :], f5_qb[:],
                                         start=(ic == 0), stop=(ic == 31))
                    return ins
                p.emit("t", mmdown, reads=(f"wd{pb}", "qb"), writes=("ppdown",))
            for oc in range(8):
                p.emit("v", lambda eng, oc=oc, tt=tt: eng.scalar_tensor_tensor(
                           out=f5_ffn[:, oc, :], in0=ps[oc][:], scalar=s_wsc[:, 2:3],
                           in1=s_inv2[:, tt * TT:(tt + 1) * TT], op0=ALU.mult, op1=ALU.mult),
                       reads=("ppdown", "wsc", "inv2"), writes=("ffn",))
            p.emit("v", lambda eng: eng.tensor_tensor(f5_ffn[:], f5_ffn[:], f5_s1r[:], op=ALU.add),
                   reads=("ffn", "s1r"), writes=("ffn",))
            emit_ln_fm(p, f"f5ln_{tt}", f5_ffn, "ffn", f5_s2, "s2", f5_sq, "f5sq", s_ones,
                       ps[0], "ppdown", ps[1], "ppdown", f5_mv, "f5mv", TT)
            p.dma("y", lambda eng, tt=tt: eng.dma_start(out=s2v[:, :, tt * TT:(tt + 1) * TT], in_=f5_s2[:]),
                  reads=("s2",))
    p.barrier()

    xch.send(2, [(lambda par: sh_halo[par],
                  lambda par: s2_hbm[:, (TOK - 128) * (1 - par):(TOK - 128) * (1 - par) + 128])],
             reads=["s2"])
    xch.poll(2)
    xch.recv_dma(lambda par: [sh_halo[1 - par], zeros_col],
                 lambda par: [halo_hbm[:, 128 * (1 - par):128 * (1 - par) + 128],
                              halo_hbm[:, 128 * par:128 * par + 128]],
                 2, ["halo"], nd=2)
    p.barrier()
    return _fusedA_part5(nc, stack, p, xch, g)


def _fusedA_part5(nc, stack, p, xch, g):
    g["bn_stack"].close()
    g["ln_stack"].close()
    s_ones, s_one1, s_ident = g["s_ones"], g["s_one1"], g["s_ident"]
    s_ms = stack.enter_context(nc.sbuf_tensor("s_ms", [128, 3, 384], F32))
    p.dma("y", lambda eng: eng.dma_start(out=s_ms[:, 0], in_=g["mask0"]), writes=("ms",))
    p.dma("y", lambda eng: eng.dma_start(out=s_ms[:, 1], in_=g["maskm"]), writes=("ms",))
    p.dma("y", lambda eng: eng.dma_start(out=s_ms[:, 2], in_=g["mask15"]), writes=("ms",))
    ps = g["ps"]
    s2_hbm, halo_hbm, k_hbm, v_hbm, s3f_hbm = g["s2_hbm"], g["halo_hbm"], g["k_hbm"], g["v_hbm"], g["s3f_hbm"]
    wqkv, bq_r, bk_r, bv_bc, wout1, bo_r = g["wqkv"], g["bq_r"], g["bk_r"], g["bv_bc"], g["wout1"], g["bo_r"]
    wq = g["wq"]
    q_out, st_out, s3t_out = g["q_out"], g["st_out"], g["s3t_out"]
    NH, DHD, W = 16, 64, 128
    NB = TOK // W
    KT = TOK + 2 * W

    x_r = s2_hbm.rearrange("(kc p) t -> p kc t", p=128)
    hx_r = halo_hbm.rearrange("(kc p) t -> p kc t", p=128)
    kv = k_hbm.rearrange("(kc p) t -> p kc t", p=128)
    vv_ = v_hbm.rearrange("(tb p) d -> p tb d", p=128)

    with ExitStack() as ph:
        s_wk = ph.enter_context(nc.sbuf_tensor("fa_wk", [128, 8, 8, 128], F32))
        s_wv = ph.enter_context(nc.sbuf_tensor("fa_wv", [128, 8, 8, 128], F32))
        s_bk = ph.enter_context(nc.sbuf_tensor("fa_bk", [128, 8], F32))
        s_bv = ph.enter_context(nc.sbuf_tensor("fa_bv", [128, DM], F32))
        s_xa = ph.enter_context(nc.sbuf_tensor("fa_x", [128, 2, 8, 512], F32))
        s_k = ph.enter_context(nc.sbuf_tensor("fa_k", [128, 8, 512], F32))
        s_v = ph.enter_context(nc.sbuf_tensor("fa_v", [128, 2, 512], F32))
        p.dma("y", lambda eng: eng.dma_start(out=s_wk[:], in_=wqkv.rearrange("(kc p) n -> p kc n", p=128)[:, :, DM:2 * DM].rearrange("p kc (o f) -> p kc o f", f=128)), writes=("wk",))
        p.dma("y", lambda eng: eng.dma_start(out=s_wv[:], in_=wqkv.rearrange("(kc p) n -> p kc n", p=128)[:, :, 2 * DM:3 * DM].rearrange("p kc (o f) -> p kc o f", f=128)), writes=("wv",))
        p.dma("y", lambda eng: eng.dma_start(out=s_bk[:], in_=bk_r), writes=("bk",))
        p.dma("y", lambda eng: eng.dma_start(out=s_bv[:], in_=bv_bc), writes=("bv",))
        groups = [("own", gg) for gg in range(4)] + [("halo", 0)]
        for gi, (kind, gg) in enumerate(groups):
            pb = gi % 2
            ntok = 512 if kind == "own" else 256
            if kind == "own":
                p.dma("y", lambda eng, gg=gg, pb=pb: eng.dma_start(out=s_xa[:, pb], in_=x_r[:, :, gg * 512:(gg + 1) * 512]),
                      writes=(f"x{pb}",))
            else:
                p.dma("y", lambda eng, pb=pb: eng.dma_start(out=s_xa[:, pb, :, 0:256], in_=hx_r[:]),
                      writes=(f"x{pb}",))
            for oc in range(8):
                bi = oc % 2
                pairs = [(s_wk[:, kc, oc, :], s_xa[:, pb, kc, 0:ntok]) for kc in range(8)]
                mm_accum(p, ps[bi][:, 0:ntok], pairs, f"pp{bi}", (f"x{pb}", "wk"))
                p.emit("s", lambda eng, oc=oc, bi=bi, ntok=ntok: eng.activation(
                           s_k[:, oc, 0:ntok], ps[bi][:, 0:ntok], ACT.Identity, bias=s_bk[:, oc:oc + 1], scale=1.0),
                       reads=(f"pp{bi}", "bk"), writes=("k",))
            if kind == "own":
                p.dma("y", lambda eng, gg=gg: eng.dma_start(out=kv[:, :, 128 + gg * 512:128 + (gg + 1) * 512], in_=s_k[:]),
                      reads=("k",))
            else:
                p.dma("y", lambda eng: eng.dma_start(out=kv[:, :, 0:128], in_=s_k[:, :, 0:128]), reads=("k",))
                p.dma("y", lambda eng: eng.dma_start(out=kv[:, :, KT - 128:KT], in_=s_k[:, :, 128:256]), reads=("k",))
            for tb in range(ntok // 128):
                for vc in range(2):
                    bi = 2 + vc
                    pairs = [(s_xa[:, pb, kc, tb * 128:(tb + 1) * 128], s_wv[:, kc, 4 * vc:4 * vc + 4, :].rearrange("p o f -> p (o f)")) for kc in range(8)]
                    mm_accum(p, ps[bi][:], pairs, f"pp{bi}", (f"x{pb}", "wv"))
                    p.emit("v", lambda eng, vc=vc, bi=bi: eng.tensor_tensor(
                               s_v[:, vc, :], ps[bi][:], s_bv[:, vc * 512:(vc + 1) * 512], op=ALU.add),
                           reads=(f"pp{bi}", "bv"), writes=("v",))
                if kind == "own":
                    p.dma("y", lambda eng, gg=gg, tb=tb: eng.dma_start(
                              out=vv_[:, 1 + gg * 4 + tb, :], in_=s_v[:].rearrange("p a b -> p (a b)")),
                          reads=("v",))
                elif tb == 0:
                    p.dma("y", lambda eng: eng.dma_start(out=vv_[:, 0, :], in_=s_v[:].rearrange("p a b -> p (a b)")),
                          reads=("v",))
                else:
                    p.dma("y", lambda eng: eng.dma_start(out=vv_[:, 17, :], in_=s_v[:].rearrange("p a b -> p (a b)")),
                          reads=("v",))
    p.barrier()

    with ExitStack() as ph:
        s_wq4 = ph.enter_context(nc.sbuf_tensor("fb_wq", [128, 8, 8, 128], F32))
        s_bq = ph.enter_context(nc.sbuf_tensor("fb_bq", [128, 8], F32))
        s_x = ph.enter_context(nc.sbuf_tensor("fb_x", [128, 1, 8, 512], F32))
        s_q = ph.enter_context(nc.sbuf_tensor("fb_q", [128, 8, 512], F32))
        s_kk = ph.enter_context(nc.sbuf_tensor("fb_kk", [128, 1, 8, 768], F32))
        s_vv = ph.enter_context(nc.sbuf_tensor("fb_vv", [128, 1, 6, DM], F32))
        s_sc = ph.enter_context(nc.sbuf_tensor("fb_sc", [128, 384], F32))
        s_sm = ph.enter_context(nc.sbuf_tensor("fb_sm", [128, 4], F32))
        s_at = ph.enter_context(nc.sbuf_tensor("fb_at", [128, 3, 128], F32))
        s_ao = ph.enter_context(nc.sbuf_tensor("fb_ao", [128, 8, 512], F32))
        s_wo = ph.enter_context(nc.sbuf_tensor("fb_wo", [128, 2, 8, 128], F32))
        s_bo = ph.enter_context(nc.sbuf_tensor("fb_bo", [128, 8], F32))
        s_o = ph.enter_context(nc.sbuf_tensor("fb_o", [128, 8, 512], F32))
        s_sq = ph.enter_context(nc.sbuf_tensor("fb_sq", [128, 8, 512], F32))
        s_s3 = ph.enter_context(nc.sbuf_tensor("fb_s3", [128, 8, 512], F32))
        s_3t = ph.enter_context(nc.sbuf_tensor("fb_3t", [128, 8, 128], F32))
        s_mvb = ph.enter_context(nc.sbuf_tensor("fb_mv", [128, 2, 512], F32))
        p.dma("y", lambda eng: eng.dma_start(out=s_wq4[:], in_=wqkv.rearrange("(kc p) n -> p kc n", p=128)[:, :, 0:DM].rearrange("p kc (o f) -> p kc o f", f=128)), writes=("wqq",))
        p.dma("y", lambda eng: eng.dma_start(out=s_bq[:], in_=bq_r), writes=("bq",))
        p.dma("y", lambda eng: eng.dma_start(out=s_bo[:], in_=bo_r), writes=("bo2",))
        s3fv = s3f_hbm.rearrange("(kc p) t -> p kc t", p=128)
        for gg in range(4):
            pb = 0
            p.dma("y", lambda eng, gg=gg: eng.dma_start(out=s_x[:, 0], in_=x_r[:, :, gg * 512:(gg + 1) * 512]),
                  writes=("xb0",))
            p.dma("y", lambda eng, gg=gg: eng.dma_start(out=s_kk[:, 0], in_=kv[:, :, gg * 512:gg * 512 + 768]),
                  writes=("kk0",))
            p.dma("y", lambda eng, gg=gg: eng.dma_start(out=s_vv[:, 0], in_=vv_[:, gg * 4:gg * 4 + 6, :]),
                  writes=("vv0",))
            for oc in range(8):
                bi = oc % 2
                pairs = [(s_wq4[:, kc, oc, :], s_x[:, 0, kc, :]) for kc in range(8)]
                mm_accum(p, ps[bi][:], pairs, f"pp{bi}", ("xb0", "wqq"))
                p.emit("s", lambda eng, oc=oc, bi=bi: eng.activation(
                           s_q[:, oc, 0:512], ps[bi][:], ACT.Identity, bias=s_bq[:, oc:oc + 1], scale=1.0),
                       reads=(f"pp{bi}", "bq"), writes=("q",))
            for hd in range(NH):
                hc, hr = hd // 2, (hd % 2) * 64
                for qb in range(4):
                    gqb = gg * 4 + qb
                    mi = 0 if gqb == 0 else (2 if gqb == NB - 1 else 1)
                    p.emit("t", lambda eng, hc=hc, hr=hr, qb=qb: eng.matmul(
                               ps[4][:, 0:384], s_q[hr:hr + 64, hc, qb * 128:(qb + 1) * 128],
                               s_kk[hr:hr + 64, 0, hc, qb * 128:qb * 128 + 384], start=True, stop=True),
                           reads=("q", "kk0"), writes=("pp4",))
                    p.emit("v", lambda eng, mi=mi: eng.scalar_tensor_tensor(
                               out=s_sc[:], in0=ps[4][:, 0:384], scalar=0.03125, in1=s_ms[:, mi],
                               op0=ALU.mult, op1=ALU.add),
                           reads=("pp4", "ms"), writes=("sc",))
                    p.emit("v", lambda eng: eng.tensor_reduce(s_sm[:, 0:1], s_sc[:], axis=AX, op=ALU.max),
                           reads=("sc",), writes=("sm",))
                    p.emit("v", lambda eng: eng.tensor_scalar(s_sm[:, 1:2], s_sm[:, 0:1], -1.0, None, op0=ALU.mult),
                           reads=("sm",), writes=("sm1",))
                    p.emit("s", lambda eng: eng.activation(s_sc[:], s_sc[:], ACT.Exp, bias=s_sm[:, 1:2], scale=1.0),
                           reads=("sc", "sm1"), writes=("sc",))
                    p.emit("v", lambda eng: eng.tensor_reduce(s_sm[:, 2:3], s_sc[:], axis=AX, op=ALU.add),
                           reads=("sc",), writes=("sm2",))
                    p.emit("v", lambda eng: eng.reciprocal(s_sm[:, 3:4], s_sm[:, 2:3]),
                           reads=("sm2",), writes=("sm3",))
                    p.emit("v", lambda eng: eng.tensor_scalar(s_sc[:], s_sc[:], s_sm[:, 3:4], None, op0=ALU.mult),
                           reads=("sc", "sm3"), writes=("sc",))
                    for j in range(3):
                        p.emit("t", lambda eng, j=j: eng.transpose(ps[5][:, j * 128:(j + 1) * 128], s_sc[:, j * 128:(j + 1) * 128], s_ident[:]),
                               reads=("sc", "ident"), writes=("pp5",))
                    p.emit("s", lambda eng: eng.copy(s_at[:].rearrange("p a b -> p (a b)"), ps[5][:, 0:384]),
                           reads=("pp5",), writes=("at",))
                    def av(eng, hc=hc, hr=hr, qb=qb):
                        ins = None
                        for j in range(3):
                            ins = eng.matmul(ps[6][hr:hr + 64, qb * 128:(qb + 1) * 128],
                                             s_vv[:, 0, qb + j, hc * 128 + hr:hc * 128 + hr + 64],
                                             s_at[:, j, :], start=(j == 0), stop=(j == 2))
                        return ins
                    p.emit("t", av, reads=("at", "vv0"), writes=("pp6",))
                    p.emit("s", lambda eng, hc=hc, hr=hr, qb=qb: eng.copy(
                               s_ao[hr:hr + 64, hc, qb * 128:(qb + 1) * 128], ps[6][hr:hr + 64, qb * 128:(qb + 1) * 128]),
                           reads=("pp6",), writes=("ao",))
            for oc in range(8):
                bi = oc % 2
                p.dma("y", lambda eng, oc=oc, bi=bi: eng.dma_start(
                          out=s_wo[:, bi], in_=wout1.rearrange("(kc p) n -> p kc n", p=128)[:, :, oc * 128:(oc + 1) * 128]),
                      writes=(f"wo1{bi}",))
                pairs = [(s_wo[:, bi, kc, :], s_ao[:, kc, :]) for kc in range(8)]
                mm_accum(p, ps[bi][:], pairs, f"pp{bi}", ("ao", f"wo1{bi}"))
                p.emit("s", lambda eng, oc=oc, bi=bi: eng.activation(
                           s_o[:, oc, :], ps[bi][:], ACT.Identity, bias=s_bo[:, oc:oc + 1], scale=1.0),
                       reads=(f"pp{bi}", "bo2"), writes=("o",))
            p.emit("v", lambda eng: eng.tensor_tensor(s_o[:], s_o[:], s_x[:, 0], op=ALU.add),
                   reads=("o", "xb0"), writes=("o",))
            emit_ln_fm(p, f"f7ln_{gg}", s_o, "o", s_s3, "s3", s_sq, "sq3", s_ones,
                       ps[4], "pp4", ps[5], "pp5", s_mvb, "f7mv", 512)
            p.dma("y", lambda eng, gg=gg: eng.dma_start(
                      out=s3fv[:, :, gg * 512:(gg + 1) * 512], in_=s_s3[:]),
                  reads=("s3",))
            for tb in range(4):
                for half in range(2):
                    bk = ps[2] if half == 0 else ps[3]
                    bkk = "pp2" if half == 0 else "pp3"
                    for k4 in range(4):
                        kc = half * 4 + k4
                        p.emit("t", lambda eng, tb=tb, kc=kc, k4=k4, bk=bk: eng.transpose(
                                   bk[:, k4 * 128:(k4 + 1) * 128], s_s3[:, kc, tb * 128:(tb + 1) * 128], s_ident[:]),
                               reads=("s3", "ident"), writes=(bkk,))
                    p.emit("s", lambda eng, half=half, bk=bk: eng.copy(
                               s_3t[:, half * 4:(half + 1) * 4, :].rearrange("p a b -> p (a b)"), bk[:, 0:512]),
                           reads=(bkk,), writes=("t3",))
                p.dma("y", lambda eng, gg=gg, tb=tb: eng.dma_start(
                          out=s3t_out[(gg * 4 + tb) * 128:(gg * 4 + tb + 1) * 128, :], in_=s_3t[:].rearrange("p a b -> p (a b)")),
                      reads=("t3",))
    p.barrier()

    with ExitStack() as ph:
        TTQ = 512
        f8_w = ph.enter_context(nc.sbuf_tensor("f8_w", [128, 8, 16, 128], F32))
        f8_x = ph.enter_context(nc.sbuf_tensor("f8_x", [128, 2, 8, TTQ], F32))
        f8_q = ph.enter_context(nc.sbuf_tensor("f8_q", [128, 16, TTQ], F32))
        f8_sq = ph.enter_context(nc.sbuf_tensor("f8_sq", [128, 16, TTQ], F32))
        f8_st = ph.enter_context(nc.sbuf_tensor("f8_st", [128, 16, 2, 4], F32))
        f8_sto = ph.enter_context(nc.sbuf_tensor("f8_sto", [128, 32], F32))
        p.dma("y", lambda eng: eng.dma_start(out=f8_w[:], in_=wq.rearrange("(kc p) (oc f) -> p kc oc f", p=128, f=128)),
              writes=("w8",))
        xv = s3f_hbm.rearrange("(kc p) t -> p kc t", p=128)
        qv = q_out.rearrange("(oc p) t -> p oc t", p=128)
        for tt in range(TOK // TTQ):
            pb = tt % 2
            p.dma("y", lambda eng, tt=tt, pb=pb: eng.dma_start(out=f8_x[:, pb], in_=xv[:, :, tt * TTQ:(tt + 1) * TTQ]),
                  writes=(f"x8{pb}",))
            for oc in range(16):
                bi = oc % 4
                pairs = [(f8_w[:, kc, oc, :], f8_x[:, pb, kc, :]) for kc in range(8)]
                mm_accum(p, ps[bi][:], pairs, f"pp{bi}", (f"x8{pb}", "w8"))
                p.emit("s", lambda eng, oc=oc, bi=bi: eng.copy(f8_q[:, oc, :], ps[bi][:]),
                       reads=(f"pp{bi}",), writes=("q8",))
            p.dma("y", lambda eng, tt=tt: eng.dma_start(out=qv[:, :, tt * TTQ:(tt + 1) * TTQ], in_=f8_q[:]),
                  reads=("q8",))
            p.emit("s", lambda eng: eng.activation(f8_sq[:], f8_q[:], ACT.Square),
                   reads=("q8",), writes=("sq8",))
            for oc in range(16):
                p.emit("v", lambda eng, oc=oc, tt=tt: eng.tensor_reduce(
                           f8_st[:, oc, 0, tt:tt + 1], f8_q[:, oc, :], axis=AX, op=ALU.add),
                       reads=("q8",), writes=("st8",))
                p.emit("v", lambda eng, oc=oc, tt=tt: eng.tensor_reduce(
                           f8_st[:, oc, 1, tt:tt + 1], f8_sq[:, oc, :], axis=AX, op=ALU.add),
                       reads=("sq8",), writes=("st8",))
        p.emit("v", lambda eng: eng.tensor_reduce(
                   f8_sto[:].rearrange("p (a b) -> p a b", b=1), f8_st[:].rearrange("p a b c -> p (a b) c"), axis=AX, op=ALU.add),
               reads=("st8",), writes=("sto8",))
        p.dma("y", lambda eng: eng.dma_start(out=st_out, in_=f8_sto[:]), reads=("sto8",))

    xch.epilogue(["sto8", "t3", "q8"])
    p.finish_outputs("y")
    with stack:
        return p.build()


def host_fusedA_inputs(ins):
    Win = ins["L0_attn_Win"]; bin_ = ins["L0_attn_bin"]
    Wx = ins["L0_attn_Wx"]; bx = ins["L0_attn_bx"]
    Wdt = ins["L0_attn_Wdt"]; bdt = ins["L0_attn_bdt"]
    Alog = ins["L0_attn_A_log"]; Dp = ins["L0_attn_Dp"]
    Wout = ins["L0_attn_Wout"]; bout = ins["L0_attn_bout"]
    x = ins["L0_attn_x"]
    wg = ins["L0_ffn_w_gate"]; wu = ins["L0_ffn_w_up"]; wd = ins["L0_ffn_w_down"]
    Wqkv = ins["L1_attn_Wqkv"]; bqkv = ins["L1_attn_bqkv"]
    Wout1 = ins["L1_attn_Wout"]; bout1 = ins["L1_attn_bout"]
    wq = ins["L1_ffn_w_q"]
    i = np.arange(128)[:, None]; j = np.arange(384)[None, :]
    base = np.where((j >= i) & (j <= i + 256), 0.0, -1e30).astype(np.float32)
    m_first = np.where(j >= 128, base, -1e30).astype(np.float32)
    m_last = np.where(j < 256, base, -1e30).astype(np.float32)
    maps = []
    for c in range(NCORES):
        b, h = c // 2, c % 2
        tsl = slice(h * TOK, (h + 1) * TOK)
        maps.append({
            "x_fm": np.ascontiguousarray(x[b].T),
            "x_res": np.ascontiguousarray(x[b].T[:, tsl]),
            "winx": np.ascontiguousarray(Win[:, h * DH:(h + 1) * DH]),
            "winz": np.ascontiguousarray(Win[:, DI + h * DH:DI + (h + 1) * DH]),
            "binx": np.ascontiguousarray(bin_[h * DH:(h + 1) * DH].reshape(-1, 128).T),
            "binz": np.ascontiguousarray(bin_[DI + h * DH:DI + (h + 1) * DH].reshape(-1, 128).T),
            "wx": np.ascontiguousarray(Wx[h * DH:(h + 1) * DH]),
            "bx_fm": np.ascontiguousarray(bx[:, None]),
            "bx_tm": np.ascontiguousarray(np.broadcast_to(bx, (128, 96))),
            "wdt": np.ascontiguousarray(Wdt[:, h * DH:(h + 1) * DH]),
            "bdt": np.ascontiguousarray(bdt[h * DH:(h + 1) * DH].reshape(8, 128).T),
            "alog": np.ascontiguousarray(Alog[h * DH:(h + 1) * DH].reshape(8, 128, NST).transpose(1, 0, 2)),
            "dp_r": np.ascontiguousarray(Dp[h * DH:(h + 1) * DH].reshape(8, 128).T),
            "wout": np.ascontiguousarray(Wout[h * DH:(h + 1) * DH]),
            "boutr": np.ascontiguousarray(bout.reshape(8, 128).T),
            "wgT": np.ascontiguousarray(wg.T),
            "wuT": np.ascontiguousarray(wu.T),
            "wdT": np.ascontiguousarray(wd.T),
            "wqkv": Wqkv,
            "bq_r": np.ascontiguousarray(bqkv[0:DM].reshape(8, 128).T),
            "bk_r": np.ascontiguousarray(bqkv[DM:2 * DM].reshape(8, 128).T),
            "bv_bc": np.ascontiguousarray(np.broadcast_to(bqkv[2 * DM:3 * DM], (128, DM))),
            "wout1": Wout1,
            "bo_r": np.ascontiguousarray(bout1.reshape(8, 128).T),
            "mask0": m_first if h == 0 else base,
            "maskm": base,
            "mask15": m_last if h == 1 else base,
            "zeros_col": np.zeros((DM, 128), np.float32),
            "wq": wq,
        })
    return maps


def kernel_fused(**ins):
    core_ids = list(range(NCORES))
    rA = run_bass_kernel_spmd(build_fusedA(), host_fusedA_inputs(ins), core_ids=core_ids)
    q_by_core = {c: np.asarray(rA.results[c]["q_out"]) for c in range(NCORES)}
    stats_by_core = {c: np.asarray(rA.results[c]["st_out"]) for c in range(NCORES)}
    s3t_by_core = {c: np.asarray(rA.results[c]["s3t_out"]) for c in range(NCORES)}
    rB = run_bass_kernel_spmd(build_l6(), host_l6_inputs(ins, q_by_core, stats_by_core, s3t_by_core),
                              core_ids=core_ids)
    out = np.empty((B, S, DM), np.float32)
    for c in range(NCORES):
        b, h = c // 2, c % 2
        out[b, h * TOK:(h + 1) * TOK, :] = np.asarray(rB.results[c]["fin_out"])
    return out


kernel = kernel_fused
